# version 09
# baseline (speedup 1.0000x reference)
import numpy as np

import concourse.bass as bass
import concourse.mybir as mybir
from concourse.bass_utils import run_bass_kernel_spmd

F32 = mybir.dt.float32
BF16 = mybir.dt.bfloat16
F16 = mybir.dt.float16
AF = mybir.ActivationFunctionType
AL = mybir.AluOpType
AX = mybir.AxisListType

LN2 = float(np.log(2.0))
INVLN2 = float(1.0 / np.log(2.0))
GAMMA = 0.9
LNG = float(np.log(GAMMA))
BIG = 1.0e30

D = 1024
NH = 6
DK = 128
DV = 256
KD = NH * DK
VD = NH * DV
HFF = 4096
C = 128
DEC_SKIP = 1100
DEC_OFF = 1664
DEC_BAND = 3456

SCALAR_CONSTS = [0.0, 1.0, -1.0, 0.5, 1e-5, 1e-12, 1.0 / 1024.0, 1.0 / 256.0,
                 128.0, 128.0e-12, float(np.log(2.0)), float(-np.log(2.0)),
                 float(np.log(2.0) * 67), float(-np.log(2.0) * 67),
                 float(np.log(2.0) * 74), float(-np.log(2.0) * 74)]


def _bf16np():
    import ml_dtypes
    return np.dtype(ml_dtypes.bfloat16)


class Sched:
    ENGS = ("pe", "act", "dve", "pool", "sp")
    DMAQ = ("sp", "pool", "act")
    NSEM = {"sp": 24, "pool": 16, "act": 4}

    def __init__(self):
        self.items = {e: [] for e in self.ENGS}
        self.n = {e: 0 for e in self.ENGS}
        self.ncomp = {e: 0 for e in self.ENGS}
        self.ndma = {e: 0 for e in self.DMAQ}
        self.semuse = {}
        self.last_w = {}
        self.readers = {}

    PSUM_BANKS = frozenset(["gp0", "gp1", "gp2", "gp3", "gp4",
                            "tbank0", "tbank1", "tfbank"])

    def task(self, eng, fn, reads=(), writes=(), dma=False, extra=()):
        pb = [r for r in reads if r in self.PSUM_BANKS]
        if pb:
            reads = [r for r in reads if r not in self.PSUM_BANKS]
            writes = list(writes) + pb
        deps = []
        for r in reads:
            c = self.last_w.get(r)
            if c is not None:
                deps.append(c)
        for w in writes:
            c = self.last_w.get(w)
            if c is not None:
                deps.append(c)
            deps.extend(self.readers.get(w, ()))
        deps.extend(extra)
        self.n[eng] += 1
        if dma:
            si = self.ndma[eng] % self.NSEM[eng]
            self.ndma[eng] += 1
            skey = ("dma", eng, si)
            cnt = self.semuse.get(skey, 0) + 1
            self.semuse[skey] = cnt
            if cnt > 1:
                deps.append((skey, 16 * (cnt - 1)))
            comp = (skey, 16 * cnt)
            dmakey = skey
        else:
            self.ncomp[eng] += 1
            comp = (("eng", eng), self.ncomp[eng])
            dmakey = None
        self.items[eng].append((deps, fn, dmakey))
        for r in reads:
            self.readers.setdefault(r, []).append(comp)
        for w in writes:
            self.last_w[w] = comp
            self.readers[w] = []
        return comp

    def emit(self, nc, block, sems):
        def make_body(ename, items):
            def body(eng):
                satisfied = {}
                own = ("eng", ename)
                if ename != "sp":
                    eng.wait_ge(sems[("dma", "sp", 0)], 16)
                    satisfied[("dma", "sp", 0)] = 16
                for deps, fn, dmakey in items:
                    for key, val in deps:
                        if satisfied.get(key, 0) >= val:
                            continue
                        eng.wait_ge(sems[key], val)
                        satisfied[key] = val
                    inst = fn(eng)
                    if dmakey is not None:
                        inst.then_inc(sems[dmakey], 16)
                    else:
                        inst.then_inc(sems[("eng", ename)], 1)
                if ename in self.DMAQ:
                    for si in range(self.NSEM[ename]):
                        skey = ("dma", ename, si)
                        cnt = self.semuse.get(skey, 0)
                        if cnt > 0:
                            eng.wait_ge(sems[skey], 16 * cnt)
            return body

        block.tensor(make_body("pe", self.items["pe"]))
        block.scalar(make_body("act", self.items["act"]))
        block.vector(make_body("dve", self.items["dve"]))
        block.gpsimd(make_body("pool", self.items["pool"]))
        block.sync(make_body("sp", self.items["sp"]))


def host_constants(L):
    bf = _bf16np()
    c = {}
    eye = np.eye(128, dtype=np.float64)
    c["c_id_bf16"] = eye.astype(bf)
    c["c_id_f32"] = eye.astype(np.float32)
    c["c_ones_f32"] = np.ones((1, 128), dtype=np.float32)
    c["c_onesmat"] = np.ones((128, 128), dtype=bf)
    c["c_zero_row_f32"] = np.zeros((6, 128), dtype=np.float32)
    r = np.arange(128)[:, None]
    t = np.arange(128)[None, :]
    c["c_maskPT"] = np.where(r <= t, 0.0, BIG).astype(bf)
    c["c_maskTT"] = np.where(r < t, 0.0, BIG).astype(bf)
    c["c_maskTS"] = np.where(t < r, 0.0, BIG).astype(bf)
    R = np.zeros((128, 128), dtype=np.float64)
    for i in range(64):
        R[2 * i, 2 * i + 1] = -1.0
        R[2 * i + 1, 2 * i] = 1.0
    c["c_rotT_bf16"] = np.ascontiguousarray(R.T).astype(bf)
    p = np.arange(128)[:, None]
    j = np.arange(DEC_BAND)[None, :]
    c["c_dlog"] = (LNG * np.abs(p - (j - DEC_OFF)).astype(np.float64)).astype(np.float32)
    half = D // 2
    sv = (np.arange(0, D, 2, dtype=np.float64) + 0.4 * D) / (1.4 * D)
    power = np.arange(L, dtype=np.float64) / 512.0
    scale = sv[None, :] ** power[:, None]
    inv_freq = 1.0 / 10000.0 ** (np.arange(half, dtype=np.float64) / half)
    sinus = np.arange(L, dtype=np.float64)[:, None] * inv_freq[None, :]
    sin = np.repeat(np.sin(sinus), 2, axis=1)
    cos = np.repeat(np.cos(sinus), 2, axis=1)
    scl = np.repeat(scale, 2, axis=1)
    c["c_cos_q"] = np.ascontiguousarray((cos * scl).T).astype(np.float32)
    c["c_sin_q"] = np.ascontiguousarray((sin * scl).T).astype(np.float32)
    c["c_cos_k"] = np.ascontiguousarray((cos / scl).T).astype(np.float32)
    c["c_sin_k"] = np.ascontiguousarray((sin / scl).T).astype(np.float32)
    c["c_scal"] = np.broadcast_to(
        np.asarray(SCALAR_CONSTS, dtype=np.float32), (128, len(SCALAR_CONSTS))
    ).copy()
    return c


class _StopBuild(Exception):
    pass


def build_core(L, phase_limit=99):
    nc = bass.Bass("TRN2", target_bir_lowering=False)
    NT = L // 128
    NCH = L // C
    NKC = L // 512
    NB = L // 512

    def din(name, shape, dtype=F32):
        return nc.dram_tensor(name, list(shape), dtype, kind="ExternalInput").ap()

    x_in = din("x", (L, D))
    Wq = din("Wq", (D, KD)); Wk = din("Wk", (D, KD)); Wv = din("Wv", (D, VD))
    Wba = din("Wba", (D, 16))
    A_log = din("A_log", (NH, 1))
    dt_bias = din("dt_bias", (NH, 1))
    convq = din("convq", (128, 24))
    convk = din("convk", (128, 24))
    convv = din("convv", (128, 48))
    Wg = din("Wg", (D, VD))
    nw_b = din("nw_b", (128, VD))
    Wo = din("Wo", (VD, D))
    w1 = din("w1", (HFF, D))
    b1c = din("b1c", (128, 32))
    w2 = din("w2", (D, HFF))
    b2r = din("b2r", (1, D))
    WQ1 = din("WQ1", (D, D)); WK1 = din("WK1", (D, D)); WV1 = din("WV1", (D, D))
    wf = din("wf", (D, D))
    bf_b = din("bf_b", (128, D))

    cid_bf = din("c_id_bf16", (128, 128), BF16)
    cid_f32 = din("c_id_f32", (128, 128), F32)
    onesr = din("c_ones_f32", (1, 128), F32)
    onesmat = din("c_onesmat", (128, 128), BF16)
    zrow6 = din("c_zero_row_f32", (6, 128), F32)
    maskPT = din("c_maskPT", (128, 128), BF16)
    maskTT = din("c_maskTT", (128, 128), BF16)
    maskTS = din("c_maskTS", (128, 128), BF16)
    rotT = din("c_rotT_bf16", (128, 128), BF16)
    dlog = din("c_dlog", (128, DEC_BAND), F32)
    ctq = din("c_cos_q", (D, L)); stq = din("c_sin_q", (D, L))
    ctk = din("c_cos_k", (D, L)); stk = din("c_sin_k", (D, L))

    out = nc.dram_tensor("out", [L, D], F32, kind="ExternalOutput").ap()

    hxn0 = nc.dram_tensor("hxn0", [8, 128, L], BF16).ap()
    hq = nc.dram_tensor("hq", [NH, NCH, 128, 128], BF16).ap()
    hk = nc.dram_tensor("hk", [NH, NCH, 128, 128], BF16).ap()
    hv = nc.dram_tensor("hv", [2 * NH, NCH, 128, 128], BF16).ap()
    hgsn = nc.dram_tensor("hgsn", [NCH, 128, VD], BF16).ap()
    hy = nc.dram_tensor("hy", [NT, 128, D], BF16).ap()
    hx1 = nc.dram_tensor("hx1", [NT, 128, D], F32).ap()
    hxn1 = nc.dram_tensor("hxn1", [8, 128, L], BF16).ap()
    hxq = nc.dram_tensor("hxq", [8, 128, L], BF16).ap()
    hq1 = nc.dram_tensor("hq1", [NT, 8, 128, 128], BF16).ap()
    hf1 = nc.dram_tensor("hf1", [NT, 128, D], F16).ap()

    sch = Sched()
    ctx = []

    def sbuf(name, shape, dtype):
        g = nc.sbuf_tensor(name, list(shape), dtype)
        h = g.__enter__()
        ctx.append(g)
        return h

    def psumt(name, shape, dtype=F32):
        g = nc.psum_tensor(name, list(shape), dtype)
        h = g.__enter__()
        ctx.append(g)
        return h

    gp = [psumt(f"gp{i}", (128, 512)) for i in range(5)]
    tbank = [psumt("tbA", (128, 1024), BF16), psumt("tbB", (128, 1024), BF16)]
    tf = psumt("tf", (128, 512))

    AB = sbuf("AB", (128, 32768), BF16)
    AB2 = sbuf("AB2", (128, 32768), BF16)
    AFX = sbuf("AFX", (128, 5120), F32)
    xtile = AFX[:, 0:1024]
    xntm = AFX[:, 1024:2048]
    qA = AFX[:, 2048:3072]
    qB = AFX[:, 3072:4096]
    qCs = AFX[:, 4096:5120]
    srow = AFX[:, 0:4096]
    sdec = AFX[:, 4096:4608]
    SROW_W = ["srow", "xtile", "xntm", "rpA", "rpB"]
    SDEC_W = ["sdec", "rpC"]

    q32 = sbuf("q32", (128, 1024), F32)
    qD = sbuf("qD", (128, 1024), F32)
    sbf = sbuf("sbf", (128, 1024), BF16)
    f16b = sbuf("f16b", (128, 1024), F16)
    lncols = sbuf("lncols", (128, 4), F32)
    q_bf = sbuf("q_bf", (128, 1024), BF16)
    qI = sbuf("qI", (128, 512), mybir.dt.int32)

    xnfm_bf = sbf
    rows = (qA, "rpA")
    pre_sb = qD[:, 0:516]
    cacc = q32[:, 0:512]
    dlt = q32[:, 0:512]
    trigc = qCs[:, 0:512]
    trigs = qCs[:, 512:1024]
    a2row = qA[0:1, 0:1024]
    bzrow = qB[0:1, 0:1024]
    b2row = qCs[0:1, 0:1024]
    bcrow_t = qCs[0:1, 0:512]

    sb_id_bf = sbuf("id_bf", (128, 128), BF16)
    sb_id_f32 = sbuf("id_f32", (128, 128), F32)
    sb_onesr = sbuf("onesr", (1, 128), F32)
    sb_onesmat = sbuf("onesmat", (128, 128), BF16)
    sb_maskPT = sbuf("maskPT", (128, 128), BF16)
    sb_maskTT = sbuf("maskTT", (128, 128), BF16)
    sb_maskTS = sbuf("maskTS", (128, 128), BF16)
    sb_rotT = sbuf("rotT", (128, 128), BF16)
    sb_zrow = sbuf("zrow", (6, 128), F32)

    def dmat(queue, dst, src, reads=(), writes=(), extra=(), slow=False):
        if dst.dtype != src.dtype:
            queue = "pool"
        return sch.task(queue, lambda e, d=dst, s=src, sl=slow: e.dma_start(
            out=d, in_=s, allow_slow_non_contiguous=sl),
            reads=reads, writes=writes, dma=True, extra=extra)

    cscal_in = din("c_scal", (128, len(SCALAR_CONSTS)), F32)
    sb_cscal = sbuf("cscal", (128, len(SCALAR_CONSTS)), F32)
    dmat("sp", sb_cscal[:], cscal_in[:], writes=["cscal"])
    for i, v in enumerate(SCALAR_CONSTS):
        nc.const_aps.register(F32, float(v), sb_cscal[:, i:i + 1])

    for dst, src, key in ((sb_id_bf, cid_bf, "idbf"), (sb_id_f32, cid_f32, "idf"),
                          (sb_onesr, onesr, "onesr"), (sb_onesmat, onesmat, "onesm"),
                          (sb_maskPT, maskPT, "mPT"), (sb_maskTT, maskTT, "mTT"),
                          (sb_maskTS, maskTS, "mTS"), (sb_rotT, rotT, "rotT"),
                          (sb_zrow, zrow6, "zrow")):
        dmat("sp", dst[:], src[:], writes=[key])

    tbrr = [0]

    def tb_slot():
        i = tbrr[0] % 8
        tbrr[0] += 1
        b = i % 2
        j = i // 2
        return tbank[b][:, j * 128:(j + 1) * 128], f"tb{i}", f"tbank{b}"

    tfrr = [0]

    def tf_slot():
        i = tfrr[0] % 4
        tfrr[0] += 1
        return tf[:, i * 128:(i + 1) * 128], f"tf{'ABCD'[i]}", "tfbank"

    gprr = [0]

    def gp_slot():
        i = gprr[0] % 5
        gprr[0] += 1
        return gp[i], f"gp{i}"

    def round_pipe(src, dst, kin, kout, mant_bits, min_p, clip=None,
                   tie_even=False):
        A = qA[:, 0:1024]; B = qB[:, 0:1024]; Cc = qCs[:, 0:1024]; Dd = qD[:]
        KA, KB, KC, KD_ = "rpA", "rpB", "rpC", "qD"
        lo = float(min_p + 64)
        bias_si = float(LN2 * (mant_bits + 64))
        bias_s = float(-LN2 * (mant_bits + 64))
        sch.task("act", lambda e: e.activation(out=A, in_=src, func=AF.Abs),
                 reads=[kin], writes=[KA])
        if clip is not None:
            sch.task("dve", lambda e: e.tensor_scalar(
                out=A, in0=A, scalar1=clip, scalar2=None, op0=AL.min),
                reads=[KA], writes=[KA])
        sch.task("dve", lambda e: e.tensor_scalar(
            out=B, in0=A, scalar1=2.0 ** -40, scalar2=None, op0=AL.max),
            reads=[KA], writes=[KB])
        sch.task("act", lambda e: e.activation(out=Cc, in_=B, func=AF.Ln),
                 reads=[KB], writes=[KC])
        sch.task("dve", lambda e: e.tensor_scalar(
            out=Cc, in0=Cc, scalar1=INVLN2, scalar2=64.0, op0=AL.mult, op1=AL.add),
            reads=[KC], writes=[KC])
        sch.task("dve", lambda e: e.tensor_scalar(
            out=Cc, in0=Cc, scalar1=lo, scalar2=-0.499, op0=AL.max, op1=AL.add),
            reads=[KC], writes=[KC])
        for hh_ in range(2):
            sch.task("dve", lambda e, hh_=hh_: e.tensor_copy(
                out=qI[:], in_=Cc[:, hh_ * 512:(hh_ + 1) * 512]),
                reads=[KC], writes=["qI"])
            sch.task("dve", lambda e, hh_=hh_: e.tensor_copy(
                out=Cc[:, hh_ * 512:(hh_ + 1) * 512], in_=qI[:]),
                reads=["qI"], writes=[KC])
        sch.task("act", lambda e: e.activation(out=B, in_=Cc, func=AF.Exp,
                                               scale=-LN2, bias=bias_si),
                 reads=[KC], writes=[KB])
        sch.task("dve", lambda e: e.tensor_tensor(out=Dd, in0=A, in1=B, op=AL.mult),
                 reads=[KA, KB], writes=[KD_])
        for hh_ in range(2):
            sch.task("dve", lambda e, hh_=hh_: e.tensor_copy(
                out=qI[:], in_=Dd[:, hh_ * 512:(hh_ + 1) * 512]),
                reads=[KD_], writes=["qI"])
            sch.task("dve", lambda e, hh_=hh_: e.tensor_copy(
                out=Dd[:, hh_ * 512:(hh_ + 1) * 512], in_=qI[:]),
                reads=["qI"], writes=[KD_])
        sch.task("act", lambda e: e.activation(out=B, in_=Cc, func=AF.Exp,
                                               scale=LN2, bias=bias_s),
                 reads=[KC], writes=[KB])
        sch.task("dve", lambda e: e.tensor_tensor(out=Dd, in0=Dd, in1=B, op=AL.mult),
                 reads=[KD_, KB], writes=[KD_])
        sch.task("act", lambda e: e.activation(out=A, in_=src, func=AF.Sign),
                 reads=[kin], writes=[KA])
        sch.task("dve", lambda e: e.tensor_tensor(out=dst, in0=Dd, in1=A,
                                                  op=AL.mult),
                 reads=[KD_, KA], writes=[kout])

    def ln_tile():
        sch.task("dve", lambda e: e.reduce_sum(lncols[:, 0:1], xtile, axis=AX.X),
                 reads=["xtile"], writes=["lnc"])
        sch.task("dve", lambda e: e.tensor_scalar(
            out=lncols[:, 1:2], in0=lncols[:, 0:1], scalar1=1.0 / D, scalar2=None,
            op0=AL.mult), reads=["lnc"], writes=["lnc"])
        sch.task("dve", lambda e: e.tensor_scalar(
            out=xntm, in0=xtile, scalar1=lncols[:, 1:2], scalar2=None,
            op0=AL.subtract), reads=["xtile", "lnc"], writes=["xntm"])
        sch.task("dve", lambda e: e.scalar_tensor_tensor(
            out=qD[:], in0=xntm, scalar=1.0, in1=xntm, op0=AL.bypass,
            op1=AL.mult, accum_out=lncols[:, 2:3]),
            reads=["xntm"], writes=["qD", "lnc"])
        sch.task("act", lambda e: e.activation(
            out=lncols[:, 3:4], in_=lncols[:, 2:3], func=AF.Sqrt,
            bias=1e-5, scale=1.0 / D), reads=["lnc"], writes=["lnc3"])
        sch.task("dve", lambda e: e.reciprocal(out=lncols[:, 2:3],
                                               in_=lncols[:, 3:4]),
                 reads=["lnc3"], writes=["lnc"])
        sch.task("dve", lambda e: e.tensor_scalar(
            out=xntm, in0=xntm, scalar1=lncols[:, 2:3], scalar2=None, op0=AL.mult),
            reads=["lnc"], writes=["xntm"])

    try:
        for tt in range(NT):
            dmat("sp", xtile, x_in[tt * 128:(tt + 1) * 128, :], writes=["xtile"])
            ln_tile()
            for dc in range(8):
                sch.task("pe", lambda e, dc=dc: e.transpose(
                    tf[:, 0:128], xntm[:, dc * 128:(dc + 1) * 128], sb_id_f32[:]),
                    reads=["xntm", "idf"], writes=["tfA", "tfbank"])
                sch.task("dve", lambda e, dc=dc: e.tensor_copy(
                    out=xnfm_bf[:, dc * 128:(dc + 1) * 128], in_=tf[:, 0:128]),
                    reads=["tfA"], writes=["sbf"])
            for dc in range(8):
                dmat("sp", hxn0[dc, :, tt * 128:(tt + 1) * 128],
                     xnfm_bf[:, dc * 128:(dc + 1) * 128],
                     reads=["sbf"], writes=["hxn0"])

        def maybe_stop(n):
            if phase_limit <= n:
                raise _StopBuild()

        maybe_stop(1)
        NCOL = NCH * NH
        assert NCOL <= 512
        NG = 8
        dcols = sbuf("dcols", (128, NG * NCOL), F32)
        G_B, G_BLNB, G_BETA, G_GAM, G_NEGW, G_C2, G_GC, G_CQ = range(NG)

        def dcol(g, c, h):
            return dcols[:, g * NCOL + c * NH + h:g * NCOL + c * NH + h + 1]

        def dgrp(g):
            return dcols[:, g * NCOL:(g + 1) * NCOL]

        bR = qA[0:6, 0:512]
        betaR = qB[0:6, 0:512]
        lnbR = qB[0:6, 512:1024]
        gR = qCs[0:6, 0:512]
        wba_sb = sbuf("wba_sb", (128, 128), BF16)
        dcol_s = sbuf("dcol_s", (NH, 2), F32)
        negA = sbuf("negA", (NH, 1), F32)
        xs_mov = sbf[:, 0:1024]

        for dc in range(8):
            dmat("sp", wba_sb[:, dc * 16:dc * 16 + 16], Wba[dc * 128:(dc + 1) * 128, :],
                 writes=["wba"])
        maybe_stop(1.02)
        dmat("sp", dcol_s[:, 0:1], A_log[:, :], writes=["dc0"])
        dmat("sp", dcol_s[:, 1:2], dt_bias[:, :], writes=["dc1"])
        maybe_stop(1.03)
        sch.task("act", lambda e: e.activation(out=negA[:], in_=dcol_s[:, 0:1],
                                               func=AF.Exp),
                 reads=["dc0"], writes=["negA"])
        maybe_stop(1.04)
        sch.task("dve", lambda e: e.tensor_scalar(
            out=negA[:], in0=negA[:], scalar1=-1.0, scalar2=None, op0=AL.mult),
            reads=["negA"], writes=["negA"])
        maybe_stop(1.05)

        for tb8 in range(NB):
            if tb8 > 0:
                maybe_stop(1.15)
            t0 = tb8 * 512
            for dc in range(8):
                dmat("sp", xs_mov[:, 0:512], hxn0[dc, :, t0:t0 + 512],
                     reads=["hxn0"], writes=["sbf"])
                sch.task("pe", lambda e, dc=dc: e.matmul(
                    gp[0][0:6, 0:512], wba_sb[:, dc * 16:dc * 16 + 6],
                    xs_mov[:, 0:512], start=(dc == 0), stop=(dc == 7)),
                    reads=["wba", "sbf"], writes=["gp0"])
                sch.task("pe", lambda e, dc=dc: e.matmul(
                    gp[1][0:6, 0:512], wba_sb[:, dc * 16 + 6:dc * 16 + 12],
                    xs_mov[:, 0:512], start=(dc == 0), stop=(dc == 7)),
                    reads=["wba", "sbf"], writes=["gp1"])
            maybe_stop(1.08)
            sch.task("act", lambda e: e.activation(
                out=betaR, in_=gp[0][0:6, 0:512], func=AF.Sigmoid),
                reads=["gp0"], writes=["rpB"])
            maybe_stop(1.09)
            sch.task("act", lambda e: e.activation(
                out=gR, in_=gp[1][0:6, 0:512], func=AF.Exp, bias=dcol_s[:, 1:2]),
                reads=["gp1", "dc1"], writes=["rpC"])
            maybe_stop(1.11)
            sch.task("act", lambda e: e.activation(
                out=gR, in_=gR, func=AF.Ln, bias=1.0),
                reads=["rpC"], writes=["rpC"])
            maybe_stop(1.12)
            sch.task("dve", lambda e: e.tensor_scalar(
                out=gR, in0=gR, scalar1=negA[:], scalar2=None, op0=AL.mult),
                reads=["rpC", "negA"], writes=["rpC"])
            maybe_stop(1.2)
            for ccl in range(4):
                u0 = ccl * C
                sch.task("dve", lambda e, u0=u0: e.tensor_tensor_scan(
                    out=qA[0:6, u0:u0 + C], data0=qCs[0:6, u0:u0 + C],
                    data1=sb_zrow[:, :], initial=0.0, op0=AL.add, op1=AL.add),
                    reads=["rpC", "zrow"], writes=["rpA"])
            maybe_stop(1.3)
            sch.task("act", lambda e: e.activation(
                out=lnbR, in_=betaR, func=AF.Ln),
                reads=["rpB"], writes=["rpB"])
            sch.task("dve", lambda e: e.tensor_tensor(
                out=gR, in0=lnbR, in1=bR, op=AL.add),
                reads=["rpB", "rpA", "rpC"], writes=["rpC"])
            for ccl in range(4):
                cc = tb8 * 4 + ccl
                u0 = ccl * C
                for src_ap, kread, goff, kw in (
                        (qA[0:6, u0:u0 + C], "rpA", G_B, "dcB"),
                        (qB[0:6, u0:u0 + C], "rpB", G_BETA, "dcBE"),
                        (qCs[0:6, u0:u0 + C], "rpC", G_BLNB, "dcBL")):
                    sch.task("pe", lambda e, src_ap=src_ap: e.transpose(
                        tf[:, 0:6], src_ap, sb_id_f32[0:6, 0:6]),
                        reads=[kread, "idf"],
                        writes=["tfA", "tfB", "tfC", "tfD", "tfbank"])
                    sch.task("dve", lambda e, cc=cc, goff=goff: e.tensor_copy(
                        out=dcols[:, goff * NCOL + cc * NH:goff * NCOL + cc * NH + 6],
                        in_=tf[:, 0:6]), reads=["tfA", "tfbank"], writes=[kw])

        maybe_stop(1.5)
        sch.task("act", lambda e: e.activation(out=dgrp(G_GAM), in_=dgrp(G_B),
                                               func=AF.Exp),
                 reads=["dcB"], writes=["dcG"])
        sch.task("dve", lambda e: e.scalar_tensor_tensor(
            out=dgrp(G_NEGW), in0=dgrp(G_BETA), scalar=-1.0, in1=dgrp(G_GAM),
            op0=AL.mult, op1=AL.mult), reads=["dcBE", "dcG"], writes=["dcNW"])
        maybe_stop(1.7)
        dmat("pool", bcrow_t[0:1, 0:NCOL],
             dcols[127:128, G_B * NCOL:(G_B + 1) * NCOL],
             reads=["dcB"], writes=["rpC"])
        maybe_stop(1.75)
        sch.task("pe", lambda e: e.matmul(
            tf[:, 0:NCOL], sb_onesr[:], bcrow_t[0:1, 0:NCOL], start=True, stop=True),
            reads=["rpC", "onesr"],
            writes=["tfA", "tfB", "tfC", "tfD", "tfbank"])
        maybe_stop(1.8)
        sch.task("act", lambda e: e.activation(out=dgrp(G_GC), in_=tf[:, 0:NCOL],
                                               func=AF.Exp),
                 reads=["tfA", "tfbank"], writes=["dcGC"])
        maybe_stop(1.85)
        sch.task("dve", lambda e: e.tensor_tensor(
            out=dgrp(G_C2), in0=tf[:, 0:NCOL], in1=dgrp(G_B), op=AL.subtract),
            reads=["tfA", "tfbank", "dcB"], writes=["dcC2"])
        maybe_stop(1.9)
        sch.task("act", lambda e: e.activation(out=dgrp(G_C2), in_=dgrp(G_C2),
                                               func=AF.Exp),
                 reads=["dcC2"], writes=["dcC2"])

        maybe_stop(2)
        wg_sb = AB2[:, 0:8 * VD]
        nw_sb = sbuf("nw_sb", (128, VD), BF16)
        xnb2 = sbuf("xstat", (128, 2048), BF16)
        xstat = xnb2[:, 0:1024]
        gtile = sbf[:, 0:512]
        for dc in range(8):
            dmat("sp", wg_sb[:, dc * VD:(dc + 1) * VD], Wg[dc * 128:(dc + 1) * 128, :],
                 writes=["wg"])
        dmat("sp", nw_sb[:], nw_b[:, :], writes=["nw"])
        for tt in range(NT):
            for dc in range(8):
                dmat("sp", xstat[:, dc * 128:(dc + 1) * 128],
                     hxn0[dc, :, tt * 128:(tt + 1) * 128],
                     reads=["hxn0"], writes=["xstat"])
            for half in range(3):
                for dc in range(8):
                    sch.task("pe", lambda e, dc=dc, half=half: e.matmul(
                        gp[half][:, 0:512], xstat[:, dc * 128:(dc + 1) * 128],
                        wg_sb[:, dc * VD + half * 512:dc * VD + (half + 1) * 512],
                        start=(dc == 0), stop=(dc == 7)),
                        reads=["xstat", "wg"], writes=[f"gp{half}"])
                sch.task("act", lambda e, half=half: e.activation(
                    out=gtile[:, 0:512], in_=gp[half][:, 0:512],
                    func=AF.Silu), reads=[f"gp{half}"], writes=["sbf"])
                sch.task("dve", lambda e, half=half: e.tensor_tensor(
                    out=gtile[:, 0:512], in0=gtile[:, 0:512],
                    in1=nw_sb[:, half * 512:(half + 1) * 512] if half < 2
                    else nw_sb[:, 1024:1536], op=AL.mult),
                    reads=["sbf", "nw"], writes=["sbf"])
                dmat("sp", hgsn[tt][:, half * 512:(half + 1) * 512], gtile[:, 0:512],
                     reads=["sbf"], writes=["hgsn"])

        maybe_stop(3)
        WSTR = KD + KD + VD
        wqkv = AB2[:, 0:8 * WSTR]
        for dc in range(8):
            b0 = dc * WSTR
            dmat("sp", wqkv[:, b0:b0 + KD], Wq[dc * 128:(dc + 1) * 128, :],
                 writes=["wqkv", "wg"])
            dmat("sp", wqkv[:, b0 + KD:b0 + 2 * KD], Wk[dc * 128:(dc + 1) * 128, :],
                 writes=["wqkv"])
            dmat("sp", wqkv[:, b0 + 2 * KD:b0 + WSTR], Wv[dc * 128:(dc + 1) * 128, :],
                 writes=["wqkv"])
        taps_sb = sbuf("taps_sb", (128, 96), F32)
        dmat("sp", taps_sb[:, 0:24], convq[:, :], writes=["taps"])
        dmat("sp", taps_sb[:, 24:48], convk[:, :], writes=["taps"])
        dmat("sp", taps_sb[:, 48:96], convv[:, :], writes=["taps"])

        xblk = AB[:, 0:4096]
        tails = sbuf("tails", (128, 72), F32)
        silu_bf = sbf[:, 0:512]

        sch.task("dve", lambda e: e.memset(tails[:], 0.0), writes=["tails"])

        cchunks = []
        for h in range(NH):
            cchunks.append((h * 128, h * 4, hq, h))
        for h in range(NH):
            cchunks.append((KD + h * 128, 24 + h * 4, hk, h))
        for j in range(2 * NH):
            cchunks.append((2 * KD + j * 128, 48 + j * 4, hv, j))

        for tb8 in range(NB):
            t0 = tb8 * 512
            for dc in range(8):
                dmat("sp", xblk[:, dc * 512:(dc + 1) * 512], hxn0[dc, :, t0:t0 + 512],
                     reads=["hxn0"], writes=["xblk"])
            for ci in range(24):
                coff, tap0, dram, slot = cchunks[ci]
                bank = gp[ci % 2]
                kb = f"gp{ci % 2}"
                for dc in range(8):
                    sch.task("pe", lambda e, dc=dc, coff=coff, bank=bank: e.matmul(
                        bank[:, 0:512], wqkv[:, dc * WSTR + coff:dc * WSTR + coff + 128],
                        xblk[:, dc * 512:(dc + 1) * 512],
                        start=(dc == 0), stop=(dc == 7)),
                        reads=["wqkv", "xblk"], writes=[kb])
                sch.task("dve", lambda e, bank=bank: e.tensor_copy(
                    out=pre_sb[:, 3:515], in_=bank[:, 0:512]),
                    reads=[kb], writes=["qD"])
                sch.task("dve", lambda e, ci=ci: e.tensor_copy(
                    out=pre_sb[:, 0:3], in_=tails[:, ci * 3:ci * 3 + 3]),
                    reads=["tails"], writes=["qD"])
                sch.task("dve", lambda e, ci=ci: e.tensor_copy(
                    out=tails[:, ci * 3:ci * 3 + 3], in_=pre_sb[:, 512:515]),
                    reads=["qD"], writes=["tails"])
                sch.task("dve", lambda e, tap0=tap0: e.tensor_scalar(
                    out=cacc[:, 0:512], in0=pre_sb[:, 0:512],
                    scalar1=taps_sb[:, tap0:tap0 + 1],
                    scalar2=None, op0=AL.mult),
                    reads=["qD", "taps"], writes=["q32"])
                for jj in (1, 2, 3):
                    sch.task("dve", lambda e, tap0=tap0, jj=jj: e.scalar_tensor_tensor(
                        out=cacc[:, 0:512], in0=pre_sb[:, jj:jj + 512],
                        scalar=taps_sb[:, tap0 + jj:tap0 + jj + 1], in1=cacc[:, 0:512],
                        op0=AL.mult, op1=AL.add),
                        reads=["qD", "taps", "q32"], writes=["q32"])
                sch.task("act", lambda e: e.activation(out=silu_bf[:, 0:512],
                                                       in_=cacc[:, 0:512],
                                                       func=AF.Silu),
                         reads=["q32"], writes=["sbf"])
                for j in range(4):
                    cix = t0 // 128 + j
                    dmat("sp", dram[slot, cix], silu_bf[:, j * 128:(j + 1) * 128],
                         reads=["sbf"], writes=["hqkv"])

        maybe_stop(4)
        wo_sb = AB2[:, 0:12 * 1024]
        for vc in range(12):
            dmat("sp", wo_sb[:, vc * 1024:(vc + 1) * 1024],
                 Wo[vc * 128:(vc + 1) * 128, :], writes=["wo", "wqkv"])

        S_sb = sbuf("S_sb", (128, NH * DV), F32)
        S_bf = sbuf("S_bf", (128, NH * DV), BF16)
        sch.task("dve", lambda e: e.memset(S_sb[:], 0.0), writes=["S"])
        sch.task("dve", lambda e: e.memset(S_bf[:], 0.0), writes=["Sbf"])

        AB5 = sbuf("AB5", (128, 6656), BF16)
        gsn_sb = AB5[:, 0:1536]
        o2fm = AB5[:, 1536:3072]
        TSP_sb = AB5[:, 3072:3840]
        expm = AB5[:, 3840:4224]
        qf_sb = AB5[:, 4224:4352]
        kf_sb = AB5[:, 4352:4480]
        khf_sb = AB5[:, 4480:4608]
        ktm_sb = AB5[:, 4608:4736]
        k2tm_sb = AB5[:, 4736:4864]
        PT_sb = AB5[:, 4864:4992]
        TSb_sb = AB5[:, 4992:5120]
        TTp_sb = AB5[:, 5120:5376]
        MT_sb = AB5[:, 5376:5504]
        jnk = AB5[:, 5504:5632]
        vtm_sb = AB5[:, 5632:5888]
        bv_sb = AB5[:, 5888:6144]
        ru_sb = AB5[:, 6144:6400]
        u_sb = AB5[:, 6400:6656]
        sso = sbuf("sso", (128, 4), F32)
        ccb = qB[:, 0:128]
        ein = qB[:, 128:256]
        osc_sb = qB[:, 256:512]
        o1s_sb = qB[:, 512:768]
        ytile = sbf[:, 0:1024]
        fence5 = sch.task("sp", lambda e: e.nop(),
                          writes=["rpB", "rpA", "rpC"])

        for cc in range(NCH):
            dmat("sp", gsn_sb[:], hgsn[cc], reads=["hgsn"], writes=["gsn"])
            for h in range(NH):
                dmat("sp", qf_sb[:], hq[h, cc], reads=["hqkv"], writes=["qf"])
                dmat("sp", kf_sb[:], hk[h, cc], reads=["hqkv"], writes=["kf"])
                tq, ktq, bkq = tb_slot()
                sch.task("pe", lambda e, tq=tq: e.transpose(
                    tq, qf_sb[:], sb_id_bf[:]), reads=["qf", "idbf"], writes=[ktq, bkq])
                sch.task("act", lambda e, tq=tq: e.activation(
                    out=jnk[:], in_=tq, func=AF.Square, accum_out=sso[:, 0:1]),
                    reads=[ktq, bkq, bkq], writes=["jnk", "ssoq"])
                sch.task("act", lambda e, cc=cc, h=h: e.activation(
                    out=dcol(G_CQ, cc, h), in_=sso[:, 0:1], func=AF.Sqrt,
                    scale=128.0, bias=128.0e-12), reads=["ssoq"], writes=["dcCQ"])
                sch.task("dve", lambda e, cc=cc, h=h: e.reciprocal(
                    out=dcol(G_CQ, cc, h), in_=dcol(G_CQ, cc, h)),
                    reads=["dcCQ"], writes=["dcCQ"])
                tk, ktk, bkk = tb_slot()
                sch.task("pe", lambda e, tk=tk: e.transpose(
                    tk, kf_sb[:], sb_id_bf[:]), reads=["kf", "idbf"], writes=[ktk, bkk])
                sch.task("act", lambda e, tk=tk: e.activation(
                    out=jnk[:], in_=tk, func=AF.Square, accum_out=sso[:, 1:2]),
                    reads=[ktk, bkk, bkk], writes=["jnk", "ssok"])
                sch.task("act", lambda e: e.activation(
                    out=sso[:, 2:3], in_=sso[:, 1:2], func=AF.Sqrt,
                    bias=1e-12), reads=["ssok"], writes=["dcIK"])
                sch.task("dve", lambda e: e.reciprocal(
                    out=sso[:, 2:3], in_=sso[:, 2:3]),
                    reads=["dcIK"], writes=["dcIK"])
                sch.task("dve", lambda e, tk=tk: e.tensor_scalar(
                    out=ktm_sb[:], in0=tk, scalar1=sso[:, 2:3], scalar2=None,
                    op0=AL.mult), reads=[ktk, bkk, "dcIK"], writes=["ktm"])
                sch.task("dve", lambda e, cc=cc, h=h: e.tensor_scalar(
                    out=k2tm_sb[:], in0=ktm_sb[:], scalar1=dcol(G_C2, cc, h),
                    scalar2=None, op0=AL.mult),
                    reads=["ktm", "dcC2"], writes=["k2tm"])
                tk2, ktk2, bkk2 = tb_slot()
                sch.task("pe", lambda e, tk2=tk2: e.transpose(
                    tk2, ktm_sb[:], sb_id_bf[:]), reads=["ktm", "idbf"], writes=[ktk2, bkk2])
                sch.task("dve", lambda e, tk2=tk2: e.tensor_copy(
                    out=khf_sb[:], in_=tk2), reads=[ktk2, bkk2, bkk2], writes=["khf"])
                tfa, ktfa, bfa = tf_slot()
                sch.task("dve", lambda e, cc=cc, h=h: e.tensor_scalar(
                    out=ccb[:], in0=sb_onesmat[:], scalar1=dcol(G_B, cc, h),
                    scalar2=None, op0=AL.mult),
                    reads=["onesm", "dcB"], writes=["ccb"], extra=[fence5])
                sch.task("pe", lambda e, tfa=tfa: e.transpose(
                    tfa, ccb[:], sb_id_f32[:]), reads=["ccb", "idf"], writes=[ktfa, bfa])
                sch.task("dve", lambda e, cc=cc, h=h, tfa=tfa: e.scalar_tensor_tensor(
                    out=ein[:], in0=tfa, scalar=dcol(G_B, cc, h), in1=sb_maskPT[:],
                    op0=AL.subtract, op1=AL.subtract),
                    reads=[ktfa, bfa, "dcB", "mPT"], writes=["ein"], extra=[fence5])
                sch.task("act", lambda e: e.activation(
                    out=expm[:, 0:128], in_=ein[:], func=AF.Exp),
                    reads=["ein"], writes=["ePT"])
                sch.task("dve", lambda e, cc=cc, h=h, tfa=tfa: e.scalar_tensor_tensor(
                    out=ein[:], in0=tfa, scalar=dcol(G_BLNB, cc, h), in1=sb_maskTS[:],
                    op0=AL.subtract, op1=AL.add),
                    reads=[ktfa, bfa, "dcBL", "mTS"], writes=["ein"])
                sch.task("act", lambda e: e.activation(
                    out=expm[:, 256:384], in_=ein[:], func=AF.Exp, scale=-1.0),
                    reads=["ein"], writes=["eTS"])
                tfb, ktfb, bfb2 = tf_slot()
                sch.task("dve", lambda e, cc=cc, h=h: e.tensor_scalar(
                    out=ccb[:], in0=sb_onesmat[:], scalar1=dcol(G_BLNB, cc, h),
                    scalar2=None, op0=AL.mult),
                    reads=["onesm", "dcBL"], writes=["ccb"])
                sch.task("pe", lambda e, tfb=tfb: e.transpose(
                    tfb, ccb[:], sb_id_f32[:]), reads=["ccb", "idf"], writes=[ktfb, bfb2])
                sch.task("dve", lambda e, cc=cc, h=h, tfb=tfb: e.scalar_tensor_tensor(
                    out=ein[:], in0=tfb, scalar=dcol(G_B, cc, h), in1=sb_maskTT[:],
                    op0=AL.subtract, op1=AL.subtract),
                    reads=[ktfb, bfb2, "dcB", "mTT"], writes=["ein"])
                sch.task("act", lambda e: e.activation(
                    out=expm[:, 128:256], in_=ein[:], func=AF.Exp),
                    reads=["ein"], writes=["eTT"])
                tg, ktg, bkg = tf_slot()
                sch.task("pe", lambda e, tg=tg: e.matmul(
                    tg, khf_sb[:], khf_sb[:], start=True, stop=True),
                    reads=["khf"], writes=[ktg, bkg])
                sch.task("dve", lambda e, tg=tg: e.tensor_tensor(
                    out=TTp_sb[:, 0:128], in0=tg, in1=expm[:, 128:256], op=AL.mult),
                    reads=[ktg, bkg, "eTT"], writes=["TTp0"])
                sch.task("dve", lambda e, tg=tg: e.tensor_tensor(
                    out=TSb_sb[:], in0=tg, in1=expm[:, 256:384], op=AL.mult),
                    reads=[ktg, bkg, "eTS"], writes=["TSb"])
                tg2, ktg2, bkg2 = tf_slot()
                sch.task("pe", lambda e, tg2=tg2: e.matmul(
                    tg2, khf_sb[:], qf_sb[:], start=True, stop=True),
                    reads=["khf", "qf"], writes=[ktg2, bkg2])
                sch.task("dve", lambda e, tg2=tg2: e.tensor_tensor(
                    out=PT_sb[:], in0=tg2, in1=expm[:, 0:128], op=AL.mult),
                    reads=[ktg2, bkg2, "ePT"], writes=["PT"])
                sch.task("dve", lambda e: e.scalar_tensor_tensor(
                    out=MT_sb[:], in0=TTp_sb[:, 0:128], scalar=-1.0, in1=sb_id_bf[:],
                    op0=AL.mult, op1=AL.add),
                    reads=["TTp0", "idbf"], writes=["MT"])
                kts_prev, ktt_prev = "TSb", "TTp0"
                ts_prev, tt_prev = TSb_sb[:], TTp_sb[:, 0:128]
                for j in range(6):
                    tp, ktp, bkp = tf_slot()
                    cur = TSP_sb[:, j * 128:(j + 1) * 128]
                    kcur = f"TSP{j}"
                    sch.task("pe", lambda e, tp=tp, a=tt_prev, b=ts_prev: e.matmul(
                        tp, a, b, start=True, stop=True),
                        reads=[kts_prev, ktt_prev], writes=[ktp, bkp])
                    sch.task("dve", lambda e, tp=tp, cur=cur: e.tensor_copy(
                        out=cur, in_=tp), reads=[ktp, bkp, bkp], writes=[kcur])
                    if j < 5:
                        tp2, ktp2, bkp2 = tf_slot()
                        curT = TTp_sb[:, ((j + 1) % 2) * 128:((j + 1) % 2 + 1) * 128]
                        kcurT = f"TTp{(j + 1) % 2}"
                        sch.task("pe", lambda e, tp2=tp2, a=ts_prev, b=tt_prev:
                                 e.matmul(tp2, a, b, start=True, stop=True),
                                 reads=[kts_prev, ktt_prev], writes=[ktp2, bkp2])
                        sch.task("dve", lambda e, tp2=tp2, curT=curT: e.tensor_copy(
                            out=curT, in_=tp2), reads=[ktp2, bkp2, bkp2], writes=[kcurT])
                        tt_prev, ktt_prev = curT, kcurT
                    ts_prev, kts_prev = cur, kcur
                for j in range(6):
                    tsl = TSP_sb[:, j * 128:(j + 1) * 128]
                    ktsl = f"TSP{j}"
                    tp3, ktp3, bkp3 = tf_slot()
                    sch.task("pe", lambda e, tp3=tp3, tsl=tsl: e.matmul(
                        tp3, tsl, MT_sb[:], start=True, stop=True),
                        reads=[ktsl, "MT"], writes=[ktp3, bkp3])
                    sch.task("dve", lambda e, tp3=tp3: e.tensor_tensor(
                        out=MT_sb[:], in0=MT_sb[:], in1=tp3, op=AL.add),
                        reads=[ktp3, bkp3, "MT"], writes=["MT"])
                for half in range(2):
                    dmat("sp", xstat[:, half * 128:(half + 1) * 128],
                         hv[2 * h + half, cc], reads=["hqkv"], writes=["xstat"])
                for half in range(2):
                    tv, ktv, bkv = tb_slot()
                    sch.task("pe", lambda e, half=half, tv=tv: e.transpose(
                        tv, xstat[:, half * 128:(half + 1) * 128], sb_id_bf[:]),
                        reads=["xstat", "idbf"], writes=[ktv, bkv])
                    sch.task("dve", lambda e, half=half, tv=tv: e.tensor_copy(
                        out=vtm_sb[:, half * 128:(half + 1) * 128], in_=tv),
                        reads=[ktv, bkv, bkv], writes=["vtm"])
                sch.task("dve", lambda e, cc=cc, h=h: e.tensor_scalar(
                    out=bv_sb[:], in0=vtm_sb[:], scalar1=dcol(G_BETA, cc, h),
                    scalar2=None, op0=AL.mult), reads=["vtm", "dcBE"], writes=["bv"])
                if cc == 0:
                    sch.task("dve", lambda e: e.tensor_copy(out=ru_sb[:], in_=bv_sb[:]),
                             reads=["bv"], writes=["ru"])
                else:
                    gb, kgb = gp_slot()
                    sch.task("pe", lambda e, h=h, gb=gb: e.matmul(
                        gb[:, 0:256], khf_sb[:],
                        S_bf[:, h * 256:(h + 1) * 256], start=True, stop=True),
                        reads=["khf", "Sbf"], writes=[kgb])
                    sch.task("dve", lambda e, cc=cc, h=h, gb=gb: e.scalar_tensor_tensor(
                        out=ru_sb[:], in0=gb[:, 0:256], scalar=dcol(G_NEGW, cc, h),
                        in1=bv_sb[:], op0=AL.mult, op1=AL.add),
                        reads=[kgb, "dcNW", "bv"], writes=["ru"])
                gu, kgu = gp_slot()
                sch.task("pe", lambda e, gu=gu: e.matmul(
                    gu[:, 0:256], MT_sb[:], ru_sb[:], start=True, stop=True),
                    reads=["MT", "ru"], writes=[kgu])
                sch.task("dve", lambda e, gu=gu: e.tensor_copy(
                    out=u_sb[:], in_=gu[:, 0:256]), reads=[kgu], writes=["u"])
                go1, kgo1 = gp_slot()
                sch.task("pe", lambda e, go1=go1: e.matmul(
                    go1[:, 0:256], PT_sb[:], u_sb[:], start=True, stop=True),
                    reads=["PT", "u"], writes=[kgo1])
                if cc == 0:
                    sch.task("dve", lambda e, cc=cc, h=h, go1=go1: e.tensor_scalar(
                        out=osc_sb[:], in0=go1[:, 0:256],
                        scalar1=dcol(G_CQ, cc, h), scalar2=None, op0=AL.mult),
                        reads=[kgo1, "dcCQ"], writes=["osc"], extra=[fence5])
                else:
                    sch.task("dve", lambda e, go1=go1: e.tensor_copy(
                        out=o1s_sb[:], in_=go1[:, 0:256]), reads=[kgo1],
                        writes=["o1s"], extra=[fence5])
                    go2, kgo2 = gp_slot()
                    sch.task("pe", lambda e, h=h, go2=go2: e.matmul(
                        go2[:, 0:256], qf_sb[:],
                        S_bf[:, h * 256:(h + 1) * 256], start=True, stop=True),
                        reads=["qf", "Sbf"], writes=[kgo2])
                    sch.task("dve", lambda e, cc=cc, h=h, go2=go2: e.scalar_tensor_tensor(
                        out=o1s_sb[:], in0=go2[:, 0:256], scalar=dcol(G_GAM, cc, h),
                        in1=o1s_sb[:], op0=AL.mult, op1=AL.add),
                        reads=[kgo2, "dcG", "o1s"], writes=["o1s"])
                    sch.task("dve", lambda e, cc=cc, h=h: e.tensor_scalar(
                        out=osc_sb[:], in0=o1s_sb[:],
                        scalar1=dcol(G_CQ, cc, h), scalar2=None, op0=AL.mult),
                        reads=["o1s", "dcCQ"], writes=["osc"])
                sch.task("dve", lambda e: e.scalar_tensor_tensor(
                    out=bv_sb[:], in0=osc_sb[:], scalar=1.0, in1=osc_sb[:],
                    op0=AL.bypass, op1=AL.mult, accum_out=sso[:, 3:4]),
                    reads=["osc"], writes=["bv", "ssoo"])
                sch.task("act", lambda e: e.activation(
                    out=sso[:, 3:4], in_=sso[:, 3:4], func=AF.Sqrt,
                    scale=1.0 / 256.0, bias=1e-5), reads=["ssoo"], writes=["ssoo"])
                sch.task("dve", lambda e: e.reciprocal(out=sso[:, 3:4], in_=sso[:, 3:4]),
                         reads=["ssoo"], writes=["ssoo"])
                sch.task("dve", lambda e, h=h: e.scalar_tensor_tensor(
                    out=osc_sb[:], in0=osc_sb[:], scalar=sso[:, 3:4],
                    in1=gsn_sb[:, h * 256:(h + 1) * 256], op0=AL.mult, op1=AL.mult),
                    reads=["ssoo", "gsn", "osc"], writes=["osc"])
                for half in range(2):
                    to, kto, bko = tf_slot()
                    sch.task("pe", lambda e, half=half, to=to: e.transpose(
                        to, osc_sb[:, half * 128:(half + 1) * 128],
                        sb_id_f32[:]), reads=["osc", "idf"], writes=[kto, bko])
                    sch.task("dve", lambda e, h=h, half=half, to=to: e.tensor_copy(
                        out=o2fm[:, (2 * h + half) * 128:(2 * h + half + 1) * 128],
                        in_=to), reads=[kto, bko, bko], writes=["o2fm"])
                gd, kgd = gp_slot()
                sch.task("pe", lambda e, gd=gd: e.matmul(
                    gd[:, 0:256], k2tm_sb[:], u_sb[:], start=True, stop=True),
                    reads=["k2tm", "u"], writes=[kgd])
                sch.task("dve", lambda e, cc=cc, h=h, gd=gd: e.scalar_tensor_tensor(
                    out=S_sb[:, h * 256:(h + 1) * 256],
                    in0=S_sb[:, h * 256:(h + 1) * 256],
                    scalar=dcol(G_GC, cc, h), in1=gd[:, 0:256],
                    op0=AL.mult, op1=AL.add),
                    reads=[kgd, "dcGC", "S"], writes=["S"])
                sch.task("dve", lambda e, h=h: e.tensor_copy(
                    out=S_bf[:, h * 256:(h + 1) * 256],
                    in_=S_sb[:, h * 256:(h + 1) * 256]),
                    reads=["S"], writes=["Sbf"])
            for dhalf in range(2):
                bank = gp[dhalf]
                kb = f"gp{dhalf}"
                for vc in range(12):
                    sch.task("pe", lambda e, vc=vc, dhalf=dhalf, bank=bank: e.matmul(
                        bank[:, 0:512], o2fm[:, vc * 128:(vc + 1) * 128],
                        wo_sb[:, vc * 1024 + dhalf * 512:vc * 1024 + (dhalf + 1) * 512],
                        start=(vc == 0), stop=(vc == 11)),
                        reads=["o2fm", "wo"], writes=[kb])
                sch.task("dve", lambda e, dhalf=dhalf, bank=bank: e.tensor_copy(
                    out=ytile[:, dhalf * 512:(dhalf + 1) * 512] if False else
                    sbf[:, dhalf * 512:(dhalf + 1) * 512], in_=bank[:, 0:512]),
                    reads=[kb], writes=["sbf"])
            dmat("sp", hy[cc], sbf[:, 0:1024], reads=["sbf"], writes=["hy"])

        maybe_stop(5)
        w1sT = AB[:, 0:16384]
        w2sT = AB[:, 16384:32768]
        fence6 = sch.task("sp", lambda e: e.nop(),
                          writes=["ccb", "ein", "osc", "o1s"])
        w1c_bf = sbf[:, 0:1024]
        w2c_bf = sbf[:, 0:1024]
        alcols = sbuf("alcols", (128, 32), F32)
        b1_sb = sbuf("b1_sb", (128, 32), F32)
        a2cols = sbuf("a2cols", (128, 8), F32)
        a2b = sbuf("a2b", (128, 1024), BF16)
        h_bf = sbuf("h_bf", (128, 256), BF16)
        yt_bf = q_bf

        dmat("sp", b1_sb[:], b1c[:, :], writes=["b1"])
        for dc2 in range(8):
            for hhx in range(2):
                dmat("pool", w2c_bf[:, 0:1024],
                     w2[dc2 * 128:(dc2 + 1) * 128, hhx * 1024:(hhx + 1) * 1024],
                     writes=["sbf"])
                dmat("pool", xstat[:, 0:1024],
                     w2[dc2 * 128:(dc2 + 1) * 128, 2048 + hhx * 1024:2048 + (hhx + 1) * 1024],
                     writes=["xstat"])
                if hhx == 0:
                    sch.task("dve", lambda e, dc2=dc2: e.tensor_reduce(
                        out=a2cols[:, dc2:dc2 + 1], in_=w2c_bf[:, 0:1024], axis=AX.X,
                        op=AL.add, apply_absolute_value=True),
                        reads=["sbf"], writes=["a2c"])
                else:
                    sch.task("dve", lambda e, dc2=dc2: e.tensor_reduce(
                        out=lncols[:, 0:1], in_=w2c_bf[:, 0:1024], axis=AX.X,
                        op=AL.add, apply_absolute_value=True),
                        reads=["sbf"], writes=["lnc"])
                    sch.task("dve", lambda e, dc2=dc2: e.scalar_tensor_tensor(
                        out=a2cols[:, dc2:dc2 + 1],
                        in0=a2cols[:, dc2:dc2 + 1] if True else None,
                        scalar=1.0, in1=lncols[:, 0:1], op0=AL.mult, op1=AL.add),
                        reads=["lnc", "a2c"], writes=["a2c"])
                sch.task("dve", lambda e: e.tensor_reduce(
                    out=lncols[:, 1:2], in_=xstat[:, 0:1024], axis=AX.X,
                    op=AL.add, apply_absolute_value=True),
                    reads=["xstat"], writes=["lnc1"])
                sch.task("dve", lambda e, dc2=dc2: e.scalar_tensor_tensor(
                    out=a2cols[:, dc2:dc2 + 1], in0=a2cols[:, dc2:dc2 + 1],
                    scalar=1.0, in1=lncols[:, 1:2], op0=AL.mult, op1=AL.add),
                    reads=["lnc1", "a2c"], writes=["a2c"])
        sch.task("dve", lambda e: e.tensor_scalar(
            out=a2cols[:], in0=a2cols[:], scalar1=1.0 / HFF, scalar2=None,
            op0=AL.mult), reads=["a2c"], writes=["a2c"])
        for dc2 in range(8):
            dmat("pool", a2row[0:1, dc2 * 128:(dc2 + 1) * 128],
                 a2cols[:, dc2:dc2 + 1], reads=["a2c"], writes=["rpA"],
                 extra=[fence6], slow=True)
        for half in range(2):
            sch.task("pe", lambda e, half=half: e.matmul(
                tf[:, 0:512], sb_onesr[:], a2row[0:1, half * 512:(half + 1) * 512],
                start=True, stop=True), reads=["rpA", "onesr"],
                writes=["tfA", "tfB", "tfC", "tfD", "tfbank"])
            sch.task("dve", lambda e, half=half: e.tensor_copy(
                out=a2b[:, half * 512:(half + 1) * 512], in_=tf[:, 0:512]),
                reads=["tfA", "tfbank"], writes=["a2b"])
        dmat("sp", b2row, b2r[:, :], writes=["rpC"], extra=[fence6])
        sch.task("dve", lambda e: e.reciprocal(out=bzrow, in_=a2row),
                 reads=["rpA"], writes=["rpB"], extra=[fence6])
        sch.task("dve", lambda e: e.tensor_tensor(
            out=bzrow, in0=bzrow, in1=b2row, op=AL.mult),
            reads=["rpB", "rpC"], writes=["rpB"])

        NT2 = L // 256
        for hh in range(2):
            for hcl in range(16):
                hcg = hh * 16 + hcl
                dmat("pool", w1c_bf[:, 0:1024], w1[hcg * 128:(hcg + 1) * 128, :],
                     writes=["sbf"])
                sch.task("dve", lambda e, hcg=hcg: e.tensor_reduce(
                    out=alcols[:, hcg:hcg + 1], in_=w1c_bf[:, 0:1024], axis=AX.X,
                    op=AL.add, apply_absolute_value=True),
                    reads=["sbf"], writes=["al1"])
                for dc in range(8):
                    ts_, kts_, bks_ = tb_slot()
                    sch.task("pe", lambda e, dc=dc, ts_=ts_: e.transpose(
                        ts_, w1c_bf[:, dc * 128:(dc + 1) * 128], sb_id_bf[:]),
                        reads=["sbf", "idbf"], writes=[kts_, bks_])
                    sch.task("act", lambda e, dc=dc, hcl=hcl, ts_=ts_: e.activation(
                        out=w1sT[:, dc * 2048 + hcl * 128:dc * 2048 + (hcl + 1) * 128],
                        in_=ts_, func=AF.Sign),
                        reads=[kts_, bks_, bks_], writes=["w1sT", "xblk"])
            for dc2 in range(8):
                for hq2 in range(2):
                    dmat("pool", w2c_bf[:, 0:1024],
                         w2[dc2 * 128:(dc2 + 1) * 128,
                            hh * 2048 + hq2 * 1024:hh * 2048 + (hq2 + 1) * 1024],
                         writes=["sbf"])
                    for hcl8 in range(8):
                        hcl = hq2 * 8 + hcl8
                        ts_, kts_, bks_ = tb_slot()
                        sch.task("pe", lambda e, hcl8=hcl8, ts_=ts_: e.transpose(
                            ts_, w2c_bf[:, hcl8 * 128:(hcl8 + 1) * 128], sb_id_bf[:]),
                            reads=["sbf", "idbf"], writes=[kts_, bks_])
                        sch.task("act", lambda e, dc2=dc2, hcl=hcl, ts_=ts_:
                                 e.activation(
                            out=w2sT[:, hcl * 1024 + dc2 * 128:hcl * 1024 + (dc2 + 1) * 128],
                            in_=ts_, func=AF.Sign), reads=[kts_, bks_, bks_], writes=["w2sT"])
            sch.task("dve", lambda e, hh=hh: e.tensor_scalar(
                out=alcols[:, hh * 16:(hh + 1) * 16],
                in0=alcols[:, hh * 16:(hh + 1) * 16], scalar1=1.0 / D, scalar2=None,
                op0=AL.mult), reads=["al1"], writes=["al1"])

            for tblk in range(NT2):
                t0 = tblk * 256
                for dc in range(8):
                    dmat("sp", xnb2[:, dc * 256:(dc + 1) * 256],
                         hxn0[dc, :, t0:t0 + 256], reads=["hxn0"], writes=["xstat"])
                if hh == 0:
                    for j in range(2):
                        for dh in range(2):
                            bank = gp[j * 2 + dh]
                            kb = f"gp{j * 2 + dh}"
                            sch.task("pe", lambda e, dh=dh, bank=bank: e.matmul(
                                bank[:, 0:512], sb_onesr[:],
                                bzrow[0:1, dh * 512:(dh + 1) * 512],
                                start=True, stop=False),
                                reads=["rpB", "onesr"], writes=[kb])
                for hcl in range(16):
                    for dc in range(8):
                        sch.task("pe", lambda e, hcl=hcl, dc=dc: e.matmul(
                            gp[4][:, 0:256],
                            w1sT[:, dc * 2048 + hcl * 128:dc * 2048 + (hcl + 1) * 128],
                            xnb2[:, dc * 256:(dc + 1) * 256],
                            start=(dc == 0), stop=(dc == 7)),
                            reads=["w1sT", "xstat"], writes=["gp4"])
                    hcg = hh * 16 + hcl
                    sch.task("act", lambda e, hcg=hcg: e.activation(
                        out=h_bf[:], in_=gp[4][:, 0:256], func=AF.Gelu,
                        bias=b1_sb[:, hcg:hcg + 1], scale=alcols[:, hcg:hcg + 1]),
                        reads=["gp4", "b1", "al1"], writes=["h_bf"])
                    for j in range(2):
                        for dh in range(2):
                            bank = gp[j * 2 + dh]
                            kb = f"gp{j * 2 + dh}"
                            first = (hh == 1 and hcl == 0)
                            sch.task("pe", lambda e, j=j, dh=dh, hcl=hcl, bank=bank,
                                     first=first: e.matmul(
                                bank[:, 0:512], h_bf[:, j * 128:(j + 1) * 128],
                                w2sT[:, hcl * 1024 + dh * 512:hcl * 1024 + (dh + 1) * 512],
                                start=first, stop=(hcl == 15)),
                                reads=["h_bf", "w2sT"], writes=[kb])
                for j in range(2):
                    tt = t0 // 128 + j
                    if hh == 0:
                        dmat("sp", qD[:], x_in[tt * 128:(tt + 1) * 128, :],
                             writes=["qD"])
                        dmat("sp", yt_bf[:], hy[tt], reads=["hy"], writes=["qbf"])
                        sch.task("dve", lambda e: e.tensor_tensor(
                            out=qD[:], in0=qD[:], in1=yt_bf[:], op=AL.add),
                            reads=["qbf", "qD"], writes=["qD"])
                    else:
                        dmat("sp", qD[:], hx1[tt], reads=["hx1w"], writes=["qD"])
                    for dh in range(2):
                        bank = gp[j * 2 + dh]
                        kb = f"gp{j * 2 + dh}"
                        sch.task("dve", lambda e, dh=dh, bank=bank: e.tensor_tensor(
                            out=q32[:, dh * 512:(dh + 1) * 512], in0=bank[:, 0:512],
                            in1=a2b[:, dh * 512:(dh + 1) * 512], op=AL.mult),
                            reads=[kb, "a2b"], writes=["q32"])
                    sch.task("dve", lambda e: e.tensor_tensor(
                        out=qD[:], in0=qD[:], in1=q32[:], op=AL.add),
                        reads=["q32", "qD"], writes=["qD"])
                    dmat("sp", hx1[tt], qD[:], reads=["qD"], writes=["hx1w"])

        maybe_stop(6)
        for tt in range(NT):
            dmat("sp", xtile, hx1[tt], reads=["hx1w"], writes=["xtile"])
            ln_tile()
            for dc in range(8):
                sch.task("pe", lambda e, dc=dc: e.transpose(
                    tf[:, 0:128], xntm[:, dc * 128:(dc + 1) * 128], sb_id_f32[:]),
                    reads=["xntm", "idf"], writes=["tfA", "tfbank"])
                sch.task("dve", lambda e, dc=dc: e.tensor_copy(
                    out=xnfm_bf[:, dc * 128:(dc + 1) * 128], in_=tf[:, 0:128]),
                    reads=["tfA"], writes=["sbf"])
                sch.task("dve", lambda e, dc=dc: e.tensor_copy(
                    out=q32[:, dc * 128:(dc + 1) * 128], in_=tf[:, 0:128]),
                    reads=["tfA", "tfbank"], writes=["q32"])
            for dc in range(8):
                dmat("sp", hxn1[dc, :, tt * 128:(tt + 1) * 128],
                     xnfm_bf[:, dc * 128:(dc + 1) * 128],
                     reads=["sbf"], writes=["hxn1"])
            if phase_limit <= 6.1:
                continue
            round_pipe(q32[:], q_bf[:], "q32", "qbf", mant_bits=3, min_p=-6,
                       clip=448.0)
            if phase_limit <= 6.3:
                continue
            for dc in range(8):
                dmat("sp", hxq[dc, :, tt * 128:(tt + 1) * 128],
                     q_bf[:, dc * 128:(dc + 1) * 128],
                     reads=["qbf"], writes=["hxq"])

        maybe_stop(7)
        wfT = AB2[:, 0:8192]
        for fc in range(8):
            dmat("pool", q32[:], wf[fc * 128:(fc + 1) * 128, :], writes=["q32"])
            round_pipe(q32[:], q_bf[:], "q32", "qbf", mant_bits=3, min_p=-6,
                       clip=448.0)
            for cc2 in range(8):
                ts_, kts_, bks_ = tb_slot()
                sch.task("pe", lambda e, cc2=cc2, ts_=ts_: e.transpose(
                    ts_, q_bf[:, cc2 * 128:(cc2 + 1) * 128], sb_id_bf[:]),
                    reads=["qbf", "idbf"], writes=[kts_, bks_])
                sch.task("dve", lambda e, cc2=cc2, fc=fc, ts_=ts_: e.tensor_copy(
                    out=wfT[:, cc2 * 1024 + fc * 128:cc2 * 1024 + (fc + 1) * 128],
                    in_=ts_), reads=[kts_, bks_, bks_], writes=["wfT", "wo"])
        for tt in range(NT):
            for dc in range(8):
                dmat("sp", xstat[:, dc * 128:(dc + 1) * 128],
                     hxq[dc, :, tt * 128:(tt + 1) * 128],
                     reads=["hxq"], writes=["xstat"])
            for dh in range(2):
                bank = gp[dh]
                kb = f"gp{dh}"
                for cc2 in range(8):
                    sch.task("pe", lambda e, cc2=cc2, dh=dh, bank=bank: e.matmul(
                        bank[:, 0:512], xstat[:, cc2 * 128:(cc2 + 1) * 128],
                        wfT[:, cc2 * 1024 + dh * 512:cc2 * 1024 + (dh + 1) * 512],
                        start=(cc2 == 0), stop=(cc2 == 7)),
                        reads=["xstat", "wfT"], writes=[kb])
                sch.task("dve", lambda e, dh=dh, bank=bank: e.tensor_copy(
                    out=q32[:, dh * 512:(dh + 1) * 512], in_=bank[:, 0:512]),
                    reads=[kb], writes=["q32"])
            round_pipe(q32[:], f16b[:], "q32", "f16b", mant_bits=10, min_p=-14,
                       clip=None, tie_even=True)
            dmat("sp", hf1[tt], f16b[:], reads=["f16b"], writes=["hf1"])

        maybe_stop(8)
        KR = AB2[:, 0:32768]
        VR = AB[:, 0:32768]
        wl1 = AB5[:, 0:4096]
        qp_bf = AB5[:, 4096:4608]
        qx_bf = AB5[:, 4608:5120]
        fence8 = sch.task("sp", lambda e: e.nop(),
                          writes=["gsn", "o2fm", "TSP0", "TSP1", "TSP2", "TSP3",
                                  "TSP4", "TSP5", "ePT", "eTT", "eTS", "qf", "kf",
                                  "khf", "ktm", "k2tm", "PT", "TSb", "TTp0",
                                  "TTp1", "MT", "jnk", "vtm", "bv", "ru", "u"])

        def l1_proj(wsrc, cost, sint, is_q):
            for wh in range(2):
                for dc in range(8):
                    dmat("pool", wl1[:, dc * 512:(dc + 1) * 512],
                         wsrc[dc * 128:(dc + 1) * 128, wh * 512:(wh + 1) * 512],
                         writes=["wl1"], extra=[fence8])
                for tb8 in range(NB):
                    t0 = tb8 * 512
                    for dc in range(8):
                        dmat("sp", xblk[:, dc * 512:(dc + 1) * 512],
                             hxn1[dc, :, t0:t0 + 512], reads=["hxn1"], writes=["xblk"])
                    for dc2l in range(4):
                        dc2 = wh * 4 + dc2l
                        bank = gp[dc2 % 2]
                        kb = f"gp{dc2 % 2}"
                        for dc in range(8):
                            sch.task("pe", lambda e, dc=dc, dc2l=dc2l, bank=bank:
                                     e.matmul(
                                bank[:, 0:512],
                                wl1[:, dc * 512 + dc2l * 128:dc * 512 + (dc2l + 1) * 128],
                                xblk[:, dc * 512:(dc + 1) * 512],
                                start=(dc == 0), stop=(dc == 7)),
                                reads=["wl1", "xblk"], writes=[kb])
                        sch.task("dve", lambda e, bank=bank: e.tensor_copy(
                            out=qp_bf[:], in_=bank[:, 0:512]), reads=[kb], writes=["qp"])
                        bank2 = gp[2 + dc2 % 2]
                        kb2 = f"gp{2 + dc2 % 2}"
                        sch.task("pe", lambda e, bank2=bank2: e.matmul(
                            bank2[:, 0:512], sb_rotT[:], qp_bf[:],
                            start=True, stop=True),
                            reads=["qp", "rotT"], writes=[kb2])
                        dmat("sp", trigc,
                             cost[dc2 * 128:(dc2 + 1) * 128, t0:t0 + 512],
                             writes=["rpC"])
                        dmat("sp", trigs,
                             sint[dc2 * 128:(dc2 + 1) * 128, t0:t0 + 512],
                             writes=["rpC2"])
                        sch.task("dve", lambda e: e.tensor_tensor(
                            out=qA[:, 0:512], in0=qp_bf[:], in1=trigc,
                            op=AL.mult), reads=["qp", "rpC"], writes=["rpA"])
                        sch.task("dve", lambda e, bank2=bank2: e.tensor_tensor(
                            out=qB[:, 0:512], in0=bank2[:, 0:512], in1=trigs,
                            op=AL.mult), reads=[kb2, "rpC2"], writes=["rpB"])
                        sch.task("dve", lambda e: e.tensor_tensor(
                            out=qx_bf[:], in0=qA[:, 0:512], in1=qB[:, 0:512],
                            op=AL.add), reads=["rpA", "rpB"], writes=["qx"])
                        if is_q:
                            for j in range(4):
                                dmat("sp", hq1[t0 // 128 + j, dc2],
                                     qx_bf[:, j * 128:(j + 1) * 128],
                                     reads=["qx"], writes=["hq1"])
                        else:
                            sch.task("dve", lambda e, dc2=dc2, t0=t0: e.tensor_copy(
                                out=KR[:, dc2 * L + t0:dc2 * L + t0 + 512],
                                in_=qx_bf[:]),
                                reads=["qx"], writes=["KR", "wfT"])

        l1_proj(WQ1, ctq, stq, True)
        l1_proj(WK1, ctk, stk, False)
        for wh in range(2):
            for dc in range(8):
                dmat("pool", wl1[:, dc * 512:(dc + 1) * 512],
                     WV1[dc * 128:(dc + 1) * 128, wh * 512:(wh + 1) * 512],
                     writes=["wl1"])
            for tt in range(NT):
                for dc in range(8):
                    dmat("sp", xstat[:, dc * 128:(dc + 1) * 128],
                         hxn1[dc, :, tt * 128:(tt + 1) * 128],
                         reads=["hxn1"], writes=["xstat"])
                bank = gp[tt % 2]
                kb = f"gp{tt % 2}"
                for dc in range(8):
                    sch.task("pe", lambda e, dc=dc, bank=bank: e.matmul(
                        bank[:, 0:512], xstat[:, dc * 128:(dc + 1) * 128],
                        wl1[:, dc * 512:(dc + 1) * 512],
                        start=(dc == 0), stop=(dc == 7)),
                        reads=["xstat", "wl1"], writes=[kb])
                sch.task("dve", lambda e, tt=tt, wh=wh, bank=bank: e.tensor_copy(
                    out=VR[:, tt * 1024 + wh * 512:tt * 1024 + (wh + 1) * 512],
                    in_=bank[:, 0:512]),
                    reads=[kb], writes=["VR", "xblk", "w1sT", "w2sT"])

        maybe_stop(9)
        qstat = xnb2[:, 0:1024]
        mcols = sbuf("mcols", (128, 16), F32)
        ed_bf = qp_bf
        pt_full = AB5[:, 5120:5632]
        zc = sbuf("zc", (128, 8), F32)
        bfb_sb = xnb2[:, 1024:2048]
        out_sb = q32
        fence9 = sch.task("sp", lambda e: e.nop(), writes=["xstat", "wl1"])
        dmat("sp", bfb_sb[:], bf_b[:, :], writes=["bfb"], extra=[fence9])

        for qt in range(NT):
            q0 = qt * 128
            dmat("sp", qstat[:], hq1[qt].rearrange("dc p c -> p dc c"),
                 reads=["hq1"], writes=["qstat"], extra=[fence9])
            for kc in range(NKC):
                bank = gp[kc % 2]
                kb = f"gp{kc % 2}"
                for dc in range(8):
                    sch.task("pe", lambda e, dc=dc, kc=kc, bank=bank: e.matmul(
                        bank[:, 0:512], qstat[:, dc * 128:(dc + 1) * 128],
                        KR[:, dc * L + kc * 512:dc * L + kc * 512 + 512],
                        start=(dc == 0), stop=(dc == 7)),
                        reads=["qstat", "KR"], writes=[kb])
                sch.task("dve", lambda e, kc=kc, bank=bank: e.reduce_max(
                    mcols[:, kc:kc + 1], bank[:, 0:512], axis=AX.X),
                    reads=[kb], writes=["mc"])
                sch.task("dve", lambda e, kc=kc, bank=bank: e.tensor_copy(
                    out=srow[:, kc * 512:(kc + 1) * 512], in_=bank[:, 0:512]),
                    reads=[kb], writes=SROW_W)
            sch.task("dve", lambda e: e.reduce_max(
                mcols[:, 9:10], mcols[:, 0:NKC], axis=AX.X),
                reads=["mc"], writes=["mfin"])
            sch.task("dve", lambda e: e.tensor_scalar(
                out=mcols[:, 10:11], in0=mcols[:, 9:10], scalar1=-1.0, scalar2=None,
                op0=AL.mult), reads=["mfin"], writes=["negm"])
            live = [kc for kc in range(NKC)
                    if max(0, q0 - (kc * 512 + 511), kc * 512 - (q0 + 127)) <= DEC_SKIP]
            for kc in range(NKC):
                k0 = kc * 512
                sch.task("act", lambda e, kc=kc: e.activation(
                    out=sbf[:, 0:512], in_=srow[:, kc * 512:(kc + 1) * 512],
                    func=AF.Exp, bias=mcols[:, 10:11], accum_out=zc[:, kc:kc + 1]),
                    reads=["srow", "negm"], writes=["sbf", f"zc{kc}"])
                if kc not in live:
                    continue
                j0 = DEC_OFF + k0 - q0
                dmat("sp", dlt[:, 0:512], dlog[:, j0:j0 + 512], writes=["q32"])
                sch.task("dve", lambda e, kc=kc: e.tensor_tensor(
                    out=sdec, in0=srow[:, kc * 512:(kc + 1) * 512], in1=dlt[:, 0:512],
                    op=AL.add), reads=["srow", "q32"], writes=SDEC_W)
                sch.task("act", lambda e: e.activation(
                    out=ed_bf[:], in_=sdec, func=AF.Exp, bias=mcols[:, 10:11]),
                    reads=["sdec", "negm"], writes=["qp"])
                for j in range(4):
                    tsl, ktsl, bksl = tb_slot()
                    sch.task("pe", lambda e, j=j, tsl=tsl: e.transpose(
                        tsl, ed_bf[:, j * 128:(j + 1) * 128], sb_id_bf[:]),
                        reads=["qp", "idbf"], writes=[ktsl, bksl])
                    sch.task("dve", lambda e, j=j, tsl=tsl: e.tensor_copy(
                        out=pt_full[:, j * 128:(j + 1) * 128], in_=tsl),
                        reads=[ktsl, bksl, bksl], writes=["ptf"], extra=[fence9])
                lastkc = (kc == live[-1])
                for j in range(4):
                    ktile = kc * 4 + j
                    st = (kc == live[0] and j == 0)
                    sp_ = (lastkc and j == 3)
                    for dh in range(2):
                        bank = gp[2 + dh]
                        kb = f"gp{2 + dh}"
                        sch.task("pe", lambda e, j=j, dh=dh, ktile=ktile, st=st,
                                 sp_=sp_, bank=bank: e.matmul(
                            bank[:, 0:512], pt_full[:, j * 128:(j + 1) * 128],
                            VR[:, ktile * 1024 + dh * 512:ktile * 1024 + (dh + 1) * 512],
                            start=st, stop=sp_),
                            reads=["ptf", "VR"], writes=[kb])
            sch.task("dve", lambda e: e.reduce_sum(
                mcols[:, 11:12], zc[:, 0:NKC], axis=AX.X),
                reads=[f"zc{kc}" for kc in range(NKC)], writes=["Z"])
            sch.task("dve", lambda e: e.reciprocal(out=mcols[:, 12:13],
                                                   in_=mcols[:, 11:12]),
                     reads=["Z"], writes=["rZ"])
            dmat("sp", qD[:], hx1[qt], reads=["hx1w"], writes=["qD"])
            dmat("sp", f16b[:], hf1[qt], reads=["hf1"], writes=["f16b"])
            sch.task("dve", lambda e: e.tensor_tensor(
                out=qD[:], in0=qD[:], in1=f16b[:], op=AL.add),
                reads=["f16b", "qD"], writes=["qD"])
            sch.task("dve", lambda e: e.tensor_tensor(
                out=qD[:], in0=qD[:], in1=bfb_sb[:], op=AL.add),
                reads=["bfb", "qD"], writes=["qD"])
            for dh in range(2):
                bank = gp[2 + dh]
                kb = f"gp{2 + dh}"
                sch.task("dve", lambda e, dh=dh, bank=bank: e.scalar_tensor_tensor(
                    out=out_sb[:, dh * 512:(dh + 1) * 512], in0=bank[:, 0:512],
                    scalar=mcols[:, 12:13], in1=qD[:, dh * 512:(dh + 1) * 512],
                    op0=AL.mult, op1=AL.add),
                    reads=[kb, "rZ", "qD"], writes=["q32"])
            dmat("sp", out[qt * 128:(qt + 1) * 128, :], out_sb[:],
                 reads=["q32"], writes=["outw"])


    except _StopBuild:
        dmat("sp", out[0:128, :], x_in[0:128, :], writes=["outw"])

    sems = {}
    for e in Sched.ENGS:
        g = nc.semaphore(f"sem_{e}")
        sems[("eng", e)] = g.__enter__()
        ctx.append(g)
    for e in Sched.DMAQ:
        for si in range(Sched.NSEM[e]):
            g = nc.semaphore(f"dmas_{e}{si}")
            sems[("dma", e, si)] = g.__enter__()
            ctx.append(g)
    blk = nc.Block()
    blk.__enter__()
    sch.emit(nc, blk, sems)
    blk.__exit__(None, None, None)
    for g in reversed(ctx):
        g.__exit__(None, None, None)
    nc._sched = sch
    return nc


def make_inputs_for_core(xb, I, consts):
    m = {"x": np.ascontiguousarray(xb, dtype=np.float32)}
    m["Wq"] = I["L0_attn_Wq"]
    m["Wk"] = I["L0_attn_Wk"]
    m["Wv"] = I["L0_attn_Wv"]
    wba = np.zeros((D, 16), dtype=np.float32)
    wba[:, 0:6] = I["L0_attn_Wb"]
    wba[:, 6:12] = I["L0_attn_Wa"]
    m["Wba"] = wba
    m["A_log"] = np.ascontiguousarray(I["L0_attn_A_log"]).reshape(NH, 1).astype(np.float32)
    m["dt_bias"] = np.ascontiguousarray(I["L0_attn_dt_bias"]).reshape(NH, 1).astype(np.float32)
    m["convq"] = np.ascontiguousarray(
        np.asarray(I["L0_attn_conv_q"]).reshape(NH, 128, 4).transpose(1, 0, 2).reshape(128, 24))
    m["convk"] = np.ascontiguousarray(
        np.asarray(I["L0_attn_conv_k"]).reshape(NH, 128, 4).transpose(1, 0, 2).reshape(128, 24))
    m["convv"] = np.ascontiguousarray(
        np.asarray(I["L0_attn_conv_v"]).reshape(12, 128, 4).transpose(1, 0, 2).reshape(128, 48))
    m["Wg"] = I["L0_attn_Wg"]
    m["nw_b"] = np.ascontiguousarray(
        np.broadcast_to(np.tile(np.asarray(I["L0_attn_norm_w"]), NH), (128, VD)))
    m["Wo"] = I["L0_attn_Wo"]
    m["w1"] = I["L0_ffn_w1"]
    m["b1c"] = np.ascontiguousarray(np.asarray(I["L0_ffn_b1"]).reshape(32, 128).T)
    m["w2"] = I["L0_ffn_w2"]
    m["b2r"] = np.ascontiguousarray(np.asarray(I["L0_ffn_b2"]).reshape(1, D))
    m["WQ1"] = I["L1_attn_W_Q"]
    m["WK1"] = I["L1_attn_W_K"]
    m["WV1"] = I["L1_attn_W_V"]
    m["wf"] = I["L1_ffn_weight"]
    m["bf_b"] = np.ascontiguousarray(np.broadcast_to(np.asarray(I["L1_ffn_bias"]), (128, D)))
    for k, v in consts.items():
        m[k] = v
    res = {}
    for k, v in m.items():
        a = np.asarray(v)
        if a.dtype == np.float64:
            a = a.astype(np.float32)
        res[k] = np.ascontiguousarray(a)
    return res


_NC_CACHE = {}


def kernel(**inputs):
    L = np.asarray(inputs["L0_attn_x"]).shape[1]
    if L not in _NC_CACHE:
        _NC_CACHE[L] = build_core(L)
    nc = _NC_CACHE[L]
    consts = host_constants(L)
    x = np.ascontiguousarray(np.asarray(inputs["L0_attn_x"], dtype=np.float32))
    n = x.shape[0]
    in_maps = [make_inputs_for_core(x[i], inputs, consts) for i in range(n)]
    res = run_bass_kernel_spmd(nc, in_maps, core_ids=list(range(n)))
    return np.stack([r["out"] for r in res.results], axis=0)


# version 11
# speedup vs baseline: 1.0530x; 1.0530x over previous
import numpy as np

import concourse.bass as bass
import concourse.mybir as mybir
from concourse.bass_utils import run_bass_kernel_spmd

F32 = mybir.dt.float32
BF16 = mybir.dt.bfloat16
F16 = mybir.dt.float16
AF = mybir.ActivationFunctionType
AL = mybir.AluOpType
AX = mybir.AxisListType

LN2 = float(np.log(2.0))
INVLN2 = float(1.0 / np.log(2.0))
GAMMA = 0.9
LNG = float(np.log(GAMMA))
BIG = 1.0e30

D = 1024
NH = 6
DK = 128
DV = 256
KD = NH * DK
VD = NH * DV
HFF = 4096
C = 128
DEC_SKIP = 1100
DEC_OFF = 1664
DEC_BAND = 3456

SCALAR_CONSTS = [0.0, 1.0, -1.0, 0.5, 1e-5, 1e-12, 1.0 / 1024.0, 1.0 / 256.0,
                 128.0, 128.0e-12, float(np.log(2.0)), float(-np.log(2.0)),
                 float(np.log(2.0) * 67), float(-np.log(2.0) * 67),
                 float(np.log(2.0) * 74), float(-np.log(2.0) * 74)]


def _bf16np():
    import ml_dtypes
    return np.dtype(ml_dtypes.bfloat16)


class Sched:
    ENGS = ("pe", "act", "dve", "pool", "sp")
    DMAQ = ("sp", "pool", "act")
    NSEM = {"sp": 24, "pool": 16, "act": 4}

    def __init__(self):
        self.items = {e: [] for e in self.ENGS}
        self.n = {e: 0 for e in self.ENGS}
        self.ncomp = {e: 0 for e in self.ENGS}
        self.ndma = {e: 0 for e in self.DMAQ}
        self.semuse = {}
        self.last_w = {}
        self.readers = {}

    PSUM_BANKS = frozenset(["gp0", "gp1", "gp2", "gp3", "gp4",
                            "tbank0", "tbank1", "tfbank"])

    def task(self, eng, fn, reads=(), writes=(), dma=False, extra=()):
        pb = [r for r in reads if r in self.PSUM_BANKS]
        if pb:
            reads = [r for r in reads if r not in self.PSUM_BANKS]
            writes = list(writes) + pb
        deps = []
        for r in reads:
            c = self.last_w.get(r)
            if c is not None:
                deps.append(c)
        for w in writes:
            c = self.last_w.get(w)
            if c is not None:
                deps.append(c)
            deps.extend(self.readers.get(w, ()))
        deps.extend(extra)
        self.n[eng] += 1
        if dma:
            si = self.ndma[eng] % self.NSEM[eng]
            self.ndma[eng] += 1
            skey = ("dma", eng, si)
            cnt = self.semuse.get(skey, 0) + 1
            self.semuse[skey] = cnt
            if cnt > 1:
                deps.append((skey, 16 * (cnt - 1)))
            comp = (skey, 16 * cnt)
            dmakey = skey
        else:
            self.ncomp[eng] += 1
            comp = (("eng", eng), self.ncomp[eng])
            dmakey = None
        self.items[eng].append((deps, fn, dmakey))
        for r in reads:
            self.readers.setdefault(r, []).append(comp)
        for w in writes:
            self.last_w[w] = comp
            self.readers[w] = []
        return comp

    def emit(self, nc, block, sems):
        def make_body(ename, items):
            def body(eng):
                satisfied = {}
                own = ("eng", ename)
                if ename != "sp":
                    eng.wait_ge(sems[("dma", "sp", 0)], 16)
                    satisfied[("dma", "sp", 0)] = 16
                for deps, fn, dmakey in items:
                    for key, val in deps:
                        if satisfied.get(key, 0) >= val:
                            continue
                        eng.wait_ge(sems[key], val)
                        satisfied[key] = val
                    inst = fn(eng)
                    if dmakey is not None:
                        inst.then_inc(sems[dmakey], 16)
                    else:
                        inst.then_inc(sems[("eng", ename)], 1)
                if ename in self.DMAQ:
                    for si in range(self.NSEM[ename]):
                        skey = ("dma", ename, si)
                        cnt = self.semuse.get(skey, 0)
                        if cnt > 0:
                            eng.wait_ge(sems[skey], 16 * cnt)
            return body

        block.tensor(make_body("pe", self.items["pe"]))
        block.scalar(make_body("act", self.items["act"]))
        block.vector(make_body("dve", self.items["dve"]))
        block.gpsimd(make_body("pool", self.items["pool"]))
        block.sync(make_body("sp", self.items["sp"]))


def host_constants(L):
    bf = _bf16np()
    c = {}
    eye = np.eye(128, dtype=np.float64)
    c["c_id_bf16"] = eye.astype(bf)
    c["c_id_f32"] = eye.astype(np.float32)
    c["c_ones_f32"] = np.ones((1, 128), dtype=np.float32)
    c["c_onesmat"] = np.ones((128, 128), dtype=bf)
    c["c_zero_row_f32"] = np.zeros((6, 128), dtype=np.float32)
    r = np.arange(128)[:, None]
    t = np.arange(128)[None, :]
    c["c_maskPT"] = np.where(r <= t, 0.0, BIG).astype(bf)
    c["c_maskTT"] = np.where(r < t, 0.0, BIG).astype(bf)
    c["c_maskTS"] = np.where(t < r, 0.0, BIG).astype(bf)
    R = np.zeros((128, 128), dtype=np.float64)
    for i in range(64):
        R[2 * i, 2 * i + 1] = -1.0
        R[2 * i + 1, 2 * i] = 1.0
    c["c_rotT_bf16"] = np.ascontiguousarray(R.T).astype(bf)
    p = np.arange(128)[:, None]
    j = np.arange(DEC_BAND)[None, :]
    c["c_dlog"] = (LNG * np.abs(p - (j - DEC_OFF)).astype(np.float64)).astype(np.float32)
    half = D // 2
    sv = (np.arange(0, D, 2, dtype=np.float64) + 0.4 * D) / (1.4 * D)
    power = np.arange(L, dtype=np.float64) / 512.0
    scale = sv[None, :] ** power[:, None]
    inv_freq = 1.0 / 10000.0 ** (np.arange(half, dtype=np.float64) / half)
    sinus = np.arange(L, dtype=np.float64)[:, None] * inv_freq[None, :]
    sin = np.repeat(np.sin(sinus), 2, axis=1)
    cos = np.repeat(np.cos(sinus), 2, axis=1)
    scl = np.repeat(scale, 2, axis=1)
    c["c_cos_q"] = np.ascontiguousarray((cos * scl).T).astype(np.float32)
    c["c_sin_q"] = np.ascontiguousarray((sin * scl).T).astype(np.float32)
    c["c_cos_k"] = np.ascontiguousarray((cos / scl).T).astype(np.float32)
    c["c_sin_k"] = np.ascontiguousarray((sin / scl).T).astype(np.float32)
    c["c_scal"] = np.broadcast_to(
        np.asarray(SCALAR_CONSTS, dtype=np.float32), (128, len(SCALAR_CONSTS))
    ).copy()
    return c


class _StopBuild(Exception):
    pass


def build_core(L, phase_limit=99):
    nc = bass.Bass("TRN2", target_bir_lowering=False)
    NT = L // 128
    NCH = L // C
    NKC = L // 512
    NB = L // 512

    def din(name, shape, dtype=F32):
        return nc.dram_tensor(name, list(shape), dtype, kind="ExternalInput").ap()

    x_in = din("x", (L, D))
    Wq = din("Wq", (D, KD)); Wk = din("Wk", (D, KD)); Wv = din("Wv", (D, VD))
    Wba = din("Wba", (D, 16))
    A_log = din("A_log", (NH, 1))
    dt_bias = din("dt_bias", (NH, 1))
    convq = din("convq", (128, 24))
    convk = din("convk", (128, 24))
    convv = din("convv", (128, 48))
    Wg = din("Wg", (D, VD))
    nw_b = din("nw_b", (128, VD))
    Wo = din("Wo", (VD, D))
    w1 = din("w1", (HFF, D))
    b1c = din("b1c", (128, 32))
    w2 = din("w2", (D, HFF))
    b2r = din("b2r", (1, D))
    WQ1 = din("WQ1", (D, D)); WK1 = din("WK1", (D, D)); WV1 = din("WV1", (D, D))
    wf = din("wf", (D, D))
    bf_b = din("bf_b", (128, D))

    cid_bf = din("c_id_bf16", (128, 128), BF16)
    cid_f32 = din("c_id_f32", (128, 128), F32)
    onesr = din("c_ones_f32", (1, 128), F32)
    onesmat = din("c_onesmat", (128, 128), BF16)
    zrow6 = din("c_zero_row_f32", (6, 128), F32)
    maskPT = din("c_maskPT", (128, 128), BF16)
    maskTT = din("c_maskTT", (128, 128), BF16)
    maskTS = din("c_maskTS", (128, 128), BF16)
    rotT = din("c_rotT_bf16", (128, 128), BF16)
    dlog = din("c_dlog", (128, DEC_BAND), F32)
    ctq = din("c_cos_q", (D, L)); stq = din("c_sin_q", (D, L))
    ctk = din("c_cos_k", (D, L)); stk = din("c_sin_k", (D, L))

    out = nc.dram_tensor("out", [L, D], F32, kind="ExternalOutput").ap()

    hxn0 = nc.dram_tensor("hxn0", [8, 128, L], BF16).ap()
    hq = nc.dram_tensor("hq", [NH, NCH, 128, 128], BF16).ap()
    hk = nc.dram_tensor("hk", [NH, NCH, 128, 128], BF16).ap()
    hv = nc.dram_tensor("hv", [2 * NH, NCH, 128, 128], BF16).ap()
    hgsn = nc.dram_tensor("hgsn", [NCH, 128, VD], BF16).ap()
    hy = nc.dram_tensor("hy", [NT, 128, D], BF16).ap()
    hx1 = nc.dram_tensor("hx1", [NT, 128, D], F32).ap()
    hxn1 = nc.dram_tensor("hxn1", [8, 128, L], BF16).ap()
    hxq = nc.dram_tensor("hxq", [8, 128, L], BF16).ap()
    hq1 = nc.dram_tensor("hq1", [NT, 8, 128, 128], BF16).ap()
    hf1 = nc.dram_tensor("hf1", [NT, 128, D], F16).ap()

    sch = Sched()
    ctx = []

    def sbuf(name, shape, dtype):
        g = nc.sbuf_tensor(name, list(shape), dtype)
        h = g.__enter__()
        ctx.append(g)
        return h

    def psumt(name, shape, dtype=F32):
        g = nc.psum_tensor(name, list(shape), dtype)
        h = g.__enter__()
        ctx.append(g)
        return h

    gp = [psumt(f"gp{i}", (128, 512)) for i in range(5)]
    tbank = [psumt("tbA", (128, 1024), BF16), psumt("tbB", (128, 1024), BF16)]
    tf = psumt("tf", (128, 512))

    AB = sbuf("AB", (128, 32768), BF16)
    AB2 = sbuf("AB2", (128, 32768), BF16)
    AFX = sbuf("AFX", (128, 5120), F32)
    xtile = AFX[:, 0:1024]
    xntm = AFX[:, 1024:2048]
    qA = AFX[:, 2048:3072]
    qB = AFX[:, 3072:4096]
    qCs = AFX[:, 4096:5120]
    srow = AFX[:, 0:4096]
    sdec = AFX[:, 4096:4608]
    SROW_W = ["srow", "xtile", "xntm", "rpA", "rpB"]
    SDEC_W = ["sdec", "rpC"]

    q32 = sbuf("q32", (128, 1024), F32)
    qD = sbuf("qD", (128, 1024), F32)
    sbf = sbuf("sbf", (128, 1024), BF16)
    f16b = sbuf("f16b", (128, 1024), F16)
    lncols = sbuf("lncols", (128, 4), F32)
    q_bf = sbuf("q_bf", (128, 1024), BF16)
    qI = sbuf("qI", (128, 512), mybir.dt.int32)

    xnfm_bf = sbf
    rows = (qA, "rpA")
    pre_sb = qD[:, 0:516]
    cacc = q32[:, 0:512]
    dlt = q32[:, 0:512]
    trigc = qCs[:, 0:512]
    trigs = qCs[:, 512:1024]
    a2row = qA[0:1, 0:1024]
    bzrow = qB[0:1, 0:1024]
    b2row = qCs[0:1, 0:1024]
    bcrow_t = qCs[0:1, 0:512]

    sb_id_bf = sbuf("id_bf", (128, 128), BF16)
    sb_id_f32 = sbuf("id_f32", (128, 128), F32)
    sb_onesr = sbuf("onesr", (1, 128), F32)
    sb_onesmat = sbuf("onesmat", (128, 128), BF16)
    sb_maskPT = sbuf("maskPT", (128, 128), BF16)
    sb_maskTT = sbuf("maskTT", (128, 128), BF16)
    sb_maskTS = sbuf("maskTS", (128, 128), BF16)
    sb_rotT = sbuf("rotT", (128, 128), BF16)
    sb_zrow = sbuf("zrow", (6, 128), F32)

    def dmat(queue, dst, src, reads=(), writes=(), extra=(), slow=False):
        if dst.dtype != src.dtype:
            queue = "pool"
        return sch.task(queue, lambda e, d=dst, s=src, sl=slow: e.dma_start(
            out=d, in_=s, allow_slow_non_contiguous=sl),
            reads=reads, writes=writes, dma=True, extra=extra)

    cscal_in = din("c_scal", (128, len(SCALAR_CONSTS)), F32)
    sb_cscal = sbuf("cscal", (128, len(SCALAR_CONSTS)), F32)
    dmat("sp", sb_cscal[:], cscal_in[:], writes=["cscal"])
    for i, v in enumerate(SCALAR_CONSTS):
        nc.const_aps.register(F32, float(v), sb_cscal[:, i:i + 1])

    for dst, src, key in ((sb_id_bf, cid_bf, "idbf"), (sb_id_f32, cid_f32, "idf"),
                          (sb_onesr, onesr, "onesr"), (sb_onesmat, onesmat, "onesm"),
                          (sb_maskPT, maskPT, "mPT"), (sb_maskTT, maskTT, "mTT"),
                          (sb_maskTS, maskTS, "mTS"), (sb_rotT, rotT, "rotT"),
                          (sb_zrow, zrow6, "zrow")):
        dmat("sp", dst[:], src[:], writes=[key])

    tbrr = [0]

    def tb_slot():
        i = tbrr[0] % 8
        tbrr[0] += 1
        b = i % 2
        j = i // 2
        return tbank[b][:, j * 128:(j + 1) * 128], f"tb{i}", f"tbank{b}"

    tfrr = [0]

    def tf_slot():
        i = tfrr[0] % 4
        tfrr[0] += 1
        return tf[:, i * 128:(i + 1) * 128], f"tf{'ABCD'[i]}", "tfbank"

    gprr = [0]

    def gp_slot():
        i = gprr[0] % 5
        gprr[0] += 1
        return gp[i], f"gp{i}"

    def round_pipe(src, dst, kin, kout, mant_bits, min_p, clip=None,
                   tie_even=False):
        A = qA[:, 0:1024]; B = qB[:, 0:1024]; Cc = qCs[:, 0:1024]; Dd = qD[:]
        KA, KB, KC, KD_ = "rpA", "rpB", "rpC", "qD"
        lo = float(min_p + 64)
        bias_si = float(LN2 * (mant_bits + 64))
        bias_s = float(-LN2 * (mant_bits + 64))
        sch.task("act", lambda e: e.activation(out=A, in_=src, func=AF.Abs),
                 reads=[kin], writes=[KA])
        if clip is not None:
            sch.task("dve", lambda e: e.tensor_scalar(
                out=A, in0=A, scalar1=clip, scalar2=None, op0=AL.min),
                reads=[KA], writes=[KA])
        sch.task("dve", lambda e: e.tensor_scalar(
            out=B, in0=A, scalar1=2.0 ** -40, scalar2=None, op0=AL.max),
            reads=[KA], writes=[KB])
        sch.task("act", lambda e: e.activation(out=Cc, in_=B, func=AF.Ln),
                 reads=[KB], writes=[KC])
        sch.task("dve", lambda e: e.tensor_scalar(
            out=Cc, in0=Cc, scalar1=INVLN2, scalar2=64.0, op0=AL.mult, op1=AL.add),
            reads=[KC], writes=[KC])
        sch.task("dve", lambda e: e.tensor_scalar(
            out=Cc, in0=Cc, scalar1=lo, scalar2=-0.499, op0=AL.max, op1=AL.add),
            reads=[KC], writes=[KC])
        for hh_ in range(2):
            sch.task("pool", lambda e, hh_=hh_: e.tensor_copy(
                out=qI[:], in_=Cc[:, hh_ * 512:(hh_ + 1) * 512]),
                reads=[KC], writes=["qI"])
            sch.task("pool", lambda e, hh_=hh_: e.tensor_copy(
                out=Cc[:, hh_ * 512:(hh_ + 1) * 512], in_=qI[:]),
                reads=["qI"], writes=[KC])
        sch.task("act", lambda e: e.activation(out=B, in_=Cc, func=AF.Exp,
                                               scale=-LN2, bias=bias_si),
                 reads=[KC], writes=[KB])
        sch.task("pool", lambda e: e.tensor_tensor(out=Dd, in0=A, in1=B, op=AL.mult),
                 reads=[KA, KB], writes=[KD_])
        for hh_ in range(2):
            sch.task("pool", lambda e, hh_=hh_: e.tensor_copy(
                out=qI[:], in_=Dd[:, hh_ * 512:(hh_ + 1) * 512]),
                reads=[KD_], writes=["qI"])
            sch.task("pool", lambda e, hh_=hh_: e.tensor_copy(
                out=Dd[:, hh_ * 512:(hh_ + 1) * 512], in_=qI[:]),
                reads=["qI"], writes=[KD_])
        sch.task("act", lambda e: e.activation(out=B, in_=Cc, func=AF.Exp,
                                               scale=LN2, bias=bias_s),
                 reads=[KC], writes=[KB])
        sch.task("pool", lambda e: e.tensor_tensor(out=Dd, in0=Dd, in1=B, op=AL.mult),
                 reads=[KD_, KB], writes=[KD_])
        sch.task("act", lambda e: e.activation(out=A, in_=src, func=AF.Sign),
                 reads=[kin], writes=[KA])
        sch.task("dve", lambda e: e.tensor_tensor(out=dst, in0=Dd, in1=A,
                                                  op=AL.mult),
                 reads=[KD_, KA], writes=[kout])

    def ln_tile():
        sch.task("dve", lambda e: e.reduce_sum(lncols[:, 0:1], xtile, axis=AX.X),
                 reads=["xtile"], writes=["lnc"])
        sch.task("dve", lambda e: e.tensor_scalar(
            out=lncols[:, 1:2], in0=lncols[:, 0:1], scalar1=1.0 / D, scalar2=None,
            op0=AL.mult), reads=["lnc"], writes=["lnc"])
        sch.task("dve", lambda e: e.tensor_scalar(
            out=xntm, in0=xtile, scalar1=lncols[:, 1:2], scalar2=None,
            op0=AL.subtract), reads=["xtile", "lnc"], writes=["xntm"])
        sch.task("dve", lambda e: e.scalar_tensor_tensor(
            out=qD[:], in0=xntm, scalar=1.0, in1=xntm, op0=AL.bypass,
            op1=AL.mult, accum_out=lncols[:, 2:3]),
            reads=["xntm"], writes=["qD", "lnc"])
        sch.task("act", lambda e: e.activation(
            out=lncols[:, 3:4], in_=lncols[:, 2:3], func=AF.Sqrt,
            bias=1e-5, scale=1.0 / D), reads=["lnc"], writes=["lnc3"])
        sch.task("dve", lambda e: e.reciprocal(out=lncols[:, 2:3],
                                               in_=lncols[:, 3:4]),
                 reads=["lnc3"], writes=["lnc"])
        sch.task("dve", lambda e: e.tensor_scalar(
            out=xntm, in0=xntm, scalar1=lncols[:, 2:3], scalar2=None, op0=AL.mult),
            reads=["lnc"], writes=["xntm"])

    try:
        for tt in range(NT):
            dmat("sp", xtile, x_in[tt * 128:(tt + 1) * 128, :], writes=["xtile"])
            ln_tile()
            for dc in range(8):
                sch.task("pe", lambda e, dc=dc: e.transpose(
                    tf[:, 0:128], xntm[:, dc * 128:(dc + 1) * 128], sb_id_f32[:]),
                    reads=["xntm", "idf"], writes=["tfA", "tfbank"])
                sch.task("dve", lambda e, dc=dc: e.tensor_copy(
                    out=xnfm_bf[:, dc * 128:(dc + 1) * 128], in_=tf[:, 0:128]),
                    reads=["tfA"], writes=["sbf"])
            for dc in range(8):
                dmat("sp", hxn0[dc, :, tt * 128:(tt + 1) * 128],
                     xnfm_bf[:, dc * 128:(dc + 1) * 128],
                     reads=["sbf"], writes=["hxn0"])

        def maybe_stop(n):
            if phase_limit <= n:
                raise _StopBuild()

        maybe_stop(1)
        NCOL = NCH * NH
        assert NCOL <= 512
        NG = 8
        dcols = sbuf("dcols", (128, NG * NCOL), F32)
        G_B, G_BLNB, G_BETA, G_GAM, G_NEGW, G_C2, G_GC, G_CQ = range(NG)

        def dcol(g, c, h):
            return dcols[:, g * NCOL + c * NH + h:g * NCOL + c * NH + h + 1]

        def dgrp(g):
            return dcols[:, g * NCOL:(g + 1) * NCOL]

        bR = qA[0:6, 0:512]
        betaR = qB[0:6, 0:512]
        lnbR = qB[0:6, 512:1024]
        gR = qCs[0:6, 0:512]
        wba_sb = sbuf("wba_sb", (128, 128), BF16)
        dcol_s = sbuf("dcol_s", (NH, 2), F32)
        negA = sbuf("negA", (NH, 1), F32)
        xs_mov = sbf[:, 0:1024]

        for dc in range(8):
            dmat("sp", wba_sb[:, dc * 16:dc * 16 + 16], Wba[dc * 128:(dc + 1) * 128, :],
                 writes=["wba"])
        maybe_stop(1.02)
        dmat("sp", dcol_s[:, 0:1], A_log[:, :], writes=["dc0"])
        dmat("sp", dcol_s[:, 1:2], dt_bias[:, :], writes=["dc1"])
        maybe_stop(1.03)
        sch.task("act", lambda e: e.activation(out=negA[:], in_=dcol_s[:, 0:1],
                                               func=AF.Exp),
                 reads=["dc0"], writes=["negA"])
        maybe_stop(1.04)
        sch.task("dve", lambda e: e.tensor_scalar(
            out=negA[:], in0=negA[:], scalar1=-1.0, scalar2=None, op0=AL.mult),
            reads=["negA"], writes=["negA"])
        maybe_stop(1.05)

        for tb8 in range(NB):
            if tb8 > 0:
                maybe_stop(1.15)
            t0 = tb8 * 512
            for dc in range(8):
                dmat("sp", xs_mov[:, 0:512], hxn0[dc, :, t0:t0 + 512],
                     reads=["hxn0"], writes=["sbf"])
                sch.task("pe", lambda e, dc=dc: e.matmul(
                    gp[0][0:6, 0:512], wba_sb[:, dc * 16:dc * 16 + 6],
                    xs_mov[:, 0:512], start=(dc == 0), stop=(dc == 7)),
                    reads=["wba", "sbf"], writes=["gp0"])
                sch.task("pe", lambda e, dc=dc: e.matmul(
                    gp[1][0:6, 0:512], wba_sb[:, dc * 16 + 6:dc * 16 + 12],
                    xs_mov[:, 0:512], start=(dc == 0), stop=(dc == 7)),
                    reads=["wba", "sbf"], writes=["gp1"])
            maybe_stop(1.08)
            sch.task("act", lambda e: e.activation(
                out=betaR, in_=gp[0][0:6, 0:512], func=AF.Sigmoid),
                reads=["gp0"], writes=["rpB"])
            maybe_stop(1.09)
            sch.task("act", lambda e: e.activation(
                out=gR, in_=gp[1][0:6, 0:512], func=AF.Exp, bias=dcol_s[:, 1:2]),
                reads=["gp1", "dc1"], writes=["rpC"])
            maybe_stop(1.11)
            sch.task("act", lambda e: e.activation(
                out=gR, in_=gR, func=AF.Ln, bias=1.0),
                reads=["rpC"], writes=["rpC"])
            maybe_stop(1.12)
            sch.task("dve", lambda e: e.tensor_scalar(
                out=gR, in0=gR, scalar1=negA[:], scalar2=None, op0=AL.mult),
                reads=["rpC", "negA"], writes=["rpC"])
            maybe_stop(1.2)
            for ccl in range(4):
                u0 = ccl * C
                sch.task("dve", lambda e, u0=u0: e.tensor_tensor_scan(
                    out=qA[0:6, u0:u0 + C], data0=qCs[0:6, u0:u0 + C],
                    data1=sb_zrow[:, :], initial=0.0, op0=AL.add, op1=AL.add),
                    reads=["rpC", "zrow"], writes=["rpA"])
            maybe_stop(1.3)
            sch.task("act", lambda e: e.activation(
                out=lnbR, in_=betaR, func=AF.Ln),
                reads=["rpB"], writes=["rpB"])
            sch.task("dve", lambda e: e.tensor_tensor(
                out=gR, in0=lnbR, in1=bR, op=AL.add),
                reads=["rpB", "rpA", "rpC"], writes=["rpC"])
            for ccl in range(4):
                cc = tb8 * 4 + ccl
                u0 = ccl * C
                for src_ap, kread, goff, kw in (
                        (qA[0:6, u0:u0 + C], "rpA", G_B, "dcB"),
                        (qB[0:6, u0:u0 + C], "rpB", G_BETA, "dcBE"),
                        (qCs[0:6, u0:u0 + C], "rpC", G_BLNB, "dcBL")):
                    sch.task("pe", lambda e, src_ap=src_ap: e.transpose(
                        tf[:, 0:6], src_ap, sb_id_f32[0:6, 0:6]),
                        reads=[kread, "idf"],
                        writes=["tfA", "tfB", "tfC", "tfD", "tfbank"])
                    sch.task("dve", lambda e, cc=cc, goff=goff: e.tensor_copy(
                        out=dcols[:, goff * NCOL + cc * NH:goff * NCOL + cc * NH + 6],
                        in_=tf[:, 0:6]), reads=["tfA", "tfbank"], writes=[kw])

        maybe_stop(1.5)
        sch.task("act", lambda e: e.activation(out=dgrp(G_GAM), in_=dgrp(G_B),
                                               func=AF.Exp),
                 reads=["dcB"], writes=["dcG"])
        sch.task("dve", lambda e: e.scalar_tensor_tensor(
            out=dgrp(G_NEGW), in0=dgrp(G_BETA), scalar=-1.0, in1=dgrp(G_GAM),
            op0=AL.mult, op1=AL.mult), reads=["dcBE", "dcG"], writes=["dcNW"])
        maybe_stop(1.7)
        dmat("pool", bcrow_t[0:1, 0:NCOL],
             dcols[127:128, G_B * NCOL:(G_B + 1) * NCOL],
             reads=["dcB"], writes=["rpC"])
        maybe_stop(1.75)
        sch.task("pe", lambda e: e.matmul(
            tf[:, 0:NCOL], sb_onesr[:], bcrow_t[0:1, 0:NCOL], start=True, stop=True),
            reads=["rpC", "onesr"],
            writes=["tfA", "tfB", "tfC", "tfD", "tfbank"])
        maybe_stop(1.8)
        sch.task("act", lambda e: e.activation(out=dgrp(G_GC), in_=tf[:, 0:NCOL],
                                               func=AF.Exp),
                 reads=["tfA", "tfbank"], writes=["dcGC"])
        maybe_stop(1.85)
        sch.task("dve", lambda e: e.tensor_tensor(
            out=dgrp(G_C2), in0=tf[:, 0:NCOL], in1=dgrp(G_B), op=AL.subtract),
            reads=["tfA", "tfbank", "dcB"], writes=["dcC2"])
        maybe_stop(1.9)
        sch.task("act", lambda e: e.activation(out=dgrp(G_C2), in_=dgrp(G_C2),
                                               func=AF.Exp),
                 reads=["dcC2"], writes=["dcC2"])

        maybe_stop(2)
        wg_sb = AB2[:, 0:8 * VD]
        nw_sb = sbuf("nw_sb", (128, VD), BF16)
        xnb2 = sbuf("xstat", (128, 2048), BF16)
        xstat = xnb2[:, 0:1024]
        gtile = sbf[:, 0:512]
        for dc in range(8):
            dmat("sp", wg_sb[:, dc * VD:(dc + 1) * VD], Wg[dc * 128:(dc + 1) * 128, :],
                 writes=["wg"])
        dmat("sp", nw_sb[:], nw_b[:, :], writes=["nw"])
        for tt in range(NT):
            for dc in range(8):
                dmat("sp", xstat[:, dc * 128:(dc + 1) * 128],
                     hxn0[dc, :, tt * 128:(tt + 1) * 128],
                     reads=["hxn0"], writes=["xstat"])
            for half in range(3):
                for dc in range(8):
                    sch.task("pe", lambda e, dc=dc, half=half: e.matmul(
                        gp[half][:, 0:512], xstat[:, dc * 128:(dc + 1) * 128],
                        wg_sb[:, dc * VD + half * 512:dc * VD + (half + 1) * 512],
                        start=(dc == 0), stop=(dc == 7)),
                        reads=["xstat", "wg"], writes=[f"gp{half}"])
                sch.task("act", lambda e, half=half: e.activation(
                    out=gtile[:, 0:512], in_=gp[half][:, 0:512],
                    func=AF.Silu), reads=[f"gp{half}"], writes=["sbf"])
                sch.task("dve", lambda e, half=half: e.tensor_tensor(
                    out=gtile[:, 0:512], in0=gtile[:, 0:512],
                    in1=nw_sb[:, half * 512:(half + 1) * 512] if half < 2
                    else nw_sb[:, 1024:1536], op=AL.mult),
                    reads=["sbf", "nw"], writes=["sbf"])
                dmat("sp", hgsn[tt][:, half * 512:(half + 1) * 512], gtile[:, 0:512],
                     reads=["sbf"], writes=["hgsn"])

        maybe_stop(3)
        WSTR = KD + KD + VD
        wqkv = AB2[:, 0:8 * WSTR]
        for dc in range(8):
            b0 = dc * WSTR
            dmat("sp", wqkv[:, b0:b0 + KD], Wq[dc * 128:(dc + 1) * 128, :],
                 writes=["wqkv", "wg"])
            dmat("sp", wqkv[:, b0 + KD:b0 + 2 * KD], Wk[dc * 128:(dc + 1) * 128, :],
                 writes=["wqkv"])
            dmat("sp", wqkv[:, b0 + 2 * KD:b0 + WSTR], Wv[dc * 128:(dc + 1) * 128, :],
                 writes=["wqkv"])
        taps_sb = sbuf("taps_sb", (128, 96), F32)
        dmat("sp", taps_sb[:, 0:24], convq[:, :], writes=["taps"])
        dmat("sp", taps_sb[:, 24:48], convk[:, :], writes=["taps"])
        dmat("sp", taps_sb[:, 48:96], convv[:, :], writes=["taps"])

        xblk = AB[:, 0:4096]
        tails = sbuf("tails", (128, 72), F32)
        silu_bf = sbf[:, 0:512]

        sch.task("dve", lambda e: e.memset(tails[:], 0.0), writes=["tails"])

        cchunks = []
        for h in range(NH):
            cchunks.append((h * 128, h * 4, hq, h))
        for h in range(NH):
            cchunks.append((KD + h * 128, 24 + h * 4, hk, h))
        for j in range(2 * NH):
            cchunks.append((2 * KD + j * 128, 48 + j * 4, hv, j))

        for tb8 in range(NB):
            t0 = tb8 * 512
            for dc in range(8):
                dmat("sp", xblk[:, dc * 512:(dc + 1) * 512], hxn0[dc, :, t0:t0 + 512],
                     reads=["hxn0"], writes=["xblk"])
            for ci in range(24):
                coff, tap0, dram, slot = cchunks[ci]
                bank = gp[ci % 2]
                kb = f"gp{ci % 2}"
                for dc in range(8):
                    sch.task("pe", lambda e, dc=dc, coff=coff, bank=bank: e.matmul(
                        bank[:, 0:512], wqkv[:, dc * WSTR + coff:dc * WSTR + coff + 128],
                        xblk[:, dc * 512:(dc + 1) * 512],
                        start=(dc == 0), stop=(dc == 7)),
                        reads=["wqkv", "xblk"], writes=[kb])
                sch.task("dve", lambda e, bank=bank: e.tensor_copy(
                    out=pre_sb[:, 3:515], in_=bank[:, 0:512]),
                    reads=[kb], writes=["qD"])
                sch.task("dve", lambda e, ci=ci: e.tensor_copy(
                    out=pre_sb[:, 0:3], in_=tails[:, ci * 3:ci * 3 + 3]),
                    reads=["tails"], writes=["qD"])
                sch.task("dve", lambda e, ci=ci: e.tensor_copy(
                    out=tails[:, ci * 3:ci * 3 + 3], in_=pre_sb[:, 512:515]),
                    reads=["qD"], writes=["tails"])
                sch.task("dve", lambda e, tap0=tap0: e.tensor_scalar(
                    out=cacc[:, 0:512], in0=pre_sb[:, 0:512],
                    scalar1=taps_sb[:, tap0:tap0 + 1],
                    scalar2=None, op0=AL.mult),
                    reads=["qD", "taps"], writes=["q32"])
                for jj in (1, 2, 3):
                    sch.task("dve", lambda e, tap0=tap0, jj=jj: e.scalar_tensor_tensor(
                        out=cacc[:, 0:512], in0=pre_sb[:, jj:jj + 512],
                        scalar=taps_sb[:, tap0 + jj:tap0 + jj + 1], in1=cacc[:, 0:512],
                        op0=AL.mult, op1=AL.add),
                        reads=["qD", "taps", "q32"], writes=["q32"])
                sch.task("act", lambda e: e.activation(out=silu_bf[:, 0:512],
                                                       in_=cacc[:, 0:512],
                                                       func=AF.Silu),
                         reads=["q32"], writes=["sbf"])
                for j in range(4):
                    cix = t0 // 128 + j
                    dmat("sp", dram[slot, cix], silu_bf[:, j * 128:(j + 1) * 128],
                         reads=["sbf"], writes=["hqkv"])

        maybe_stop(4)
        wo_sb = AB2[:, 0:12 * 1024]
        for vc in range(12):
            dmat("sp", wo_sb[:, vc * 1024:(vc + 1) * 1024],
                 Wo[vc * 128:(vc + 1) * 128, :], writes=["wo", "wqkv"])

        S_sb = sbuf("S_sb", (128, NH * DV), F32)
        S_bf = sbuf("S_bf", (128, NH * DV), BF16)
        for h in range(NH):
            sch.task("dve", lambda e, h=h: e.memset(S_sb[:, h * 256:(h + 1) * 256], 0.0),
                     writes=[f"S{h}"])
            sch.task("dve", lambda e, h=h: e.memset(S_bf[:, h * 256:(h + 1) * 256], 0.0),
                     writes=[f"Sbf{h}"])

        HSTR = 3584
        def hb(h, off, n):
            b0 = 4096 + h * HSTR + off
            return AB[:, b0:b0 + n]
        ktm_h = lambda h: hb(h, 0, 128)
        k2tm_h = lambda h: hb(h, 128, 128)
        khf_h = lambda h: hb(h, 256, 128)
        PT_h = lambda h: hb(h, 384, 128)
        TSb_h = lambda h: hb(h, 512, 128)
        TTp_h = lambda h, j: hb(h, 640 + j * 128, 128)
        TSP_h = lambda h, j: hb(h, 896 + j * 128, 128)
        MT_h = lambda h: hb(h, 1664, 128)
        vtm_h = lambda h: hb(h, 1792, 256)
        bv_h = lambda h: hb(h, 2048, 256)
        ru_h = lambda h: hb(h, 2304, 256)
        u_h = lambda h: hb(h, 2560, 256)
        expm_h = lambda h: hb(h, 2816, 384)
        qf_h = lambda h: hb(h, 3200, 128)
        kf_h = lambda h: hb(h, 3328, 128)
        vfm_h = lambda h: hb(h, 3456, 128)
        osc_h = lambda h: AFX[:, 2048 + h * 256:2048 + (h + 1) * 256]
        o1s_h = lambda h: AFX[:, 3584 + h * 256:3584 + (h + 1) * 256]
        ccb_h = lambda h: q32[:, h * 128:(h + 1) * 128]
        ein_h = lambda h: qD[:, h * 128:(h + 1) * 128]
        gsn_sb = sbuf("gsn_sb", (128, VD), BF16)
        o2fm = sbuf("o2fm", (128, 12 * 128), BF16)
        jnk = sbuf("jnk", (128, 128), BF16)
        sso = sbuf("sso", (128, 24), F32)
        fence5 = sch.task("sp", lambda e: e.nop(),
                          writes=["rpB", "rpA", "rpC", "q32", "qD", "xblk"])
        HS = range(NH)

        for cc in range(NCH):
            dmat("sp", gsn_sb[:], hgsn[cc], reads=["hgsn"], writes=["gsn"])
            for h in HS:
                dmat("sp", qf_h(h), hq[h, cc], reads=["hqkv"],
                     writes=[f"qf{h}"], extra=[fence5])
                dmat("sp", kf_h(h), hk[h, cc], reads=["hqkv"],
                     writes=[f"kf{h}"], extra=[fence5])
            for h in HS:
                tq, ktq, bkq = tb_slot()
                sch.task("pe", lambda e, h=h, tq=tq: e.transpose(
                    tq, qf_h(h), sb_id_bf[:]), reads=[f"qf{h}", "idbf"],
                    writes=[ktq, bkq])
                sch.task("act", lambda e, h=h, tq=tq: e.activation(
                    out=jnk[:], in_=tq, func=AF.Square,
                    accum_out=sso[:, h * 4:h * 4 + 1]),
                    reads=[ktq, bkq], writes=["jnk", f"ssoq{h}"])
                sch.task("act", lambda e, h=h, cc=cc: e.activation(
                    out=dcol(G_CQ, cc, h), in_=sso[:, h * 4:h * 4 + 1],
                    func=AF.Sqrt, scale=128.0, bias=128.0e-12),
                    reads=[f"ssoq{h}"], writes=[f"dcCQ{h}"])
                sch.task("dve", lambda e, h=h, cc=cc: e.reciprocal(
                    out=dcol(G_CQ, cc, h), in_=dcol(G_CQ, cc, h)),
                    reads=[f"dcCQ{h}"], writes=[f"dcCQ{h}"])
            for h in HS:
                tk, ktk, bkk = tb_slot()
                sch.task("pe", lambda e, h=h, tk=tk: e.transpose(
                    tk, kf_h(h), sb_id_bf[:]), reads=[f"kf{h}", "idbf"],
                    writes=[ktk, bkk])
                sch.task("act", lambda e, h=h, tk=tk: e.activation(
                    out=jnk[:], in_=tk, func=AF.Square,
                    accum_out=sso[:, h * 4 + 1:h * 4 + 2]),
                    reads=[ktk, bkk], writes=["jnk", f"ssok{h}"])
                sch.task("act", lambda e, h=h: e.activation(
                    out=sso[:, h * 4 + 2:h * 4 + 3],
                    in_=sso[:, h * 4 + 1:h * 4 + 2], func=AF.Sqrt, bias=1e-12),
                    reads=[f"ssok{h}"], writes=[f"dcIK{h}"])
                sch.task("dve", lambda e, h=h: e.reciprocal(
                    out=sso[:, h * 4 + 2:h * 4 + 3],
                    in_=sso[:, h * 4 + 2:h * 4 + 3]),
                    reads=[f"dcIK{h}"], writes=[f"dcIK{h}"])
                sch.task("dve", lambda e, h=h, tk=tk: e.tensor_scalar(
                    out=ktm_h(h), in0=tk, scalar1=sso[:, h * 4 + 2:h * 4 + 3],
                    scalar2=None, op0=AL.mult),
                    reads=[ktk, bkk, f"dcIK{h}"], writes=[f"ktm{h}"])
                sch.task("dve", lambda e, h=h, cc=cc: e.tensor_scalar(
                    out=k2tm_h(h), in0=ktm_h(h), scalar1=dcol(G_C2, cc, h),
                    scalar2=None, op0=AL.mult),
                    reads=[f"ktm{h}", "dcC2"], writes=[f"k2tm{h}"])
                tk2, ktk2, bkk2 = tb_slot()
                sch.task("pe", lambda e, h=h, tk2=tk2: e.transpose(
                    tk2, ktm_h(h), sb_id_bf[:]), reads=[f"ktm{h}", "idbf"],
                    writes=[ktk2, bkk2])
                sch.task("dve", lambda e, h=h, tk2=tk2: e.tensor_copy(
                    out=khf_h(h), in_=tk2), reads=[ktk2, bkk2],
                    writes=[f"khf{h}"])
            for h in HS:
                for half in range(2):
                    dmat("sp", vfm_h(h), hv[2 * h + half, cc],
                         reads=["hqkv"], writes=[f"vf{h}"], extra=[fence5])
                    tv, ktv, bkv = tb_slot()
                    sch.task("pe", lambda e, h=h, tv=tv: e.transpose(
                        tv, vfm_h(h), sb_id_bf[:]), reads=[f"vf{h}", "idbf"],
                        writes=[ktv, bkv])
                    sch.task("dve", lambda e, h=h, half=half, tv=tv: e.tensor_copy(
                        out=vtm_h(h)[:, half * 128:(half + 1) * 128], in_=tv),
                        reads=[ktv, bkv], writes=[f"vtm{h}"])
                sch.task("dve", lambda e, h=h, cc=cc: e.tensor_scalar(
                    out=bv_h(h), in0=vtm_h(h), scalar1=dcol(G_BETA, cc, h),
                    scalar2=None, op0=AL.mult),
                    reads=[f"vtm{h}", "dcBE"], writes=[f"bv{h}"])
            for h in HS:
                tfa, ktfa, bfa = tf_slot()
                sch.task("dve", lambda e, h=h, cc=cc: e.tensor_scalar(
                    out=ccb_h(h), in0=sb_onesmat[:], scalar1=dcol(G_B, cc, h),
                    scalar2=None, op0=AL.mult),
                    reads=["onesm", "dcB"], writes=[f"ccb{h}"], extra=[fence5])
                sch.task("pe", lambda e, h=h, tfa=tfa: e.transpose(
                    tfa, ccb_h(h), sb_id_f32[:]), reads=[f"ccb{h}", "idf"],
                    writes=[ktfa, bfa])
                sch.task("dve", lambda e, h=h, cc=cc, tfa=tfa: e.scalar_tensor_tensor(
                    out=ein_h(h), in0=tfa, scalar=dcol(G_B, cc, h),
                    in1=sb_maskPT[:], op0=AL.subtract, op1=AL.subtract),
                    reads=[ktfa, bfa, "dcB", "mPT"], writes=[f"ein{h}"],
                    extra=[fence5])
                sch.task("act", lambda e, h=h: e.activation(
                    out=expm_h(h)[:, 0:128], in_=ein_h(h), func=AF.Exp),
                    reads=[f"ein{h}"], writes=[f"ePT{h}"])
                sch.task("dve", lambda e, h=h, cc=cc, tfa=tfa: e.scalar_tensor_tensor(
                    out=ein_h(h), in0=tfa, scalar=dcol(G_BLNB, cc, h),
                    in1=sb_maskTS[:], op0=AL.subtract, op1=AL.add),
                    reads=[ktfa, bfa, "dcBL", "mTS", f"ePT{h}"], writes=[f"ein{h}"])
                sch.task("act", lambda e, h=h: e.activation(
                    out=expm_h(h)[:, 256:384], in_=ein_h(h), func=AF.Exp,
                    scale=-1.0), reads=[f"ein{h}"], writes=[f"eTS{h}"])
                tfb, ktfb, bfb2 = tf_slot()
                sch.task("dve", lambda e, h=h, cc=cc: e.tensor_scalar(
                    out=ccb_h(h), in0=sb_onesmat[:], scalar1=dcol(G_BLNB, cc, h),
                    scalar2=None, op0=AL.mult),
                    reads=["onesm", "dcBL", f"ein{h}"], writes=[f"ccb{h}"])
                sch.task("pe", lambda e, h=h, tfb=tfb: e.transpose(
                    tfb, ccb_h(h), sb_id_f32[:]), reads=[f"ccb{h}", "idf"],
                    writes=[ktfb, bfb2])
                sch.task("dve", lambda e, h=h, cc=cc, tfb=tfb: e.scalar_tensor_tensor(
                    out=ein_h(h), in0=tfb, scalar=dcol(G_B, cc, h),
                    in1=sb_maskTT[:], op0=AL.subtract, op1=AL.subtract),
                    reads=[ktfb, bfb2, "dcB", "mTT", f"eTS{h}"], writes=[f"ein{h}"])
                sch.task("act", lambda e, h=h: e.activation(
                    out=expm_h(h)[:, 128:256], in_=ein_h(h), func=AF.Exp),
                    reads=[f"ein{h}"], writes=[f"eTT{h}"])
            for h in HS:
                tg, ktg = gp_slot()
                sch.task("pe", lambda e, h=h, tg=tg: e.matmul(
                    tg[:, 0:128], khf_h(h), khf_h(h), start=True, stop=True),
                    reads=[f"khf{h}"], writes=[ktg])
                sch.task("dve", lambda e, h=h, tg=tg: e.tensor_tensor(
                    out=TTp_h(h, 0), in0=tg[:, 0:128], in1=expm_h(h)[:, 128:256],
                    op=AL.mult), reads=[ktg, f"eTT{h}"], writes=[f"TT{h}_0"])
                sch.task("dve", lambda e, h=h, tg=tg: e.tensor_tensor(
                    out=TSb_h(h), in0=tg[:, 0:128], in1=expm_h(h)[:, 256:384],
                    op=AL.mult), reads=[ktg, f"eTS{h}"], writes=[f"TSb{h}"])
                tg2, ktg2 = gp_slot()
                sch.task("pe", lambda e, h=h, tg2=tg2: e.matmul(
                    tg2[:, 0:128], khf_h(h), qf_h(h), start=True, stop=True),
                    reads=[f"khf{h}", f"qf{h}"], writes=[ktg2])
                sch.task("dve", lambda e, h=h, tg2=tg2: e.tensor_tensor(
                    out=PT_h(h), in0=tg2[:, 0:128], in1=expm_h(h)[:, 0:128],
                    op=AL.mult), reads=[ktg2, f"ePT{h}"], writes=[f"PT{h}"])
                sch.task("dve", lambda e, h=h: e.scalar_tensor_tensor(
                    out=MT_h(h), in0=TTp_h(h, 0), scalar=-1.0, in1=sb_id_bf[:],
                    op0=AL.mult, op1=AL.add),
                    reads=[f"TT{h}_0", "idbf"], writes=[f"MT{h}"])
            prevs = {h: (TSb_h(h), f"TSb{h}", TTp_h(h, 0), f"TT{h}_0") for h in HS}
            for j in range(6):
                for h in HS:
                    ts_p, kts_p, tt_p, ktt_p = prevs[h]
                    tp, ktp = gp_slot()
                    cur = TSP_h(h, j)
                    kcur = f"TSP{h}_{j}"
                    sch.task("pe", lambda e, tp=tp, a=tt_p, b=ts_p: e.matmul(
                        tp[:, 0:128], a, b, start=True, stop=True),
                        reads=[kts_p, ktt_p], writes=[ktp])
                    sch.task("dve", lambda e, tp=tp, cur=cur: e.tensor_copy(
                        out=cur, in_=tp[:, 0:128]), reads=[ktp], writes=[kcur])
                    if j < 5:
                        tp2, ktp2 = gp_slot()
                        curT = TTp_h(h, (j + 1) % 2)
                        kcurT = f"TT{h}_{(j + 1) % 2}"
                        sch.task("pe", lambda e, tp2=tp2, a=ts_p, b=tt_p: e.matmul(
                            tp2[:, 0:128], a, b, start=True, stop=True),
                            reads=[kts_p, ktt_p], writes=[ktp2])
                        sch.task("dve", lambda e, tp2=tp2, curT=curT: e.tensor_copy(
                            out=curT, in_=tp2[:, 0:128]), reads=[ktp2],
                            writes=[kcurT])
                        prevs[h] = (cur, kcur, curT, kcurT)
                    else:
                        prevs[h] = (cur, kcur, None, None)
            for j in range(6):
                for h in HS:
                    tsl = TSP_h(h, j)
                    ktsl = f"TSP{h}_{j}"
                    tp3, ktp3 = gp_slot()
                    sch.task("pe", lambda e, h=h, tp3=tp3, tsl=tsl: e.matmul(
                        tp3[:, 0:128], tsl, MT_h(h), start=True, stop=True),
                        reads=[ktsl, f"MT{h}"], writes=[ktp3])
                    sch.task("dve", lambda e, h=h, tp3=tp3: e.tensor_tensor(
                        out=MT_h(h), in0=MT_h(h), in1=tp3[:, 0:128], op=AL.add),
                        reads=[ktp3, f"MT{h}"], writes=[f"MT{h}"])
            for h in HS:
                if cc == 0:
                    sch.task("dve", lambda e, h=h: e.tensor_copy(
                        out=ru_h(h), in_=bv_h(h)),
                        reads=[f"bv{h}"], writes=[f"ru{h}"])
                else:
                    gb, kgb = gp_slot()
                    sch.task("pe", lambda e, h=h, gb=gb: e.matmul(
                        gb[:, 0:256], khf_h(h), S_bf[:, h * 256:(h + 1) * 256],
                        start=True, stop=True),
                        reads=[f"khf{h}", f"Sbf{h}"], writes=[kgb])
                    sch.task("dve", lambda e, h=h, cc=cc, gb=gb: e.scalar_tensor_tensor(
                        out=ru_h(h), in0=gb[:, 0:256], scalar=dcol(G_NEGW, cc, h),
                        in1=bv_h(h), op0=AL.mult, op1=AL.add),
                        reads=[kgb, "dcNW", f"bv{h}"], writes=[f"ru{h}"])
            for h in HS:
                gu, kgu = gp_slot()
                sch.task("pe", lambda e, h=h, gu=gu: e.matmul(
                    gu[:, 0:256], MT_h(h), ru_h(h), start=True, stop=True),
                    reads=[f"MT{h}", f"ru{h}"], writes=[kgu])
                sch.task("dve", lambda e, h=h, gu=gu: e.tensor_copy(
                    out=u_h(h), in_=gu[:, 0:256]), reads=[kgu], writes=[f"u{h}"])
            for h in HS:
                go1, kgo1 = gp_slot()
                sch.task("pe", lambda e, h=h, go1=go1: e.matmul(
                    go1[:, 0:256], PT_h(h), u_h(h), start=True, stop=True),
                    reads=[f"PT{h}", f"u{h}"], writes=[kgo1])
                if cc == 0:
                    sch.task("dve", lambda e, h=h, cc=cc, go1=go1: e.tensor_scalar(
                        out=osc_h(h), in0=go1[:, 0:256],
                        scalar1=dcol(G_CQ, cc, h), scalar2=None, op0=AL.mult),
                        reads=[kgo1, f"dcCQ{h}"], writes=[f"osc{h}"],
                        extra=[fence5])
                else:
                    sch.task("dve", lambda e, h=h, go1=go1: e.tensor_copy(
                        out=o1s_h(h), in_=go1[:, 0:256]), reads=[kgo1],
                        writes=[f"o1s{h}"], extra=[fence5])
                    go2, kgo2 = gp_slot()
                    sch.task("pe", lambda e, h=h, go2=go2: e.matmul(
                        go2[:, 0:256], qf_h(h), S_bf[:, h * 256:(h + 1) * 256],
                        start=True, stop=True),
                        reads=[f"qf{h}", f"Sbf{h}"], writes=[kgo2])
                    sch.task("dve", lambda e, h=h, cc=cc, go2=go2: e.scalar_tensor_tensor(
                        out=o1s_h(h), in0=go2[:, 0:256], scalar=dcol(G_GAM, cc, h),
                        in1=o1s_h(h), op0=AL.mult, op1=AL.add),
                        reads=[kgo2, "dcG", f"o1s{h}"], writes=[f"o1s{h}"])
                    sch.task("dve", lambda e, h=h, cc=cc: e.tensor_scalar(
                        out=osc_h(h), in0=o1s_h(h),
                        scalar1=dcol(G_CQ, cc, h), scalar2=None, op0=AL.mult),
                        reads=[f"o1s{h}", f"dcCQ{h}"], writes=[f"osc{h}"])
                gd, kgd = gp_slot()
                sch.task("pe", lambda e, h=h, gd=gd: e.matmul(
                    gd[:, 0:256], k2tm_h(h), u_h(h), start=True, stop=True),
                    reads=[f"k2tm{h}", f"u{h}"], writes=[kgd])
                sch.task("dve", lambda e, h=h, cc=cc, gd=gd: e.scalar_tensor_tensor(
                    out=S_sb[:, h * 256:(h + 1) * 256],
                    in0=S_sb[:, h * 256:(h + 1) * 256],
                    scalar=dcol(G_GC, cc, h), in1=gd[:, 0:256],
                    op0=AL.mult, op1=AL.add),
                    reads=[kgd, "dcGC", f"S{h}"], writes=[f"S{h}"])
                sch.task("dve", lambda e, h=h: e.tensor_copy(
                    out=S_bf[:, h * 256:(h + 1) * 256],
                    in_=S_sb[:, h * 256:(h + 1) * 256]),
                    reads=[f"S{h}"], writes=[f"Sbf{h}"])
            for h in HS:
                sch.task("dve", lambda e, h=h: e.scalar_tensor_tensor(
                    out=jnk[:] if False else bv_h(h), in0=osc_h(h), scalar=1.0,
                    in1=osc_h(h), op0=AL.bypass, op1=AL.mult,
                    accum_out=sso[:, h * 4 + 3:h * 4 + 4]),
                    reads=[f"osc{h}"], writes=[f"bv{h}", f"ssoo{h}"])
                sch.task("act", lambda e, h=h: e.activation(
                    out=sso[:, h * 4 + 3:h * 4 + 4],
                    in_=sso[:, h * 4 + 3:h * 4 + 4], func=AF.Sqrt,
                    scale=1.0 / 256.0, bias=1e-5),
                    reads=[f"ssoo{h}"], writes=[f"ssoo{h}"])
                sch.task("dve", lambda e, h=h: e.reciprocal(
                    out=sso[:, h * 4 + 3:h * 4 + 4],
                    in_=sso[:, h * 4 + 3:h * 4 + 4]),
                    reads=[f"ssoo{h}"], writes=[f"ssoo{h}"])
                sch.task("dve", lambda e, h=h: e.scalar_tensor_tensor(
                    out=osc_h(h), in0=osc_h(h), scalar=sso[:, h * 4 + 3:h * 4 + 4],
                    in1=gsn_sb[:, h * 256:(h + 1) * 256], op0=AL.mult,
                    op1=AL.mult),
                    reads=[f"ssoo{h}", "gsn", f"osc{h}"], writes=[f"osc{h}"])
                for half in range(2):
                    to, kto, bko = tf_slot()
                    sch.task("pe", lambda e, h=h, half=half, to=to: e.transpose(
                        to, osc_h(h)[:, half * 128:(half + 1) * 128],
                        sb_id_f32[:]), reads=[f"osc{h}", "idf"],
                        writes=[kto, bko])
                    sch.task("dve", lambda e, h=h, half=half, to=to: e.tensor_copy(
                        out=o2fm[:, (2 * h + half) * 128:(2 * h + half + 1) * 128],
                        in_=to), reads=[kto, bko], writes=["o2fm"])
            for dhalf in range(2):
                bank = gp[dhalf]
                kb = f"gp{dhalf}"
                for vc in range(12):
                    sch.task("pe", lambda e, vc=vc, dhalf=dhalf, bank=bank: e.matmul(
                        bank[:, 0:512], o2fm[:, vc * 128:(vc + 1) * 128],
                        wo_sb[:, vc * 1024 + dhalf * 512:vc * 1024 + (dhalf + 1) * 512],
                        start=(vc == 0), stop=(vc == 11)),
                        reads=["o2fm", "wo"], writes=[kb])
                sch.task("dve", lambda e, dhalf=dhalf, bank=bank: e.tensor_copy(
                    out=sbf[:, dhalf * 512:(dhalf + 1) * 512], in_=bank[:, 0:512]),
                    reads=[kb], writes=["sbf"])
            dmat("sp", hy[cc], sbf[:, 0:1024], reads=["sbf"], writes=["hy"])

        maybe_stop(5)
        w1sT = AB[:, 0:16384]
        w2sT = AB[:, 16384:32768]
        p5keys = []
        for h in range(NH):
            p5keys += [f"qf{h}", f"kf{h}", f"vf{h}", f"ktm{h}", f"k2tm{h}",
                       f"khf{h}", f"PT{h}", f"TSb{h}", f"TT{h}_0", f"TT{h}_1",
                       f"MT{h}", f"vtm{h}", f"bv{h}", f"ru{h}", f"u{h}",
                       f"ePT{h}", f"eTT{h}", f"eTS{h}", f"osc{h}", f"o1s{h}",
                       f"ccb{h}", f"ein{h}"]
            p5keys += [f"TSP{h}_{j}" for j in range(6)]
        fence6 = sch.task("sp", lambda e: e.nop(), writes=p5keys)
        w1c_bf = sbf[:, 0:1024]
        w2c_bf = sbf[:, 0:1024]
        alcols = sbuf("alcols", (128, 32), F32)
        b1_sb = sbuf("b1_sb", (128, 32), F32)
        a2cols = sbuf("a2cols", (128, 8), F32)
        a2b = sbuf("a2b", (128, 1024), BF16)
        h_bf = sbuf("h_bf", (128, 256), BF16)
        yt_bf = q_bf

        dmat("sp", b1_sb[:], b1c[:, :], writes=["b1"])
        for dc2 in range(8):
            for hhx in range(2):
                dmat("pool", w2c_bf[:, 0:1024],
                     w2[dc2 * 128:(dc2 + 1) * 128, hhx * 1024:(hhx + 1) * 1024],
                     writes=["sbf"])
                dmat("pool", xstat[:, 0:1024],
                     w2[dc2 * 128:(dc2 + 1) * 128, 2048 + hhx * 1024:2048 + (hhx + 1) * 1024],
                     writes=["xstat"])
                if hhx == 0:
                    sch.task("dve", lambda e, dc2=dc2: e.tensor_reduce(
                        out=a2cols[:, dc2:dc2 + 1], in_=w2c_bf[:, 0:1024], axis=AX.X,
                        op=AL.add, apply_absolute_value=True),
                        reads=["sbf"], writes=["a2c"])
                else:
                    sch.task("dve", lambda e, dc2=dc2: e.tensor_reduce(
                        out=lncols[:, 0:1], in_=w2c_bf[:, 0:1024], axis=AX.X,
                        op=AL.add, apply_absolute_value=True),
                        reads=["sbf"], writes=["lnc"])
                    sch.task("dve", lambda e, dc2=dc2: e.scalar_tensor_tensor(
                        out=a2cols[:, dc2:dc2 + 1],
                        in0=a2cols[:, dc2:dc2 + 1] if True else None,
                        scalar=1.0, in1=lncols[:, 0:1], op0=AL.mult, op1=AL.add),
                        reads=["lnc", "a2c"], writes=["a2c"])
                sch.task("dve", lambda e: e.tensor_reduce(
                    out=lncols[:, 1:2], in_=xstat[:, 0:1024], axis=AX.X,
                    op=AL.add, apply_absolute_value=True),
                    reads=["xstat"], writes=["lnc1"])
                sch.task("dve", lambda e, dc2=dc2: e.scalar_tensor_tensor(
                    out=a2cols[:, dc2:dc2 + 1], in0=a2cols[:, dc2:dc2 + 1],
                    scalar=1.0, in1=lncols[:, 1:2], op0=AL.mult, op1=AL.add),
                    reads=["lnc1", "a2c"], writes=["a2c"])
        sch.task("dve", lambda e: e.tensor_scalar(
            out=a2cols[:], in0=a2cols[:], scalar1=1.0 / HFF, scalar2=None,
            op0=AL.mult), reads=["a2c"], writes=["a2c"])
        for dc2 in range(8):
            dmat("pool", a2row[0:1, dc2 * 128:(dc2 + 1) * 128],
                 a2cols[:, dc2:dc2 + 1], reads=["a2c"], writes=["rpA"],
                 extra=[fence6], slow=True)
        for half in range(2):
            sch.task("pe", lambda e, half=half: e.matmul(
                tf[:, 0:512], sb_onesr[:], a2row[0:1, half * 512:(half + 1) * 512],
                start=True, stop=True), reads=["rpA", "onesr"],
                writes=["tfA", "tfB", "tfC", "tfD", "tfbank"])
            sch.task("dve", lambda e, half=half: e.tensor_copy(
                out=a2b[:, half * 512:(half + 1) * 512], in_=tf[:, 0:512]),
                reads=["tfA", "tfbank"], writes=["a2b"])
        dmat("sp", b2row, b2r[:, :], writes=["rpC"], extra=[fence6])
        sch.task("dve", lambda e: e.reciprocal(out=bzrow, in_=a2row),
                 reads=["rpA"], writes=["rpB"], extra=[fence6])
        sch.task("dve", lambda e: e.tensor_tensor(
            out=bzrow, in0=bzrow, in1=b2row, op=AL.mult),
            reads=["rpB", "rpC"], writes=["rpB"])

        NT2 = L // 256
        for hh in range(2):
            for hcl in range(16):
                hcg = hh * 16 + hcl
                dmat("pool", w1c_bf[:, 0:1024], w1[hcg * 128:(hcg + 1) * 128, :],
                     writes=["sbf"])
                sch.task("dve", lambda e, hcg=hcg: e.tensor_reduce(
                    out=alcols[:, hcg:hcg + 1], in_=w1c_bf[:, 0:1024], axis=AX.X,
                    op=AL.add, apply_absolute_value=True),
                    reads=["sbf"], writes=["al1"])
                for dc in range(8):
                    ts_, kts_, bks_ = tb_slot()
                    sch.task("pe", lambda e, dc=dc, ts_=ts_: e.transpose(
                        ts_, w1c_bf[:, dc * 128:(dc + 1) * 128], sb_id_bf[:]),
                        reads=["sbf", "idbf"], writes=[kts_, bks_])
                    sch.task("act", lambda e, dc=dc, hcl=hcl, ts_=ts_: e.activation(
                        out=w1sT[:, dc * 2048 + hcl * 128:dc * 2048 + (hcl + 1) * 128],
                        in_=ts_, func=AF.Sign),
                        reads=[kts_, bks_, bks_], writes=["w1sT", "xblk"])
            for dc2 in range(8):
                for hq2 in range(2):
                    dmat("pool", w2c_bf[:, 0:1024],
                         w2[dc2 * 128:(dc2 + 1) * 128,
                            hh * 2048 + hq2 * 1024:hh * 2048 + (hq2 + 1) * 1024],
                         writes=["sbf"])
                    for hcl8 in range(8):
                        hcl = hq2 * 8 + hcl8
                        ts_, kts_, bks_ = tb_slot()
                        sch.task("pe", lambda e, hcl8=hcl8, ts_=ts_: e.transpose(
                            ts_, w2c_bf[:, hcl8 * 128:(hcl8 + 1) * 128], sb_id_bf[:]),
                            reads=["sbf", "idbf"], writes=[kts_, bks_])
                        sch.task("act", lambda e, dc2=dc2, hcl=hcl, ts_=ts_:
                                 e.activation(
                            out=w2sT[:, hcl * 1024 + dc2 * 128:hcl * 1024 + (dc2 + 1) * 128],
                            in_=ts_, func=AF.Sign), reads=[kts_, bks_, bks_], writes=["w2sT"])
            sch.task("dve", lambda e, hh=hh: e.tensor_scalar(
                out=alcols[:, hh * 16:(hh + 1) * 16],
                in0=alcols[:, hh * 16:(hh + 1) * 16], scalar1=1.0 / D, scalar2=None,
                op0=AL.mult), reads=["al1"], writes=["al1"])

            for tblk in range(NT2):
                t0 = tblk * 256
                for dc in range(8):
                    dmat("sp", xnb2[:, dc * 256:(dc + 1) * 256],
                         hxn0[dc, :, t0:t0 + 256], reads=["hxn0"], writes=["xstat"])
                if hh == 0:
                    for j in range(2):
                        for dh in range(2):
                            bank = gp[j * 2 + dh]
                            kb = f"gp{j * 2 + dh}"
                            sch.task("pe", lambda e, dh=dh, bank=bank: e.matmul(
                                bank[:, 0:512], sb_onesr[:],
                                bzrow[0:1, dh * 512:(dh + 1) * 512],
                                start=True, stop=False),
                                reads=["rpB", "onesr"], writes=[kb])
                for hcl in range(16):
                    for dc in range(8):
                        sch.task("pe", lambda e, hcl=hcl, dc=dc: e.matmul(
                            gp[4][:, 0:256],
                            w1sT[:, dc * 2048 + hcl * 128:dc * 2048 + (hcl + 1) * 128],
                            xnb2[:, dc * 256:(dc + 1) * 256],
                            start=(dc == 0), stop=(dc == 7)),
                            reads=["w1sT", "xstat"], writes=["gp4"])
                    hcg = hh * 16 + hcl
                    sch.task("act", lambda e, hcg=hcg: e.activation(
                        out=h_bf[:], in_=gp[4][:, 0:256], func=AF.Gelu,
                        bias=b1_sb[:, hcg:hcg + 1], scale=alcols[:, hcg:hcg + 1]),
                        reads=["gp4", "b1", "al1"], writes=["h_bf"])
                    for j in range(2):
                        for dh in range(2):
                            bank = gp[j * 2 + dh]
                            kb = f"gp{j * 2 + dh}"
                            first = (hh == 1 and hcl == 0)
                            sch.task("pe", lambda e, j=j, dh=dh, hcl=hcl, bank=bank,
                                     first=first: e.matmul(
                                bank[:, 0:512], h_bf[:, j * 128:(j + 1) * 128],
                                w2sT[:, hcl * 1024 + dh * 512:hcl * 1024 + (dh + 1) * 512],
                                start=first, stop=(hcl == 15)),
                                reads=["h_bf", "w2sT"], writes=[kb])
                for j in range(2):
                    tt = t0 // 128 + j
                    if hh == 0:
                        dmat("sp", qD[:], x_in[tt * 128:(tt + 1) * 128, :],
                             writes=["qD"])
                        dmat("sp", yt_bf[:], hy[tt], reads=["hy"], writes=["qbf"])
                        sch.task("dve", lambda e: e.tensor_tensor(
                            out=qD[:], in0=qD[:], in1=yt_bf[:], op=AL.add),
                            reads=["qbf", "qD"], writes=["qD"])
                    else:
                        dmat("sp", qD[:], hx1[tt], reads=["hx1w"], writes=["qD"])
                    for dh in range(2):
                        bank = gp[j * 2 + dh]
                        kb = f"gp{j * 2 + dh}"
                        sch.task("dve", lambda e, dh=dh, bank=bank: e.tensor_tensor(
                            out=q32[:, dh * 512:(dh + 1) * 512], in0=bank[:, 0:512],
                            in1=a2b[:, dh * 512:(dh + 1) * 512], op=AL.mult),
                            reads=[kb, "a2b"], writes=["q32"])
                    sch.task("dve", lambda e: e.tensor_tensor(
                        out=qD[:], in0=qD[:], in1=q32[:], op=AL.add),
                        reads=["q32", "qD"], writes=["qD"])
                    dmat("sp", hx1[tt], qD[:], reads=["qD"], writes=["hx1w"])

        maybe_stop(6)
        for tt in range(NT):
            dmat("sp", xtile, hx1[tt], reads=["hx1w"], writes=["xtile"])
            ln_tile()
            for dc in range(8):
                sch.task("pe", lambda e, dc=dc: e.transpose(
                    tf[:, 0:128], xntm[:, dc * 128:(dc + 1) * 128], sb_id_f32[:]),
                    reads=["xntm", "idf"], writes=["tfA", "tfbank"])
                sch.task("dve", lambda e, dc=dc: e.tensor_copy(
                    out=xnfm_bf[:, dc * 128:(dc + 1) * 128], in_=tf[:, 0:128]),
                    reads=["tfA"], writes=["sbf"])
                sch.task("dve", lambda e, dc=dc: e.tensor_copy(
                    out=q32[:, dc * 128:(dc + 1) * 128], in_=tf[:, 0:128]),
                    reads=["tfA", "tfbank"], writes=["q32"])
            for dc in range(8):
                dmat("sp", hxn1[dc, :, tt * 128:(tt + 1) * 128],
                     xnfm_bf[:, dc * 128:(dc + 1) * 128],
                     reads=["sbf"], writes=["hxn1"])
            if phase_limit <= 6.1:
                continue
            round_pipe(q32[:], q_bf[:], "q32", "qbf", mant_bits=3, min_p=-6,
                       clip=448.0)
            if phase_limit <= 6.3:
                continue
            for dc in range(8):
                dmat("sp", hxq[dc, :, tt * 128:(tt + 1) * 128],
                     q_bf[:, dc * 128:(dc + 1) * 128],
                     reads=["qbf"], writes=["hxq"])

        maybe_stop(7)
        wfT = AB2[:, 0:8192]
        for fc in range(8):
            dmat("pool", q32[:], wf[fc * 128:(fc + 1) * 128, :], writes=["q32"])
            round_pipe(q32[:], q_bf[:], "q32", "qbf", mant_bits=3, min_p=-6,
                       clip=448.0)
            for cc2 in range(8):
                ts_, kts_, bks_ = tb_slot()
                sch.task("pe", lambda e, cc2=cc2, ts_=ts_: e.transpose(
                    ts_, q_bf[:, cc2 * 128:(cc2 + 1) * 128], sb_id_bf[:]),
                    reads=["qbf", "idbf"], writes=[kts_, bks_])
                sch.task("dve", lambda e, cc2=cc2, fc=fc, ts_=ts_: e.tensor_copy(
                    out=wfT[:, cc2 * 1024 + fc * 128:cc2 * 1024 + (fc + 1) * 128],
                    in_=ts_), reads=[kts_, bks_, bks_], writes=["wfT", "wo"])
        for tt in range(NT):
            for dc in range(8):
                dmat("sp", xstat[:, dc * 128:(dc + 1) * 128],
                     hxq[dc, :, tt * 128:(tt + 1) * 128],
                     reads=["hxq"], writes=["xstat"])
            for dh in range(2):
                bank = gp[dh]
                kb = f"gp{dh}"
                for cc2 in range(8):
                    sch.task("pe", lambda e, cc2=cc2, dh=dh, bank=bank: e.matmul(
                        bank[:, 0:512], xstat[:, cc2 * 128:(cc2 + 1) * 128],
                        wfT[:, cc2 * 1024 + dh * 512:cc2 * 1024 + (dh + 1) * 512],
                        start=(cc2 == 0), stop=(cc2 == 7)),
                        reads=["xstat", "wfT"], writes=[kb])
                sch.task("dve", lambda e, dh=dh, bank=bank: e.tensor_copy(
                    out=q32[:, dh * 512:(dh + 1) * 512], in_=bank[:, 0:512]),
                    reads=[kb], writes=["q32"])
            round_pipe(q32[:], f16b[:], "q32", "f16b", mant_bits=10, min_p=-14,
                       clip=None, tie_even=True)
            dmat("sp", hf1[tt], f16b[:], reads=["f16b"], writes=["hf1"])

        maybe_stop(8)
        KR = AB2[:, 0:32768]
        VR = AB[:, 0:32768]
        AB5 = sbuf("AB5", (128, 4096), BF16)
        wl1 = AB5[:, 0:4096]
        qp_bf = gsn_sb[:, 0:512]
        qx_bf = gsn_sb[:, 512:1024]
        fence8 = sch.task("sp", lambda e: e.nop(),
                          writes=["wl1", "qp", "qx", "xblk0", "xblk1",
                                  "xblk", "w1sT", "gsn"])

        def l1_proj(wsrc, cost, sint, is_q):
            for wh in range(2):
                for dc in range(8):
                    dmat("pool", wl1[:, dc * 512:(dc + 1) * 512],
                         wsrc[dc * 128:(dc + 1) * 128, wh * 512:(wh + 1) * 512],
                         writes=["wl1"], extra=[fence8])
                for tb8 in range(NB):
                    t0 = tb8 * 512
                    par = (wh * NB + tb8) % 2
                    xbuf = AB[:, par * 4096:(par + 1) * 4096]
                    kxb = f"xblk{par}"
                    for dc in range(8):
                        dmat("sp", xbuf[:, dc * 512:(dc + 1) * 512],
                             hxn1[dc, :, t0:t0 + 512], reads=["hxn1"],
                             writes=[kxb], extra=[fence8])
                    for dc2l in range(4):
                        dc2 = wh * 4 + dc2l
                        bank = gp[dc2 % 2]
                        kb = f"gp{dc2 % 2}"
                        for dc in range(8):
                            sch.task("pe", lambda e, dc=dc, dc2l=dc2l, bank=bank,
                                     xbuf=xbuf: e.matmul(
                                bank[:, 0:512],
                                wl1[:, dc * 512 + dc2l * 128:dc * 512 + (dc2l + 1) * 128],
                                xbuf[:, dc * 512:(dc + 1) * 512],
                                start=(dc == 0), stop=(dc == 7)),
                                reads=["wl1", kxb], writes=[kb])
                        sch.task("dve", lambda e, bank=bank: e.tensor_copy(
                            out=qp_bf[:], in_=bank[:, 0:512]), reads=[kb], writes=["qp"])
                        bank2 = gp[2 + dc2 % 2]
                        kb2 = f"gp{2 + dc2 % 2}"
                        sch.task("pe", lambda e, bank2=bank2: e.matmul(
                            bank2[:, 0:512], sb_rotT[:], qp_bf[:],
                            start=True, stop=True),
                            reads=["qp", "rotT"], writes=[kb2])
                        dmat("sp", trigc,
                             cost[dc2 * 128:(dc2 + 1) * 128, t0:t0 + 512],
                             writes=["rpC"])
                        dmat("sp", trigs,
                             sint[dc2 * 128:(dc2 + 1) * 128, t0:t0 + 512],
                             writes=["rpC2"])
                        sch.task("dve", lambda e: e.tensor_tensor(
                            out=qA[:, 0:512], in0=qp_bf[:], in1=trigc,
                            op=AL.mult), reads=["qp", "rpC"], writes=["rpA"])
                        sch.task("dve", lambda e, bank2=bank2: e.tensor_tensor(
                            out=qB[:, 0:512], in0=bank2[:, 0:512], in1=trigs,
                            op=AL.mult), reads=[kb2, "rpC2"], writes=["rpB"])
                        sch.task("dve", lambda e: e.tensor_tensor(
                            out=qx_bf[:], in0=qA[:, 0:512], in1=qB[:, 0:512],
                            op=AL.add), reads=["rpA", "rpB"], writes=["qx"])
                        if is_q:
                            for j in range(4):
                                dmat("sp", hq1[t0 // 128 + j, dc2],
                                     qx_bf[:, j * 128:(j + 1) * 128],
                                     reads=["qx"], writes=["hq1"])
                        else:
                            sch.task("dve", lambda e, dc2=dc2, t0=t0: e.tensor_copy(
                                out=KR[:, dc2 * L + t0:dc2 * L + t0 + 512],
                                in_=qx_bf[:]),
                                reads=["qx"], writes=["KR", "wfT"])

        l1_proj(WQ1, ctq, stq, True)
        l1_proj(WK1, ctk, stk, False)
        for wh in range(2):
            for dc in range(8):
                dmat("pool", wl1[:, dc * 512:(dc + 1) * 512],
                     WV1[dc * 128:(dc + 1) * 128, wh * 512:(wh + 1) * 512],
                     writes=["wl1"])
            for tt in range(NT):
                for dc in range(8):
                    dmat("sp", xstat[:, dc * 128:(dc + 1) * 128],
                         hxn1[dc, :, tt * 128:(tt + 1) * 128],
                         reads=["hxn1"], writes=["xstat"])
                bank = gp[tt % 2]
                kb = f"gp{tt % 2}"
                for dc in range(8):
                    sch.task("pe", lambda e, dc=dc, bank=bank: e.matmul(
                        bank[:, 0:512], xstat[:, dc * 128:(dc + 1) * 128],
                        wl1[:, dc * 512:(dc + 1) * 512],
                        start=(dc == 0), stop=(dc == 7)),
                        reads=["xstat", "wl1"], writes=[kb])
                sch.task("dve", lambda e, tt=tt, wh=wh, bank=bank: e.tensor_copy(
                    out=VR[:, tt * 1024 + wh * 512:tt * 1024 + (wh + 1) * 512],
                    in_=bank[:, 0:512]),
                    reads=[kb], writes=["VR", "xblk0", "xblk1", "w1sT", "w2sT"])

        maybe_stop(9)
        qstat = xnb2[:, 0:1024]
        mcols = sbuf("mcols", (128, 16), F32)
        ed_bf = qp_bf
        pt_full = o2fm[:, 0:512]
        zc = sbuf("zc", (128, 8), F32)
        bfb_sb = xnb2[:, 1024:2048]
        out_sb = q32
        fence9 = sch.task("sp", lambda e: e.nop(),
                          writes=["xstat", "wl1", "o2fm", "ptf"])
        dmat("sp", bfb_sb[:], bf_b[:, :], writes=["bfb"], extra=[fence9])

        for qt in range(NT):
            q0 = qt * 128
            dmat("sp", qstat[:], hq1[qt].rearrange("dc p c -> p dc c"),
                 reads=["hq1"], writes=["qstat"], extra=[fence9])
            for kc in range(NKC):
                bank = gp[kc % 2]
                kb = f"gp{kc % 2}"
                for dc in range(8):
                    sch.task("pe", lambda e, dc=dc, kc=kc, bank=bank: e.matmul(
                        bank[:, 0:512], qstat[:, dc * 128:(dc + 1) * 128],
                        KR[:, dc * L + kc * 512:dc * L + kc * 512 + 512],
                        start=(dc == 0), stop=(dc == 7)),
                        reads=["qstat", "KR"], writes=[kb])
                sch.task("dve", lambda e, kc=kc, bank=bank: e.reduce_max(
                    mcols[:, kc:kc + 1], bank[:, 0:512], axis=AX.X),
                    reads=[kb], writes=["mc"])
                sch.task("dve", lambda e, kc=kc, bank=bank: e.tensor_copy(
                    out=srow[:, kc * 512:(kc + 1) * 512], in_=bank[:, 0:512]),
                    reads=[kb], writes=SROW_W)
            sch.task("dve", lambda e: e.reduce_max(
                mcols[:, 9:10], mcols[:, 0:NKC], axis=AX.X),
                reads=["mc"], writes=["mfin"])
            sch.task("dve", lambda e: e.tensor_scalar(
                out=mcols[:, 10:11], in0=mcols[:, 9:10], scalar1=-1.0, scalar2=None,
                op0=AL.mult), reads=["mfin"], writes=["negm"])
            live = [kc for kc in range(NKC)
                    if max(0, q0 - (kc * 512 + 511), kc * 512 - (q0 + 127)) <= DEC_SKIP]
            for kc in range(NKC):
                k0 = kc * 512
                sch.task("act", lambda e, kc=kc: e.activation(
                    out=sbf[:, 0:512], in_=srow[:, kc * 512:(kc + 1) * 512],
                    func=AF.Exp, bias=mcols[:, 10:11], accum_out=zc[:, kc:kc + 1]),
                    reads=["srow", "negm"], writes=["sbf", f"zc{kc}"])
                if kc not in live:
                    continue
                j0 = DEC_OFF + k0 - q0
                dmat("sp", dlt[:, 0:512], dlog[:, j0:j0 + 512], writes=["q32"])
                sch.task("dve", lambda e, kc=kc: e.tensor_tensor(
                    out=sdec, in0=srow[:, kc * 512:(kc + 1) * 512], in1=dlt[:, 0:512],
                    op=AL.add), reads=["srow", "q32"], writes=SDEC_W)
                sch.task("act", lambda e: e.activation(
                    out=ed_bf[:], in_=sdec, func=AF.Exp, bias=mcols[:, 10:11]),
                    reads=["sdec", "negm"], writes=["qp"])
                for j in range(4):
                    tsl, ktsl, bksl = tb_slot()
                    sch.task("pe", lambda e, j=j, tsl=tsl: e.transpose(
                        tsl, ed_bf[:, j * 128:(j + 1) * 128], sb_id_bf[:]),
                        reads=["qp", "idbf"], writes=[ktsl, bksl])
                    sch.task("dve", lambda e, j=j, tsl=tsl: e.tensor_copy(
                        out=pt_full[:, j * 128:(j + 1) * 128], in_=tsl),
                        reads=[ktsl, bksl, bksl], writes=["ptf"], extra=[fence9])
                lastkc = (kc == live[-1])
                for j in range(4):
                    ktile = kc * 4 + j
                    st = (kc == live[0] and j == 0)
                    sp_ = (lastkc and j == 3)
                    for dh in range(2):
                        bank = gp[2 + dh]
                        kb = f"gp{2 + dh}"
                        sch.task("pe", lambda e, j=j, dh=dh, ktile=ktile, st=st,
                                 sp_=sp_, bank=bank: e.matmul(
                            bank[:, 0:512], pt_full[:, j * 128:(j + 1) * 128],
                            VR[:, ktile * 1024 + dh * 512:ktile * 1024 + (dh + 1) * 512],
                            start=st, stop=sp_),
                            reads=["ptf", "VR"], writes=[kb])
            sch.task("dve", lambda e: e.reduce_sum(
                mcols[:, 11:12], zc[:, 0:NKC], axis=AX.X),
                reads=[f"zc{kc}" for kc in range(NKC)], writes=["Z"])
            sch.task("dve", lambda e: e.reciprocal(out=mcols[:, 12:13],
                                                   in_=mcols[:, 11:12]),
                     reads=["Z"], writes=["rZ"])
            dmat("sp", qD[:], hx1[qt], reads=["hx1w"], writes=["qD"])
            dmat("sp", f16b[:], hf1[qt], reads=["hf1"], writes=["f16b"])
            sch.task("dve", lambda e: e.tensor_tensor(
                out=qD[:], in0=qD[:], in1=f16b[:], op=AL.add),
                reads=["f16b", "qD"], writes=["qD"])
            sch.task("dve", lambda e: e.tensor_tensor(
                out=qD[:], in0=qD[:], in1=bfb_sb[:], op=AL.add),
                reads=["bfb", "qD"], writes=["qD"])
            for dh in range(2):
                bank = gp[2 + dh]
                kb = f"gp{2 + dh}"
                sch.task("dve", lambda e, dh=dh, bank=bank: e.scalar_tensor_tensor(
                    out=out_sb[:, dh * 512:(dh + 1) * 512], in0=bank[:, 0:512],
                    scalar=mcols[:, 12:13], in1=qD[:, dh * 512:(dh + 1) * 512],
                    op0=AL.mult, op1=AL.add),
                    reads=[kb, "rZ", "qD"], writes=["q32"])
            dmat("sp", out[qt * 128:(qt + 1) * 128, :], out_sb[:],
                 reads=["q32"], writes=["outw"])


    except _StopBuild:
        dmat("sp", out[0:128, :], x_in[0:128, :], writes=["outw"])

    sems = {}
    for e in Sched.ENGS:
        g = nc.semaphore(f"sem_{e}")
        sems[("eng", e)] = g.__enter__()
        ctx.append(g)
    for e in Sched.DMAQ:
        for si in range(Sched.NSEM[e]):
            g = nc.semaphore(f"dmas_{e}{si}")
            sems[("dma", e, si)] = g.__enter__()
            ctx.append(g)
    blk = nc.Block()
    blk.__enter__()
    sch.emit(nc, blk, sems)
    blk.__exit__(None, None, None)
    for g in reversed(ctx):
        g.__exit__(None, None, None)
    nc._sched = sch
    return nc


def make_inputs_for_core(xb, I, consts):
    m = {"x": np.ascontiguousarray(xb, dtype=np.float32)}
    m["Wq"] = I["L0_attn_Wq"]
    m["Wk"] = I["L0_attn_Wk"]
    m["Wv"] = I["L0_attn_Wv"]
    wba = np.zeros((D, 16), dtype=np.float32)
    wba[:, 0:6] = I["L0_attn_Wb"]
    wba[:, 6:12] = I["L0_attn_Wa"]
    m["Wba"] = wba
    m["A_log"] = np.ascontiguousarray(I["L0_attn_A_log"]).reshape(NH, 1).astype(np.float32)
    m["dt_bias"] = np.ascontiguousarray(I["L0_attn_dt_bias"]).reshape(NH, 1).astype(np.float32)
    m["convq"] = np.ascontiguousarray(
        np.asarray(I["L0_attn_conv_q"]).reshape(NH, 128, 4).transpose(1, 0, 2).reshape(128, 24))
    m["convk"] = np.ascontiguousarray(
        np.asarray(I["L0_attn_conv_k"]).reshape(NH, 128, 4).transpose(1, 0, 2).reshape(128, 24))
    m["convv"] = np.ascontiguousarray(
        np.asarray(I["L0_attn_conv_v"]).reshape(12, 128, 4).transpose(1, 0, 2).reshape(128, 48))
    m["Wg"] = I["L0_attn_Wg"]
    m["nw_b"] = np.ascontiguousarray(
        np.broadcast_to(np.tile(np.asarray(I["L0_attn_norm_w"]), NH), (128, VD)))
    m["Wo"] = I["L0_attn_Wo"]
    m["w1"] = I["L0_ffn_w1"]
    m["b1c"] = np.ascontiguousarray(np.asarray(I["L0_ffn_b1"]).reshape(32, 128).T)
    m["w2"] = I["L0_ffn_w2"]
    m["b2r"] = np.ascontiguousarray(np.asarray(I["L0_ffn_b2"]).reshape(1, D))
    m["WQ1"] = I["L1_attn_W_Q"]
    m["WK1"] = I["L1_attn_W_K"]
    m["WV1"] = I["L1_attn_W_V"]
    m["wf"] = I["L1_ffn_weight"]
    m["bf_b"] = np.ascontiguousarray(np.broadcast_to(np.asarray(I["L1_ffn_bias"]), (128, D)))
    for k, v in consts.items():
        m[k] = v
    res = {}
    for k, v in m.items():
        a = np.asarray(v)
        if a.dtype == np.float64:
            a = a.astype(np.float32)
        res[k] = np.ascontiguousarray(a)
    return res


_NC_CACHE = {}


def kernel(**inputs):
    L = np.asarray(inputs["L0_attn_x"]).shape[1]
    if L not in _NC_CACHE:
        _NC_CACHE[L] = build_core(L)
    nc = _NC_CACHE[L]
    consts = host_constants(L)
    x = np.ascontiguousarray(np.asarray(inputs["L0_attn_x"], dtype=np.float32))
    n = x.shape[0]
    in_maps = [make_inputs_for_core(x[i], inputs, consts) for i in range(n)]
    res = run_bass_kernel_spmd(nc, in_maps, core_ids=list(range(n)))
    return np.stack([r["out"] for r in res.results], axis=0)


# version 12
# speedup vs baseline: 1.1020x; 1.0465x over previous
import numpy as np

import concourse.bass as bass
import concourse.mybir as mybir
from concourse.bass_utils import run_bass_kernel_spmd

F32 = mybir.dt.float32
BF16 = mybir.dt.bfloat16
F16 = mybir.dt.float16
AF = mybir.ActivationFunctionType
AL = mybir.AluOpType
AX = mybir.AxisListType

LN2 = float(np.log(2.0))
INVLN2 = float(1.0 / np.log(2.0))
GAMMA = 0.9
LNG = float(np.log(GAMMA))
BIG = 1.0e30

D = 1024
NH = 6
DK = 128
DV = 256
KD = NH * DK
VD = NH * DV
HFF = 4096
C = 128
DEC_SKIP = 1100
DEC_OFF = 1664
DEC_BAND = 3456

SCALAR_CONSTS = [0.0, 1.0, -1.0, 0.5, 1e-5, 1e-12, 1.0 / 1024.0, 1.0 / 256.0,
                 128.0, 128.0e-12, float(np.log(2.0)), float(-np.log(2.0)),
                 float(np.log(2.0) * 67), float(-np.log(2.0) * 67),
                 float(np.log(2.0) * 74), float(-np.log(2.0) * 74)]


def _bf16np():
    import ml_dtypes
    return np.dtype(ml_dtypes.bfloat16)


class Sched:
    ENGS = ("pe", "act", "dve", "pool", "sp")
    DMAQ = ("sp", "pool", "act")
    NSEM = {"sp": 24, "pool": 16, "act": 4}

    def __init__(self):
        self.items = {e: [] for e in self.ENGS}
        self.n = {e: 0 for e in self.ENGS}
        self.ncomp = {e: 0 for e in self.ENGS}
        self.ndma = {e: 0 for e in self.DMAQ}
        self.semuse = {}
        self.last_w = {}
        self.readers = {}

    PSUM_BANKS = frozenset(["gp0", "gp1", "gp2", "gp3", "gp4",
                            "tbank0", "tbank1", "tfbank"])

    def task(self, eng, fn, reads=(), writes=(), dma=False, extra=()):
        pb = [r for r in reads if r in self.PSUM_BANKS]
        if pb:
            reads = [r for r in reads if r not in self.PSUM_BANKS]
            writes = list(writes) + pb
        deps = []
        for r in reads:
            c = self.last_w.get(r)
            if c is not None:
                deps.append(c)
        for w in writes:
            c = self.last_w.get(w)
            if c is not None:
                deps.append(c)
            deps.extend(self.readers.get(w, ()))
        deps.extend(extra)
        self.n[eng] += 1
        if dma:
            si = self.ndma[eng] % self.NSEM[eng]
            self.ndma[eng] += 1
            skey = ("dma", eng, si)
            cnt = self.semuse.get(skey, 0) + 1
            self.semuse[skey] = cnt
            if cnt > 1:
                deps.append((skey, 16 * (cnt - 1)))
            comp = (skey, 16 * cnt)
            dmakey = skey
        else:
            self.ncomp[eng] += 1
            comp = (("eng", eng), self.ncomp[eng])
            dmakey = None
        self.items[eng].append((deps, fn, dmakey))
        for r in reads:
            self.readers.setdefault(r, []).append(comp)
        for w in writes:
            self.last_w[w] = comp
            self.readers[w] = []
        return comp

    def emit(self, nc, block, sems):
        def make_body(ename, items):
            def body(eng):
                satisfied = {}
                own = ("eng", ename)
                if ename != "sp":
                    eng.wait_ge(sems[("dma", "sp", 0)], 16)
                    satisfied[("dma", "sp", 0)] = 16
                for deps, fn, dmakey in items:
                    for key, val in deps:
                        if satisfied.get(key, 0) >= val:
                            continue
                        eng.wait_ge(sems[key], val)
                        satisfied[key] = val
                    inst = fn(eng)
                    if dmakey is not None:
                        inst.then_inc(sems[dmakey], 16)
                    else:
                        inst.then_inc(sems[("eng", ename)], 1)
                if ename in self.DMAQ:
                    for si in range(self.NSEM[ename]):
                        skey = ("dma", ename, si)
                        cnt = self.semuse.get(skey, 0)
                        if cnt > 0:
                            eng.wait_ge(sems[skey], 16 * cnt)
            return body

        block.tensor(make_body("pe", self.items["pe"]))
        block.scalar(make_body("act", self.items["act"]))
        block.vector(make_body("dve", self.items["dve"]))
        block.gpsimd(make_body("pool", self.items["pool"]))
        block.sync(make_body("sp", self.items["sp"]))


def host_constants(L):
    bf = _bf16np()
    c = {}
    eye = np.eye(128, dtype=np.float64)
    c["c_id_bf16"] = eye.astype(bf)
    c["c_id_f32"] = eye.astype(np.float32)
    c["c_ones_f32"] = np.ones((1, 128), dtype=np.float32)
    c["c_onesmat"] = np.ones((128, 128), dtype=bf)
    c["c_zero_row_f32"] = np.zeros((6, 128), dtype=np.float32)
    r = np.arange(128)[:, None]
    t = np.arange(128)[None, :]
    c["c_maskPT"] = np.where(r <= t, 0.0, BIG).astype(bf)
    c["c_maskTT"] = np.where(r < t, 0.0, BIG).astype(bf)
    c["c_maskTS"] = np.where(t < r, 0.0, BIG).astype(bf)
    R = np.zeros((128, 128), dtype=np.float64)
    for i in range(64):
        R[2 * i, 2 * i + 1] = -1.0
        R[2 * i + 1, 2 * i] = 1.0
    c["c_rotT_bf16"] = np.ascontiguousarray(R.T).astype(bf)
    p = np.arange(128)[:, None]
    j = np.arange(DEC_BAND)[None, :]
    c["c_dlog"] = (LNG * np.abs(p - (j - DEC_OFF)).astype(np.float64)).astype(np.float32)
    half = D // 2
    sv = (np.arange(0, D, 2, dtype=np.float64) + 0.4 * D) / (1.4 * D)
    power = np.arange(L, dtype=np.float64) / 512.0
    scale = sv[None, :] ** power[:, None]
    inv_freq = 1.0 / 10000.0 ** (np.arange(half, dtype=np.float64) / half)
    sinus = np.arange(L, dtype=np.float64)[:, None] * inv_freq[None, :]
    sin = np.repeat(np.sin(sinus), 2, axis=1)
    cos = np.repeat(np.cos(sinus), 2, axis=1)
    scl = np.repeat(scale, 2, axis=1)
    c["c_cos_q"] = np.ascontiguousarray((cos * scl).T).astype(np.float32)
    c["c_sin_q"] = np.ascontiguousarray((sin * scl).T).astype(np.float32)
    c["c_cos_k"] = np.ascontiguousarray((cos / scl).T).astype(np.float32)
    c["c_sin_k"] = np.ascontiguousarray((sin / scl).T).astype(np.float32)
    c["c_scal"] = np.broadcast_to(
        np.asarray(SCALAR_CONSTS, dtype=np.float32), (128, len(SCALAR_CONSTS))
    ).copy()
    return c


class _StopBuild(Exception):
    pass


def build_core(L, phase_limit=99):
    nc = bass.Bass("TRN2", target_bir_lowering=False)
    NT = L // 128
    NCH = L // C
    NKC = L // 512
    NB = L // 512

    def din(name, shape, dtype=F32):
        return nc.dram_tensor(name, list(shape), dtype, kind="ExternalInput").ap()

    x_in = din("x", (L, D))
    Wq = din("Wq", (D, KD)); Wk = din("Wk", (D, KD)); Wv = din("Wv", (D, VD))
    Wba = din("Wba", (D, 16))
    A_log = din("A_log", (NH, 1))
    dt_bias = din("dt_bias", (NH, 1))
    convq = din("convq", (128, 24))
    convk = din("convk", (128, 24))
    convv = din("convv", (128, 48))
    Wg = din("Wg", (D, VD))
    nw_b = din("nw_b", (128, VD))
    Wo = din("Wo", (VD, D))
    w1 = din("w1", (HFF, D))
    b1c = din("b1c", (128, 32))
    w2 = din("w2", (D, HFF))
    b2r = din("b2r", (1, D))
    WQ1 = din("WQ1", (D, D)); WK1 = din("WK1", (D, D)); WV1 = din("WV1", (D, D))
    wf = din("wf", (D, D))
    bf_b = din("bf_b", (128, D))

    cid_bf = din("c_id_bf16", (128, 128), BF16)
    cid_f32 = din("c_id_f32", (128, 128), F32)
    onesr = din("c_ones_f32", (1, 128), F32)
    onesmat = din("c_onesmat", (128, 128), BF16)
    zrow6 = din("c_zero_row_f32", (6, 128), F32)
    maskPT = din("c_maskPT", (128, 128), BF16)
    maskTT = din("c_maskTT", (128, 128), BF16)
    maskTS = din("c_maskTS", (128, 128), BF16)
    rotT = din("c_rotT_bf16", (128, 128), BF16)
    dlog = din("c_dlog", (128, DEC_BAND), F32)
    ctq = din("c_cos_q", (D, L)); stq = din("c_sin_q", (D, L))
    ctk = din("c_cos_k", (D, L)); stk = din("c_sin_k", (D, L))

    out = nc.dram_tensor("out", [L, D], F32, kind="ExternalOutput").ap()

    hxn0 = nc.dram_tensor("hxn0", [8, 128, L], BF16).ap()
    hq = nc.dram_tensor("hq", [NH, NCH, 128, 128], BF16).ap()
    hk = nc.dram_tensor("hk", [NH, NCH, 128, 128], BF16).ap()
    hv = nc.dram_tensor("hv", [2 * NH, NCH, 128, 128], BF16).ap()
    hgsn = nc.dram_tensor("hgsn", [NCH, 128, VD], BF16).ap()
    hy = nc.dram_tensor("hy", [NT, 128, D], BF16).ap()
    hx1 = nc.dram_tensor("hx1", [NT, 128, D], F32).ap()
    hxn1 = nc.dram_tensor("hxn1", [8, 128, L], BF16).ap()
    hxq = nc.dram_tensor("hxq", [8, 128, L], BF16).ap()
    hq1 = nc.dram_tensor("hq1", [NT, 8, 128, 128], BF16).ap()
    hf1 = nc.dram_tensor("hf1", [NT, 128, D], F16).ap()

    sch = Sched()
    ctx = []

    def sbuf(name, shape, dtype):
        g = nc.sbuf_tensor(name, list(shape), dtype)
        h = g.__enter__()
        ctx.append(g)
        return h

    def psumt(name, shape, dtype=F32):
        g = nc.psum_tensor(name, list(shape), dtype)
        h = g.__enter__()
        ctx.append(g)
        return h

    gp = [psumt(f"gp{i}", (128, 512)) for i in range(5)]
    tbank = [psumt("tbA", (128, 1024), BF16), psumt("tbB", (128, 1024), BF16)]
    tf = psumt("tf", (128, 512))

    AB = sbuf("AB", (128, 32768), BF16)
    AB2 = sbuf("AB2", (128, 32768), BF16)
    AFX = sbuf("AFX", (128, 5120), F32)
    xtile = AFX[:, 0:1024]
    xntm = AFX[:, 1024:2048]
    qA = AFX[:, 2048:3072]
    qB = AFX[:, 3072:4096]
    qCs = AFX[:, 4096:5120]
    srow = AFX[:, 0:4096]
    sdec = AFX[:, 4096:4608]
    SROW_W = ["srow", "xtile", "xntm", "rpA", "rpB"]
    SDEC_W = ["sdec", "rpC"]

    q32 = sbuf("q32", (128, 1024), F32)
    qD = sbuf("qD", (128, 1024), F32)
    sbf = sbuf("sbf", (128, 1024), BF16)
    f16b = sbuf("f16b", (128, 1024), F16)
    lncols = sbuf("lncols", (128, 4), F32)
    q_bf = sbuf("q_bf", (128, 1024), BF16)
    qI = sbuf("qI", (128, 512), mybir.dt.int32)

    xnfm_bf = sbf
    rows = (qA, "rpA")
    pre_sb = qD[:, 0:516]
    cacc = q32[:, 0:512]
    dlt = q32[:, 0:512]
    trigc = qCs[:, 0:512]
    trigs = qCs[:, 512:1024]
    a2row = qA[0:1, 0:1024]
    bzrow = qB[0:1, 0:1024]
    b2row = qCs[0:1, 0:1024]
    bcrow_t = qCs[0:1, 0:512]

    sb_id_bf = sbuf("id_bf", (128, 128), BF16)
    sb_id_f32 = sbuf("id_f32", (128, 128), F32)
    sb_onesr = sbuf("onesr", (1, 128), F32)
    sb_onesmat = sbuf("onesmat", (128, 128), BF16)
    sb_maskPT = sbuf("maskPT", (128, 128), BF16)
    sb_maskTT = sbuf("maskTT", (128, 128), BF16)
    sb_maskTS = sbuf("maskTS", (128, 128), BF16)
    sb_rotT = sbuf("rotT", (128, 128), BF16)
    sb_zrow = sbuf("zrow", (6, 128), F32)

    def dmat(queue, dst, src, reads=(), writes=(), extra=(), slow=False):
        if dst.dtype != src.dtype:
            queue = "pool"
        return sch.task(queue, lambda e, d=dst, s=src, sl=slow: e.dma_start(
            out=d, in_=s, allow_slow_non_contiguous=sl),
            reads=reads, writes=writes, dma=True, extra=extra)

    cscal_in = din("c_scal", (128, len(SCALAR_CONSTS)), F32)
    sb_cscal = sbuf("cscal", (128, len(SCALAR_CONSTS)), F32)
    dmat("sp", sb_cscal[:], cscal_in[:], writes=["cscal"])
    for i, v in enumerate(SCALAR_CONSTS):
        nc.const_aps.register(F32, float(v), sb_cscal[:, i:i + 1])

    for dst, src, key in ((sb_id_bf, cid_bf, "idbf"), (sb_id_f32, cid_f32, "idf"),
                          (sb_onesr, onesr, "onesr"), (sb_onesmat, onesmat, "onesm"),
                          (sb_maskPT, maskPT, "mPT"), (sb_maskTT, maskTT, "mTT"),
                          (sb_maskTS, maskTS, "mTS"), (sb_rotT, rotT, "rotT"),
                          (sb_zrow, zrow6, "zrow")):
        dmat("sp", dst[:], src[:], writes=[key])

    tbrr = [0]

    def tb_slot():
        i = tbrr[0] % 8
        tbrr[0] += 1
        b = i % 2
        j = i // 2
        return tbank[b][:, j * 128:(j + 1) * 128], f"tb{i}", f"tbank{b}"

    tfrr = [0]

    def tf_slot():
        i = tfrr[0] % 4
        tfrr[0] += 1
        return tf[:, i * 128:(i + 1) * 128], f"tf{'ABCD'[i]}", "tfbank"

    gprr = [0]

    def gp_slot():
        i = gprr[0] % 5
        gprr[0] += 1
        return gp[i], f"gp{i}"

    def round_pipe(src, dst, kin, kout, mant_bits, min_p, clip=None,
                   tie_even=False):
        A = qA[:, 0:1024]; B = qB[:, 0:1024]; Cc = qCs[:, 0:1024]; Dd = qD[:]
        KA, KB, KC, KD_ = "rpA", "rpB", "rpC", "qD"
        lo = float(min_p + 64)
        bias_si = float(LN2 * (mant_bits + 64))
        bias_s = float(-LN2 * (mant_bits + 64))
        sch.task("act", lambda e: e.activation(out=A, in_=src, func=AF.Abs),
                 reads=[kin], writes=[KA])
        if clip is not None:
            sch.task("dve", lambda e: e.tensor_scalar(
                out=A, in0=A, scalar1=clip, scalar2=None, op0=AL.min),
                reads=[KA], writes=[KA])
        sch.task("dve", lambda e: e.tensor_scalar(
            out=B, in0=A, scalar1=2.0 ** -40, scalar2=None, op0=AL.max),
            reads=[KA], writes=[KB])
        sch.task("act", lambda e: e.activation(out=Cc, in_=B, func=AF.Ln),
                 reads=[KB], writes=[KC])
        sch.task("dve", lambda e: e.tensor_scalar(
            out=Cc, in0=Cc, scalar1=INVLN2, scalar2=64.0, op0=AL.mult, op1=AL.add),
            reads=[KC], writes=[KC])
        sch.task("dve", lambda e: e.tensor_scalar(
            out=Cc, in0=Cc, scalar1=lo, scalar2=-0.499, op0=AL.max, op1=AL.add),
            reads=[KC], writes=[KC])
        for hh_ in range(2):
            sch.task("pool", lambda e, hh_=hh_: e.tensor_copy(
                out=qI[:], in_=Cc[:, hh_ * 512:(hh_ + 1) * 512]),
                reads=[KC], writes=["qI"])
            sch.task("pool", lambda e, hh_=hh_: e.tensor_copy(
                out=Cc[:, hh_ * 512:(hh_ + 1) * 512], in_=qI[:]),
                reads=["qI"], writes=[KC])
        sch.task("act", lambda e: e.activation(out=B, in_=Cc, func=AF.Exp,
                                               scale=-LN2, bias=bias_si),
                 reads=[KC], writes=[KB])
        sch.task("pool", lambda e: e.tensor_tensor(out=Dd, in0=A, in1=B, op=AL.mult),
                 reads=[KA, KB], writes=[KD_])
        for hh_ in range(2):
            sch.task("pool", lambda e, hh_=hh_: e.tensor_copy(
                out=qI[:], in_=Dd[:, hh_ * 512:(hh_ + 1) * 512]),
                reads=[KD_], writes=["qI"])
            sch.task("pool", lambda e, hh_=hh_: e.tensor_copy(
                out=Dd[:, hh_ * 512:(hh_ + 1) * 512], in_=qI[:]),
                reads=["qI"], writes=[KD_])
        sch.task("act", lambda e: e.activation(out=B, in_=Cc, func=AF.Exp,
                                               scale=LN2, bias=bias_s),
                 reads=[KC], writes=[KB])
        sch.task("pool", lambda e: e.tensor_tensor(out=Dd, in0=Dd, in1=B, op=AL.mult),
                 reads=[KD_, KB], writes=[KD_])
        sch.task("act", lambda e: e.activation(out=A, in_=src, func=AF.Sign),
                 reads=[kin], writes=[KA])
        sch.task("dve", lambda e: e.tensor_tensor(out=dst, in0=Dd, in1=A,
                                                  op=AL.mult),
                 reads=[KD_, KA], writes=[kout])

    def ln_tile():
        sch.task("dve", lambda e: e.reduce_sum(lncols[:, 0:1], xtile, axis=AX.X),
                 reads=["xtile"], writes=["lnc"])
        sch.task("dve", lambda e: e.tensor_scalar(
            out=lncols[:, 1:2], in0=lncols[:, 0:1], scalar1=1.0 / D, scalar2=None,
            op0=AL.mult), reads=["lnc"], writes=["lnc"])
        sch.task("dve", lambda e: e.tensor_scalar(
            out=xntm, in0=xtile, scalar1=lncols[:, 1:2], scalar2=None,
            op0=AL.subtract), reads=["xtile", "lnc"], writes=["xntm"])
        sch.task("dve", lambda e: e.scalar_tensor_tensor(
            out=qD[:], in0=xntm, scalar=1.0, in1=xntm, op0=AL.bypass,
            op1=AL.mult, accum_out=lncols[:, 2:3]),
            reads=["xntm"], writes=["qD", "lnc"])
        sch.task("act", lambda e: e.activation(
            out=lncols[:, 3:4], in_=lncols[:, 2:3], func=AF.Sqrt,
            bias=1e-5, scale=1.0 / D), reads=["lnc"], writes=["lnc3"])
        sch.task("dve", lambda e: e.reciprocal(out=lncols[:, 2:3],
                                               in_=lncols[:, 3:4]),
                 reads=["lnc3"], writes=["lnc"])
        sch.task("dve", lambda e: e.tensor_scalar(
            out=xntm, in0=xntm, scalar1=lncols[:, 2:3], scalar2=None, op0=AL.mult),
            reads=["lnc"], writes=["xntm"])

    try:
        for tt in range(NT):
            dmat("sp", xtile, x_in[tt * 128:(tt + 1) * 128, :], writes=["xtile"])
            ln_tile()
            for dc in range(8):
                sch.task("pe", lambda e, dc=dc: e.transpose(
                    tf[:, 0:128], xntm[:, dc * 128:(dc + 1) * 128], sb_id_f32[:]),
                    reads=["xntm", "idf"], writes=["tfA", "tfbank"])
                sch.task("dve", lambda e, dc=dc: e.tensor_copy(
                    out=xnfm_bf[:, dc * 128:(dc + 1) * 128], in_=tf[:, 0:128]),
                    reads=["tfA"], writes=["sbf"])
            for dc in range(8):
                dmat("sp", hxn0[dc, :, tt * 128:(tt + 1) * 128],
                     xnfm_bf[:, dc * 128:(dc + 1) * 128],
                     reads=["sbf"], writes=["hxn0"])

        def maybe_stop(n):
            if phase_limit <= n:
                raise _StopBuild()

        maybe_stop(1)
        NCOL = NCH * NH
        assert NCOL <= 512
        NG = 8
        dcols = sbuf("dcols", (128, NG * NCOL), F32)
        G_B, G_BLNB, G_BETA, G_GAM, G_NEGW, G_C2, G_GC, G_CQ = range(NG)

        def dcol(g, c, h):
            return dcols[:, g * NCOL + c * NH + h:g * NCOL + c * NH + h + 1]

        def dgrp(g):
            return dcols[:, g * NCOL:(g + 1) * NCOL]

        bR = qA[0:6, 0:512]
        betaR = qB[0:6, 0:512]
        lnbR = qB[0:6, 512:1024]
        gR = qCs[0:6, 0:512]
        wba_sb = sbuf("wba_sb", (128, 128), BF16)
        dcol_s = sbuf("dcol_s", (NH, 2), F32)
        negA = sbuf("negA", (NH, 1), F32)
        xs_mov = sbf[:, 0:1024]

        for dc in range(8):
            dmat("sp", wba_sb[:, dc * 16:dc * 16 + 16], Wba[dc * 128:(dc + 1) * 128, :],
                 writes=["wba"])
        maybe_stop(1.02)
        dmat("sp", dcol_s[:, 0:1], A_log[:, :], writes=["dc0"])
        dmat("sp", dcol_s[:, 1:2], dt_bias[:, :], writes=["dc1"])
        maybe_stop(1.03)
        sch.task("act", lambda e: e.activation(out=negA[:], in_=dcol_s[:, 0:1],
                                               func=AF.Exp),
                 reads=["dc0"], writes=["negA"])
        maybe_stop(1.04)
        sch.task("dve", lambda e: e.tensor_scalar(
            out=negA[:], in0=negA[:], scalar1=-1.0, scalar2=None, op0=AL.mult),
            reads=["negA"], writes=["negA"])
        maybe_stop(1.05)

        for tb8 in range(NB):
            if tb8 > 0:
                maybe_stop(1.15)
            t0 = tb8 * 512
            for dc in range(8):
                dmat("sp", xs_mov[:, 0:512], hxn0[dc, :, t0:t0 + 512],
                     reads=["hxn0"], writes=["sbf"])
                sch.task("pe", lambda e, dc=dc: e.matmul(
                    gp[0][0:6, 0:512], wba_sb[:, dc * 16:dc * 16 + 6],
                    xs_mov[:, 0:512], start=(dc == 0), stop=(dc == 7)),
                    reads=["wba", "sbf"], writes=["gp0"])
                sch.task("pe", lambda e, dc=dc: e.matmul(
                    gp[1][0:6, 0:512], wba_sb[:, dc * 16 + 6:dc * 16 + 12],
                    xs_mov[:, 0:512], start=(dc == 0), stop=(dc == 7)),
                    reads=["wba", "sbf"], writes=["gp1"])
            maybe_stop(1.08)
            sch.task("act", lambda e: e.activation(
                out=betaR, in_=gp[0][0:6, 0:512], func=AF.Sigmoid),
                reads=["gp0"], writes=["rpB"])
            maybe_stop(1.09)
            sch.task("act", lambda e: e.activation(
                out=gR, in_=gp[1][0:6, 0:512], func=AF.Exp, bias=dcol_s[:, 1:2]),
                reads=["gp1", "dc1"], writes=["rpC"])
            maybe_stop(1.11)
            sch.task("act", lambda e: e.activation(
                out=gR, in_=gR, func=AF.Ln, bias=1.0),
                reads=["rpC"], writes=["rpC"])
            maybe_stop(1.12)
            sch.task("dve", lambda e: e.tensor_scalar(
                out=gR, in0=gR, scalar1=negA[:], scalar2=None, op0=AL.mult),
                reads=["rpC", "negA"], writes=["rpC"])
            maybe_stop(1.2)
            for ccl in range(4):
                u0 = ccl * C
                sch.task("dve", lambda e, u0=u0: e.tensor_tensor_scan(
                    out=qA[0:6, u0:u0 + C], data0=qCs[0:6, u0:u0 + C],
                    data1=sb_zrow[:, :], initial=0.0, op0=AL.add, op1=AL.add),
                    reads=["rpC", "zrow"], writes=["rpA"])
            maybe_stop(1.3)
            sch.task("act", lambda e: e.activation(
                out=lnbR, in_=betaR, func=AF.Ln),
                reads=["rpB"], writes=["rpB"])
            sch.task("dve", lambda e: e.tensor_tensor(
                out=gR, in0=lnbR, in1=bR, op=AL.add),
                reads=["rpB", "rpA", "rpC"], writes=["rpC"])
            for ccl in range(4):
                cc = tb8 * 4 + ccl
                u0 = ccl * C
                for src_ap, kread, goff, kw in (
                        (qA[0:6, u0:u0 + C], "rpA", G_B, "dcB"),
                        (qB[0:6, u0:u0 + C], "rpB", G_BETA, "dcBE"),
                        (qCs[0:6, u0:u0 + C], "rpC", G_BLNB, "dcBL")):
                    sch.task("pe", lambda e, src_ap=src_ap: e.transpose(
                        tf[:, 0:6], src_ap, sb_id_f32[0:6, 0:6]),
                        reads=[kread, "idf"],
                        writes=["tfA", "tfB", "tfC", "tfD", "tfbank"])
                    sch.task("dve", lambda e, cc=cc, goff=goff: e.tensor_copy(
                        out=dcols[:, goff * NCOL + cc * NH:goff * NCOL + cc * NH + 6],
                        in_=tf[:, 0:6]), reads=["tfA", "tfbank"], writes=[kw])

        maybe_stop(1.5)
        sch.task("act", lambda e: e.activation(out=dgrp(G_GAM), in_=dgrp(G_B),
                                               func=AF.Exp),
                 reads=["dcB"], writes=["dcG"])
        sch.task("dve", lambda e: e.scalar_tensor_tensor(
            out=dgrp(G_NEGW), in0=dgrp(G_BETA), scalar=-1.0, in1=dgrp(G_GAM),
            op0=AL.mult, op1=AL.mult), reads=["dcBE", "dcG"], writes=["dcNW"])
        maybe_stop(1.7)
        dmat("pool", bcrow_t[0:1, 0:NCOL],
             dcols[127:128, G_B * NCOL:(G_B + 1) * NCOL],
             reads=["dcB"], writes=["rpC"])
        maybe_stop(1.75)
        sch.task("pe", lambda e: e.matmul(
            tf[:, 0:NCOL], sb_onesr[:], bcrow_t[0:1, 0:NCOL], start=True, stop=True),
            reads=["rpC", "onesr"],
            writes=["tfA", "tfB", "tfC", "tfD", "tfbank"])
        maybe_stop(1.8)
        sch.task("act", lambda e: e.activation(out=dgrp(G_GC), in_=tf[:, 0:NCOL],
                                               func=AF.Exp),
                 reads=["tfA", "tfbank"], writes=["dcGC"])
        maybe_stop(1.85)
        sch.task("dve", lambda e: e.tensor_tensor(
            out=dgrp(G_C2), in0=tf[:, 0:NCOL], in1=dgrp(G_B), op=AL.subtract),
            reads=["tfA", "tfbank", "dcB"], writes=["dcC2"])
        maybe_stop(1.9)
        sch.task("act", lambda e: e.activation(out=dgrp(G_C2), in_=dgrp(G_C2),
                                               func=AF.Exp),
                 reads=["dcC2"], writes=["dcC2"])

        maybe_stop(2)
        wg_sb = AB2[:, 0:8 * VD]
        nw_sb = sbuf("nw_sb", (128, VD), BF16)
        xnb2 = sbuf("xstat", (128, 2048), BF16)
        xstat = xnb2[:, 0:1024]
        gtile = sbf[:, 0:512]
        for dc in range(8):
            dmat("sp", wg_sb[:, dc * VD:(dc + 1) * VD], Wg[dc * 128:(dc + 1) * 128, :],
                 writes=["wg"])
        dmat("sp", nw_sb[:], nw_b[:, :], writes=["nw"])
        for tt in range(NT):
            for dc in range(8):
                dmat("sp", xstat[:, dc * 128:(dc + 1) * 128],
                     hxn0[dc, :, tt * 128:(tt + 1) * 128],
                     reads=["hxn0"], writes=["xstat"])
            for half in range(3):
                for dc in range(8):
                    sch.task("pe", lambda e, dc=dc, half=half: e.matmul(
                        gp[half][:, 0:512], xstat[:, dc * 128:(dc + 1) * 128],
                        wg_sb[:, dc * VD + half * 512:dc * VD + (half + 1) * 512],
                        start=(dc == 0), stop=(dc == 7)),
                        reads=["xstat", "wg"], writes=[f"gp{half}"])
                sch.task("act", lambda e, half=half: e.activation(
                    out=gtile[:, 0:512], in_=gp[half][:, 0:512],
                    func=AF.Silu), reads=[f"gp{half}"], writes=["sbf"])
                sch.task("dve", lambda e, half=half: e.tensor_tensor(
                    out=gtile[:, 0:512], in0=gtile[:, 0:512],
                    in1=nw_sb[:, half * 512:(half + 1) * 512] if half < 2
                    else nw_sb[:, 1024:1536], op=AL.mult),
                    reads=["sbf", "nw"], writes=["sbf"])
                dmat("sp", hgsn[tt][:, half * 512:(half + 1) * 512], gtile[:, 0:512],
                     reads=["sbf"], writes=["hgsn"])

        maybe_stop(3)
        WSTR = KD + KD + VD
        wqkv = AB2[:, 0:8 * WSTR]
        for dc in range(8):
            b0 = dc * WSTR
            dmat("sp", wqkv[:, b0:b0 + KD], Wq[dc * 128:(dc + 1) * 128, :],
                 writes=["wqkv", "wg"])
            dmat("sp", wqkv[:, b0 + KD:b0 + 2 * KD], Wk[dc * 128:(dc + 1) * 128, :],
                 writes=["wqkv"])
            dmat("sp", wqkv[:, b0 + 2 * KD:b0 + WSTR], Wv[dc * 128:(dc + 1) * 128, :],
                 writes=["wqkv"])
        taps_sb = sbuf("taps_sb", (128, 96), F32)
        dmat("sp", taps_sb[:, 0:24], convq[:, :], writes=["taps"])
        dmat("sp", taps_sb[:, 24:48], convk[:, :], writes=["taps"])
        dmat("sp", taps_sb[:, 48:96], convv[:, :], writes=["taps"])

        xblk = AB[:, 0:4096]
        tails = sbuf("tails", (128, 72), F32)
        silu_bf = sbf[:, 0:512]

        sch.task("dve", lambda e: e.memset(tails[:], 0.0), writes=["tails"])

        cchunks = []
        for h in range(NH):
            cchunks.append((h * 128, h * 4, hq, h))
        for h in range(NH):
            cchunks.append((KD + h * 128, 24 + h * 4, hk, h))
        for j in range(2 * NH):
            cchunks.append((2 * KD + j * 128, 48 + j * 4, hv, j))

        for tb8 in range(NB):
            t0 = tb8 * 512
            for dc in range(8):
                dmat("sp", xblk[:, dc * 512:(dc + 1) * 512], hxn0[dc, :, t0:t0 + 512],
                     reads=["hxn0"], writes=["xblk"])
            for ci in range(24):
                coff, tap0, dram, slot = cchunks[ci]
                bank = gp[ci % 2]
                kb = f"gp{ci % 2}"
                for dc in range(8):
                    sch.task("pe", lambda e, dc=dc, coff=coff, bank=bank: e.matmul(
                        bank[:, 0:512], wqkv[:, dc * WSTR + coff:dc * WSTR + coff + 128],
                        xblk[:, dc * 512:(dc + 1) * 512],
                        start=(dc == 0), stop=(dc == 7)),
                        reads=["wqkv", "xblk"], writes=[kb])
                sch.task("dve", lambda e, bank=bank: e.tensor_copy(
                    out=pre_sb[:, 3:515], in_=bank[:, 0:512]),
                    reads=[kb], writes=["qD"])
                sch.task("dve", lambda e, ci=ci: e.tensor_copy(
                    out=pre_sb[:, 0:3], in_=tails[:, ci * 3:ci * 3 + 3]),
                    reads=["tails"], writes=["qD"])
                sch.task("dve", lambda e, ci=ci: e.tensor_copy(
                    out=tails[:, ci * 3:ci * 3 + 3], in_=pre_sb[:, 512:515]),
                    reads=["qD"], writes=["tails"])
                sch.task("dve", lambda e, tap0=tap0: e.tensor_scalar(
                    out=cacc[:, 0:512], in0=pre_sb[:, 0:512],
                    scalar1=taps_sb[:, tap0:tap0 + 1],
                    scalar2=None, op0=AL.mult),
                    reads=["qD", "taps"], writes=["q32"])
                for jj in (1, 2, 3):
                    sch.task("dve", lambda e, tap0=tap0, jj=jj: e.scalar_tensor_tensor(
                        out=cacc[:, 0:512], in0=pre_sb[:, jj:jj + 512],
                        scalar=taps_sb[:, tap0 + jj:tap0 + jj + 1], in1=cacc[:, 0:512],
                        op0=AL.mult, op1=AL.add),
                        reads=["qD", "taps", "q32"], writes=["q32"])
                sch.task("act", lambda e: e.activation(out=silu_bf[:, 0:512],
                                                       in_=cacc[:, 0:512],
                                                       func=AF.Silu),
                         reads=["q32"], writes=["sbf"])
                for j in range(4):
                    cix = t0 // 128 + j
                    dmat("sp", dram[slot, cix], silu_bf[:, j * 128:(j + 1) * 128],
                         reads=["sbf"], writes=["hqkv"])

        maybe_stop(4)
        wo_sb = AB2[:, 0:12 * 1024]
        for vc in range(12):
            dmat("sp", wo_sb[:, vc * 1024:(vc + 1) * 1024],
                 Wo[vc * 128:(vc + 1) * 128, :], writes=["wo", "wqkv"])

        S_sb = sbuf("S_sb", (128, NH * DV), F32)
        S_bf = sbuf("S_bf", (128, NH * DV), BF16)
        for h in range(NH):
            sch.task("dve", lambda e, h=h: e.memset(S_sb[:, h * 256:(h + 1) * 256], 0.0),
                     writes=[f"S{h}"])
            sch.task("dve", lambda e, h=h: e.memset(S_bf[:, h * 256:(h + 1) * 256], 0.0),
                     writes=[f"Sbf{h}"])

        HSTR = 3584
        def hb(h, off, n):
            b0 = 4096 + h * HSTR + off
            return AB[:, b0:b0 + n]
        ktm_h = lambda h: hb(h, 0, 128)
        k2tm_h = lambda h: hb(h, 128, 128)
        khf_h = lambda h: hb(h, 256, 128)
        PT_h = lambda h: hb(h, 384, 128)
        TSb_h = lambda h: hb(h, 512, 128)
        TTp_h = lambda h, j: hb(h, 640 + j * 128, 128)
        TSP_h = lambda h, j: hb(h, 896 + j * 128, 128)
        MT_h = lambda h: hb(h, 1664, 128)
        vtm_h = lambda h: hb(h, 1792, 256)
        bv_h = lambda h: hb(h, 2048, 256)
        ru_h = lambda h: hb(h, 2304, 256)
        u_h = lambda h: hb(h, 2560, 256)
        expm_h = lambda h: hb(h, 2816, 384)
        qf_h = lambda h: hb(h, 3200, 128)
        kf_h = lambda h: hb(h, 3328, 128)
        vfm_h = lambda h: hb(h, 3456, 128)
        osc_h = lambda h: AFX[:, 2048 + h * 256:2048 + (h + 1) * 256]
        o1s_h = lambda h: AFX[:, 3584 + h * 256:3584 + (h + 1) * 256]
        ccb_h = lambda h: q32[:, h * 128:(h + 1) * 128]
        ein_h = lambda h: qD[:, h * 128:(h + 1) * 128]
        gsn_sb = sbuf("gsn_sb", (128, VD), BF16)
        o2fm = sbuf("o2fm", (128, 12 * 128), BF16)
        jnk = sbuf("jnk", (128, 128), BF16)
        sso = sbuf("sso", (128, 24), F32)
        fence5 = sch.task("sp", lambda e: e.nop(),
                          writes=["rpB", "rpA", "rpC", "q32", "qD", "xblk"])
        HS = range(NH)

        for cc in range(NCH):
            dmat("sp", gsn_sb[:], hgsn[cc], reads=["hgsn"], writes=["gsn"])
            for h in HS:
                dmat("sp", qf_h(h), hq[h, cc], reads=["hqkv"],
                     writes=[f"qf{h}"], extra=[fence5])
                dmat("sp", kf_h(h), hk[h, cc], reads=["hqkv"],
                     writes=[f"kf{h}"], extra=[fence5])
            for h in HS:
                tq, ktq, bkq = tb_slot()
                sch.task("pe", lambda e, h=h, tq=tq: e.transpose(
                    tq, qf_h(h), sb_id_bf[:]), reads=[f"qf{h}", "idbf"],
                    writes=[ktq, bkq])
                sch.task("act", lambda e, h=h, tq=tq: e.activation(
                    out=jnk[:], in_=tq, func=AF.Square,
                    accum_out=sso[:, h * 4:h * 4 + 1]),
                    reads=[ktq, bkq], writes=["jnk", f"ssoq{h}"])
                sch.task("act", lambda e, h=h, cc=cc: e.activation(
                    out=dcol(G_CQ, cc, h), in_=sso[:, h * 4:h * 4 + 1],
                    func=AF.Sqrt, scale=128.0, bias=128.0e-12),
                    reads=[f"ssoq{h}"], writes=[f"dcCQ{h}"])
                sch.task("dve", lambda e, h=h, cc=cc: e.reciprocal(
                    out=dcol(G_CQ, cc, h), in_=dcol(G_CQ, cc, h)),
                    reads=[f"dcCQ{h}"], writes=[f"dcCQ{h}"])
            for h in HS:
                tk, ktk, bkk = tb_slot()
                sch.task("pe", lambda e, h=h, tk=tk: e.transpose(
                    tk, kf_h(h), sb_id_bf[:]), reads=[f"kf{h}", "idbf"],
                    writes=[ktk, bkk])
                sch.task("act", lambda e, h=h, tk=tk: e.activation(
                    out=jnk[:], in_=tk, func=AF.Square,
                    accum_out=sso[:, h * 4 + 1:h * 4 + 2]),
                    reads=[ktk, bkk], writes=["jnk", f"ssok{h}"])
                sch.task("act", lambda e, h=h: e.activation(
                    out=sso[:, h * 4 + 2:h * 4 + 3],
                    in_=sso[:, h * 4 + 1:h * 4 + 2], func=AF.Sqrt, bias=1e-12),
                    reads=[f"ssok{h}"], writes=[f"dcIK{h}"])
                sch.task("dve", lambda e, h=h: e.reciprocal(
                    out=sso[:, h * 4 + 2:h * 4 + 3],
                    in_=sso[:, h * 4 + 2:h * 4 + 3]),
                    reads=[f"dcIK{h}"], writes=[f"dcIK{h}"])
                sch.task("dve", lambda e, h=h, tk=tk: e.tensor_scalar(
                    out=ktm_h(h), in0=tk, scalar1=sso[:, h * 4 + 2:h * 4 + 3],
                    scalar2=None, op0=AL.mult),
                    reads=[ktk, bkk, f"dcIK{h}"], writes=[f"ktm{h}"])
                sch.task("dve", lambda e, h=h, cc=cc: e.tensor_scalar(
                    out=k2tm_h(h), in0=ktm_h(h), scalar1=dcol(G_C2, cc, h),
                    scalar2=None, op0=AL.mult),
                    reads=[f"ktm{h}", "dcC2"], writes=[f"k2tm{h}"])
                tk2, ktk2, bkk2 = tb_slot()
                sch.task("pe", lambda e, h=h, tk2=tk2: e.transpose(
                    tk2, ktm_h(h), sb_id_bf[:]), reads=[f"ktm{h}", "idbf"],
                    writes=[ktk2, bkk2])
                sch.task("dve", lambda e, h=h, tk2=tk2: e.tensor_copy(
                    out=khf_h(h), in_=tk2), reads=[ktk2, bkk2],
                    writes=[f"khf{h}"])
            for h in HS:
                for half in range(2):
                    dmat("sp", vfm_h(h), hv[2 * h + half, cc],
                         reads=["hqkv"], writes=[f"vf{h}"], extra=[fence5])
                    tv, ktv, bkv = tb_slot()
                    sch.task("pe", lambda e, h=h, tv=tv: e.transpose(
                        tv, vfm_h(h), sb_id_bf[:]), reads=[f"vf{h}", "idbf"],
                        writes=[ktv, bkv])
                    sch.task("dve", lambda e, h=h, half=half, tv=tv: e.tensor_copy(
                        out=vtm_h(h)[:, half * 128:(half + 1) * 128], in_=tv),
                        reads=[ktv, bkv], writes=[f"vtm{h}"])
                sch.task("dve", lambda e, h=h, cc=cc: e.tensor_scalar(
                    out=bv_h(h), in0=vtm_h(h), scalar1=dcol(G_BETA, cc, h),
                    scalar2=None, op0=AL.mult),
                    reads=[f"vtm{h}", "dcBE"], writes=[f"bv{h}"])
            for h in HS:
                tfa, ktfa, bfa = tf_slot()
                sch.task("dve", lambda e, h=h, cc=cc: e.tensor_scalar(
                    out=ccb_h(h), in0=sb_onesmat[:], scalar1=dcol(G_B, cc, h),
                    scalar2=None, op0=AL.mult),
                    reads=["onesm", "dcB"], writes=[f"ccb{h}"], extra=[fence5])
                sch.task("pe", lambda e, h=h, tfa=tfa: e.transpose(
                    tfa, ccb_h(h), sb_id_f32[:]), reads=[f"ccb{h}", "idf"],
                    writes=[ktfa, bfa])
                sch.task("dve", lambda e, h=h, cc=cc, tfa=tfa: e.scalar_tensor_tensor(
                    out=ein_h(h), in0=tfa, scalar=dcol(G_B, cc, h),
                    in1=sb_maskPT[:], op0=AL.subtract, op1=AL.subtract),
                    reads=[ktfa, bfa, "dcB", "mPT"], writes=[f"ein{h}"],
                    extra=[fence5])
                sch.task("act", lambda e, h=h: e.activation(
                    out=expm_h(h)[:, 0:128], in_=ein_h(h), func=AF.Exp),
                    reads=[f"ein{h}"], writes=[f"ePT{h}"])
                sch.task("dve", lambda e, h=h, cc=cc, tfa=tfa: e.scalar_tensor_tensor(
                    out=ein_h(h), in0=tfa, scalar=dcol(G_BLNB, cc, h),
                    in1=sb_maskTS[:], op0=AL.subtract, op1=AL.add),
                    reads=[ktfa, bfa, "dcBL", "mTS", f"ePT{h}"], writes=[f"ein{h}"])
                sch.task("act", lambda e, h=h: e.activation(
                    out=expm_h(h)[:, 256:384], in_=ein_h(h), func=AF.Exp,
                    scale=-1.0), reads=[f"ein{h}"], writes=[f"eTS{h}"])
                tfb, ktfb, bfb2 = tf_slot()
                sch.task("dve", lambda e, h=h, cc=cc: e.tensor_scalar(
                    out=ccb_h(h), in0=sb_onesmat[:], scalar1=dcol(G_BLNB, cc, h),
                    scalar2=None, op0=AL.mult),
                    reads=["onesm", "dcBL", f"ein{h}"], writes=[f"ccb{h}"])
                sch.task("pe", lambda e, h=h, tfb=tfb: e.transpose(
                    tfb, ccb_h(h), sb_id_f32[:]), reads=[f"ccb{h}", "idf"],
                    writes=[ktfb, bfb2])
                sch.task("dve", lambda e, h=h, cc=cc, tfb=tfb: e.scalar_tensor_tensor(
                    out=ein_h(h), in0=tfb, scalar=dcol(G_B, cc, h),
                    in1=sb_maskTT[:], op0=AL.subtract, op1=AL.subtract),
                    reads=[ktfb, bfb2, "dcB", "mTT", f"eTS{h}"], writes=[f"ein{h}"])
                sch.task("act", lambda e, h=h: e.activation(
                    out=expm_h(h)[:, 128:256], in_=ein_h(h), func=AF.Exp),
                    reads=[f"ein{h}"], writes=[f"eTT{h}"])
            for h in HS:
                tg, ktg = gp_slot()
                sch.task("pe", lambda e, h=h, tg=tg: e.matmul(
                    tg[:, 0:128], khf_h(h), khf_h(h), start=True, stop=True),
                    reads=[f"khf{h}"], writes=[ktg])
                sch.task("dve", lambda e, h=h, tg=tg: e.tensor_tensor(
                    out=TTp_h(h, 0), in0=tg[:, 0:128], in1=expm_h(h)[:, 128:256],
                    op=AL.mult), reads=[ktg, f"eTT{h}"], writes=[f"TT{h}_0"])
                sch.task("dve", lambda e, h=h, tg=tg: e.tensor_tensor(
                    out=TSb_h(h), in0=tg[:, 0:128], in1=expm_h(h)[:, 256:384],
                    op=AL.mult), reads=[ktg, f"eTS{h}"], writes=[f"TSb{h}"])
                tg2, ktg2 = gp_slot()
                sch.task("pe", lambda e, h=h, tg2=tg2: e.matmul(
                    tg2[:, 0:128], khf_h(h), qf_h(h), start=True, stop=True),
                    reads=[f"khf{h}", f"qf{h}"], writes=[ktg2])
                sch.task("dve", lambda e, h=h, tg2=tg2: e.tensor_tensor(
                    out=PT_h(h), in0=tg2[:, 0:128], in1=expm_h(h)[:, 0:128],
                    op=AL.mult), reads=[ktg2, f"ePT{h}"], writes=[f"PT{h}"])
                sch.task("dve", lambda e, h=h: e.scalar_tensor_tensor(
                    out=MT_h(h), in0=TTp_h(h, 0), scalar=-1.0, in1=sb_id_bf[:],
                    op0=AL.mult, op1=AL.add),
                    reads=[f"TT{h}_0", "idbf"], writes=[f"MT{h}"])
            prevs = {h: (TSb_h(h), f"TSb{h}", TTp_h(h, 0), f"TT{h}_0") for h in HS}
            for j in range(6):
                for h in HS:
                    ts_p, kts_p, tt_p, ktt_p = prevs[h]
                    tp, ktp = gp_slot()
                    cur = TSP_h(h, j)
                    kcur = f"TSP{h}_{j}"
                    sch.task("pe", lambda e, tp=tp, a=tt_p, b=ts_p: e.matmul(
                        tp[:, 0:128], a, b, start=True, stop=True),
                        reads=[kts_p, ktt_p], writes=[ktp])
                    sch.task("dve", lambda e, tp=tp, cur=cur: e.tensor_copy(
                        out=cur, in_=tp[:, 0:128]), reads=[ktp], writes=[kcur])
                    if j < 5:
                        tp2, ktp2 = gp_slot()
                        curT = TTp_h(h, (j + 1) % 2)
                        kcurT = f"TT{h}_{(j + 1) % 2}"
                        sch.task("pe", lambda e, tp2=tp2, a=ts_p, b=tt_p: e.matmul(
                            tp2[:, 0:128], a, b, start=True, stop=True),
                            reads=[kts_p, ktt_p], writes=[ktp2])
                        sch.task("dve", lambda e, tp2=tp2, curT=curT: e.tensor_copy(
                            out=curT, in_=tp2[:, 0:128]), reads=[ktp2],
                            writes=[kcurT])
                        prevs[h] = (cur, kcur, curT, kcurT)
                    else:
                        prevs[h] = (cur, kcur, None, None)
            for j in range(6):
                for h in HS:
                    tsl = TSP_h(h, j)
                    ktsl = f"TSP{h}_{j}"
                    tp3, ktp3 = gp_slot()
                    sch.task("pe", lambda e, h=h, tp3=tp3, tsl=tsl: e.matmul(
                        tp3[:, 0:128], tsl, MT_h(h), start=True, stop=True),
                        reads=[ktsl, f"MT{h}"], writes=[ktp3])
                    sch.task("dve", lambda e, h=h, tp3=tp3: e.tensor_tensor(
                        out=MT_h(h), in0=MT_h(h), in1=tp3[:, 0:128], op=AL.add),
                        reads=[ktp3, f"MT{h}"], writes=[f"MT{h}"])
            for h in HS:
                if cc == 0:
                    sch.task("dve", lambda e, h=h: e.tensor_copy(
                        out=ru_h(h), in_=bv_h(h)),
                        reads=[f"bv{h}"], writes=[f"ru{h}"])
                else:
                    gb, kgb = gp_slot()
                    sch.task("pe", lambda e, h=h, gb=gb: e.matmul(
                        gb[:, 0:256], khf_h(h), S_bf[:, h * 256:(h + 1) * 256],
                        start=True, stop=True),
                        reads=[f"khf{h}", f"Sbf{h}"], writes=[kgb])
                    sch.task("dve", lambda e, h=h, cc=cc, gb=gb: e.scalar_tensor_tensor(
                        out=ru_h(h), in0=gb[:, 0:256], scalar=dcol(G_NEGW, cc, h),
                        in1=bv_h(h), op0=AL.mult, op1=AL.add),
                        reads=[kgb, "dcNW", f"bv{h}"], writes=[f"ru{h}"])
            for h in HS:
                gu, kgu = gp_slot()
                sch.task("pe", lambda e, h=h, gu=gu: e.matmul(
                    gu[:, 0:256], MT_h(h), ru_h(h), start=True, stop=True),
                    reads=[f"MT{h}", f"ru{h}"], writes=[kgu])
                sch.task("dve", lambda e, h=h, gu=gu: e.tensor_copy(
                    out=u_h(h), in_=gu[:, 0:256]), reads=[kgu], writes=[f"u{h}"])
            for h in HS:
                go1, kgo1 = gp_slot()
                sch.task("pe", lambda e, h=h, go1=go1: e.matmul(
                    go1[:, 0:256], PT_h(h), u_h(h), start=True, stop=True),
                    reads=[f"PT{h}", f"u{h}"], writes=[kgo1])
                if cc == 0:
                    sch.task("dve", lambda e, h=h, cc=cc, go1=go1: e.tensor_scalar(
                        out=osc_h(h), in0=go1[:, 0:256],
                        scalar1=dcol(G_CQ, cc, h), scalar2=None, op0=AL.mult),
                        reads=[kgo1, f"dcCQ{h}"], writes=[f"osc{h}"],
                        extra=[fence5])
                else:
                    sch.task("dve", lambda e, h=h, go1=go1: e.tensor_copy(
                        out=o1s_h(h), in_=go1[:, 0:256]), reads=[kgo1],
                        writes=[f"o1s{h}"], extra=[fence5])
                    go2, kgo2 = gp_slot()
                    sch.task("pe", lambda e, h=h, go2=go2: e.matmul(
                        go2[:, 0:256], qf_h(h), S_bf[:, h * 256:(h + 1) * 256],
                        start=True, stop=True),
                        reads=[f"qf{h}", f"Sbf{h}"], writes=[kgo2])
                    sch.task("dve", lambda e, h=h, cc=cc, go2=go2: e.scalar_tensor_tensor(
                        out=o1s_h(h), in0=go2[:, 0:256], scalar=dcol(G_GAM, cc, h),
                        in1=o1s_h(h), op0=AL.mult, op1=AL.add),
                        reads=[kgo2, "dcG", f"o1s{h}"], writes=[f"o1s{h}"])
                    sch.task("dve", lambda e, h=h, cc=cc: e.tensor_scalar(
                        out=osc_h(h), in0=o1s_h(h),
                        scalar1=dcol(G_CQ, cc, h), scalar2=None, op0=AL.mult),
                        reads=[f"o1s{h}", f"dcCQ{h}"], writes=[f"osc{h}"])
                gd, kgd = gp_slot()
                sch.task("pe", lambda e, h=h, gd=gd: e.matmul(
                    gd[:, 0:256], k2tm_h(h), u_h(h), start=True, stop=True),
                    reads=[f"k2tm{h}", f"u{h}"], writes=[kgd])
                sch.task("dve", lambda e, h=h, cc=cc, gd=gd: e.scalar_tensor_tensor(
                    out=S_sb[:, h * 256:(h + 1) * 256],
                    in0=S_sb[:, h * 256:(h + 1) * 256],
                    scalar=dcol(G_GC, cc, h), in1=gd[:, 0:256],
                    op0=AL.mult, op1=AL.add),
                    reads=[kgd, "dcGC", f"S{h}"], writes=[f"S{h}"])
                sch.task("dve", lambda e, h=h: e.tensor_copy(
                    out=S_bf[:, h * 256:(h + 1) * 256],
                    in_=S_sb[:, h * 256:(h + 1) * 256]),
                    reads=[f"S{h}"], writes=[f"Sbf{h}"])
            for h in HS:
                sch.task("dve", lambda e, h=h: e.scalar_tensor_tensor(
                    out=jnk[:] if False else bv_h(h), in0=osc_h(h), scalar=1.0,
                    in1=osc_h(h), op0=AL.bypass, op1=AL.mult,
                    accum_out=sso[:, h * 4 + 3:h * 4 + 4]),
                    reads=[f"osc{h}"], writes=[f"bv{h}", f"ssoo{h}"])
                sch.task("act", lambda e, h=h: e.activation(
                    out=sso[:, h * 4 + 3:h * 4 + 4],
                    in_=sso[:, h * 4 + 3:h * 4 + 4], func=AF.Sqrt,
                    scale=1.0 / 256.0, bias=1e-5),
                    reads=[f"ssoo{h}"], writes=[f"ssoo{h}"])
                sch.task("dve", lambda e, h=h: e.reciprocal(
                    out=sso[:, h * 4 + 3:h * 4 + 4],
                    in_=sso[:, h * 4 + 3:h * 4 + 4]),
                    reads=[f"ssoo{h}"], writes=[f"ssoo{h}"])
                sch.task("dve", lambda e, h=h: e.scalar_tensor_tensor(
                    out=osc_h(h), in0=osc_h(h), scalar=sso[:, h * 4 + 3:h * 4 + 4],
                    in1=gsn_sb[:, h * 256:(h + 1) * 256], op0=AL.mult,
                    op1=AL.mult),
                    reads=[f"ssoo{h}", "gsn", f"osc{h}"], writes=[f"osc{h}"])
                for half in range(2):
                    to, kto, bko = tf_slot()
                    sch.task("pe", lambda e, h=h, half=half, to=to: e.transpose(
                        to, osc_h(h)[:, half * 128:(half + 1) * 128],
                        sb_id_f32[:]), reads=[f"osc{h}", "idf"],
                        writes=[kto, bko])
                    sch.task("dve", lambda e, h=h, half=half, to=to: e.tensor_copy(
                        out=o2fm[:, (2 * h + half) * 128:(2 * h + half + 1) * 128],
                        in_=to), reads=[kto, bko], writes=["o2fm"])
            for dhalf in range(2):
                bank = gp[dhalf]
                kb = f"gp{dhalf}"
                for vc in range(12):
                    sch.task("pe", lambda e, vc=vc, dhalf=dhalf, bank=bank: e.matmul(
                        bank[:, 0:512], o2fm[:, vc * 128:(vc + 1) * 128],
                        wo_sb[:, vc * 1024 + dhalf * 512:vc * 1024 + (dhalf + 1) * 512],
                        start=(vc == 0), stop=(vc == 11)),
                        reads=["o2fm", "wo"], writes=[kb])
                sch.task("dve", lambda e, dhalf=dhalf, bank=bank: e.tensor_copy(
                    out=sbf[:, dhalf * 512:(dhalf + 1) * 512], in_=bank[:, 0:512]),
                    reads=[kb], writes=["sbf"])
            dmat("sp", hy[cc], sbf[:, 0:1024], reads=["sbf"], writes=["hy"])

        maybe_stop(5)
        w1sT = AB[:, 0:16384]
        w2sT = AB[:, 16384:32768]
        p5keys = []
        for h in range(NH):
            p5keys += [f"qf{h}", f"kf{h}", f"vf{h}", f"ktm{h}", f"k2tm{h}",
                       f"khf{h}", f"PT{h}", f"TSb{h}", f"TT{h}_0", f"TT{h}_1",
                       f"MT{h}", f"vtm{h}", f"bv{h}", f"ru{h}", f"u{h}",
                       f"ePT{h}", f"eTT{h}", f"eTS{h}", f"osc{h}", f"o1s{h}",
                       f"ccb{h}", f"ein{h}"]
            p5keys += [f"TSP{h}_{j}" for j in range(6)]
        fence6 = sch.task("sp", lambda e: e.nop(), writes=p5keys)
        w1c_bf = sbf[:, 0:1024]
        w2c_bf = sbf[:, 0:1024]
        alcols = sbuf("alcols", (128, 32), F32)
        b1_sb = sbuf("b1_sb", (128, 32), F32)
        a2cols = sbuf("a2cols", (128, 8), F32)
        a2b = sbuf("a2b", (128, 1024), BF16)
        h_bf = sbuf("h_bf", (128, 256), BF16)
        yt_bf = q_bf

        dmat("sp", b1_sb[:], b1c[:, :], writes=["b1"])
        for dc2 in range(8):
            for hhx in range(2):
                dmat("pool", w2c_bf[:, 0:1024],
                     w2[dc2 * 128:(dc2 + 1) * 128, hhx * 1024:(hhx + 1) * 1024],
                     writes=["sbf"])
                dmat("pool", xstat[:, 0:1024],
                     w2[dc2 * 128:(dc2 + 1) * 128, 2048 + hhx * 1024:2048 + (hhx + 1) * 1024],
                     writes=["xstat"])
                if hhx == 0:
                    sch.task("dve", lambda e, dc2=dc2: e.tensor_reduce(
                        out=a2cols[:, dc2:dc2 + 1], in_=w2c_bf[:, 0:1024], axis=AX.X,
                        op=AL.add, apply_absolute_value=True),
                        reads=["sbf"], writes=["a2c"])
                else:
                    sch.task("dve", lambda e, dc2=dc2: e.tensor_reduce(
                        out=lncols[:, 0:1], in_=w2c_bf[:, 0:1024], axis=AX.X,
                        op=AL.add, apply_absolute_value=True),
                        reads=["sbf"], writes=["lnc"])
                    sch.task("dve", lambda e, dc2=dc2: e.scalar_tensor_tensor(
                        out=a2cols[:, dc2:dc2 + 1],
                        in0=a2cols[:, dc2:dc2 + 1] if True else None,
                        scalar=1.0, in1=lncols[:, 0:1], op0=AL.mult, op1=AL.add),
                        reads=["lnc", "a2c"], writes=["a2c"])
                sch.task("dve", lambda e: e.tensor_reduce(
                    out=lncols[:, 1:2], in_=xstat[:, 0:1024], axis=AX.X,
                    op=AL.add, apply_absolute_value=True),
                    reads=["xstat"], writes=["lnc1"])
                sch.task("dve", lambda e, dc2=dc2: e.scalar_tensor_tensor(
                    out=a2cols[:, dc2:dc2 + 1], in0=a2cols[:, dc2:dc2 + 1],
                    scalar=1.0, in1=lncols[:, 1:2], op0=AL.mult, op1=AL.add),
                    reads=["lnc1", "a2c"], writes=["a2c"])
        sch.task("dve", lambda e: e.tensor_scalar(
            out=a2cols[:], in0=a2cols[:], scalar1=1.0 / HFF, scalar2=None,
            op0=AL.mult), reads=["a2c"], writes=["a2c"])
        for dc2 in range(8):
            dmat("pool", a2row[0:1, dc2 * 128:(dc2 + 1) * 128],
                 a2cols[:, dc2:dc2 + 1], reads=["a2c"], writes=["rpA"],
                 extra=[fence6], slow=True)
        for half in range(2):
            sch.task("pe", lambda e, half=half: e.matmul(
                tf[:, 0:512], sb_onesr[:], a2row[0:1, half * 512:(half + 1) * 512],
                start=True, stop=True), reads=["rpA", "onesr"],
                writes=["tfA", "tfB", "tfC", "tfD", "tfbank"])
            sch.task("dve", lambda e, half=half: e.tensor_copy(
                out=a2b[:, half * 512:(half + 1) * 512], in_=tf[:, 0:512]),
                reads=["tfA", "tfbank"], writes=["a2b"])
        dmat("sp", b2row, b2r[:, :], writes=["rpC"], extra=[fence6])
        sch.task("dve", lambda e: e.reciprocal(out=bzrow, in_=a2row),
                 reads=["rpA"], writes=["rpB"], extra=[fence6])
        sch.task("dve", lambda e: e.tensor_tensor(
            out=bzrow, in0=bzrow, in1=b2row, op=AL.mult),
            reads=["rpB", "rpC"], writes=["rpB"])

        NT2 = L // 256
        for hh in range(2):
            for hcl in range(16):
                hcg = hh * 16 + hcl
                dmat("pool", w1c_bf[:, 0:1024], w1[hcg * 128:(hcg + 1) * 128, :],
                     writes=["sbf"])
                sch.task("dve", lambda e, hcg=hcg: e.tensor_reduce(
                    out=alcols[:, hcg:hcg + 1], in_=w1c_bf[:, 0:1024], axis=AX.X,
                    op=AL.add, apply_absolute_value=True),
                    reads=["sbf"], writes=["al1"])
                for dc in range(8):
                    ts_, kts_, bks_ = tb_slot()
                    sch.task("pe", lambda e, dc=dc, ts_=ts_: e.transpose(
                        ts_, w1c_bf[:, dc * 128:(dc + 1) * 128], sb_id_bf[:]),
                        reads=["sbf", "idbf"], writes=[kts_, bks_])
                    sch.task("act", lambda e, dc=dc, hcl=hcl, ts_=ts_: e.activation(
                        out=w1sT[:, dc * 2048 + hcl * 128:dc * 2048 + (hcl + 1) * 128],
                        in_=ts_, func=AF.Sign),
                        reads=[kts_, bks_, bks_], writes=["w1sT", "xblk"])
            for dc2 in range(8):
                for hq2 in range(2):
                    dmat("pool", w2c_bf[:, 0:1024],
                         w2[dc2 * 128:(dc2 + 1) * 128,
                            hh * 2048 + hq2 * 1024:hh * 2048 + (hq2 + 1) * 1024],
                         writes=["sbf"])
                    for hcl8 in range(8):
                        hcl = hq2 * 8 + hcl8
                        ts_, kts_, bks_ = tb_slot()
                        sch.task("pe", lambda e, hcl8=hcl8, ts_=ts_: e.transpose(
                            ts_, w2c_bf[:, hcl8 * 128:(hcl8 + 1) * 128], sb_id_bf[:]),
                            reads=["sbf", "idbf"], writes=[kts_, bks_])
                        sch.task("act", lambda e, dc2=dc2, hcl=hcl, ts_=ts_:
                                 e.activation(
                            out=w2sT[:, hcl * 1024 + dc2 * 128:hcl * 1024 + (dc2 + 1) * 128],
                            in_=ts_, func=AF.Sign), reads=[kts_, bks_, bks_], writes=["w2sT"])
            sch.task("dve", lambda e, hh=hh: e.tensor_scalar(
                out=alcols[:, hh * 16:(hh + 1) * 16],
                in0=alcols[:, hh * 16:(hh + 1) * 16], scalar1=1.0 / D, scalar2=None,
                op0=AL.mult), reads=["al1"], writes=["al1"])

            for tblk in range(NT2):
                t0 = tblk * 256
                for dc in range(8):
                    dmat("sp", xnb2[:, dc * 256:(dc + 1) * 256],
                         hxn0[dc, :, t0:t0 + 256], reads=["hxn0"], writes=["xstat"])
                if hh == 0:
                    for j in range(2):
                        for dh in range(2):
                            bank = gp[j * 2 + dh]
                            kb = f"gp{j * 2 + dh}"
                            sch.task("pe", lambda e, dh=dh, bank=bank: e.matmul(
                                bank[:, 0:512], sb_onesr[:],
                                bzrow[0:1, dh * 512:(dh + 1) * 512],
                                start=True, stop=False),
                                reads=["rpB", "onesr"], writes=[kb])
                for hcl in range(16):
                    for dc in range(8):
                        sch.task("pe", lambda e, hcl=hcl, dc=dc: e.matmul(
                            gp[4][:, 0:256],
                            w1sT[:, dc * 2048 + hcl * 128:dc * 2048 + (hcl + 1) * 128],
                            xnb2[:, dc * 256:(dc + 1) * 256],
                            start=(dc == 0), stop=(dc == 7)),
                            reads=["w1sT", "xstat"], writes=["gp4"])
                    hcg = hh * 16 + hcl
                    sch.task("act", lambda e, hcg=hcg: e.activation(
                        out=h_bf[:], in_=gp[4][:, 0:256], func=AF.Gelu,
                        bias=b1_sb[:, hcg:hcg + 1], scale=alcols[:, hcg:hcg + 1]),
                        reads=["gp4", "b1", "al1"], writes=["h_bf"])
                    for j in range(2):
                        for dh in range(2):
                            bank = gp[j * 2 + dh]
                            kb = f"gp{j * 2 + dh}"
                            first = (hh == 1 and hcl == 0)
                            sch.task("pe", lambda e, j=j, dh=dh, hcl=hcl, bank=bank,
                                     first=first: e.matmul(
                                bank[:, 0:512], h_bf[:, j * 128:(j + 1) * 128],
                                w2sT[:, hcl * 1024 + dh * 512:hcl * 1024 + (dh + 1) * 512],
                                start=first, stop=(hcl == 15)),
                                reads=["h_bf", "w2sT"], writes=[kb])
                for j in range(2):
                    tt = t0 // 128 + j
                    if hh == 0:
                        dmat("sp", qD[:], x_in[tt * 128:(tt + 1) * 128, :],
                             writes=["qD"])
                        dmat("sp", yt_bf[:], hy[tt], reads=["hy"], writes=["qbf"])
                        sch.task("dve", lambda e: e.tensor_tensor(
                            out=qD[:], in0=qD[:], in1=yt_bf[:], op=AL.add),
                            reads=["qbf", "qD"], writes=["qD"])
                    else:
                        dmat("sp", qD[:], hx1[tt], reads=["hx1w"], writes=["qD"])
                    for dh in range(2):
                        bank = gp[j * 2 + dh]
                        kb = f"gp{j * 2 + dh}"
                        sch.task("dve", lambda e, dh=dh, bank=bank: e.tensor_tensor(
                            out=q32[:, dh * 512:(dh + 1) * 512], in0=bank[:, 0:512],
                            in1=a2b[:, dh * 512:(dh + 1) * 512], op=AL.mult),
                            reads=[kb, "a2b"], writes=["q32"])
                    sch.task("dve", lambda e: e.tensor_tensor(
                        out=qD[:], in0=qD[:], in1=q32[:], op=AL.add),
                        reads=["q32", "qD"], writes=["qD"])
                    dmat("sp", hx1[tt], qD[:], reads=["qD"], writes=["hx1w"])

        maybe_stop(6)
        wfT = AB2[:, 0:8192]
        for fc in range(8):
            dmat("pool", q32[:], wf[fc * 128:(fc + 1) * 128, :], writes=["q32"])
            round_pipe(q32[:], q_bf[:], "q32", "qbf", mant_bits=3, min_p=-6,
                       clip=448.0)
            for cc2 in range(8):
                ts_, kts_, bks_ = tb_slot()
                sch.task("pe", lambda e, cc2=cc2, ts_=ts_: e.transpose(
                    ts_, q_bf[:, cc2 * 128:(cc2 + 1) * 128], sb_id_bf[:]),
                    reads=["qbf", "idbf"], writes=[kts_, bks_])
                sch.task("dve", lambda e, cc2=cc2, fc=fc, ts_=ts_: e.tensor_copy(
                    out=wfT[:, cc2 * 1024 + fc * 128:cc2 * 1024 + (fc + 1) * 128],
                    in_=ts_), reads=[kts_, bks_, bks_], writes=["wfT", "wo"])
        for tt in range(NT):
            dmat("sp", xtile, hx1[tt], reads=["hx1w"], writes=["xtile"])
            ln_tile()
            for dc in range(8):
                sch.task("pe", lambda e, dc=dc: e.transpose(
                    tf[:, 0:128], xntm[:, dc * 128:(dc + 1) * 128], sb_id_f32[:]),
                    reads=["xntm", "idf"], writes=["tfA", "tfbank"])
                sch.task("dve", lambda e, dc=dc: e.tensor_copy(
                    out=xnfm_bf[:, dc * 128:(dc + 1) * 128], in_=tf[:, 0:128]),
                    reads=["tfA"], writes=["sbf"])
                sch.task("dve", lambda e, dc=dc: e.tensor_copy(
                    out=q32[:, dc * 128:(dc + 1) * 128], in_=tf[:, 0:128]),
                    reads=["tfA", "tfbank"], writes=["q32"])
            for dc in range(8):
                dmat("sp", hxn1[dc, :, tt * 128:(tt + 1) * 128],
                     xnfm_bf[:, dc * 128:(dc + 1) * 128],
                     reads=["sbf"], writes=["hxn1"])
            if phase_limit <= 6.1:
                continue
            round_pipe(q32[:], q_bf[:], "q32", "qbf", mant_bits=3, min_p=-6,
                       clip=448.0)
            if phase_limit <= 6.3:
                continue
            for dc in range(8):
                dmat("sp", hxq[dc, :, tt * 128:(tt + 1) * 128],
                     q_bf[:, dc * 128:(dc + 1) * 128],
                     reads=["qbf"], writes=["hxq"])
            for dh in range(2):
                bank = gp[dh]
                kb = f"gp{dh}"
                for cc2 in range(8):
                    sch.task("pe", lambda e, cc2=cc2, dh=dh, bank=bank: e.matmul(
                        bank[:, 0:512], q_bf[:, cc2 * 128:(cc2 + 1) * 128],
                        wfT[:, cc2 * 1024 + dh * 512:cc2 * 1024 + (dh + 1) * 512],
                        start=(cc2 == 0), stop=(cc2 == 7)),
                        reads=["qbf", "wfT"], writes=[kb])
                sch.task("dve", lambda e, dh=dh, bank=bank: e.tensor_copy(
                    out=q32[:, dh * 512:(dh + 1) * 512], in_=bank[:, 0:512]),
                    reads=[kb], writes=["q32"])
            round_pipe(q32[:], f16b[:], "q32", "f16b", mant_bits=10, min_p=-14,
                       clip=None, tie_even=True)
            dmat("sp", hf1[tt], f16b[:], reads=["f16b"], writes=["hf1"])

        maybe_stop(8)
        KR = AB2[:, 0:32768]
        VR = AB[:, 0:32768]
        AB5 = sbuf("AB5", (128, 4096), BF16)
        wl1 = AB5[:, 0:4096]
        qp_bf = gsn_sb[:, 0:512]
        qx_bf = gsn_sb[:, 512:1024]
        fence8 = sch.task("sp", lambda e: e.nop(),
                          writes=["wl1", "qp", "qx", "xblk0", "xblk1",
                                  "xblk", "w1sT", "gsn"])

        def l1_proj(wsrc, cost, sint, is_q):
            for wh in range(2):
                for dc in range(8):
                    dmat("pool", wl1[:, dc * 512:(dc + 1) * 512],
                         wsrc[dc * 128:(dc + 1) * 128, wh * 512:(wh + 1) * 512],
                         writes=["wl1"], extra=[fence8])
                for tb8 in range(NB):
                    t0 = tb8 * 512
                    par = (wh * NB + tb8) % 2
                    xbuf = AB[:, par * 4096:(par + 1) * 4096]
                    kxb = f"xblk{par}"
                    for dc in range(8):
                        dmat("sp", xbuf[:, dc * 512:(dc + 1) * 512],
                             hxn1[dc, :, t0:t0 + 512], reads=["hxn1"],
                             writes=[kxb], extra=[fence8])
                    for dc2l in range(4):
                        dc2 = wh * 4 + dc2l
                        bank = gp[dc2 % 2]
                        kb = f"gp{dc2 % 2}"
                        for dc in range(8):
                            sch.task("pe", lambda e, dc=dc, dc2l=dc2l, bank=bank,
                                     xbuf=xbuf: e.matmul(
                                bank[:, 0:512],
                                wl1[:, dc * 512 + dc2l * 128:dc * 512 + (dc2l + 1) * 128],
                                xbuf[:, dc * 512:(dc + 1) * 512],
                                start=(dc == 0), stop=(dc == 7)),
                                reads=["wl1", kxb], writes=[kb])
                        sch.task("dve", lambda e, bank=bank: e.tensor_copy(
                            out=qp_bf[:], in_=bank[:, 0:512]), reads=[kb], writes=["qp"])
                        bank2 = gp[2 + dc2 % 2]
                        kb2 = f"gp{2 + dc2 % 2}"
                        sch.task("pe", lambda e, bank2=bank2: e.matmul(
                            bank2[:, 0:512], sb_rotT[:], qp_bf[:],
                            start=True, stop=True),
                            reads=["qp", "rotT"], writes=[kb2])
                        dmat("sp", trigc,
                             cost[dc2 * 128:(dc2 + 1) * 128, t0:t0 + 512],
                             writes=["rpC"])
                        dmat("sp", trigs,
                             sint[dc2 * 128:(dc2 + 1) * 128, t0:t0 + 512],
                             writes=["rpC2"])
                        sch.task("dve", lambda e: e.tensor_tensor(
                            out=qA[:, 0:512], in0=qp_bf[:], in1=trigc,
                            op=AL.mult), reads=["qp", "rpC"], writes=["rpA"])
                        sch.task("dve", lambda e, bank2=bank2: e.tensor_tensor(
                            out=qB[:, 0:512], in0=bank2[:, 0:512], in1=trigs,
                            op=AL.mult), reads=[kb2, "rpC2"], writes=["rpB"])
                        sch.task("dve", lambda e: e.tensor_tensor(
                            out=qx_bf[:], in0=qA[:, 0:512], in1=qB[:, 0:512],
                            op=AL.add), reads=["rpA", "rpB"], writes=["qx"])
                        if is_q:
                            for j in range(4):
                                dmat("sp", hq1[t0 // 128 + j, dc2],
                                     qx_bf[:, j * 128:(j + 1) * 128],
                                     reads=["qx"], writes=["hq1"])
                        else:
                            sch.task("dve", lambda e, dc2=dc2, t0=t0: e.tensor_copy(
                                out=KR[:, dc2 * L + t0:dc2 * L + t0 + 512],
                                in_=qx_bf[:]),
                                reads=["qx"], writes=["KR", "wfT"])

        l1_proj(WQ1, ctq, stq, True)
        l1_proj(WK1, ctk, stk, False)
        for wh in range(2):
            for dc in range(8):
                dmat("pool", wl1[:, dc * 512:(dc + 1) * 512],
                     WV1[dc * 128:(dc + 1) * 128, wh * 512:(wh + 1) * 512],
                     writes=["wl1"])
            for tt in range(NT):
                for dc in range(8):
                    dmat("sp", xstat[:, dc * 128:(dc + 1) * 128],
                         hxn1[dc, :, tt * 128:(tt + 1) * 128],
                         reads=["hxn1"], writes=["xstat"])
                bank = gp[tt % 2]
                kb = f"gp{tt % 2}"
                for dc in range(8):
                    sch.task("pe", lambda e, dc=dc, bank=bank: e.matmul(
                        bank[:, 0:512], xstat[:, dc * 128:(dc + 1) * 128],
                        wl1[:, dc * 512:(dc + 1) * 512],
                        start=(dc == 0), stop=(dc == 7)),
                        reads=["xstat", "wl1"], writes=[kb])
                sch.task("dve", lambda e, tt=tt, wh=wh, bank=bank: e.tensor_copy(
                    out=VR[:, tt * 1024 + wh * 512:tt * 1024 + (wh + 1) * 512],
                    in_=bank[:, 0:512]),
                    reads=[kb], writes=["VR", "xblk0", "xblk1", "w1sT", "w2sT"])

        maybe_stop(9)
        qstat = xnb2[:, 0:1024]
        mcols = sbuf("mcols", (128, 16), F32)
        ed_bf = qp_bf
        pt_full = o2fm[:, 0:512]
        zc = sbuf("zc", (128, 8), F32)
        bfb_sb = xnb2[:, 1024:2048]
        out_sb = q32
        fence9 = sch.task("sp", lambda e: e.nop(),
                          writes=["xstat", "wl1", "o2fm", "ptf"])
        dmat("sp", bfb_sb[:], bf_b[:, :], writes=["bfb"], extra=[fence9])

        for qt in range(NT):
            q0 = qt * 128
            dmat("sp", qstat[:], hq1[qt].rearrange("dc p c -> p dc c"),
                 reads=["hq1"], writes=["qstat"], extra=[fence9])
            for kc in range(NKC):
                bank = gp[kc % 2]
                kb = f"gp{kc % 2}"
                for dc in range(8):
                    sch.task("pe", lambda e, dc=dc, kc=kc, bank=bank: e.matmul(
                        bank[:, 0:512], qstat[:, dc * 128:(dc + 1) * 128],
                        KR[:, dc * L + kc * 512:dc * L + kc * 512 + 512],
                        start=(dc == 0), stop=(dc == 7)),
                        reads=["qstat", "KR"], writes=[kb])
                sch.task("dve", lambda e, kc=kc, bank=bank: e.reduce_max(
                    mcols[:, kc:kc + 1], bank[:, 0:512], axis=AX.X),
                    reads=[kb], writes=["mc"])
                sch.task("dve", lambda e, kc=kc, bank=bank: e.tensor_copy(
                    out=srow[:, kc * 512:(kc + 1) * 512], in_=bank[:, 0:512]),
                    reads=[kb], writes=SROW_W)
            sch.task("dve", lambda e: e.reduce_max(
                mcols[:, 9:10], mcols[:, 0:NKC], axis=AX.X),
                reads=["mc"], writes=["mfin"])
            sch.task("dve", lambda e: e.tensor_scalar(
                out=mcols[:, 10:11], in0=mcols[:, 9:10], scalar1=-1.0, scalar2=None,
                op0=AL.mult), reads=["mfin"], writes=["negm"])
            live = [kc for kc in range(NKC)
                    if max(0, q0 - (kc * 512 + 511), kc * 512 - (q0 + 127)) <= DEC_SKIP]
            for kc in range(NKC):
                k0 = kc * 512
                sch.task("act", lambda e, kc=kc: e.activation(
                    out=sbf[:, 0:512], in_=srow[:, kc * 512:(kc + 1) * 512],
                    func=AF.Exp, bias=mcols[:, 10:11], accum_out=zc[:, kc:kc + 1]),
                    reads=["srow", "negm"], writes=["sbf", f"zc{kc}"])
                if kc not in live:
                    continue
                j0 = DEC_OFF + k0 - q0
                dmat("sp", dlt[:, 0:512], dlog[:, j0:j0 + 512], writes=["q32"])
                sch.task("dve", lambda e, kc=kc: e.tensor_tensor(
                    out=sdec, in0=srow[:, kc * 512:(kc + 1) * 512], in1=dlt[:, 0:512],
                    op=AL.add), reads=["srow", "q32"], writes=SDEC_W)
                sch.task("act", lambda e: e.activation(
                    out=ed_bf[:], in_=sdec, func=AF.Exp, bias=mcols[:, 10:11]),
                    reads=["sdec", "negm"], writes=["qp"])
                for j in range(4):
                    tsl, ktsl, bksl = tb_slot()
                    sch.task("pe", lambda e, j=j, tsl=tsl: e.transpose(
                        tsl, ed_bf[:, j * 128:(j + 1) * 128], sb_id_bf[:]),
                        reads=["qp", "idbf"], writes=[ktsl, bksl])
                    sch.task("dve", lambda e, j=j, tsl=tsl: e.tensor_copy(
                        out=pt_full[:, j * 128:(j + 1) * 128], in_=tsl),
                        reads=[ktsl, bksl, bksl], writes=["ptf"], extra=[fence9])
                lastkc = (kc == live[-1])
                for j in range(4):
                    ktile = kc * 4 + j
                    st = (kc == live[0] and j == 0)
                    sp_ = (lastkc and j == 3)
                    for dh in range(2):
                        bank = gp[2 + dh]
                        kb = f"gp{2 + dh}"
                        sch.task("pe", lambda e, j=j, dh=dh, ktile=ktile, st=st,
                                 sp_=sp_, bank=bank: e.matmul(
                            bank[:, 0:512], pt_full[:, j * 128:(j + 1) * 128],
                            VR[:, ktile * 1024 + dh * 512:ktile * 1024 + (dh + 1) * 512],
                            start=st, stop=sp_),
                            reads=["ptf", "VR"], writes=[kb])
            sch.task("dve", lambda e: e.reduce_sum(
                mcols[:, 11:12], zc[:, 0:NKC], axis=AX.X),
                reads=[f"zc{kc}" for kc in range(NKC)], writes=["Z"])
            sch.task("dve", lambda e: e.reciprocal(out=mcols[:, 12:13],
                                                   in_=mcols[:, 11:12]),
                     reads=["Z"], writes=["rZ"])
            dmat("sp", qD[:], hx1[qt], reads=["hx1w"], writes=["qD"])
            dmat("sp", f16b[:], hf1[qt], reads=["hf1"], writes=["f16b"])
            sch.task("dve", lambda e: e.tensor_tensor(
                out=qD[:], in0=qD[:], in1=f16b[:], op=AL.add),
                reads=["f16b", "qD"], writes=["qD"])
            sch.task("dve", lambda e: e.tensor_tensor(
                out=qD[:], in0=qD[:], in1=bfb_sb[:], op=AL.add),
                reads=["bfb", "qD"], writes=["qD"])
            for dh in range(2):
                bank = gp[2 + dh]
                kb = f"gp{2 + dh}"
                sch.task("dve", lambda e, dh=dh, bank=bank: e.scalar_tensor_tensor(
                    out=out_sb[:, dh * 512:(dh + 1) * 512], in0=bank[:, 0:512],
                    scalar=mcols[:, 12:13], in1=qD[:, dh * 512:(dh + 1) * 512],
                    op0=AL.mult, op1=AL.add),
                    reads=[kb, "rZ", "qD"], writes=["q32"])
            dmat("sp", out[qt * 128:(qt + 1) * 128, :], out_sb[:],
                 reads=["q32"], writes=["outw"])


    except _StopBuild:
        dmat("sp", out[0:128, :], x_in[0:128, :], writes=["outw"])

    sems = {}
    for e in Sched.ENGS:
        g = nc.semaphore(f"sem_{e}")
        sems[("eng", e)] = g.__enter__()
        ctx.append(g)
    for e in Sched.DMAQ:
        for si in range(Sched.NSEM[e]):
            g = nc.semaphore(f"dmas_{e}{si}")
            sems[("dma", e, si)] = g.__enter__()
            ctx.append(g)
    blk = nc.Block()
    blk.__enter__()
    sch.emit(nc, blk, sems)
    blk.__exit__(None, None, None)
    for g in reversed(ctx):
        g.__exit__(None, None, None)
    nc._sched = sch
    return nc


def make_inputs_for_core(xb, I, consts):
    m = {"x": np.ascontiguousarray(xb, dtype=np.float32)}
    m["Wq"] = I["L0_attn_Wq"]
    m["Wk"] = I["L0_attn_Wk"]
    m["Wv"] = I["L0_attn_Wv"]
    wba = np.zeros((D, 16), dtype=np.float32)
    wba[:, 0:6] = I["L0_attn_Wb"]
    wba[:, 6:12] = I["L0_attn_Wa"]
    m["Wba"] = wba
    m["A_log"] = np.ascontiguousarray(I["L0_attn_A_log"]).reshape(NH, 1).astype(np.float32)
    m["dt_bias"] = np.ascontiguousarray(I["L0_attn_dt_bias"]).reshape(NH, 1).astype(np.float32)
    m["convq"] = np.ascontiguousarray(
        np.asarray(I["L0_attn_conv_q"]).reshape(NH, 128, 4).transpose(1, 0, 2).reshape(128, 24))
    m["convk"] = np.ascontiguousarray(
        np.asarray(I["L0_attn_conv_k"]).reshape(NH, 128, 4).transpose(1, 0, 2).reshape(128, 24))
    m["convv"] = np.ascontiguousarray(
        np.asarray(I["L0_attn_conv_v"]).reshape(12, 128, 4).transpose(1, 0, 2).reshape(128, 48))
    m["Wg"] = I["L0_attn_Wg"]
    m["nw_b"] = np.ascontiguousarray(
        np.broadcast_to(np.tile(np.asarray(I["L0_attn_norm_w"]), NH), (128, VD)))
    m["Wo"] = I["L0_attn_Wo"]
    m["w1"] = I["L0_ffn_w1"]
    m["b1c"] = np.ascontiguousarray(np.asarray(I["L0_ffn_b1"]).reshape(32, 128).T)
    m["w2"] = I["L0_ffn_w2"]
    m["b2r"] = np.ascontiguousarray(np.asarray(I["L0_ffn_b2"]).reshape(1, D))
    m["WQ1"] = I["L1_attn_W_Q"]
    m["WK1"] = I["L1_attn_W_K"]
    m["WV1"] = I["L1_attn_W_V"]
    m["wf"] = I["L1_ffn_weight"]
    m["bf_b"] = np.ascontiguousarray(np.broadcast_to(np.asarray(I["L1_ffn_bias"]), (128, D)))
    for k, v in consts.items():
        m[k] = v
    res = {}
    for k, v in m.items():
        a = np.asarray(v)
        if a.dtype == np.float64:
            a = a.astype(np.float32)
        res[k] = np.ascontiguousarray(a)
    return res


_NC_CACHE = {}


def kernel(**inputs):
    L = np.asarray(inputs["L0_attn_x"]).shape[1]
    if L not in _NC_CACHE:
        _NC_CACHE[L] = build_core(L)
    nc = _NC_CACHE[L]
    consts = host_constants(L)
    x = np.ascontiguousarray(np.asarray(inputs["L0_attn_x"], dtype=np.float32))
    n = x.shape[0]
    in_maps = [make_inputs_for_core(x[i], inputs, consts) for i in range(n)]
    res = run_bass_kernel_spmd(nc, in_maps, core_ids=list(range(n)))
    return np.stack([r["out"] for r in res.results], axis=0)


# version 13
# speedup vs baseline: 1.1097x; 1.0070x over previous
import numpy as np

import concourse.bass as bass
import concourse.mybir as mybir
from concourse.bass_utils import run_bass_kernel_spmd

F32 = mybir.dt.float32
BF16 = mybir.dt.bfloat16
F16 = mybir.dt.float16
AF = mybir.ActivationFunctionType
AL = mybir.AluOpType
AX = mybir.AxisListType

LN2 = float(np.log(2.0))
INVLN2 = float(1.0 / np.log(2.0))
GAMMA = 0.9
LNG = float(np.log(GAMMA))
BIG = 1.0e30

D = 1024
NH = 6
DK = 128
DV = 256
KD = NH * DK
VD = NH * DV
HFF = 4096
C = 128
DEC_SKIP = 1100
DEC_OFF = 1664
DEC_BAND = 3456

SCALAR_CONSTS = [0.0, 1.0, -1.0, 0.5, 1e-5, 1e-12, 1.0 / 1024.0, 1.0 / 256.0,
                 128.0, 128.0e-12, float(np.log(2.0)), float(-np.log(2.0)),
                 float(np.log(2.0) * 67), float(-np.log(2.0) * 67),
                 float(np.log(2.0) * 74), float(-np.log(2.0) * 74)]


def _bf16np():
    import ml_dtypes
    return np.dtype(ml_dtypes.bfloat16)


class Sched:
    ENGS = ("pe", "act", "dve", "pool", "sp")
    DMAQ = ("sp", "pool", "act")
    NSEM = {"sp": 24, "pool": 16, "act": 4}

    def __init__(self):
        self.items = {e: [] for e in self.ENGS}
        self.n = {e: 0 for e in self.ENGS}
        self.ncomp = {e: 0 for e in self.ENGS}
        self.ndma = {e: 0 for e in self.DMAQ}
        self.semuse = {}
        self.last_w = {}
        self.readers = {}

    PSUM_BANKS = frozenset(["gp0", "gp1", "gp2", "gp3", "gp4",
                            "tbank0", "tbank1", "tfbank"])

    def task(self, eng, fn, reads=(), writes=(), dma=False, extra=()):
        pb = [r for r in reads if r in self.PSUM_BANKS]
        if pb:
            reads = [r for r in reads if r not in self.PSUM_BANKS]
            writes = list(writes) + pb
        deps = []
        for r in reads:
            c = self.last_w.get(r)
            if c is not None:
                deps.append(c)
        for w in writes:
            c = self.last_w.get(w)
            if c is not None:
                deps.append(c)
            deps.extend(self.readers.get(w, ()))
        deps.extend(extra)
        self.n[eng] += 1
        if dma:
            si = self.ndma[eng] % self.NSEM[eng]
            self.ndma[eng] += 1
            skey = ("dma", eng, si)
            cnt = self.semuse.get(skey, 0) + 1
            self.semuse[skey] = cnt
            if cnt > 1:
                deps.append((skey, 16 * (cnt - 1)))
            comp = (skey, 16 * cnt)
            dmakey = skey
        else:
            self.ncomp[eng] += 1
            comp = (("eng", eng), self.ncomp[eng])
            dmakey = None
        self.items[eng].append((deps, fn, dmakey))
        for r in reads:
            self.readers.setdefault(r, []).append(comp)
        for w in writes:
            self.last_w[w] = comp
            self.readers[w] = []
        return comp

    def emit(self, nc, block, sems):
        def make_body(ename, items):
            def body(eng):
                satisfied = {}
                own = ("eng", ename)
                if ename != "sp":
                    eng.wait_ge(sems[("dma", "sp", 0)], 16)
                    satisfied[("dma", "sp", 0)] = 16
                for deps, fn, dmakey in items:
                    for key, val in deps:
                        if satisfied.get(key, 0) >= val:
                            continue
                        eng.wait_ge(sems[key], val)
                        satisfied[key] = val
                    inst = fn(eng)
                    if dmakey is not None:
                        inst.then_inc(sems[dmakey], 16)
                    else:
                        inst.then_inc(sems[("eng", ename)], 1)
                if ename in self.DMAQ:
                    for si in range(self.NSEM[ename]):
                        skey = ("dma", ename, si)
                        cnt = self.semuse.get(skey, 0)
                        if cnt > 0:
                            eng.wait_ge(sems[skey], 16 * cnt)
            return body

        block.tensor(make_body("pe", self.items["pe"]))
        block.scalar(make_body("act", self.items["act"]))
        block.vector(make_body("dve", self.items["dve"]))
        block.gpsimd(make_body("pool", self.items["pool"]))
        block.sync(make_body("sp", self.items["sp"]))


def host_constants(L):
    bf = _bf16np()
    c = {}
    eye = np.eye(128, dtype=np.float64)
    c["c_id_bf16"] = eye.astype(bf)
    c["c_id_f32"] = eye.astype(np.float32)
    c["c_ones_f32"] = np.ones((1, 128), dtype=np.float32)
    c["c_onesmat"] = np.ones((128, 128), dtype=bf)
    c["c_zero_row_f32"] = np.zeros((6, 128), dtype=np.float32)
    r = np.arange(128)[:, None]
    t = np.arange(128)[None, :]
    c["c_maskPT"] = np.where(r <= t, 0.0, BIG).astype(bf)
    c["c_maskTT"] = np.where(r < t, 0.0, BIG).astype(bf)
    c["c_maskTS"] = np.where(t < r, 0.0, BIG).astype(bf)
    R = np.zeros((128, 128), dtype=np.float64)
    for i in range(64):
        R[2 * i, 2 * i + 1] = -1.0
        R[2 * i + 1, 2 * i] = 1.0
    c["c_rotT_bf16"] = np.ascontiguousarray(R.T).astype(bf)
    p = np.arange(128)[:, None]
    j = np.arange(DEC_BAND)[None, :]
    c["c_dlog"] = (LNG * np.abs(p - (j - DEC_OFF)).astype(np.float64)).astype(np.float32)
    half = D // 2
    sv = (np.arange(0, D, 2, dtype=np.float64) + 0.4 * D) / (1.4 * D)
    power = np.arange(L, dtype=np.float64) / 512.0
    scale = sv[None, :] ** power[:, None]
    inv_freq = 1.0 / 10000.0 ** (np.arange(half, dtype=np.float64) / half)
    sinus = np.arange(L, dtype=np.float64)[:, None] * inv_freq[None, :]
    sin = np.repeat(np.sin(sinus), 2, axis=1)
    cos = np.repeat(np.cos(sinus), 2, axis=1)
    scl = np.repeat(scale, 2, axis=1)
    c["c_cos_q"] = np.ascontiguousarray((cos * scl).T).astype(np.float32)
    c["c_sin_q"] = np.ascontiguousarray((sin * scl).T).astype(np.float32)
    c["c_cos_k"] = np.ascontiguousarray((cos / scl).T).astype(np.float32)
    c["c_sin_k"] = np.ascontiguousarray((sin / scl).T).astype(np.float32)
    c["c_scal"] = np.broadcast_to(
        np.asarray(SCALAR_CONSTS, dtype=np.float32), (128, len(SCALAR_CONSTS))
    ).copy()
    return c


class _StopBuild(Exception):
    pass


def build_core(L, phase_limit=99):
    nc = bass.Bass("TRN2", target_bir_lowering=False)
    NT = L // 128
    NCH = L // C
    NKC = L // 512
    NB = L // 512

    def din(name, shape, dtype=F32):
        return nc.dram_tensor(name, list(shape), dtype, kind="ExternalInput").ap()

    x_in = din("x", (L, D))
    Wq = din("Wq", (D, KD)); Wk = din("Wk", (D, KD)); Wv = din("Wv", (D, VD))
    Wba = din("Wba", (D, 16))
    A_log = din("A_log", (NH, 1))
    dt_bias = din("dt_bias", (NH, 1))
    convq = din("convq", (128, 24))
    convk = din("convk", (128, 24))
    convv = din("convv", (128, 48))
    Wg = din("Wg", (D, VD))
    nw_b = din("nw_b", (128, VD))
    Wo = din("Wo", (VD, D))
    w1 = din("w1", (HFF, D))
    b1c = din("b1c", (128, 32))
    w2 = din("w2", (D, HFF))
    b2r = din("b2r", (1, D))
    WQ1 = din("WQ1", (D, D)); WK1 = din("WK1", (D, D)); WV1 = din("WV1", (D, D))
    wf = din("wf", (D, D))
    bf_b = din("bf_b", (128, D))

    cid_bf = din("c_id_bf16", (128, 128), BF16)
    cid_f32 = din("c_id_f32", (128, 128), F32)
    onesr = din("c_ones_f32", (1, 128), F32)
    onesmat = din("c_onesmat", (128, 128), BF16)
    zrow6 = din("c_zero_row_f32", (6, 128), F32)
    maskPT = din("c_maskPT", (128, 128), BF16)
    maskTT = din("c_maskTT", (128, 128), BF16)
    maskTS = din("c_maskTS", (128, 128), BF16)
    rotT = din("c_rotT_bf16", (128, 128), BF16)
    dlog = din("c_dlog", (128, DEC_BAND), F32)
    ctq = din("c_cos_q", (D, L)); stq = din("c_sin_q", (D, L))
    ctk = din("c_cos_k", (D, L)); stk = din("c_sin_k", (D, L))

    out = nc.dram_tensor("out", [L, D], F32, kind="ExternalOutput").ap()

    hxn0 = nc.dram_tensor("hxn0", [8, 128, L], BF16).ap()
    hq = nc.dram_tensor("hq", [NH, NCH, 128, 128], BF16).ap()
    hk = nc.dram_tensor("hk", [NH, NCH, 128, 128], BF16).ap()
    hv = nc.dram_tensor("hv", [2 * NH, NCH, 128, 128], BF16).ap()
    hgsn = nc.dram_tensor("hgsn", [NCH, 128, VD], BF16).ap()
    hy = nc.dram_tensor("hy", [NT, 128, D], BF16).ap()
    hx1 = nc.dram_tensor("hx1", [NT, 128, D], F32).ap()
    hxn1 = nc.dram_tensor("hxn1", [8, 128, L], BF16).ap()
    hxq = nc.dram_tensor("hxq", [8, 128, L], BF16).ap()
    hq1 = nc.dram_tensor("hq1", [NT, 8, 128, 128], BF16).ap()
    hf1 = nc.dram_tensor("hf1", [NT, 128, D], F16).ap()

    sch = Sched()
    ctx = []

    def sbuf(name, shape, dtype):
        g = nc.sbuf_tensor(name, list(shape), dtype)
        h = g.__enter__()
        ctx.append(g)
        return h

    def psumt(name, shape, dtype=F32):
        g = nc.psum_tensor(name, list(shape), dtype)
        h = g.__enter__()
        ctx.append(g)
        return h

    gp = [psumt(f"gp{i}", (128, 512)) for i in range(5)]
    tbank = [psumt("tbA", (128, 1024), BF16), psumt("tbB", (128, 1024), BF16)]
    tf = psumt("tf", (128, 512))

    AB = sbuf("AB", (128, 32768), BF16)
    AB2 = sbuf("AB2", (128, 32768), BF16)
    AFX = sbuf("AFX", (128, 5120), F32)
    xtile = AFX[:, 0:1024]
    xntm = AFX[:, 1024:2048]
    qA = AFX[:, 2048:3072]
    qB = AFX[:, 3072:4096]
    qCs = AFX[:, 4096:5120]
    srow = AFX[:, 0:4096]
    sdec = AFX[:, 4096:4608]
    SROW_W = ["srow", "xtile", "xntm", "rpA", "rpB"]
    SDEC_W = ["sdec", "rpC"]

    q32 = sbuf("q32", (128, 1024), F32)
    qD = sbuf("qD", (128, 1024), F32)
    sbf = sbuf("sbf", (128, 1024), BF16)
    f16b = sbuf("f16b", (128, 1024), F16)
    lncols = sbuf("lncols", (128, 4), F32)
    q_bf = sbuf("q_bf", (128, 1024), BF16)
    qI = sbuf("qI", (128, 512), mybir.dt.int32)

    xnfm_bf = sbf
    rows = (qA, "rpA")
    pre_sb = qD[:, 0:516]
    cacc = q32[:, 0:512]
    dlt = q32[:, 0:512]
    trigc = qCs[:, 0:512]
    trigs = qCs[:, 512:1024]
    a2row = qA[0:1, 0:1024]
    bzrow = qB[0:1, 0:1024]
    b2row = qCs[0:1, 0:1024]
    bcrow_t = qCs[0:1, 0:512]

    sb_id_bf = sbuf("id_bf", (128, 128), BF16)
    sb_id_f32 = sbuf("id_f32", (128, 128), F32)
    sb_onesr = sbuf("onesr", (1, 128), F32)
    sb_onesmat = sbuf("onesmat", (128, 128), BF16)
    sb_maskPT = sbuf("maskPT", (128, 128), BF16)
    sb_maskTT = sbuf("maskTT", (128, 128), BF16)
    sb_maskTS = sbuf("maskTS", (128, 128), BF16)
    sb_rotT = sbuf("rotT", (128, 128), BF16)
    sb_zrow = sbuf("zrow", (6, 128), F32)

    def dmat(queue, dst, src, reads=(), writes=(), extra=(), slow=False):
        if dst.dtype != src.dtype:
            queue = "pool"
        return sch.task(queue, lambda e, d=dst, s=src, sl=slow: e.dma_start(
            out=d, in_=s, allow_slow_non_contiguous=sl),
            reads=reads, writes=writes, dma=True, extra=extra)

    cscal_in = din("c_scal", (128, len(SCALAR_CONSTS)), F32)
    sb_cscal = sbuf("cscal", (128, len(SCALAR_CONSTS)), F32)
    dmat("sp", sb_cscal[:], cscal_in[:], writes=["cscal"])
    for i, v in enumerate(SCALAR_CONSTS):
        nc.const_aps.register(F32, float(v), sb_cscal[:, i:i + 1])

    for dst, src, key in ((sb_id_bf, cid_bf, "idbf"), (sb_id_f32, cid_f32, "idf"),
                          (sb_onesr, onesr, "onesr"), (sb_onesmat, onesmat, "onesm"),
                          (sb_maskPT, maskPT, "mPT"), (sb_maskTT, maskTT, "mTT"),
                          (sb_maskTS, maskTS, "mTS"), (sb_rotT, rotT, "rotT"),
                          (sb_zrow, zrow6, "zrow")):
        dmat("sp", dst[:], src[:], writes=[key])

    tbrr = [0]

    def tb_slot():
        i = tbrr[0] % 8
        tbrr[0] += 1
        b = i % 2
        j = i // 2
        return tbank[b][:, j * 128:(j + 1) * 128], f"tb{i}", f"tbank{b}"

    tfrr = [0]

    def tf_slot():
        i = tfrr[0] % 4
        tfrr[0] += 1
        return tf[:, i * 128:(i + 1) * 128], f"tf{'ABCD'[i]}", "tfbank"

    gprr = [0]

    def gp_slot():
        i = gprr[0] % 5
        gprr[0] += 1
        return gp[i], f"gp{i}"

    def round_pipe(src, dst, kin, kout, mant_bits, min_p, clip=None,
                   tie_even=False, bb=None, kbb=None, dd=None, kdd=None):
        A = qA[:, 0:1024]; Cc = qCs[:, 0:1024]
        B = bb if bb is not None else qB[:, 0:1024]
        Dd = dd if dd is not None else qD[:]
        KA, KC = "rpA", "rpC"
        KB = kbb if kbb is not None else "rpB"
        KD_ = kdd if kdd is not None else "qD"
        lo = float(min_p + 64)
        bias_si = float(LN2 * (mant_bits + 64))
        bias_s = float(-LN2 * (mant_bits + 64))
        sch.task("act", lambda e: e.activation(out=A, in_=src, func=AF.Abs),
                 reads=[kin], writes=[KA])
        if clip is not None:
            sch.task("dve", lambda e: e.tensor_scalar(
                out=A, in0=A, scalar1=clip, scalar2=None, op0=AL.min),
                reads=[KA], writes=[KA])
        sch.task("dve", lambda e: e.tensor_scalar(
            out=B, in0=A, scalar1=2.0 ** -40, scalar2=None, op0=AL.max),
            reads=[KA], writes=[KB])
        sch.task("act", lambda e: e.activation(out=Cc, in_=B, func=AF.Ln),
                 reads=[KB], writes=[KC])
        sch.task("dve", lambda e: e.tensor_scalar(
            out=Cc, in0=Cc, scalar1=INVLN2, scalar2=64.0, op0=AL.mult, op1=AL.add),
            reads=[KC], writes=[KC])
        sch.task("dve", lambda e: e.tensor_scalar(
            out=Cc, in0=Cc, scalar1=lo, scalar2=-0.499, op0=AL.max, op1=AL.add),
            reads=[KC], writes=[KC])
        for hh_ in range(2):
            sch.task("pool", lambda e, hh_=hh_: e.tensor_copy(
                out=qI[:], in_=Cc[:, hh_ * 512:(hh_ + 1) * 512]),
                reads=[KC], writes=["qI"])
            sch.task("pool", lambda e, hh_=hh_: e.tensor_copy(
                out=Cc[:, hh_ * 512:(hh_ + 1) * 512], in_=qI[:]),
                reads=["qI"], writes=[KC])
        sch.task("act", lambda e: e.activation(out=B, in_=Cc, func=AF.Exp,
                                               scale=-LN2, bias=bias_si),
                 reads=[KC], writes=[KB])
        sch.task("pool", lambda e: e.tensor_tensor(out=Dd, in0=A, in1=B, op=AL.mult),
                 reads=[KA, KB], writes=[KD_])
        for hh_ in range(2):
            sch.task("pool", lambda e, hh_=hh_: e.tensor_copy(
                out=qI[:], in_=Dd[:, hh_ * 512:(hh_ + 1) * 512]),
                reads=[KD_], writes=["qI"])
            sch.task("pool", lambda e, hh_=hh_: e.tensor_copy(
                out=Dd[:, hh_ * 512:(hh_ + 1) * 512], in_=qI[:]),
                reads=["qI"], writes=[KD_])
        sch.task("act", lambda e: e.activation(out=B, in_=Cc, func=AF.Exp,
                                               scale=LN2, bias=bias_s),
                 reads=[KC], writes=[KB])
        sch.task("pool", lambda e: e.tensor_tensor(out=Dd, in0=Dd, in1=B, op=AL.mult),
                 reads=[KD_, KB], writes=[KD_])
        sch.task("act", lambda e: e.activation(out=A, in_=src, func=AF.Sign),
                 reads=[kin], writes=[KA])
        sch.task("dve", lambda e: e.tensor_tensor(out=dst, in0=Dd, in1=A,
                                                  op=AL.mult),
                 reads=[KD_, KA], writes=[kout])

    def ln_tile(gdst=None, gkey="qD"):
        sch.task("dve", lambda e: e.reduce_sum(lncols[:, 0:1], xtile, axis=AX.X),
                 reads=["xtile"], writes=["lnc"])
        sch.task("dve", lambda e: e.tensor_scalar(
            out=lncols[:, 1:2], in0=lncols[:, 0:1], scalar1=1.0 / D, scalar2=None,
            op0=AL.mult), reads=["lnc"], writes=["lnc"])
        sch.task("dve", lambda e: e.tensor_scalar(
            out=xntm, in0=xtile, scalar1=lncols[:, 1:2], scalar2=None,
            op0=AL.subtract), reads=["xtile", "lnc"], writes=["xntm"])
        gd = gdst if gdst is not None else qD[:]
        sch.task("dve", lambda e, gd=gd: e.scalar_tensor_tensor(
            out=gd, in0=xntm, scalar=1.0, in1=xntm, op0=AL.bypass,
            op1=AL.mult, accum_out=lncols[:, 2:3]),
            reads=["xntm"], writes=[gkey, "lnc"])
        sch.task("act", lambda e: e.activation(
            out=lncols[:, 3:4], in_=lncols[:, 2:3], func=AF.Sqrt,
            bias=1e-5, scale=1.0 / D), reads=["lnc"], writes=["lnc3"])
        sch.task("dve", lambda e: e.reciprocal(out=lncols[:, 2:3],
                                               in_=lncols[:, 3:4]),
                 reads=["lnc3"], writes=["lnc"])
        sch.task("dve", lambda e: e.tensor_scalar(
            out=xntm, in0=xntm, scalar1=lncols[:, 2:3], scalar2=None, op0=AL.mult),
            reads=["lnc"], writes=["xntm"])

    try:
        for tt in range(NT):
            dmat("sp", xtile, x_in[tt * 128:(tt + 1) * 128, :], writes=["xtile"])
            ln_tile()
            for dc in range(8):
                sch.task("pe", lambda e, dc=dc: e.transpose(
                    tf[:, 0:128], xntm[:, dc * 128:(dc + 1) * 128], sb_id_f32[:]),
                    reads=["xntm", "idf"], writes=["tfA", "tfbank"])
                sch.task("dve", lambda e, dc=dc: e.tensor_copy(
                    out=xnfm_bf[:, dc * 128:(dc + 1) * 128], in_=tf[:, 0:128]),
                    reads=["tfA"], writes=["sbf"])
            for dc in range(8):
                dmat("sp", hxn0[dc, :, tt * 128:(tt + 1) * 128],
                     xnfm_bf[:, dc * 128:(dc + 1) * 128],
                     reads=["sbf"], writes=["hxn0"])

        def maybe_stop(n):
            if phase_limit <= n:
                raise _StopBuild()

        maybe_stop(1)
        NCOL = NCH * NH
        assert NCOL <= 512
        NG = 8
        dcols = sbuf("dcols", (128, NG * NCOL), F32)
        G_B, G_BLNB, G_BETA, G_GAM, G_NEGW, G_C2, G_GC, G_CQ = range(NG)

        def dcol(g, c, h):
            return dcols[:, g * NCOL + c * NH + h:g * NCOL + c * NH + h + 1]

        def dgrp(g):
            return dcols[:, g * NCOL:(g + 1) * NCOL]

        bR = qA[0:6, 0:512]
        betaR = qB[0:6, 0:512]
        lnbR = qB[0:6, 512:1024]
        gR = qCs[0:6, 0:512]
        wba_sb = sbuf("wba_sb", (128, 128), BF16)
        dcol_s = sbuf("dcol_s", (NH, 2), F32)
        negA = sbuf("negA", (NH, 1), F32)
        xs_mov = sbf[:, 0:1024]

        for dc in range(8):
            dmat("sp", wba_sb[:, dc * 16:dc * 16 + 16], Wba[dc * 128:(dc + 1) * 128, :],
                 writes=["wba"])
        maybe_stop(1.02)
        dmat("sp", dcol_s[:, 0:1], A_log[:, :], writes=["dc0"])
        dmat("sp", dcol_s[:, 1:2], dt_bias[:, :], writes=["dc1"])
        maybe_stop(1.03)
        sch.task("act", lambda e: e.activation(out=negA[:], in_=dcol_s[:, 0:1],
                                               func=AF.Exp),
                 reads=["dc0"], writes=["negA"])
        maybe_stop(1.04)
        sch.task("dve", lambda e: e.tensor_scalar(
            out=negA[:], in0=negA[:], scalar1=-1.0, scalar2=None, op0=AL.mult),
            reads=["negA"], writes=["negA"])
        maybe_stop(1.05)

        for tb8 in range(NB):
            if tb8 > 0:
                maybe_stop(1.15)
            t0 = tb8 * 512
            for dc in range(8):
                dmat("sp", xs_mov[:, 0:512], hxn0[dc, :, t0:t0 + 512],
                     reads=["hxn0"], writes=["sbf"])
                sch.task("pe", lambda e, dc=dc: e.matmul(
                    gp[0][0:6, 0:512], wba_sb[:, dc * 16:dc * 16 + 6],
                    xs_mov[:, 0:512], start=(dc == 0), stop=(dc == 7)),
                    reads=["wba", "sbf"], writes=["gp0"])
                sch.task("pe", lambda e, dc=dc: e.matmul(
                    gp[1][0:6, 0:512], wba_sb[:, dc * 16 + 6:dc * 16 + 12],
                    xs_mov[:, 0:512], start=(dc == 0), stop=(dc == 7)),
                    reads=["wba", "sbf"], writes=["gp1"])
            maybe_stop(1.08)
            sch.task("act", lambda e: e.activation(
                out=betaR, in_=gp[0][0:6, 0:512], func=AF.Sigmoid),
                reads=["gp0"], writes=["rpB"])
            maybe_stop(1.09)
            sch.task("act", lambda e: e.activation(
                out=gR, in_=gp[1][0:6, 0:512], func=AF.Exp, bias=dcol_s[:, 1:2]),
                reads=["gp1", "dc1"], writes=["rpC"])
            maybe_stop(1.11)
            sch.task("act", lambda e: e.activation(
                out=gR, in_=gR, func=AF.Ln, bias=1.0),
                reads=["rpC"], writes=["rpC"])
            maybe_stop(1.12)
            sch.task("dve", lambda e: e.tensor_scalar(
                out=gR, in0=gR, scalar1=negA[:], scalar2=None, op0=AL.mult),
                reads=["rpC", "negA"], writes=["rpC"])
            maybe_stop(1.2)
            for ccl in range(4):
                u0 = ccl * C
                sch.task("dve", lambda e, u0=u0: e.tensor_tensor_scan(
                    out=qA[0:6, u0:u0 + C], data0=qCs[0:6, u0:u0 + C],
                    data1=sb_zrow[:, :], initial=0.0, op0=AL.add, op1=AL.add),
                    reads=["rpC", "zrow"], writes=["rpA"])
            maybe_stop(1.3)
            sch.task("act", lambda e: e.activation(
                out=lnbR, in_=betaR, func=AF.Ln),
                reads=["rpB"], writes=["rpB"])
            sch.task("dve", lambda e: e.tensor_tensor(
                out=gR, in0=lnbR, in1=bR, op=AL.add),
                reads=["rpB", "rpA", "rpC"], writes=["rpC"])
            for ccl in range(4):
                cc = tb8 * 4 + ccl
                u0 = ccl * C
                for src_ap, kread, goff, kw in (
                        (qA[0:6, u0:u0 + C], "rpA", G_B, "dcB"),
                        (qB[0:6, u0:u0 + C], "rpB", G_BETA, "dcBE"),
                        (qCs[0:6, u0:u0 + C], "rpC", G_BLNB, "dcBL")):
                    sch.task("pe", lambda e, src_ap=src_ap: e.transpose(
                        tf[:, 0:6], src_ap, sb_id_f32[0:6, 0:6]),
                        reads=[kread, "idf"],
                        writes=["tfA", "tfB", "tfC", "tfD", "tfbank"])
                    sch.task("dve", lambda e, cc=cc, goff=goff: e.tensor_copy(
                        out=dcols[:, goff * NCOL + cc * NH:goff * NCOL + cc * NH + 6],
                        in_=tf[:, 0:6]), reads=["tfA", "tfbank"], writes=[kw])

        maybe_stop(1.5)
        sch.task("act", lambda e: e.activation(out=dgrp(G_GAM), in_=dgrp(G_B),
                                               func=AF.Exp),
                 reads=["dcB"], writes=["dcG"])
        sch.task("dve", lambda e: e.scalar_tensor_tensor(
            out=dgrp(G_NEGW), in0=dgrp(G_BETA), scalar=-1.0, in1=dgrp(G_GAM),
            op0=AL.mult, op1=AL.mult), reads=["dcBE", "dcG"], writes=["dcNW"])
        maybe_stop(1.7)
        dmat("pool", bcrow_t[0:1, 0:NCOL],
             dcols[127:128, G_B * NCOL:(G_B + 1) * NCOL],
             reads=["dcB"], writes=["rpC"])
        maybe_stop(1.75)
        sch.task("pe", lambda e: e.matmul(
            tf[:, 0:NCOL], sb_onesr[:], bcrow_t[0:1, 0:NCOL], start=True, stop=True),
            reads=["rpC", "onesr"],
            writes=["tfA", "tfB", "tfC", "tfD", "tfbank"])
        maybe_stop(1.8)
        sch.task("act", lambda e: e.activation(out=dgrp(G_GC), in_=tf[:, 0:NCOL],
                                               func=AF.Exp),
                 reads=["tfA", "tfbank"], writes=["dcGC"])
        maybe_stop(1.85)
        sch.task("dve", lambda e: e.tensor_tensor(
            out=dgrp(G_C2), in0=tf[:, 0:NCOL], in1=dgrp(G_B), op=AL.subtract),
            reads=["tfA", "tfbank", "dcB"], writes=["dcC2"])
        maybe_stop(1.9)
        sch.task("act", lambda e: e.activation(out=dgrp(G_C2), in_=dgrp(G_C2),
                                               func=AF.Exp),
                 reads=["dcC2"], writes=["dcC2"])

        maybe_stop(2)
        wg_sb = AB2[:, 0:8 * VD]
        nw_sb = sbuf("nw_sb", (128, VD), BF16)
        xnb2 = sbuf("xstat", (128, 2048), BF16)
        xstat = xnb2[:, 0:1024]
        gtile = sbf[:, 0:512]
        for dc in range(8):
            dmat("sp", wg_sb[:, dc * VD:(dc + 1) * VD], Wg[dc * 128:(dc + 1) * 128, :],
                 writes=["wg"])
        dmat("sp", nw_sb[:], nw_b[:, :], writes=["nw"])
        for tt in range(NT):
            for dc in range(8):
                dmat("sp", xstat[:, dc * 128:(dc + 1) * 128],
                     hxn0[dc, :, tt * 128:(tt + 1) * 128],
                     reads=["hxn0"], writes=["xstat"])
            for half in range(3):
                for dc in range(8):
                    sch.task("pe", lambda e, dc=dc, half=half: e.matmul(
                        gp[half][:, 0:512], xstat[:, dc * 128:(dc + 1) * 128],
                        wg_sb[:, dc * VD + half * 512:dc * VD + (half + 1) * 512],
                        start=(dc == 0), stop=(dc == 7)),
                        reads=["xstat", "wg"], writes=[f"gp{half}"])
                sch.task("act", lambda e, half=half: e.activation(
                    out=gtile[:, 0:512], in_=gp[half][:, 0:512],
                    func=AF.Silu), reads=[f"gp{half}"], writes=["sbf"])
                sch.task("dve", lambda e, half=half: e.tensor_tensor(
                    out=gtile[:, 0:512], in0=gtile[:, 0:512],
                    in1=nw_sb[:, half * 512:(half + 1) * 512] if half < 2
                    else nw_sb[:, 1024:1536], op=AL.mult),
                    reads=["sbf", "nw"], writes=["sbf"])
                dmat("sp", hgsn[tt][:, half * 512:(half + 1) * 512], gtile[:, 0:512],
                     reads=["sbf"], writes=["hgsn"])

        maybe_stop(3)
        WSTR = KD + KD + VD
        wqkv = AB2[:, 0:8 * WSTR]
        for dc in range(8):
            b0 = dc * WSTR
            dmat("sp", wqkv[:, b0:b0 + KD], Wq[dc * 128:(dc + 1) * 128, :],
                 writes=["wqkv", "wg"])
            dmat("sp", wqkv[:, b0 + KD:b0 + 2 * KD], Wk[dc * 128:(dc + 1) * 128, :],
                 writes=["wqkv"])
            dmat("sp", wqkv[:, b0 + 2 * KD:b0 + WSTR], Wv[dc * 128:(dc + 1) * 128, :],
                 writes=["wqkv"])
        taps_sb = sbuf("taps_sb", (128, 96), F32)
        dmat("sp", taps_sb[:, 0:24], convq[:, :], writes=["taps"])
        dmat("sp", taps_sb[:, 24:48], convk[:, :], writes=["taps"])
        dmat("sp", taps_sb[:, 48:96], convv[:, :], writes=["taps"])

        xblk = AB[:, 0:4096]
        tails = sbuf("tails", (128, 72), F32)
        silu_bf = sbf[:, 0:512]

        sch.task("dve", lambda e: e.memset(tails[:], 0.0), writes=["tails"])

        cchunks = []
        for h in range(NH):
            cchunks.append((h * 128, h * 4, hq, h))
        for h in range(NH):
            cchunks.append((KD + h * 128, 24 + h * 4, hk, h))
        for j in range(2 * NH):
            cchunks.append((2 * KD + j * 128, 48 + j * 4, hv, j))

        for tb8 in range(NB):
            t0 = tb8 * 512
            for dc in range(8):
                dmat("sp", xblk[:, dc * 512:(dc + 1) * 512], hxn0[dc, :, t0:t0 + 512],
                     reads=["hxn0"], writes=["xblk"])
            for ci in range(24):
                coff, tap0, dram, slot = cchunks[ci]
                bank = gp[ci % 2]
                kb = f"gp{ci % 2}"
                for dc in range(8):
                    sch.task("pe", lambda e, dc=dc, coff=coff, bank=bank: e.matmul(
                        bank[:, 0:512], wqkv[:, dc * WSTR + coff:dc * WSTR + coff + 128],
                        xblk[:, dc * 512:(dc + 1) * 512],
                        start=(dc == 0), stop=(dc == 7)),
                        reads=["wqkv", "xblk"], writes=[kb])
                sch.task("dve", lambda e, bank=bank: e.tensor_copy(
                    out=pre_sb[:, 3:515], in_=bank[:, 0:512]),
                    reads=[kb], writes=["qD"])
                sch.task("dve", lambda e, ci=ci: e.tensor_copy(
                    out=pre_sb[:, 0:3], in_=tails[:, ci * 3:ci * 3 + 3]),
                    reads=["tails"], writes=["qD"])
                sch.task("dve", lambda e, ci=ci: e.tensor_copy(
                    out=tails[:, ci * 3:ci * 3 + 3], in_=pre_sb[:, 512:515]),
                    reads=["qD"], writes=["tails"])
                sch.task("dve", lambda e, tap0=tap0: e.tensor_scalar(
                    out=cacc[:, 0:512], in0=pre_sb[:, 0:512],
                    scalar1=taps_sb[:, tap0:tap0 + 1],
                    scalar2=None, op0=AL.mult),
                    reads=["qD", "taps"], writes=["q32"])
                for jj in (1, 2, 3):
                    sch.task("dve", lambda e, tap0=tap0, jj=jj: e.scalar_tensor_tensor(
                        out=cacc[:, 0:512], in0=pre_sb[:, jj:jj + 512],
                        scalar=taps_sb[:, tap0 + jj:tap0 + jj + 1], in1=cacc[:, 0:512],
                        op0=AL.mult, op1=AL.add),
                        reads=["qD", "taps", "q32"], writes=["q32"])
                sch.task("act", lambda e: e.activation(out=silu_bf[:, 0:512],
                                                       in_=cacc[:, 0:512],
                                                       func=AF.Silu),
                         reads=["q32"], writes=["sbf"])
                for j in range(4):
                    cix = t0 // 128 + j
                    dmat("sp", dram[slot, cix], silu_bf[:, j * 128:(j + 1) * 128],
                         reads=["sbf"], writes=["hqkv"])

        maybe_stop(4)
        wo_sb = AB2[:, 0:12 * 1024]
        for vc in range(12):
            dmat("sp", wo_sb[:, vc * 1024:(vc + 1) * 1024],
                 Wo[vc * 128:(vc + 1) * 128, :], writes=["wo", "wqkv"])

        S_sb = sbuf("S_sb", (128, NH * DV), F32)
        S_bf = sbuf("S_bf", (128, NH * DV), BF16)
        for h in range(NH):
            sch.task("dve", lambda e, h=h: e.memset(S_sb[:, h * 256:(h + 1) * 256], 0.0),
                     writes=[f"S{h}"])
            sch.task("dve", lambda e, h=h: e.memset(S_bf[:, h * 256:(h + 1) * 256], 0.0),
                     writes=[f"Sbf{h}"])

        HSTR = 3584
        def hb(h, off, n):
            b0 = 4096 + h * HSTR + off
            return AB[:, b0:b0 + n]
        ktm_h = lambda h: hb(h, 0, 128)
        k2tm_h = lambda h: hb(h, 128, 128)
        khf_h = lambda h: hb(h, 256, 128)
        PT_h = lambda h: hb(h, 384, 128)
        TSb_h = lambda h: hb(h, 512, 128)
        TTp_h = lambda h, j: hb(h, 640 + j * 128, 128)
        TSP_h = lambda h, j: hb(h, 896 + j * 128, 128)
        MT_h = lambda h: hb(h, 1664, 128)
        vtm_h = lambda h: hb(h, 1792, 256)
        bv_h = lambda h: hb(h, 2048, 256)
        ru_h = lambda h: hb(h, 2304, 256)
        u_h = lambda h: hb(h, 2560, 256)
        expm_h = lambda h: hb(h, 2816, 384)
        qf_h = lambda h: hb(h, 3200, 128)
        kf_h = lambda h: hb(h, 3328, 128)
        vfm_h = lambda h: hb(h, 3456, 128)
        osc_h = lambda h: AFX[:, 2048 + h * 256:2048 + (h + 1) * 256]
        o1s_h = lambda h: AFX[:, 3584 + h * 256:3584 + (h + 1) * 256]
        ccb_h = lambda h: q32[:, h * 128:(h + 1) * 128]
        ein_h = lambda h: qD[:, h * 128:(h + 1) * 128]
        gsn_sb = sbuf("gsn_sb", (128, VD), BF16)
        o2fm = sbuf("o2fm", (128, 12 * 128), BF16)
        jnk = sbuf("jnk", (128, 128), BF16)
        sso = sbuf("sso", (128, 24), F32)
        fence5 = sch.task("sp", lambda e: e.nop(),
                          writes=["rpB", "rpA", "rpC", "q32", "qD", "xblk"])
        HS = range(NH)

        for cc in range(NCH):
            dmat("sp", gsn_sb[:], hgsn[cc], reads=["hgsn"], writes=["gsn"])
            for h in HS:
                dmat("sp", qf_h(h), hq[h, cc], reads=["hqkv"],
                     writes=[f"qf{h}"], extra=[fence5])
                dmat("sp", kf_h(h), hk[h, cc], reads=["hqkv"],
                     writes=[f"kf{h}"], extra=[fence5])
            for h in HS:
                tq, ktq, bkq = tb_slot()
                sch.task("pe", lambda e, h=h, tq=tq: e.transpose(
                    tq, qf_h(h), sb_id_bf[:]), reads=[f"qf{h}", "idbf"],
                    writes=[ktq, bkq])
                sch.task("act", lambda e, h=h, tq=tq: e.activation(
                    out=jnk[:], in_=tq, func=AF.Square,
                    accum_out=sso[:, h * 4:h * 4 + 1]),
                    reads=[ktq, bkq], writes=["jnk", f"ssoq{h}"])
                sch.task("act", lambda e, h=h, cc=cc: e.activation(
                    out=dcol(G_CQ, cc, h), in_=sso[:, h * 4:h * 4 + 1],
                    func=AF.Sqrt, scale=128.0, bias=128.0e-12),
                    reads=[f"ssoq{h}"], writes=[f"dcCQ{h}"])
                sch.task("dve", lambda e, h=h, cc=cc: e.reciprocal(
                    out=dcol(G_CQ, cc, h), in_=dcol(G_CQ, cc, h)),
                    reads=[f"dcCQ{h}"], writes=[f"dcCQ{h}"])
            for h in HS:
                tk, ktk, bkk = tb_slot()
                sch.task("pe", lambda e, h=h, tk=tk: e.transpose(
                    tk, kf_h(h), sb_id_bf[:]), reads=[f"kf{h}", "idbf"],
                    writes=[ktk, bkk])
                sch.task("act", lambda e, h=h, tk=tk: e.activation(
                    out=jnk[:], in_=tk, func=AF.Square,
                    accum_out=sso[:, h * 4 + 1:h * 4 + 2]),
                    reads=[ktk, bkk], writes=["jnk", f"ssok{h}"])
                sch.task("act", lambda e, h=h: e.activation(
                    out=sso[:, h * 4 + 2:h * 4 + 3],
                    in_=sso[:, h * 4 + 1:h * 4 + 2], func=AF.Sqrt, bias=1e-12),
                    reads=[f"ssok{h}"], writes=[f"dcIK{h}"])
                sch.task("dve", lambda e, h=h: e.reciprocal(
                    out=sso[:, h * 4 + 2:h * 4 + 3],
                    in_=sso[:, h * 4 + 2:h * 4 + 3]),
                    reads=[f"dcIK{h}"], writes=[f"dcIK{h}"])
                sch.task("dve", lambda e, h=h, tk=tk: e.tensor_scalar(
                    out=ktm_h(h), in0=tk, scalar1=sso[:, h * 4 + 2:h * 4 + 3],
                    scalar2=None, op0=AL.mult),
                    reads=[ktk, bkk, f"dcIK{h}"], writes=[f"ktm{h}"])
                sch.task("dve", lambda e, h=h, cc=cc: e.tensor_scalar(
                    out=k2tm_h(h), in0=ktm_h(h), scalar1=dcol(G_C2, cc, h),
                    scalar2=None, op0=AL.mult),
                    reads=[f"ktm{h}", "dcC2"], writes=[f"k2tm{h}"])
                tk2, ktk2, bkk2 = tb_slot()
                sch.task("pe", lambda e, h=h, tk2=tk2: e.transpose(
                    tk2, ktm_h(h), sb_id_bf[:]), reads=[f"ktm{h}", "idbf"],
                    writes=[ktk2, bkk2])
                sch.task("dve", lambda e, h=h, tk2=tk2: e.tensor_copy(
                    out=khf_h(h), in_=tk2), reads=[ktk2, bkk2],
                    writes=[f"khf{h}"])
            for h in HS:
                for half in range(2):
                    dmat("sp", vfm_h(h), hv[2 * h + half, cc],
                         reads=["hqkv"], writes=[f"vf{h}"], extra=[fence5])
                    tv, ktv, bkv = tb_slot()
                    sch.task("pe", lambda e, h=h, tv=tv: e.transpose(
                        tv, vfm_h(h), sb_id_bf[:]), reads=[f"vf{h}", "idbf"],
                        writes=[ktv, bkv])
                    sch.task("dve", lambda e, h=h, half=half, tv=tv: e.tensor_copy(
                        out=vtm_h(h)[:, half * 128:(half + 1) * 128], in_=tv),
                        reads=[ktv, bkv], writes=[f"vtm{h}"])
                sch.task("dve", lambda e, h=h, cc=cc: e.tensor_scalar(
                    out=bv_h(h), in0=vtm_h(h), scalar1=dcol(G_BETA, cc, h),
                    scalar2=None, op0=AL.mult),
                    reads=[f"vtm{h}", "dcBE"], writes=[f"bv{h}"])
            for h in HS:
                tfa, ktfa, bfa = tf_slot()
                sch.task("dve", lambda e, h=h, cc=cc: e.tensor_scalar(
                    out=ccb_h(h), in0=sb_onesmat[:], scalar1=dcol(G_B, cc, h),
                    scalar2=None, op0=AL.mult),
                    reads=["onesm", "dcB"], writes=[f"ccb{h}"], extra=[fence5])
                sch.task("pe", lambda e, h=h, tfa=tfa: e.transpose(
                    tfa, ccb_h(h), sb_id_f32[:]), reads=[f"ccb{h}", "idf"],
                    writes=[ktfa, bfa])
                sch.task("dve", lambda e, h=h, cc=cc, tfa=tfa: e.scalar_tensor_tensor(
                    out=ein_h(h), in0=tfa, scalar=dcol(G_B, cc, h),
                    in1=sb_maskPT[:], op0=AL.subtract, op1=AL.subtract),
                    reads=[ktfa, bfa, "dcB", "mPT"], writes=[f"ein{h}"],
                    extra=[fence5])
                sch.task("act", lambda e, h=h: e.activation(
                    out=expm_h(h)[:, 0:128], in_=ein_h(h), func=AF.Exp),
                    reads=[f"ein{h}"], writes=[f"ePT{h}"])
                sch.task("dve", lambda e, h=h, cc=cc, tfa=tfa: e.scalar_tensor_tensor(
                    out=ein_h(h), in0=tfa, scalar=dcol(G_BLNB, cc, h),
                    in1=sb_maskTS[:], op0=AL.subtract, op1=AL.add),
                    reads=[ktfa, bfa, "dcBL", "mTS", f"ePT{h}"], writes=[f"ein{h}"])
                sch.task("act", lambda e, h=h: e.activation(
                    out=expm_h(h)[:, 256:384], in_=ein_h(h), func=AF.Exp,
                    scale=-1.0), reads=[f"ein{h}"], writes=[f"eTS{h}"])
                tfb, ktfb, bfb2 = tf_slot()
                sch.task("dve", lambda e, h=h, cc=cc: e.tensor_scalar(
                    out=ccb_h(h), in0=sb_onesmat[:], scalar1=dcol(G_BLNB, cc, h),
                    scalar2=None, op0=AL.mult),
                    reads=["onesm", "dcBL", f"ein{h}"], writes=[f"ccb{h}"])
                sch.task("pe", lambda e, h=h, tfb=tfb: e.transpose(
                    tfb, ccb_h(h), sb_id_f32[:]), reads=[f"ccb{h}", "idf"],
                    writes=[ktfb, bfb2])
                sch.task("dve", lambda e, h=h, cc=cc, tfb=tfb: e.scalar_tensor_tensor(
                    out=ein_h(h), in0=tfb, scalar=dcol(G_B, cc, h),
                    in1=sb_maskTT[:], op0=AL.subtract, op1=AL.subtract),
                    reads=[ktfb, bfb2, "dcB", "mTT", f"eTS{h}"], writes=[f"ein{h}"])
                sch.task("act", lambda e, h=h: e.activation(
                    out=expm_h(h)[:, 128:256], in_=ein_h(h), func=AF.Exp),
                    reads=[f"ein{h}"], writes=[f"eTT{h}"])
            for h in HS:
                tg, ktg = gp_slot()
                sch.task("pe", lambda e, h=h, tg=tg: e.matmul(
                    tg[:, 0:128], khf_h(h), khf_h(h), start=True, stop=True),
                    reads=[f"khf{h}"], writes=[ktg])
                sch.task("dve", lambda e, h=h, tg=tg: e.tensor_tensor(
                    out=TTp_h(h, 0), in0=tg[:, 0:128], in1=expm_h(h)[:, 128:256],
                    op=AL.mult), reads=[ktg, f"eTT{h}"], writes=[f"TT{h}_0"])
                sch.task("dve", lambda e, h=h, tg=tg: e.tensor_tensor(
                    out=TSb_h(h), in0=tg[:, 0:128], in1=expm_h(h)[:, 256:384],
                    op=AL.mult), reads=[ktg, f"eTS{h}"], writes=[f"TSb{h}"])
                tg2, ktg2 = gp_slot()
                sch.task("pe", lambda e, h=h, tg2=tg2: e.matmul(
                    tg2[:, 0:128], khf_h(h), qf_h(h), start=True, stop=True),
                    reads=[f"khf{h}", f"qf{h}"], writes=[ktg2])
                sch.task("dve", lambda e, h=h, tg2=tg2: e.tensor_tensor(
                    out=PT_h(h), in0=tg2[:, 0:128], in1=expm_h(h)[:, 0:128],
                    op=AL.mult), reads=[ktg2, f"ePT{h}"], writes=[f"PT{h}"])
                sch.task("dve", lambda e, h=h: e.scalar_tensor_tensor(
                    out=MT_h(h), in0=TTp_h(h, 0), scalar=-1.0, in1=sb_id_bf[:],
                    op0=AL.mult, op1=AL.add),
                    reads=[f"TT{h}_0", "idbf"], writes=[f"MT{h}"])
            prevs = {h: (TSb_h(h), f"TSb{h}", TTp_h(h, 0), f"TT{h}_0") for h in HS}
            for j in range(6):
                for h in HS:
                    ts_p, kts_p, tt_p, ktt_p = prevs[h]
                    tp, ktp = gp_slot()
                    cur = TSP_h(h, j)
                    kcur = f"TSP{h}_{j}"
                    sch.task("pe", lambda e, tp=tp, a=tt_p, b=ts_p: e.matmul(
                        tp[:, 0:128], a, b, start=True, stop=True),
                        reads=[kts_p, ktt_p], writes=[ktp])
                    sch.task("dve", lambda e, tp=tp, cur=cur: e.tensor_copy(
                        out=cur, in_=tp[:, 0:128]), reads=[ktp], writes=[kcur])
                    if j < 5:
                        tp2, ktp2 = gp_slot()
                        curT = TTp_h(h, (j + 1) % 2)
                        kcurT = f"TT{h}_{(j + 1) % 2}"
                        sch.task("pe", lambda e, tp2=tp2, a=ts_p, b=tt_p: e.matmul(
                            tp2[:, 0:128], a, b, start=True, stop=True),
                            reads=[kts_p, ktt_p], writes=[ktp2])
                        sch.task("dve", lambda e, tp2=tp2, curT=curT: e.tensor_copy(
                            out=curT, in_=tp2[:, 0:128]), reads=[ktp2],
                            writes=[kcurT])
                        prevs[h] = (cur, kcur, curT, kcurT)
                    else:
                        prevs[h] = (cur, kcur, None, None)
            for j in range(6):
                for h in HS:
                    tsl = TSP_h(h, j)
                    ktsl = f"TSP{h}_{j}"
                    tp3, ktp3 = gp_slot()
                    sch.task("pe", lambda e, h=h, tp3=tp3, tsl=tsl: e.matmul(
                        tp3[:, 0:128], tsl, MT_h(h), start=True, stop=True),
                        reads=[ktsl, f"MT{h}"], writes=[ktp3])
                    sch.task("dve", lambda e, h=h, tp3=tp3: e.tensor_tensor(
                        out=MT_h(h), in0=MT_h(h), in1=tp3[:, 0:128], op=AL.add),
                        reads=[ktp3, f"MT{h}"], writes=[f"MT{h}"])
            for h in HS:
                if cc == 0:
                    sch.task("dve", lambda e, h=h: e.tensor_copy(
                        out=ru_h(h), in_=bv_h(h)),
                        reads=[f"bv{h}"], writes=[f"ru{h}"])
                else:
                    gb, kgb = gp_slot()
                    sch.task("pe", lambda e, h=h, gb=gb: e.matmul(
                        gb[:, 0:256], khf_h(h), S_bf[:, h * 256:(h + 1) * 256],
                        start=True, stop=True),
                        reads=[f"khf{h}", f"Sbf{h}"], writes=[kgb])
                    sch.task("dve", lambda e, h=h, cc=cc, gb=gb: e.scalar_tensor_tensor(
                        out=ru_h(h), in0=gb[:, 0:256], scalar=dcol(G_NEGW, cc, h),
                        in1=bv_h(h), op0=AL.mult, op1=AL.add),
                        reads=[kgb, "dcNW", f"bv{h}"], writes=[f"ru{h}"])
            for h in HS:
                gu, kgu = gp_slot()
                sch.task("pe", lambda e, h=h, gu=gu: e.matmul(
                    gu[:, 0:256], MT_h(h), ru_h(h), start=True, stop=True),
                    reads=[f"MT{h}", f"ru{h}"], writes=[kgu])
                sch.task("dve", lambda e, h=h, gu=gu: e.tensor_copy(
                    out=u_h(h), in_=gu[:, 0:256]), reads=[kgu], writes=[f"u{h}"])
            for h in HS:
                go1, kgo1 = gp_slot()
                sch.task("pe", lambda e, h=h, go1=go1: e.matmul(
                    go1[:, 0:256], PT_h(h), u_h(h), start=True, stop=True),
                    reads=[f"PT{h}", f"u{h}"], writes=[kgo1])
                if cc == 0:
                    sch.task("dve", lambda e, h=h, cc=cc, go1=go1: e.tensor_scalar(
                        out=osc_h(h), in0=go1[:, 0:256],
                        scalar1=dcol(G_CQ, cc, h), scalar2=None, op0=AL.mult),
                        reads=[kgo1, f"dcCQ{h}"], writes=[f"osc{h}"],
                        extra=[fence5])
                else:
                    sch.task("dve", lambda e, h=h, go1=go1: e.tensor_copy(
                        out=o1s_h(h), in_=go1[:, 0:256]), reads=[kgo1],
                        writes=[f"o1s{h}"], extra=[fence5])
                    go2, kgo2 = gp_slot()
                    sch.task("pe", lambda e, h=h, go2=go2: e.matmul(
                        go2[:, 0:256], qf_h(h), S_bf[:, h * 256:(h + 1) * 256],
                        start=True, stop=True),
                        reads=[f"qf{h}", f"Sbf{h}"], writes=[kgo2])
                    sch.task("dve", lambda e, h=h, cc=cc, go2=go2: e.scalar_tensor_tensor(
                        out=o1s_h(h), in0=go2[:, 0:256], scalar=dcol(G_GAM, cc, h),
                        in1=o1s_h(h), op0=AL.mult, op1=AL.add),
                        reads=[kgo2, "dcG", f"o1s{h}"], writes=[f"o1s{h}"])
                    sch.task("dve", lambda e, h=h, cc=cc: e.tensor_scalar(
                        out=osc_h(h), in0=o1s_h(h),
                        scalar1=dcol(G_CQ, cc, h), scalar2=None, op0=AL.mult),
                        reads=[f"o1s{h}", f"dcCQ{h}"], writes=[f"osc{h}"])
                gd, kgd = gp_slot()
                sch.task("pe", lambda e, h=h, gd=gd: e.matmul(
                    gd[:, 0:256], k2tm_h(h), u_h(h), start=True, stop=True),
                    reads=[f"k2tm{h}", f"u{h}"], writes=[kgd])
                sch.task("dve", lambda e, h=h, cc=cc, gd=gd: e.scalar_tensor_tensor(
                    out=S_sb[:, h * 256:(h + 1) * 256],
                    in0=S_sb[:, h * 256:(h + 1) * 256],
                    scalar=dcol(G_GC, cc, h), in1=gd[:, 0:256],
                    op0=AL.mult, op1=AL.add),
                    reads=[kgd, "dcGC", f"S{h}"], writes=[f"S{h}"])
                sch.task("dve", lambda e, h=h: e.tensor_copy(
                    out=S_bf[:, h * 256:(h + 1) * 256],
                    in_=S_sb[:, h * 256:(h + 1) * 256]),
                    reads=[f"S{h}"], writes=[f"Sbf{h}"])
            for h in HS:
                sch.task("dve", lambda e, h=h: e.scalar_tensor_tensor(
                    out=jnk[:] if False else bv_h(h), in0=osc_h(h), scalar=1.0,
                    in1=osc_h(h), op0=AL.bypass, op1=AL.mult,
                    accum_out=sso[:, h * 4 + 3:h * 4 + 4]),
                    reads=[f"osc{h}"], writes=[f"bv{h}", f"ssoo{h}"])
                sch.task("act", lambda e, h=h: e.activation(
                    out=sso[:, h * 4 + 3:h * 4 + 4],
                    in_=sso[:, h * 4 + 3:h * 4 + 4], func=AF.Sqrt,
                    scale=1.0 / 256.0, bias=1e-5),
                    reads=[f"ssoo{h}"], writes=[f"ssoo{h}"])
                sch.task("dve", lambda e, h=h: e.reciprocal(
                    out=sso[:, h * 4 + 3:h * 4 + 4],
                    in_=sso[:, h * 4 + 3:h * 4 + 4]),
                    reads=[f"ssoo{h}"], writes=[f"ssoo{h}"])
                sch.task("dve", lambda e, h=h: e.scalar_tensor_tensor(
                    out=osc_h(h), in0=osc_h(h), scalar=sso[:, h * 4 + 3:h * 4 + 4],
                    in1=gsn_sb[:, h * 256:(h + 1) * 256], op0=AL.mult,
                    op1=AL.mult),
                    reads=[f"ssoo{h}", "gsn", f"osc{h}"], writes=[f"osc{h}"])
                for half in range(2):
                    to, kto, bko = tf_slot()
                    sch.task("pe", lambda e, h=h, half=half, to=to: e.transpose(
                        to, osc_h(h)[:, half * 128:(half + 1) * 128],
                        sb_id_f32[:]), reads=[f"osc{h}", "idf"],
                        writes=[kto, bko])
                    sch.task("dve", lambda e, h=h, half=half, to=to: e.tensor_copy(
                        out=o2fm[:, (2 * h + half) * 128:(2 * h + half + 1) * 128],
                        in_=to), reads=[kto, bko], writes=["o2fm"])
            for dhalf in range(2):
                bank = gp[dhalf]
                kb = f"gp{dhalf}"
                for vc in range(12):
                    sch.task("pe", lambda e, vc=vc, dhalf=dhalf, bank=bank: e.matmul(
                        bank[:, 0:512], o2fm[:, vc * 128:(vc + 1) * 128],
                        wo_sb[:, vc * 1024 + dhalf * 512:vc * 1024 + (dhalf + 1) * 512],
                        start=(vc == 0), stop=(vc == 11)),
                        reads=["o2fm", "wo"], writes=[kb])
                sch.task("dve", lambda e, dhalf=dhalf, bank=bank: e.tensor_copy(
                    out=sbf[:, dhalf * 512:(dhalf + 1) * 512], in_=bank[:, 0:512]),
                    reads=[kb], writes=["sbf"])
            dmat("sp", hy[cc], sbf[:, 0:1024], reads=["sbf"], writes=["hy"])

        maybe_stop(5)
        w1sT = AB[:, 0:16384]
        w2sT = AB[:, 16384:32768]
        p5keys = []
        for h in range(NH):
            p5keys += [f"qf{h}", f"kf{h}", f"vf{h}", f"ktm{h}", f"k2tm{h}",
                       f"khf{h}", f"PT{h}", f"TSb{h}", f"TT{h}_0", f"TT{h}_1",
                       f"MT{h}", f"vtm{h}", f"bv{h}", f"ru{h}", f"u{h}",
                       f"ePT{h}", f"eTT{h}", f"eTS{h}", f"osc{h}", f"o1s{h}",
                       f"ccb{h}", f"ein{h}"]
            p5keys += [f"TSP{h}_{j}" for j in range(6)]
        fence6 = sch.task("sp", lambda e: e.nop(), writes=p5keys)
        w1c_bf = sbf[:, 0:1024]
        w2c_bf = sbf[:, 0:1024]
        alcols = sbuf("alcols", (128, 32), F32)
        b1_sb = sbuf("b1_sb", (128, 32), F32)
        a2cols = sbuf("a2cols", (128, 8), F32)
        a2b = sbuf("a2b", (128, 1024), BF16)
        h_bf = sbuf("h_bf", (128, 256), BF16)
        yt_bf = q_bf
        wfT = AB2[:, 0:8192]

        def fc_unit(fc):
            dmat("pool", q32[:], wf[fc * 128:(fc + 1) * 128, :], writes=["q32"])
            round_pipe(q32[:], q_bf[:], "q32", "qbf", mant_bits=3, min_p=-6,
                       clip=448.0, bb=xntm, kbb="xntm", dd=xtile, kdd="xtile")
            for cc2 in range(8):
                ts_, kts_, bks_ = tb_slot()
                sch.task("pe", lambda e, cc2=cc2, ts_=ts_: e.transpose(
                    ts_, q_bf[:, cc2 * 128:(cc2 + 1) * 128], sb_id_bf[:]),
                    reads=["qbf", "idbf"], writes=[kts_, bks_])
                sch.task("dve", lambda e, cc2=cc2, fc=fc, ts_=ts_: e.tensor_copy(
                    out=wfT[:, cc2 * 1024 + fc * 128:cc2 * 1024 + (fc + 1) * 128],
                    in_=ts_), reads=[kts_, bks_, bks_], writes=["wfT", "wo"])

        def merged_tile(tt):
            dmat("sp", xtile, hx1[tt], reads=["hx1w"], writes=["xtile"])
            ln_tile(gdst=xtile, gkey="xtile")
            for dc in range(8):
                sch.task("pe", lambda e, dc=dc: e.transpose(
                    tf[:, 0:128], xntm[:, dc * 128:(dc + 1) * 128], sb_id_f32[:]),
                    reads=["xntm", "idf"], writes=["tfA", "tfbank"])
                sch.task("dve", lambda e, dc=dc: e.tensor_copy(
                    out=xnfm_bf[:, dc * 128:(dc + 1) * 128], in_=tf[:, 0:128]),
                    reads=["tfA"], writes=["sbf"])
                sch.task("dve", lambda e, dc=dc: e.tensor_copy(
                    out=q32[:, dc * 128:(dc + 1) * 128], in_=tf[:, 0:128]),
                    reads=["tfA", "tfbank"], writes=["q32"])
            for dc in range(8):
                dmat("sp", hxn1[dc, :, tt * 128:(tt + 1) * 128],
                     xnfm_bf[:, dc * 128:(dc + 1) * 128],
                     reads=["sbf"], writes=["hxn1"])
            round_pipe(q32[:], q_bf[:], "q32", "qbf", mant_bits=3, min_p=-6,
                       clip=448.0, bb=xntm, kbb="xntm", dd=xtile, kdd="xtile")
            for dc in range(8):
                dmat("sp", hxq[dc, :, tt * 128:(tt + 1) * 128],
                     q_bf[:, dc * 128:(dc + 1) * 128],
                     reads=["qbf"], writes=["hxq"])
            for dh in range(2):
                bank = gp[dh]
                kb = f"gp{dh}"
                for cc2 in range(8):
                    sch.task("pe", lambda e, cc2=cc2, dh=dh, bank=bank: e.matmul(
                        bank[:, 0:512], q_bf[:, cc2 * 128:(cc2 + 1) * 128],
                        wfT[:, cc2 * 1024 + dh * 512:cc2 * 1024 + (dh + 1) * 512],
                        start=(cc2 == 0), stop=(cc2 == 7)),
                        reads=["qbf", "wfT"], writes=[kb])
                sch.task("dve", lambda e, dh=dh, bank=bank: e.tensor_copy(
                    out=q32[:, dh * 512:(dh + 1) * 512], in_=bank[:, 0:512]),
                    reads=[kb], writes=["q32"])
            round_pipe(q32[:], f16b[:], "q32", "f16b", mant_bits=10, min_p=-14,
                       clip=None, tie_even=True, bb=xntm, kbb="xntm",
                       dd=xtile, kdd="xtile")
            dmat("sp", hf1[tt], f16b[:], reads=["f16b"], writes=["hf1"])

        dmat("sp", b1_sb[:], b1c[:, :], writes=["b1"])
        for dc2 in range(8):
            for hhx in range(2):
                dmat("pool", w2c_bf[:, 0:1024],
                     w2[dc2 * 128:(dc2 + 1) * 128, hhx * 1024:(hhx + 1) * 1024],
                     writes=["sbf"])
                dmat("pool", xstat[:, 0:1024],
                     w2[dc2 * 128:(dc2 + 1) * 128, 2048 + hhx * 1024:2048 + (hhx + 1) * 1024],
                     writes=["xstat"])
                if hhx == 0:
                    sch.task("dve", lambda e, dc2=dc2: e.tensor_reduce(
                        out=a2cols[:, dc2:dc2 + 1], in_=w2c_bf[:, 0:1024], axis=AX.X,
                        op=AL.add, apply_absolute_value=True),
                        reads=["sbf"], writes=["a2c"])
                else:
                    sch.task("dve", lambda e, dc2=dc2: e.tensor_reduce(
                        out=lncols[:, 0:1], in_=w2c_bf[:, 0:1024], axis=AX.X,
                        op=AL.add, apply_absolute_value=True),
                        reads=["sbf"], writes=["lnc"])
                    sch.task("dve", lambda e, dc2=dc2: e.scalar_tensor_tensor(
                        out=a2cols[:, dc2:dc2 + 1],
                        in0=a2cols[:, dc2:dc2 + 1] if True else None,
                        scalar=1.0, in1=lncols[:, 0:1], op0=AL.mult, op1=AL.add),
                        reads=["lnc", "a2c"], writes=["a2c"])
                sch.task("dve", lambda e: e.tensor_reduce(
                    out=lncols[:, 1:2], in_=xstat[:, 0:1024], axis=AX.X,
                    op=AL.add, apply_absolute_value=True),
                    reads=["xstat"], writes=["lnc1"])
                sch.task("dve", lambda e, dc2=dc2: e.scalar_tensor_tensor(
                    out=a2cols[:, dc2:dc2 + 1], in0=a2cols[:, dc2:dc2 + 1],
                    scalar=1.0, in1=lncols[:, 1:2], op0=AL.mult, op1=AL.add),
                    reads=["lnc1", "a2c"], writes=["a2c"])
        sch.task("dve", lambda e: e.tensor_scalar(
            out=a2cols[:], in0=a2cols[:], scalar1=1.0 / HFF, scalar2=None,
            op0=AL.mult), reads=["a2c"], writes=["a2c"])
        for dc2 in range(8):
            dmat("pool", a2row[0:1, dc2 * 128:(dc2 + 1) * 128],
                 a2cols[:, dc2:dc2 + 1], reads=["a2c"], writes=["rpA"],
                 extra=[fence6], slow=True)
        for half in range(2):
            sch.task("pe", lambda e, half=half: e.matmul(
                tf[:, 0:512], sb_onesr[:], a2row[0:1, half * 512:(half + 1) * 512],
                start=True, stop=True), reads=["rpA", "onesr"],
                writes=["tfA", "tfB", "tfC", "tfD", "tfbank"])
            sch.task("dve", lambda e, half=half: e.tensor_copy(
                out=a2b[:, half * 512:(half + 1) * 512], in_=tf[:, 0:512]),
                reads=["tfA", "tfbank"], writes=["a2b"])
        dmat("sp", b2row, b2r[:, :], writes=["rpC"], extra=[fence6])
        sch.task("dve", lambda e: e.reciprocal(out=bzrow, in_=a2row),
                 reads=["rpA"], writes=["rpB"], extra=[fence6])
        sch.task("dve", lambda e: e.tensor_tensor(
            out=bzrow, in0=bzrow, in1=b2row, op=AL.mult),
            reads=["rpB", "rpC"], writes=["rpB"])

        NT2 = L // 256
        for hh in range(2):
            for hcl in range(16):
                hcg = hh * 16 + hcl
                dmat("pool", w1c_bf[:, 0:1024], w1[hcg * 128:(hcg + 1) * 128, :],
                     writes=["sbf"])
                sch.task("dve", lambda e, hcg=hcg: e.tensor_reduce(
                    out=alcols[:, hcg:hcg + 1], in_=w1c_bf[:, 0:1024], axis=AX.X,
                    op=AL.add, apply_absolute_value=True),
                    reads=["sbf"], writes=["al1"])
                for dc in range(8):
                    ts_, kts_, bks_ = tb_slot()
                    sch.task("pe", lambda e, dc=dc, ts_=ts_: e.transpose(
                        ts_, w1c_bf[:, dc * 128:(dc + 1) * 128], sb_id_bf[:]),
                        reads=["sbf", "idbf"], writes=[kts_, bks_])
                    sch.task("act", lambda e, dc=dc, hcl=hcl, ts_=ts_: e.activation(
                        out=w1sT[:, dc * 2048 + hcl * 128:dc * 2048 + (hcl + 1) * 128],
                        in_=ts_, func=AF.Sign),
                        reads=[kts_, bks_, bks_], writes=["w1sT", "xblk"])
            for dc2 in range(8):
                for hq2 in range(2):
                    dmat("pool", w2c_bf[:, 0:1024],
                         w2[dc2 * 128:(dc2 + 1) * 128,
                            hh * 2048 + hq2 * 1024:hh * 2048 + (hq2 + 1) * 1024],
                         writes=["sbf"])
                    for hcl8 in range(8):
                        hcl = hq2 * 8 + hcl8
                        ts_, kts_, bks_ = tb_slot()
                        sch.task("pe", lambda e, hcl8=hcl8, ts_=ts_: e.transpose(
                            ts_, w2c_bf[:, hcl8 * 128:(hcl8 + 1) * 128], sb_id_bf[:]),
                            reads=["sbf", "idbf"], writes=[kts_, bks_])
                        sch.task("act", lambda e, dc2=dc2, hcl=hcl, ts_=ts_:
                                 e.activation(
                            out=w2sT[:, hcl * 1024 + dc2 * 128:hcl * 1024 + (dc2 + 1) * 128],
                            in_=ts_, func=AF.Sign), reads=[kts_, bks_, bks_], writes=["w2sT"])
            sch.task("dve", lambda e, hh=hh: e.tensor_scalar(
                out=alcols[:, hh * 16:(hh + 1) * 16],
                in0=alcols[:, hh * 16:(hh + 1) * 16], scalar1=1.0 / D, scalar2=None,
                op0=AL.mult), reads=["al1"], writes=["al1"])

            for tblk in range(NT2):
                t0 = tblk * 256
                if hh == 0:
                    for fc in range(tblk * 8 // NT2, (tblk + 1) * 8 // NT2):
                        fc_unit(fc)
                for dc in range(8):
                    dmat("sp", xnb2[:, dc * 256:(dc + 1) * 256],
                         hxn0[dc, :, t0:t0 + 256], reads=["hxn0"], writes=["xstat"])
                if hh == 0:
                    for j in range(2):
                        for dh in range(2):
                            bank = gp[j * 2 + dh]
                            kb = f"gp{j * 2 + dh}"
                            sch.task("pe", lambda e, dh=dh, bank=bank: e.matmul(
                                bank[:, 0:512], sb_onesr[:],
                                bzrow[0:1, dh * 512:(dh + 1) * 512],
                                start=True, stop=False),
                                reads=["rpB", "onesr"], writes=[kb])
                for hcl in range(16):
                    for dc in range(8):
                        sch.task("pe", lambda e, hcl=hcl, dc=dc: e.matmul(
                            gp[4][:, 0:256],
                            w1sT[:, dc * 2048 + hcl * 128:dc * 2048 + (hcl + 1) * 128],
                            xnb2[:, dc * 256:(dc + 1) * 256],
                            start=(dc == 0), stop=(dc == 7)),
                            reads=["w1sT", "xstat"], writes=["gp4"])
                    hcg = hh * 16 + hcl
                    sch.task("act", lambda e, hcg=hcg: e.activation(
                        out=h_bf[:], in_=gp[4][:, 0:256], func=AF.Gelu,
                        bias=b1_sb[:, hcg:hcg + 1], scale=alcols[:, hcg:hcg + 1]),
                        reads=["gp4", "b1", "al1"], writes=["h_bf"])
                    for j in range(2):
                        for dh in range(2):
                            bank = gp[j * 2 + dh]
                            kb = f"gp{j * 2 + dh}"
                            first = (hh == 1 and hcl == 0)
                            sch.task("pe", lambda e, j=j, dh=dh, hcl=hcl, bank=bank,
                                     first=first: e.matmul(
                                bank[:, 0:512], h_bf[:, j * 128:(j + 1) * 128],
                                w2sT[:, hcl * 1024 + dh * 512:hcl * 1024 + (dh + 1) * 512],
                                start=first, stop=(hcl == 15)),
                                reads=["h_bf", "w2sT"], writes=[kb])
                for j in range(2):
                    tt = t0 // 128 + j
                    if hh == 0:
                        dmat("sp", qD[:], x_in[tt * 128:(tt + 1) * 128, :],
                             writes=["qD"])
                        dmat("sp", yt_bf[:], hy[tt], reads=["hy"], writes=["qbf"])
                        sch.task("dve", lambda e: e.tensor_tensor(
                            out=qD[:], in0=qD[:], in1=yt_bf[:], op=AL.add),
                            reads=["qbf", "qD"], writes=["qD"])
                    else:
                        dmat("sp", qD[:], hx1[tt], reads=["hx1w"], writes=["qD"])
                    for dh in range(2):
                        bank = gp[j * 2 + dh]
                        kb = f"gp{j * 2 + dh}"
                        sch.task("dve", lambda e, dh=dh, bank=bank: e.tensor_tensor(
                            out=q32[:, dh * 512:(dh + 1) * 512], in0=bank[:, 0:512],
                            in1=a2b[:, dh * 512:(dh + 1) * 512], op=AL.mult),
                            reads=[kb, "a2b"], writes=["q32"])
                    sch.task("dve", lambda e: e.tensor_tensor(
                        out=qD[:], in0=qD[:], in1=q32[:], op=AL.add),
                        reads=["q32", "qD"], writes=["qD"])
                    dmat("sp", hx1[tt], qD[:], reads=["qD"], writes=["hx1w"])
                    if hh == 1 and tblk >= 1:
                        merged_tile(2 * (tblk - 1) + j)

        merged_tile(L // 128 - 2)
        merged_tile(L // 128 - 1)
        maybe_stop(6)
        maybe_stop(8)
        KR = AB2[:, 0:32768]
        VR = AB[:, 0:32768]
        AB5 = sbuf("AB5", (128, 4096), BF16)
        wl1 = AB5[:, 0:4096]
        qp_bf = gsn_sb[:, 0:512]
        qx_bf = gsn_sb[:, 512:1024]
        fence8 = sch.task("sp", lambda e: e.nop(),
                          writes=["wl1", "qp", "qx", "xblk0", "xblk1",
                                  "xblk", "w1sT", "gsn"])

        def l1_proj(wsrc, cost, sint, is_q):
            for wh in range(2):
                for dc in range(8):
                    dmat("pool", wl1[:, dc * 512:(dc + 1) * 512],
                         wsrc[dc * 128:(dc + 1) * 128, wh * 512:(wh + 1) * 512],
                         writes=["wl1"], extra=[fence8])
                for tb8 in range(NB):
                    t0 = tb8 * 512
                    par = (wh * NB + tb8) % 2
                    xbuf = AB[:, par * 4096:(par + 1) * 4096]
                    kxb = f"xblk{par}"
                    for dc in range(8):
                        dmat("sp", xbuf[:, dc * 512:(dc + 1) * 512],
                             hxn1[dc, :, t0:t0 + 512], reads=["hxn1"],
                             writes=[kxb], extra=[fence8])
                    for dc2l in range(4):
                        dc2 = wh * 4 + dc2l
                        bank = gp[dc2 % 2]
                        kb = f"gp{dc2 % 2}"
                        for dc in range(8):
                            sch.task("pe", lambda e, dc=dc, dc2l=dc2l, bank=bank,
                                     xbuf=xbuf: e.matmul(
                                bank[:, 0:512],
                                wl1[:, dc * 512 + dc2l * 128:dc * 512 + (dc2l + 1) * 128],
                                xbuf[:, dc * 512:(dc + 1) * 512],
                                start=(dc == 0), stop=(dc == 7)),
                                reads=["wl1", kxb], writes=[kb])
                        sch.task("dve", lambda e, bank=bank: e.tensor_copy(
                            out=qp_bf[:], in_=bank[:, 0:512]), reads=[kb], writes=["qp"])
                        bank2 = gp[2 + dc2 % 2]
                        kb2 = f"gp{2 + dc2 % 2}"
                        sch.task("pe", lambda e, bank2=bank2: e.matmul(
                            bank2[:, 0:512], sb_rotT[:], qp_bf[:],
                            start=True, stop=True),
                            reads=["qp", "rotT"], writes=[kb2])
                        dmat("sp", trigc,
                             cost[dc2 * 128:(dc2 + 1) * 128, t0:t0 + 512],
                             writes=["rpC"])
                        dmat("sp", trigs,
                             sint[dc2 * 128:(dc2 + 1) * 128, t0:t0 + 512],
                             writes=["rpC2"])
                        sch.task("dve", lambda e: e.tensor_tensor(
                            out=qA[:, 0:512], in0=qp_bf[:], in1=trigc,
                            op=AL.mult), reads=["qp", "rpC"], writes=["rpA"])
                        sch.task("dve", lambda e, bank2=bank2: e.tensor_tensor(
                            out=qB[:, 0:512], in0=bank2[:, 0:512], in1=trigs,
                            op=AL.mult), reads=[kb2, "rpC2"], writes=["rpB"])
                        sch.task("dve", lambda e: e.tensor_tensor(
                            out=qx_bf[:], in0=qA[:, 0:512], in1=qB[:, 0:512],
                            op=AL.add), reads=["rpA", "rpB"], writes=["qx"])
                        if is_q:
                            for j in range(4):
                                dmat("sp", hq1[t0 // 128 + j, dc2],
                                     qx_bf[:, j * 128:(j + 1) * 128],
                                     reads=["qx"], writes=["hq1"])
                        else:
                            sch.task("dve", lambda e, dc2=dc2, t0=t0: e.tensor_copy(
                                out=KR[:, dc2 * L + t0:dc2 * L + t0 + 512],
                                in_=qx_bf[:]),
                                reads=["qx"], writes=["KR", "wfT"])

        l1_proj(WQ1, ctq, stq, True)
        l1_proj(WK1, ctk, stk, False)
        for wh in range(2):
            for dc in range(8):
                dmat("pool", wl1[:, dc * 512:(dc + 1) * 512],
                     WV1[dc * 128:(dc + 1) * 128, wh * 512:(wh + 1) * 512],
                     writes=["wl1"])
            for tt in range(NT):
                for dc in range(8):
                    dmat("sp", xstat[:, dc * 128:(dc + 1) * 128],
                         hxn1[dc, :, tt * 128:(tt + 1) * 128],
                         reads=["hxn1"], writes=["xstat"])
                bank = gp[tt % 2]
                kb = f"gp{tt % 2}"
                for dc in range(8):
                    sch.task("pe", lambda e, dc=dc, bank=bank: e.matmul(
                        bank[:, 0:512], xstat[:, dc * 128:(dc + 1) * 128],
                        wl1[:, dc * 512:(dc + 1) * 512],
                        start=(dc == 0), stop=(dc == 7)),
                        reads=["xstat", "wl1"], writes=[kb])
                sch.task("dve", lambda e, tt=tt, wh=wh, bank=bank: e.tensor_copy(
                    out=VR[:, tt * 1024 + wh * 512:tt * 1024 + (wh + 1) * 512],
                    in_=bank[:, 0:512]),
                    reads=[kb], writes=["VR", "xblk0", "xblk1", "w1sT", "w2sT"])

        maybe_stop(9)
        qstat = xnb2[:, 0:1024]
        mcols = sbuf("mcols", (128, 16), F32)
        ed_bf = qp_bf
        pt_full = o2fm[:, 0:512]
        zc = sbuf("zc", (128, 8), F32)
        bfb_sb = xnb2[:, 1024:2048]
        out_sb = q32
        fence9 = sch.task("sp", lambda e: e.nop(),
                          writes=["xstat", "wl1", "o2fm", "ptf"])
        dmat("sp", bfb_sb[:], bf_b[:, :], writes=["bfb"], extra=[fence9])

        for qt in range(NT):
            q0 = qt * 128
            dmat("sp", qstat[:], hq1[qt].rearrange("dc p c -> p dc c"),
                 reads=["hq1"], writes=["qstat"], extra=[fence9])
            for kc in range(NKC):
                bank = gp[kc % 2]
                kb = f"gp{kc % 2}"
                for dc in range(8):
                    sch.task("pe", lambda e, dc=dc, kc=kc, bank=bank: e.matmul(
                        bank[:, 0:512], qstat[:, dc * 128:(dc + 1) * 128],
                        KR[:, dc * L + kc * 512:dc * L + kc * 512 + 512],
                        start=(dc == 0), stop=(dc == 7)),
                        reads=["qstat", "KR"], writes=[kb])
                sch.task("dve", lambda e, kc=kc, bank=bank: e.reduce_max(
                    mcols[:, kc:kc + 1], bank[:, 0:512], axis=AX.X),
                    reads=[kb], writes=["mc"])
                sch.task("dve", lambda e, kc=kc, bank=bank: e.tensor_copy(
                    out=srow[:, kc * 512:(kc + 1) * 512], in_=bank[:, 0:512]),
                    reads=[kb], writes=SROW_W)
            sch.task("dve", lambda e: e.reduce_max(
                mcols[:, 9:10], mcols[:, 0:NKC], axis=AX.X),
                reads=["mc"], writes=["mfin"])
            sch.task("dve", lambda e: e.tensor_scalar(
                out=mcols[:, 10:11], in0=mcols[:, 9:10], scalar1=-1.0, scalar2=None,
                op0=AL.mult), reads=["mfin"], writes=["negm"])
            live = [kc for kc in range(NKC)
                    if max(0, q0 - (kc * 512 + 511), kc * 512 - (q0 + 127)) <= DEC_SKIP]
            for kc in range(NKC):
                k0 = kc * 512
                sch.task("act", lambda e, kc=kc: e.activation(
                    out=sbf[:, 0:512], in_=srow[:, kc * 512:(kc + 1) * 512],
                    func=AF.Exp, bias=mcols[:, 10:11], accum_out=zc[:, kc:kc + 1]),
                    reads=["srow", "negm"], writes=["sbf", f"zc{kc}"])
                if kc not in live:
                    continue
                j0 = DEC_OFF + k0 - q0
                dmat("sp", dlt[:, 0:512], dlog[:, j0:j0 + 512], writes=["q32"])
                sch.task("dve", lambda e, kc=kc: e.tensor_tensor(
                    out=sdec, in0=srow[:, kc * 512:(kc + 1) * 512], in1=dlt[:, 0:512],
                    op=AL.add), reads=["srow", "q32"], writes=SDEC_W)
                sch.task("act", lambda e: e.activation(
                    out=ed_bf[:], in_=sdec, func=AF.Exp, bias=mcols[:, 10:11]),
                    reads=["sdec", "negm"], writes=["qp"])
                for j in range(4):
                    tsl, ktsl, bksl = tb_slot()
                    sch.task("pe", lambda e, j=j, tsl=tsl: e.transpose(
                        tsl, ed_bf[:, j * 128:(j + 1) * 128], sb_id_bf[:]),
                        reads=["qp", "idbf"], writes=[ktsl, bksl])
                    sch.task("dve", lambda e, j=j, tsl=tsl: e.tensor_copy(
                        out=pt_full[:, j * 128:(j + 1) * 128], in_=tsl),
                        reads=[ktsl, bksl, bksl], writes=["ptf"], extra=[fence9])
                lastkc = (kc == live[-1])
                for j in range(4):
                    ktile = kc * 4 + j
                    st = (kc == live[0] and j == 0)
                    sp_ = (lastkc and j == 3)
                    for dh in range(2):
                        bank = gp[2 + dh]
                        kb = f"gp{2 + dh}"
                        sch.task("pe", lambda e, j=j, dh=dh, ktile=ktile, st=st,
                                 sp_=sp_, bank=bank: e.matmul(
                            bank[:, 0:512], pt_full[:, j * 128:(j + 1) * 128],
                            VR[:, ktile * 1024 + dh * 512:ktile * 1024 + (dh + 1) * 512],
                            start=st, stop=sp_),
                            reads=["ptf", "VR"], writes=[kb])
            sch.task("dve", lambda e: e.reduce_sum(
                mcols[:, 11:12], zc[:, 0:NKC], axis=AX.X),
                reads=[f"zc{kc}" for kc in range(NKC)], writes=["Z"])
            sch.task("dve", lambda e: e.reciprocal(out=mcols[:, 12:13],
                                                   in_=mcols[:, 11:12]),
                     reads=["Z"], writes=["rZ"])
            dmat("sp", qD[:], hx1[qt], reads=["hx1w"], writes=["qD"])
            dmat("sp", f16b[:], hf1[qt], reads=["hf1"], writes=["f16b"])
            sch.task("dve", lambda e: e.tensor_tensor(
                out=qD[:], in0=qD[:], in1=f16b[:], op=AL.add),
                reads=["f16b", "qD"], writes=["qD"])
            sch.task("dve", lambda e: e.tensor_tensor(
                out=qD[:], in0=qD[:], in1=bfb_sb[:], op=AL.add),
                reads=["bfb", "qD"], writes=["qD"])
            for dh in range(2):
                bank = gp[2 + dh]
                kb = f"gp{2 + dh}"
                sch.task("dve", lambda e, dh=dh, bank=bank: e.scalar_tensor_tensor(
                    out=out_sb[:, dh * 512:(dh + 1) * 512], in0=bank[:, 0:512],
                    scalar=mcols[:, 12:13], in1=qD[:, dh * 512:(dh + 1) * 512],
                    op0=AL.mult, op1=AL.add),
                    reads=[kb, "rZ", "qD"], writes=["q32"])
            dmat("sp", out[qt * 128:(qt + 1) * 128, :], out_sb[:],
                 reads=["q32"], writes=["outw"])


    except _StopBuild:
        dmat("sp", out[0:128, :], x_in[0:128, :], writes=["outw"])

    sems = {}
    for e in Sched.ENGS:
        g = nc.semaphore(f"sem_{e}")
        sems[("eng", e)] = g.__enter__()
        ctx.append(g)
    for e in Sched.DMAQ:
        for si in range(Sched.NSEM[e]):
            g = nc.semaphore(f"dmas_{e}{si}")
            sems[("dma", e, si)] = g.__enter__()
            ctx.append(g)
    blk = nc.Block()
    blk.__enter__()
    sch.emit(nc, blk, sems)
    blk.__exit__(None, None, None)
    for g in reversed(ctx):
        g.__exit__(None, None, None)
    nc._sched = sch
    return nc


def make_inputs_for_core(xb, I, consts):
    m = {"x": np.ascontiguousarray(xb, dtype=np.float32)}
    m["Wq"] = I["L0_attn_Wq"]
    m["Wk"] = I["L0_attn_Wk"]
    m["Wv"] = I["L0_attn_Wv"]
    wba = np.zeros((D, 16), dtype=np.float32)
    wba[:, 0:6] = I["L0_attn_Wb"]
    wba[:, 6:12] = I["L0_attn_Wa"]
    m["Wba"] = wba
    m["A_log"] = np.ascontiguousarray(I["L0_attn_A_log"]).reshape(NH, 1).astype(np.float32)
    m["dt_bias"] = np.ascontiguousarray(I["L0_attn_dt_bias"]).reshape(NH, 1).astype(np.float32)
    m["convq"] = np.ascontiguousarray(
        np.asarray(I["L0_attn_conv_q"]).reshape(NH, 128, 4).transpose(1, 0, 2).reshape(128, 24))
    m["convk"] = np.ascontiguousarray(
        np.asarray(I["L0_attn_conv_k"]).reshape(NH, 128, 4).transpose(1, 0, 2).reshape(128, 24))
    m["convv"] = np.ascontiguousarray(
        np.asarray(I["L0_attn_conv_v"]).reshape(12, 128, 4).transpose(1, 0, 2).reshape(128, 48))
    m["Wg"] = I["L0_attn_Wg"]
    m["nw_b"] = np.ascontiguousarray(
        np.broadcast_to(np.tile(np.asarray(I["L0_attn_norm_w"]), NH), (128, VD)))
    m["Wo"] = I["L0_attn_Wo"]
    m["w1"] = I["L0_ffn_w1"]
    m["b1c"] = np.ascontiguousarray(np.asarray(I["L0_ffn_b1"]).reshape(32, 128).T)
    m["w2"] = I["L0_ffn_w2"]
    m["b2r"] = np.ascontiguousarray(np.asarray(I["L0_ffn_b2"]).reshape(1, D))
    m["WQ1"] = I["L1_attn_W_Q"]
    m["WK1"] = I["L1_attn_W_K"]
    m["WV1"] = I["L1_attn_W_V"]
    m["wf"] = I["L1_ffn_weight"]
    m["bf_b"] = np.ascontiguousarray(np.broadcast_to(np.asarray(I["L1_ffn_bias"]), (128, D)))
    for k, v in consts.items():
        m[k] = v
    res = {}
    for k, v in m.items():
        a = np.asarray(v)
        if a.dtype == np.float64:
            a = a.astype(np.float32)
        res[k] = np.ascontiguousarray(a)
    return res


_NC_CACHE = {}


def kernel(**inputs):
    L = np.asarray(inputs["L0_attn_x"]).shape[1]
    if L not in _NC_CACHE:
        _NC_CACHE[L] = build_core(L)
    nc = _NC_CACHE[L]
    consts = host_constants(L)
    x = np.ascontiguousarray(np.asarray(inputs["L0_attn_x"], dtype=np.float32))
    n = x.shape[0]
    in_maps = [make_inputs_for_core(x[i], inputs, consts) for i in range(n)]
    res = run_bass_kernel_spmd(nc, in_maps, core_ids=list(range(n)))
    return np.stack([r["out"] for r in res.results], axis=0)


# version 14
# speedup vs baseline: 1.1136x; 1.0035x over previous
import numpy as np

import concourse.bass as bass
import concourse.mybir as mybir
from concourse.bass_utils import run_bass_kernel_spmd

F32 = mybir.dt.float32
BF16 = mybir.dt.bfloat16
F16 = mybir.dt.float16
AF = mybir.ActivationFunctionType
AL = mybir.AluOpType
AX = mybir.AxisListType

LN2 = float(np.log(2.0))
INVLN2 = float(1.0 / np.log(2.0))
GAMMA = 0.9
LNG = float(np.log(GAMMA))
BIG = 1.0e30

D = 1024
NH = 6
DK = 128
DV = 256
KD = NH * DK
VD = NH * DV
HFF = 4096
C = 128
DEC_SKIP = 1100
DEC_OFF = 1664
DEC_BAND = 3456

SCALAR_CONSTS = [0.0, 1.0, -1.0, 0.5, 1e-5, 1e-12, 1.0 / 1024.0, 1.0 / 256.0,
                 128.0, 128.0e-12, float(np.log(2.0)), float(-np.log(2.0)),
                 float(np.log(2.0) * 67), float(-np.log(2.0) * 67),
                 float(np.log(2.0) * 74), float(-np.log(2.0) * 74)]


def _bf16np():
    import ml_dtypes
    return np.dtype(ml_dtypes.bfloat16)


class Sched:
    ENGS = ("pe", "act", "dve", "pool", "sp")
    DMAQ = ("sp", "pool", "act")
    NSEM = {"sp": 24, "pool": 16, "act": 4}

    def __init__(self):
        self.items = {e: [] for e in self.ENGS}
        self.n = {e: 0 for e in self.ENGS}
        self.ncomp = {e: 0 for e in self.ENGS}
        self.ndma = {e: 0 for e in self.DMAQ}
        self.semuse = {}
        self.last_w = {}
        self.readers = {}

    PSUM_BANKS = frozenset(["gp0", "gp1", "gp2", "gp3", "gp4",
                            "tbank0", "tbank1", "tfbank"])

    def task(self, eng, fn, reads=(), writes=(), dma=False, extra=()):
        pb = [r for r in reads if r in self.PSUM_BANKS]
        if pb:
            reads = [r for r in reads if r not in self.PSUM_BANKS]
            writes = list(writes) + pb
        deps = []
        for r in reads:
            c = self.last_w.get(r)
            if c is not None:
                deps.append(c)
        for w in writes:
            c = self.last_w.get(w)
            if c is not None:
                deps.append(c)
            deps.extend(self.readers.get(w, ()))
        deps.extend(extra)
        self.n[eng] += 1
        if dma:
            si = self.ndma[eng] % self.NSEM[eng]
            self.ndma[eng] += 1
            skey = ("dma", eng, si)
            cnt = self.semuse.get(skey, 0) + 1
            self.semuse[skey] = cnt
            if cnt > 1:
                deps.append((skey, 16 * (cnt - 1)))
            comp = (skey, 16 * cnt)
            dmakey = skey
        else:
            self.ncomp[eng] += 1
            comp = (("eng", eng), self.ncomp[eng])
            dmakey = None
        self.items[eng].append((deps, fn, dmakey))
        for r in reads:
            self.readers.setdefault(r, []).append(comp)
        for w in writes:
            self.last_w[w] = comp
            self.readers[w] = []
        return comp

    def emit(self, nc, block, sems):
        def make_body(ename, items):
            def body(eng):
                satisfied = {}
                own = ("eng", ename)
                if ename != "sp":
                    eng.wait_ge(sems[("dma", "sp", 0)], 16)
                    satisfied[("dma", "sp", 0)] = 16
                for deps, fn, dmakey in items:
                    for key, val in deps:
                        if satisfied.get(key, 0) >= val:
                            continue
                        eng.wait_ge(sems[key], val)
                        satisfied[key] = val
                    inst = fn(eng)
                    if dmakey is not None:
                        inst.then_inc(sems[dmakey], 16)
                    else:
                        inst.then_inc(sems[("eng", ename)], 1)
                if ename in self.DMAQ:
                    for si in range(self.NSEM[ename]):
                        skey = ("dma", ename, si)
                        cnt = self.semuse.get(skey, 0)
                        if cnt > 0:
                            eng.wait_ge(sems[skey], 16 * cnt)
            return body

        block.tensor(make_body("pe", self.items["pe"]))
        block.scalar(make_body("act", self.items["act"]))
        block.vector(make_body("dve", self.items["dve"]))
        block.gpsimd(make_body("pool", self.items["pool"]))
        block.sync(make_body("sp", self.items["sp"]))


def host_constants(L):
    bf = _bf16np()
    c = {}
    eye = np.eye(128, dtype=np.float64)
    c["c_id_bf16"] = eye.astype(bf)
    c["c_id_f32"] = eye.astype(np.float32)
    c["c_ones_f32"] = np.ones((1, 128), dtype=np.float32)
    c["c_onesmat"] = np.ones((128, 128), dtype=bf)
    c["c_zero_row_f32"] = np.zeros((6, 128), dtype=np.float32)
    r = np.arange(128)[:, None]
    t = np.arange(128)[None, :]
    c["c_maskPT"] = np.where(r <= t, 0.0, BIG).astype(bf)
    c["c_maskTT"] = np.where(r < t, 0.0, BIG).astype(bf)
    c["c_maskTS"] = np.where(t < r, 0.0, BIG).astype(bf)
    R = np.zeros((128, 128), dtype=np.float64)
    for i in range(64):
        R[2 * i, 2 * i + 1] = -1.0
        R[2 * i + 1, 2 * i] = 1.0
    c["c_rotT_bf16"] = np.ascontiguousarray(R.T).astype(bf)
    p = np.arange(128)[:, None]
    j = np.arange(DEC_BAND)[None, :]
    c["c_dlog"] = (LNG * np.abs(p - (j - DEC_OFF)).astype(np.float64)).astype(np.float32)
    half = D // 2
    sv = (np.arange(0, D, 2, dtype=np.float64) + 0.4 * D) / (1.4 * D)
    power = np.arange(L, dtype=np.float64) / 512.0
    scale = sv[None, :] ** power[:, None]
    inv_freq = 1.0 / 10000.0 ** (np.arange(half, dtype=np.float64) / half)
    sinus = np.arange(L, dtype=np.float64)[:, None] * inv_freq[None, :]
    sin = np.repeat(np.sin(sinus), 2, axis=1)
    cos = np.repeat(np.cos(sinus), 2, axis=1)
    scl = np.repeat(scale, 2, axis=1)
    c["c_cos_q"] = np.ascontiguousarray((cos * scl).T).astype(np.float32)
    c["c_sin_q"] = np.ascontiguousarray((sin * scl).T).astype(np.float32)
    c["c_cos_k"] = np.ascontiguousarray((cos / scl).T).astype(np.float32)
    c["c_sin_k"] = np.ascontiguousarray((sin / scl).T).astype(np.float32)
    c["c_scal"] = np.broadcast_to(
        np.asarray(SCALAR_CONSTS, dtype=np.float32), (128, len(SCALAR_CONSTS))
    ).copy()
    return c


class _StopBuild(Exception):
    pass


def build_core(L, phase_limit=99):
    nc = bass.Bass("TRN2", target_bir_lowering=False)
    NT = L // 128
    NCH = L // C
    NKC = L // 512
    NB = L // 512

    def din(name, shape, dtype=F32):
        return nc.dram_tensor(name, list(shape), dtype, kind="ExternalInput").ap()

    x_in = din("x", (L, D))
    Wq = din("Wq", (D, KD)); Wk = din("Wk", (D, KD)); Wv = din("Wv", (D, VD))
    Wba = din("Wba", (D, 16))
    A_log = din("A_log", (NH, 1))
    dt_bias = din("dt_bias", (NH, 1))
    convq = din("convq", (128, 24))
    convk = din("convk", (128, 24))
    convv = din("convv", (128, 48))
    Wg = din("Wg", (D, VD))
    nw_b = din("nw_b", (128, VD))
    Wo = din("Wo", (VD, D))
    w1 = din("w1", (HFF, D))
    b1c = din("b1c", (128, 32))
    w2 = din("w2", (D, HFF))
    b2r = din("b2r", (1, D))
    WQ1 = din("WQ1", (D, D)); WK1 = din("WK1", (D, D)); WV1 = din("WV1", (D, D))
    wf = din("wf", (D, D))
    bf_b = din("bf_b", (128, D))

    cid_bf = din("c_id_bf16", (128, 128), BF16)
    cid_f32 = din("c_id_f32", (128, 128), F32)
    onesr = din("c_ones_f32", (1, 128), F32)
    onesmat = din("c_onesmat", (128, 128), BF16)
    zrow6 = din("c_zero_row_f32", (6, 128), F32)
    maskPT = din("c_maskPT", (128, 128), BF16)
    maskTT = din("c_maskTT", (128, 128), BF16)
    maskTS = din("c_maskTS", (128, 128), BF16)
    rotT = din("c_rotT_bf16", (128, 128), BF16)
    dlog = din("c_dlog", (128, DEC_BAND), F32)
    ctq = din("c_cos_q", (D, L)); stq = din("c_sin_q", (D, L))
    ctk = din("c_cos_k", (D, L)); stk = din("c_sin_k", (D, L))

    out = nc.dram_tensor("out", [L, D], F32, kind="ExternalOutput").ap()

    hxn0 = nc.dram_tensor("hxn0", [8, 128, L], BF16).ap()
    hq = nc.dram_tensor("hq", [NH, NCH, 128, 128], BF16).ap()
    hk = nc.dram_tensor("hk", [NH, NCH, 128, 128], BF16).ap()
    hv = nc.dram_tensor("hv", [2 * NH, NCH, 128, 128], BF16).ap()
    hgsn = nc.dram_tensor("hgsn", [NCH, 128, VD], BF16).ap()
    hy = nc.dram_tensor("hy", [NT, 128, D], BF16).ap()
    hx1 = nc.dram_tensor("hx1", [NT, 128, D], F32).ap()
    hxn1 = nc.dram_tensor("hxn1", [8, 128, L], BF16).ap()
    hxq = nc.dram_tensor("hxq", [8, 128, L], BF16).ap()
    hq1 = nc.dram_tensor("hq1", [NT, 8, 128, 128], BF16).ap()
    hf1 = nc.dram_tensor("hf1", [NT, 128, D], F16).ap()

    sch = Sched()
    ctx = []

    def sbuf(name, shape, dtype):
        g = nc.sbuf_tensor(name, list(shape), dtype)
        h = g.__enter__()
        ctx.append(g)
        return h

    def psumt(name, shape, dtype=F32):
        g = nc.psum_tensor(name, list(shape), dtype)
        h = g.__enter__()
        ctx.append(g)
        return h

    gp = [psumt(f"gp{i}", (128, 512)) for i in range(5)]
    tbank = [psumt("tbA", (128, 1024), BF16), psumt("tbB", (128, 1024), BF16)]
    tf = psumt("tf", (128, 512))

    AB = sbuf("AB", (128, 32768), BF16)
    AB2 = sbuf("AB2", (128, 32768), BF16)
    AFX = sbuf("AFX", (128, 5120), F32)
    xtile = AFX[:, 0:1024]
    xntm = AFX[:, 1024:2048]
    qA = AFX[:, 2048:3072]
    qB = AFX[:, 3072:4096]
    qCs = AFX[:, 4096:5120]
    srow = AFX[:, 0:4096]
    sdec = AFX[:, 4096:4608]
    SROW_W = ["srow", "xtile", "xntm", "rpA", "rpB"]
    SDEC_W = ["sdec", "rpC"]

    q32 = sbuf("q32", (128, 1024), F32)
    qD = sbuf("qD", (128, 1024), F32)
    sbf = sbuf("sbf", (128, 1024), BF16)
    f16b = sbuf("f16b", (128, 1024), F16)
    lncols = sbuf("lncols", (128, 4), F32)
    q_bf = sbuf("q_bf", (128, 1024), BF16)
    qI = sbuf("qI", (128, 512), mybir.dt.int32)

    xnfm_bf = sbf
    rows = (qA, "rpA")
    pre_sb = qD[:, 0:516]
    cacc = q32[:, 0:512]
    dlt = q32[:, 0:512]
    trigc = qCs[:, 0:512]
    trigs = qCs[:, 512:1024]
    a2row = qA[0:1, 0:1024]
    bzrow = qB[0:1, 0:1024]
    b2row = qCs[0:1, 0:1024]
    bcrow_t = qCs[0:1, 0:512]

    sb_id_bf = sbuf("id_bf", (128, 128), BF16)
    sb_id_f32 = sbuf("id_f32", (128, 128), F32)
    sb_onesr = sbuf("onesr", (1, 128), F32)
    sb_onesmat = sbuf("onesmat", (128, 128), BF16)
    sb_maskPT = sbuf("maskPT", (128, 128), BF16)
    sb_maskTT = sbuf("maskTT", (128, 128), BF16)
    sb_maskTS = sbuf("maskTS", (128, 128), BF16)
    sb_rotT = sbuf("rotT", (128, 128), BF16)
    sb_zrow = sbuf("zrow", (6, 128), F32)

    def dmat(queue, dst, src, reads=(), writes=(), extra=(), slow=False):
        if dst.dtype != src.dtype:
            queue = "pool"
        return sch.task(queue, lambda e, d=dst, s=src, sl=slow: e.dma_start(
            out=d, in_=s, allow_slow_non_contiguous=sl),
            reads=reads, writes=writes, dma=True, extra=extra)

    cscal_in = din("c_scal", (128, len(SCALAR_CONSTS)), F32)
    sb_cscal = sbuf("cscal", (128, len(SCALAR_CONSTS)), F32)
    dmat("sp", sb_cscal[:], cscal_in[:], writes=["cscal"])
    for i, v in enumerate(SCALAR_CONSTS):
        nc.const_aps.register(F32, float(v), sb_cscal[:, i:i + 1])

    for dst, src, key in ((sb_id_bf, cid_bf, "idbf"), (sb_id_f32, cid_f32, "idf"),
                          (sb_onesr, onesr, "onesr"), (sb_onesmat, onesmat, "onesm"),
                          (sb_maskPT, maskPT, "mPT"), (sb_maskTT, maskTT, "mTT"),
                          (sb_maskTS, maskTS, "mTS"), (sb_rotT, rotT, "rotT"),
                          (sb_zrow, zrow6, "zrow")):
        dmat("sp", dst[:], src[:], writes=[key])

    tbrr = [0]

    def tb_slot():
        i = tbrr[0] % 8
        tbrr[0] += 1
        b = i % 2
        j = i // 2
        return tbank[b][:, j * 128:(j + 1) * 128], f"tb{i}", f"tbank{b}"

    tfrr = [0]

    def tf_slot():
        i = tfrr[0] % 4
        tfrr[0] += 1
        return tf[:, i * 128:(i + 1) * 128], f"tf{'ABCD'[i]}", "tfbank"

    gprr = [0]

    def gp_slot():
        i = gprr[0] % 5
        gprr[0] += 1
        return gp[i], f"gp{i}"

    def round_pipe(src, dst, kin, kout, mant_bits, min_p, clip=None,
                   tie_even=False, bb=None, kbb=None, dd=None, kdd=None):
        A = qA[:, 0:1024]; Cc = qCs[:, 0:1024]
        B = bb if bb is not None else qB[:, 0:1024]
        Dd = dd if dd is not None else qD[:]
        KA, KC = "rpA", "rpC"
        KB = kbb if kbb is not None else "rpB"
        KD_ = kdd if kdd is not None else "qD"
        lo = float(min_p + 64)
        bias_si = float(LN2 * (mant_bits + 64))
        bias_s = float(-LN2 * (mant_bits + 64))
        sch.task("act", lambda e: e.activation(out=A, in_=src, func=AF.Abs),
                 reads=[kin], writes=[KA])
        if clip is not None:
            sch.task("dve", lambda e: e.tensor_scalar(
                out=A, in0=A, scalar1=clip, scalar2=None, op0=AL.min),
                reads=[KA], writes=[KA])
        sch.task("dve", lambda e: e.tensor_scalar(
            out=B, in0=A, scalar1=2.0 ** -40, scalar2=None, op0=AL.max),
            reads=[KA], writes=[KB])
        sch.task("act", lambda e: e.activation(out=Cc, in_=B, func=AF.Ln),
                 reads=[KB], writes=[KC])
        sch.task("dve", lambda e: e.tensor_scalar(
            out=Cc, in0=Cc, scalar1=INVLN2, scalar2=64.0, op0=AL.mult, op1=AL.add),
            reads=[KC], writes=[KC])
        sch.task("dve", lambda e: e.tensor_scalar(
            out=Cc, in0=Cc, scalar1=lo, scalar2=-0.499, op0=AL.max, op1=AL.add),
            reads=[KC], writes=[KC])
        for hh_ in range(2):
            sch.task("pool", lambda e, hh_=hh_: e.tensor_copy(
                out=qI[:], in_=Cc[:, hh_ * 512:(hh_ + 1) * 512]),
                reads=[KC], writes=["qI"])
            sch.task("pool", lambda e, hh_=hh_: e.tensor_copy(
                out=Cc[:, hh_ * 512:(hh_ + 1) * 512], in_=qI[:]),
                reads=["qI"], writes=[KC])
        sch.task("act", lambda e: e.activation(out=B, in_=Cc, func=AF.Exp,
                                               scale=-LN2, bias=bias_si),
                 reads=[KC], writes=[KB])
        sch.task("pool", lambda e: e.tensor_tensor(out=Dd, in0=A, in1=B, op=AL.mult),
                 reads=[KA, KB], writes=[KD_])
        for hh_ in range(2):
            sch.task("pool", lambda e, hh_=hh_: e.tensor_copy(
                out=qI[:], in_=Dd[:, hh_ * 512:(hh_ + 1) * 512]),
                reads=[KD_], writes=["qI"])
            sch.task("pool", lambda e, hh_=hh_: e.tensor_copy(
                out=Dd[:, hh_ * 512:(hh_ + 1) * 512], in_=qI[:]),
                reads=["qI"], writes=[KD_])
        sch.task("act", lambda e: e.activation(out=B, in_=Cc, func=AF.Exp,
                                               scale=LN2, bias=bias_s),
                 reads=[KC], writes=[KB])
        sch.task("pool", lambda e: e.tensor_tensor(out=Dd, in0=Dd, in1=B, op=AL.mult),
                 reads=[KD_, KB], writes=[KD_])
        sch.task("act", lambda e: e.activation(out=A, in_=src, func=AF.Sign),
                 reads=[kin], writes=[KA])
        sch.task("dve", lambda e: e.tensor_tensor(out=dst, in0=Dd, in1=A,
                                                  op=AL.mult),
                 reads=[KD_, KA], writes=[kout])

    def ln_tile(gdst=None, gkey="qD"):
        sch.task("dve", lambda e: e.reduce_sum(lncols[:, 0:1], xtile, axis=AX.X),
                 reads=["xtile"], writes=["lnc"])
        sch.task("dve", lambda e: e.tensor_scalar(
            out=lncols[:, 1:2], in0=lncols[:, 0:1], scalar1=1.0 / D, scalar2=None,
            op0=AL.mult), reads=["lnc"], writes=["lnc"])
        sch.task("dve", lambda e: e.tensor_scalar(
            out=xntm, in0=xtile, scalar1=lncols[:, 1:2], scalar2=None,
            op0=AL.subtract), reads=["xtile", "lnc"], writes=["xntm"])
        gd = gdst if gdst is not None else qD[:]
        sch.task("dve", lambda e, gd=gd: e.scalar_tensor_tensor(
            out=gd, in0=xntm, scalar=1.0, in1=xntm, op0=AL.bypass,
            op1=AL.mult, accum_out=lncols[:, 2:3]),
            reads=["xntm"], writes=[gkey, "lnc"])
        sch.task("act", lambda e: e.activation(
            out=lncols[:, 3:4], in_=lncols[:, 2:3], func=AF.Sqrt,
            bias=1e-5, scale=1.0 / D), reads=["lnc"], writes=["lnc3"])
        sch.task("dve", lambda e: e.reciprocal(out=lncols[:, 2:3],
                                               in_=lncols[:, 3:4]),
                 reads=["lnc3"], writes=["lnc"])
        sch.task("dve", lambda e: e.tensor_scalar(
            out=xntm, in0=xntm, scalar1=lncols[:, 2:3], scalar2=None, op0=AL.mult),
            reads=["lnc"], writes=["xntm"])

    try:
        for tt in range(NT):
            dmat("sp", xtile, x_in[tt * 128:(tt + 1) * 128, :], writes=["xtile"])
            ln_tile()
            for dc in range(8):
                sch.task("pe", lambda e, dc=dc: e.transpose(
                    tf[:, 0:128], xntm[:, dc * 128:(dc + 1) * 128], sb_id_f32[:]),
                    reads=["xntm", "idf"], writes=["tfA", "tfbank"])
                sch.task("dve", lambda e, dc=dc: e.tensor_copy(
                    out=xnfm_bf[:, dc * 128:(dc + 1) * 128], in_=tf[:, 0:128]),
                    reads=["tfA"], writes=["sbf"])
            for dc in range(8):
                dmat("sp", hxn0[dc, :, tt * 128:(tt + 1) * 128],
                     xnfm_bf[:, dc * 128:(dc + 1) * 128],
                     reads=["sbf"], writes=["hxn0"])

        def maybe_stop(n):
            if phase_limit <= n:
                raise _StopBuild()

        maybe_stop(1)
        NCOL = NCH * NH
        assert NCOL <= 512
        NG = 8
        dcols = sbuf("dcols", (128, NG * NCOL), F32)
        G_B, G_BLNB, G_BETA, G_GAM, G_NEGW, G_C2, G_GC, G_CQ = range(NG)

        def dcol(g, c, h):
            return dcols[:, g * NCOL + c * NH + h:g * NCOL + c * NH + h + 1]

        def dgrp(g):
            return dcols[:, g * NCOL:(g + 1) * NCOL]

        bR = qA[0:6, 0:512]
        betaR = qB[0:6, 0:512]
        lnbR = qB[0:6, 512:1024]
        gR = qCs[0:6, 0:512]
        wba_sb = sbuf("wba_sb", (128, 128), BF16)
        dcol_s = sbuf("dcol_s", (NH, 2), F32)
        negA = sbuf("negA", (NH, 1), F32)
        xs_mov = sbf[:, 0:1024]

        for dc in range(8):
            dmat("sp", wba_sb[:, dc * 16:dc * 16 + 16], Wba[dc * 128:(dc + 1) * 128, :],
                 writes=["wba"])
        maybe_stop(1.02)
        dmat("sp", dcol_s[:, 0:1], A_log[:, :], writes=["dc0"])
        dmat("sp", dcol_s[:, 1:2], dt_bias[:, :], writes=["dc1"])
        maybe_stop(1.03)
        sch.task("act", lambda e: e.activation(out=negA[:], in_=dcol_s[:, 0:1],
                                               func=AF.Exp),
                 reads=["dc0"], writes=["negA"])
        maybe_stop(1.04)
        sch.task("dve", lambda e: e.tensor_scalar(
            out=negA[:], in0=negA[:], scalar1=-1.0, scalar2=None, op0=AL.mult),
            reads=["negA"], writes=["negA"])
        maybe_stop(1.05)

        for tb8 in range(NB):
            if tb8 > 0:
                maybe_stop(1.15)
            t0 = tb8 * 512
            for dc in range(8):
                dmat("sp", xs_mov[:, 0:512], hxn0[dc, :, t0:t0 + 512],
                     reads=["hxn0"], writes=["sbf"])
                sch.task("pe", lambda e, dc=dc: e.matmul(
                    gp[0][0:6, 0:512], wba_sb[:, dc * 16:dc * 16 + 6],
                    xs_mov[:, 0:512], start=(dc == 0), stop=(dc == 7)),
                    reads=["wba", "sbf"], writes=["gp0"])
                sch.task("pe", lambda e, dc=dc: e.matmul(
                    gp[1][0:6, 0:512], wba_sb[:, dc * 16 + 6:dc * 16 + 12],
                    xs_mov[:, 0:512], start=(dc == 0), stop=(dc == 7)),
                    reads=["wba", "sbf"], writes=["gp1"])
            maybe_stop(1.08)
            sch.task("act", lambda e: e.activation(
                out=betaR, in_=gp[0][0:6, 0:512], func=AF.Sigmoid),
                reads=["gp0"], writes=["rpB"])
            maybe_stop(1.09)
            sch.task("act", lambda e: e.activation(
                out=gR, in_=gp[1][0:6, 0:512], func=AF.Exp, bias=dcol_s[:, 1:2]),
                reads=["gp1", "dc1"], writes=["rpC"])
            maybe_stop(1.11)
            sch.task("act", lambda e: e.activation(
                out=gR, in_=gR, func=AF.Ln, bias=1.0),
                reads=["rpC"], writes=["rpC"])
            maybe_stop(1.12)
            sch.task("dve", lambda e: e.tensor_scalar(
                out=gR, in0=gR, scalar1=negA[:], scalar2=None, op0=AL.mult),
                reads=["rpC", "negA"], writes=["rpC"])
            maybe_stop(1.2)
            for ccl in range(4):
                u0 = ccl * C
                sch.task("dve", lambda e, u0=u0: e.tensor_tensor_scan(
                    out=qA[0:6, u0:u0 + C], data0=qCs[0:6, u0:u0 + C],
                    data1=sb_zrow[:, :], initial=0.0, op0=AL.add, op1=AL.add),
                    reads=["rpC", "zrow"], writes=["rpA"])
            maybe_stop(1.3)
            sch.task("act", lambda e: e.activation(
                out=lnbR, in_=betaR, func=AF.Ln),
                reads=["rpB"], writes=["rpB"])
            sch.task("dve", lambda e: e.tensor_tensor(
                out=gR, in0=lnbR, in1=bR, op=AL.add),
                reads=["rpB", "rpA", "rpC"], writes=["rpC"])
            for ccl in range(4):
                cc = tb8 * 4 + ccl
                u0 = ccl * C
                for src_ap, kread, goff, kw in (
                        (qA[0:6, u0:u0 + C], "rpA", G_B, "dcB"),
                        (qB[0:6, u0:u0 + C], "rpB", G_BETA, "dcBE"),
                        (qCs[0:6, u0:u0 + C], "rpC", G_BLNB, "dcBL")):
                    sch.task("pe", lambda e, src_ap=src_ap: e.transpose(
                        tf[:, 0:6], src_ap, sb_id_f32[0:6, 0:6]),
                        reads=[kread, "idf"],
                        writes=["tfA", "tfB", "tfC", "tfD", "tfbank"])
                    sch.task("dve", lambda e, cc=cc, goff=goff: e.tensor_copy(
                        out=dcols[:, goff * NCOL + cc * NH:goff * NCOL + cc * NH + 6],
                        in_=tf[:, 0:6]), reads=["tfA", "tfbank"], writes=[kw])

        maybe_stop(1.5)
        sch.task("act", lambda e: e.activation(out=dgrp(G_GAM), in_=dgrp(G_B),
                                               func=AF.Exp),
                 reads=["dcB"], writes=["dcG"])
        sch.task("dve", lambda e: e.scalar_tensor_tensor(
            out=dgrp(G_NEGW), in0=dgrp(G_BETA), scalar=-1.0, in1=dgrp(G_GAM),
            op0=AL.mult, op1=AL.mult), reads=["dcBE", "dcG"], writes=["dcNW"])
        maybe_stop(1.7)
        dmat("pool", bcrow_t[0:1, 0:NCOL],
             dcols[127:128, G_B * NCOL:(G_B + 1) * NCOL],
             reads=["dcB"], writes=["rpC"])
        maybe_stop(1.75)
        sch.task("pe", lambda e: e.matmul(
            tf[:, 0:NCOL], sb_onesr[:], bcrow_t[0:1, 0:NCOL], start=True, stop=True),
            reads=["rpC", "onesr"],
            writes=["tfA", "tfB", "tfC", "tfD", "tfbank"])
        maybe_stop(1.8)
        sch.task("act", lambda e: e.activation(out=dgrp(G_GC), in_=tf[:, 0:NCOL],
                                               func=AF.Exp),
                 reads=["tfA", "tfbank"], writes=["dcGC"])
        maybe_stop(1.85)
        sch.task("dve", lambda e: e.tensor_tensor(
            out=dgrp(G_C2), in0=tf[:, 0:NCOL], in1=dgrp(G_B), op=AL.subtract),
            reads=["tfA", "tfbank", "dcB"], writes=["dcC2"])
        maybe_stop(1.9)
        sch.task("act", lambda e: e.activation(out=dgrp(G_C2), in_=dgrp(G_C2),
                                               func=AF.Exp),
                 reads=["dcC2"], writes=["dcC2"])

        maybe_stop(2)
        wg_sb = AB2[:, 0:8 * VD]
        nw_sb = sbuf("nw_sb", (128, VD), BF16)
        xnb2 = sbuf("xstat", (128, 2048), BF16)
        xstat = xnb2[:, 0:1024]
        gtile = sbf[:, 0:512]
        for dc in range(8):
            dmat("sp", wg_sb[:, dc * VD:(dc + 1) * VD], Wg[dc * 128:(dc + 1) * 128, :],
                 writes=["wg"])
        dmat("sp", nw_sb[:], nw_b[:, :], writes=["nw"])
        for tt in range(NT):
            for dc in range(8):
                dmat("sp", xstat[:, dc * 128:(dc + 1) * 128],
                     hxn0[dc, :, tt * 128:(tt + 1) * 128],
                     reads=["hxn0"], writes=["xstat"])
            for half in range(3):
                for dc in range(8):
                    sch.task("pe", lambda e, dc=dc, half=half: e.matmul(
                        gp[half][:, 0:512], xstat[:, dc * 128:(dc + 1) * 128],
                        wg_sb[:, dc * VD + half * 512:dc * VD + (half + 1) * 512],
                        start=(dc == 0), stop=(dc == 7)),
                        reads=["xstat", "wg"], writes=[f"gp{half}"])
                sch.task("act", lambda e, half=half: e.activation(
                    out=gtile[:, 0:512], in_=gp[half][:, 0:512],
                    func=AF.Silu), reads=[f"gp{half}"], writes=["sbf"])
                sch.task("dve", lambda e, half=half: e.tensor_tensor(
                    out=gtile[:, 0:512], in0=gtile[:, 0:512],
                    in1=nw_sb[:, half * 512:(half + 1) * 512] if half < 2
                    else nw_sb[:, 1024:1536], op=AL.mult),
                    reads=["sbf", "nw"], writes=["sbf"])
                dmat("sp", hgsn[tt][:, half * 512:(half + 1) * 512], gtile[:, 0:512],
                     reads=["sbf"], writes=["hgsn"])

        maybe_stop(3)
        WSTR = KD + KD + VD
        wqkv = AB2[:, 0:8 * WSTR]
        for dc in range(8):
            b0 = dc * WSTR
            dmat("sp", wqkv[:, b0:b0 + KD], Wq[dc * 128:(dc + 1) * 128, :],
                 writes=["wqkv", "wg"])
            dmat("sp", wqkv[:, b0 + KD:b0 + 2 * KD], Wk[dc * 128:(dc + 1) * 128, :],
                 writes=["wqkv"])
            dmat("sp", wqkv[:, b0 + 2 * KD:b0 + WSTR], Wv[dc * 128:(dc + 1) * 128, :],
                 writes=["wqkv"])
        taps_sb = sbuf("taps_sb", (128, 96), F32)
        dmat("sp", taps_sb[:, 0:24], convq[:, :], writes=["taps"])
        dmat("sp", taps_sb[:, 24:48], convk[:, :], writes=["taps"])
        dmat("sp", taps_sb[:, 48:96], convv[:, :], writes=["taps"])

        xblk = AB[:, 0:4096]
        tails = sbuf("tails", (128, 72), F32)
        silu_bf = sbf[:, 0:512]

        sch.task("dve", lambda e: e.memset(tails[:], 0.0), writes=["tails"])

        cchunks = []
        for h in range(NH):
            cchunks.append((h * 128, h * 4, hq, h))
        for h in range(NH):
            cchunks.append((KD + h * 128, 24 + h * 4, hk, h))
        for j in range(2 * NH):
            cchunks.append((2 * KD + j * 128, 48 + j * 4, hv, j))

        for tb8 in range(NB):
            t0 = tb8 * 512
            for dc in range(8):
                dmat("sp", xblk[:, dc * 512:(dc + 1) * 512], hxn0[dc, :, t0:t0 + 512],
                     reads=["hxn0"], writes=["xblk"])
            for ci in range(24):
                coff, tap0, dram, slot = cchunks[ci]
                bank = gp[ci % 2]
                kb = f"gp{ci % 2}"
                for dc in range(8):
                    sch.task("pe", lambda e, dc=dc, coff=coff, bank=bank: e.matmul(
                        bank[:, 0:512], wqkv[:, dc * WSTR + coff:dc * WSTR + coff + 128],
                        xblk[:, dc * 512:(dc + 1) * 512],
                        start=(dc == 0), stop=(dc == 7)),
                        reads=["wqkv", "xblk"], writes=[kb])
                sch.task("dve", lambda e, bank=bank: e.tensor_copy(
                    out=pre_sb[:, 3:515], in_=bank[:, 0:512]),
                    reads=[kb], writes=["qD"])
                sch.task("dve", lambda e, ci=ci: e.tensor_copy(
                    out=pre_sb[:, 0:3], in_=tails[:, ci * 3:ci * 3 + 3]),
                    reads=["tails"], writes=["qD"])
                sch.task("dve", lambda e, ci=ci: e.tensor_copy(
                    out=tails[:, ci * 3:ci * 3 + 3], in_=pre_sb[:, 512:515]),
                    reads=["qD"], writes=["tails"])
                sch.task("dve", lambda e, tap0=tap0: e.tensor_scalar(
                    out=cacc[:, 0:512], in0=pre_sb[:, 0:512],
                    scalar1=taps_sb[:, tap0:tap0 + 1],
                    scalar2=None, op0=AL.mult),
                    reads=["qD", "taps"], writes=["q32"])
                for jj in (1, 2, 3):
                    sch.task("dve", lambda e, tap0=tap0, jj=jj: e.scalar_tensor_tensor(
                        out=cacc[:, 0:512], in0=pre_sb[:, jj:jj + 512],
                        scalar=taps_sb[:, tap0 + jj:tap0 + jj + 1], in1=cacc[:, 0:512],
                        op0=AL.mult, op1=AL.add),
                        reads=["qD", "taps", "q32"], writes=["q32"])
                sch.task("act", lambda e: e.activation(out=silu_bf[:, 0:512],
                                                       in_=cacc[:, 0:512],
                                                       func=AF.Silu),
                         reads=["q32"], writes=["sbf"])
                for j in range(4):
                    cix = t0 // 128 + j
                    dmat("sp", dram[slot, cix], silu_bf[:, j * 128:(j + 1) * 128],
                         reads=["sbf"], writes=["hqkv"])

        maybe_stop(4)
        wo_sb = AB2[:, 0:12 * 1024]
        for vc in range(12):
            dmat("sp", wo_sb[:, vc * 1024:(vc + 1) * 1024],
                 Wo[vc * 128:(vc + 1) * 128, :], writes=["wo", "wqkv"])

        S_sb = sbuf("S_sb", (128, NH * DV), F32)
        S_bf = sbuf("S_bf", (128, NH * DV), BF16)
        for h in range(NH):
            sch.task("dve", lambda e, h=h: e.memset(S_sb[:, h * 256:(h + 1) * 256], 0.0),
                     writes=[f"S{h}"])
            sch.task("dve", lambda e, h=h: e.memset(S_bf[:, h * 256:(h + 1) * 256], 0.0),
                     writes=[f"Sbf{h}"])

        HSTR = 3584
        def hb(h, off, n):
            b0 = 4096 + h * HSTR + off
            return AB[:, b0:b0 + n]
        ktm_h = lambda h: hb(h, 0, 128)
        k2tm_h = lambda h: hb(h, 128, 128)
        khf_h = lambda h: hb(h, 256, 128)
        PT_h = lambda h: hb(h, 384, 128)
        TSb_h = lambda h: hb(h, 512, 128)
        TTp_h = lambda h, j: hb(h, 640 + j * 128, 128)
        TSP_h = lambda h, j: hb(h, 896 + j * 128, 128)
        MT_h = lambda h: hb(h, 1664, 128)
        vtm_h = lambda h: hb(h, 1792, 256)
        bv_h = lambda h: hb(h, 2048, 256)
        ru_h = lambda h: hb(h, 2304, 256)
        u_h = lambda h: hb(h, 2560, 256)
        expm_h = lambda h: hb(h, 2816, 384)
        qf_h = lambda h: hb(h, 3200, 128)
        kf_h = lambda h: hb(h, 3328, 128)
        vfm_h = lambda h: hb(h, 3456, 128)
        osc_h = lambda h: AFX[:, 2048 + h * 256:2048 + (h + 1) * 256]
        o1s_h = lambda h: AFX[:, 3584 + h * 256:3584 + (h + 1) * 256]
        ccb_h = lambda h: q32[:, h * 128:(h + 1) * 128]
        ein_h = lambda h: qD[:, h * 128:(h + 1) * 128]
        gsn_sb = sbuf("gsn_sb", (128, VD), BF16)
        o2fm = sbuf("o2fm", (128, 12 * 128), BF16)
        jnk = sbuf("jnk", (128, 128), BF16)
        sso = sbuf("sso", (128, 24), F32)
        fence5 = sch.task("sp", lambda e: e.nop(),
                          writes=["rpB", "rpA", "rpC", "q32", "qD", "xblk"])
        HS = range(NH)

        for cc in range(NCH):
            dmat("sp", gsn_sb[:], hgsn[cc], reads=["hgsn"], writes=["gsn"])
            for h in HS:
                dmat("sp", qf_h(h), hq[h, cc], reads=["hqkv"],
                     writes=[f"qf{h}"], extra=[fence5])
                dmat("sp", kf_h(h), hk[h, cc], reads=["hqkv"],
                     writes=[f"kf{h}"], extra=[fence5])
            for h in HS:
                tq, ktq, bkq = tb_slot()
                sch.task("pe", lambda e, h=h, tq=tq: e.transpose(
                    tq, qf_h(h), sb_id_bf[:]), reads=[f"qf{h}", "idbf"],
                    writes=[ktq, bkq])
                sch.task("act", lambda e, h=h, tq=tq: e.activation(
                    out=jnk[:], in_=tq, func=AF.Square,
                    accum_out=sso[:, h * 4:h * 4 + 1]),
                    reads=[ktq, bkq], writes=["jnk", f"ssoq{h}"])
                sch.task("act", lambda e, h=h, cc=cc: e.activation(
                    out=dcol(G_CQ, cc, h), in_=sso[:, h * 4:h * 4 + 1],
                    func=AF.Sqrt, scale=128.0, bias=128.0e-12),
                    reads=[f"ssoq{h}"], writes=[f"dcCQ{h}"])
                sch.task("dve", lambda e, h=h, cc=cc: e.reciprocal(
                    out=dcol(G_CQ, cc, h), in_=dcol(G_CQ, cc, h)),
                    reads=[f"dcCQ{h}"], writes=[f"dcCQ{h}"])
            for h in HS:
                tk, ktk, bkk = tb_slot()
                sch.task("pe", lambda e, h=h, tk=tk: e.transpose(
                    tk, kf_h(h), sb_id_bf[:]), reads=[f"kf{h}", "idbf"],
                    writes=[ktk, bkk])
                sch.task("act", lambda e, h=h, tk=tk: e.activation(
                    out=jnk[:], in_=tk, func=AF.Square,
                    accum_out=sso[:, h * 4 + 1:h * 4 + 2]),
                    reads=[ktk, bkk], writes=["jnk", f"ssok{h}"])
                sch.task("act", lambda e, h=h: e.activation(
                    out=sso[:, h * 4 + 2:h * 4 + 3],
                    in_=sso[:, h * 4 + 1:h * 4 + 2], func=AF.Sqrt, bias=1e-12),
                    reads=[f"ssok{h}"], writes=[f"dcIK{h}"])
                sch.task("dve", lambda e, h=h: e.reciprocal(
                    out=sso[:, h * 4 + 2:h * 4 + 3],
                    in_=sso[:, h * 4 + 2:h * 4 + 3]),
                    reads=[f"dcIK{h}"], writes=[f"dcIK{h}"])
                sch.task("dve", lambda e, h=h, tk=tk: e.tensor_scalar(
                    out=ktm_h(h), in0=tk, scalar1=sso[:, h * 4 + 2:h * 4 + 3],
                    scalar2=None, op0=AL.mult),
                    reads=[ktk, bkk, f"dcIK{h}"], writes=[f"ktm{h}"])
                sch.task("dve", lambda e, h=h, cc=cc: e.tensor_scalar(
                    out=k2tm_h(h), in0=ktm_h(h), scalar1=dcol(G_C2, cc, h),
                    scalar2=None, op0=AL.mult),
                    reads=[f"ktm{h}", "dcC2"], writes=[f"k2tm{h}"])
                tk2, ktk2, bkk2 = tb_slot()
                sch.task("pe", lambda e, h=h, tk2=tk2: e.transpose(
                    tk2, ktm_h(h), sb_id_bf[:]), reads=[f"ktm{h}", "idbf"],
                    writes=[ktk2, bkk2])
                sch.task("dve", lambda e, h=h, tk2=tk2: e.tensor_copy(
                    out=khf_h(h), in_=tk2), reads=[ktk2, bkk2],
                    writes=[f"khf{h}"])
            for h in HS:
                for half in range(2):
                    dmat("sp", vfm_h(h), hv[2 * h + half, cc],
                         reads=["hqkv"], writes=[f"vf{h}"], extra=[fence5])
                    tv, ktv, bkv = tb_slot()
                    sch.task("pe", lambda e, h=h, tv=tv: e.transpose(
                        tv, vfm_h(h), sb_id_bf[:]), reads=[f"vf{h}", "idbf"],
                        writes=[ktv, bkv])
                    sch.task("dve", lambda e, h=h, half=half, tv=tv: e.tensor_copy(
                        out=vtm_h(h)[:, half * 128:(half + 1) * 128], in_=tv),
                        reads=[ktv, bkv], writes=[f"vtm{h}"])
                sch.task("dve", lambda e, h=h, cc=cc: e.tensor_scalar(
                    out=bv_h(h), in0=vtm_h(h), scalar1=dcol(G_BETA, cc, h),
                    scalar2=None, op0=AL.mult),
                    reads=[f"vtm{h}", "dcBE"], writes=[f"bv{h}"])
            for h in HS:
                tfa, ktfa, bfa = tf_slot()
                sch.task("dve", lambda e, h=h, cc=cc: e.tensor_scalar(
                    out=ccb_h(h), in0=sb_onesmat[:], scalar1=dcol(G_B, cc, h),
                    scalar2=None, op0=AL.mult),
                    reads=["onesm", "dcB"], writes=[f"ccb{h}"], extra=[fence5])
                sch.task("pe", lambda e, h=h, tfa=tfa: e.transpose(
                    tfa, ccb_h(h), sb_id_f32[:]), reads=[f"ccb{h}", "idf"],
                    writes=[ktfa, bfa])
                sch.task("dve", lambda e, h=h, cc=cc, tfa=tfa: e.scalar_tensor_tensor(
                    out=ein_h(h), in0=tfa, scalar=dcol(G_B, cc, h),
                    in1=sb_maskPT[:], op0=AL.subtract, op1=AL.subtract),
                    reads=[ktfa, bfa, "dcB", "mPT"], writes=[f"ein{h}"],
                    extra=[fence5])
                sch.task("act", lambda e, h=h: e.activation(
                    out=expm_h(h)[:, 0:128], in_=ein_h(h), func=AF.Exp),
                    reads=[f"ein{h}"], writes=[f"ePT{h}"])
                sch.task("dve", lambda e, h=h, cc=cc, tfa=tfa: e.scalar_tensor_tensor(
                    out=ein_h(h), in0=tfa, scalar=dcol(G_BLNB, cc, h),
                    in1=sb_maskTS[:], op0=AL.subtract, op1=AL.add),
                    reads=[ktfa, bfa, "dcBL", "mTS", f"ePT{h}"], writes=[f"ein{h}"])
                sch.task("act", lambda e, h=h: e.activation(
                    out=expm_h(h)[:, 256:384], in_=ein_h(h), func=AF.Exp,
                    scale=-1.0), reads=[f"ein{h}"], writes=[f"eTS{h}"])
                tfb, ktfb, bfb2 = tf_slot()
                sch.task("dve", lambda e, h=h, cc=cc: e.tensor_scalar(
                    out=ccb_h(h), in0=sb_onesmat[:], scalar1=dcol(G_BLNB, cc, h),
                    scalar2=None, op0=AL.mult),
                    reads=["onesm", "dcBL", f"ein{h}"], writes=[f"ccb{h}"])
                sch.task("pe", lambda e, h=h, tfb=tfb: e.transpose(
                    tfb, ccb_h(h), sb_id_f32[:]), reads=[f"ccb{h}", "idf"],
                    writes=[ktfb, bfb2])
                sch.task("dve", lambda e, h=h, cc=cc, tfb=tfb: e.scalar_tensor_tensor(
                    out=ein_h(h), in0=tfb, scalar=dcol(G_B, cc, h),
                    in1=sb_maskTT[:], op0=AL.subtract, op1=AL.subtract),
                    reads=[ktfb, bfb2, "dcB", "mTT", f"eTS{h}"], writes=[f"ein{h}"])
                sch.task("act", lambda e, h=h: e.activation(
                    out=expm_h(h)[:, 128:256], in_=ein_h(h), func=AF.Exp),
                    reads=[f"ein{h}"], writes=[f"eTT{h}"])
            for h in HS:
                tg, ktg = gp_slot()
                sch.task("pe", lambda e, h=h, tg=tg: e.matmul(
                    tg[:, 0:128], khf_h(h), khf_h(h), start=True, stop=True),
                    reads=[f"khf{h}"], writes=[ktg])
                sch.task("dve", lambda e, h=h, tg=tg: e.tensor_tensor(
                    out=TTp_h(h, 0), in0=tg[:, 0:128], in1=expm_h(h)[:, 128:256],
                    op=AL.mult), reads=[ktg, f"eTT{h}"], writes=[f"TT{h}_0"])
                sch.task("dve", lambda e, h=h, tg=tg: e.tensor_tensor(
                    out=TSb_h(h), in0=tg[:, 0:128], in1=expm_h(h)[:, 256:384],
                    op=AL.mult), reads=[ktg, f"eTS{h}"], writes=[f"TSb{h}"])
                tg2, ktg2 = gp_slot()
                sch.task("pe", lambda e, h=h, tg2=tg2: e.matmul(
                    tg2[:, 0:128], khf_h(h), qf_h(h), start=True, stop=True),
                    reads=[f"khf{h}", f"qf{h}"], writes=[ktg2])
                sch.task("dve", lambda e, h=h, tg2=tg2: e.tensor_tensor(
                    out=PT_h(h), in0=tg2[:, 0:128], in1=expm_h(h)[:, 0:128],
                    op=AL.mult), reads=[ktg2, f"ePT{h}"], writes=[f"PT{h}"])
                sch.task("dve", lambda e, h=h: e.scalar_tensor_tensor(
                    out=MT_h(h), in0=TTp_h(h, 0), scalar=-1.0, in1=sb_id_bf[:],
                    op0=AL.mult, op1=AL.add),
                    reads=[f"TT{h}_0", "idbf"], writes=[f"MT{h}"])
            prevs = {h: (TSb_h(h), f"TSb{h}", TTp_h(h, 0), f"TT{h}_0") for h in HS}
            for j in range(6):
                for h in HS:
                    ts_p, kts_p, tt_p, ktt_p = prevs[h]
                    tp, ktp = gp_slot()
                    cur = TSP_h(h, j)
                    kcur = f"TSP{h}_{j}"
                    sch.task("pe", lambda e, tp=tp, a=tt_p, b=ts_p: e.matmul(
                        tp[:, 0:128], a, b, start=True, stop=True),
                        reads=[kts_p, ktt_p], writes=[ktp])
                    sch.task("dve", lambda e, tp=tp, cur=cur: e.tensor_copy(
                        out=cur, in_=tp[:, 0:128]), reads=[ktp], writes=[kcur])
                    if j < 5:
                        tp2, ktp2 = gp_slot()
                        curT = TTp_h(h, (j + 1) % 2)
                        kcurT = f"TT{h}_{(j + 1) % 2}"
                        sch.task("pe", lambda e, tp2=tp2, a=ts_p, b=tt_p: e.matmul(
                            tp2[:, 0:128], a, b, start=True, stop=True),
                            reads=[kts_p, ktt_p], writes=[ktp2])
                        sch.task("dve", lambda e, tp2=tp2, curT=curT: e.tensor_copy(
                            out=curT, in_=tp2[:, 0:128]), reads=[ktp2],
                            writes=[kcurT])
                        prevs[h] = (cur, kcur, curT, kcurT)
                    else:
                        prevs[h] = (cur, kcur, None, None)
            for j in range(6):
                for h in HS:
                    tsl = TSP_h(h, j)
                    ktsl = f"TSP{h}_{j}"
                    tp3, ktp3 = gp_slot()
                    sch.task("pe", lambda e, h=h, tp3=tp3, tsl=tsl: e.matmul(
                        tp3[:, 0:128], tsl, MT_h(h), start=True, stop=True),
                        reads=[ktsl, f"MT{h}"], writes=[ktp3])
                    sch.task("dve", lambda e, h=h, tp3=tp3: e.tensor_tensor(
                        out=MT_h(h), in0=MT_h(h), in1=tp3[:, 0:128], op=AL.add),
                        reads=[ktp3, f"MT{h}"], writes=[f"MT{h}"])
            for h in HS:
                if cc == 0:
                    sch.task("dve", lambda e, h=h: e.tensor_copy(
                        out=ru_h(h), in_=bv_h(h)),
                        reads=[f"bv{h}"], writes=[f"ru{h}"])
                else:
                    gb, kgb = gp_slot()
                    sch.task("pe", lambda e, h=h, gb=gb: e.matmul(
                        gb[:, 0:256], khf_h(h), S_bf[:, h * 256:(h + 1) * 256],
                        start=True, stop=True),
                        reads=[f"khf{h}", f"Sbf{h}"], writes=[kgb])
                    sch.task("dve", lambda e, h=h, cc=cc, gb=gb: e.scalar_tensor_tensor(
                        out=ru_h(h), in0=gb[:, 0:256], scalar=dcol(G_NEGW, cc, h),
                        in1=bv_h(h), op0=AL.mult, op1=AL.add),
                        reads=[kgb, "dcNW", f"bv{h}"], writes=[f"ru{h}"])
            for h in HS:
                gu, kgu = gp_slot()
                sch.task("pe", lambda e, h=h, gu=gu: e.matmul(
                    gu[:, 0:256], MT_h(h), ru_h(h), start=True, stop=True),
                    reads=[f"MT{h}", f"ru{h}"], writes=[kgu])
                sch.task("dve", lambda e, h=h, gu=gu: e.tensor_copy(
                    out=u_h(h), in_=gu[:, 0:256]), reads=[kgu], writes=[f"u{h}"])
            for h in HS:
                go1, kgo1 = gp_slot()
                sch.task("pe", lambda e, h=h, go1=go1: e.matmul(
                    go1[:, 0:256], PT_h(h), u_h(h), start=True, stop=True),
                    reads=[f"PT{h}", f"u{h}"], writes=[kgo1])
                if cc == 0:
                    sch.task("dve", lambda e, h=h, cc=cc, go1=go1: e.tensor_scalar(
                        out=osc_h(h), in0=go1[:, 0:256],
                        scalar1=dcol(G_CQ, cc, h), scalar2=None, op0=AL.mult),
                        reads=[kgo1, f"dcCQ{h}"], writes=[f"osc{h}"],
                        extra=[fence5])
                else:
                    sch.task("dve", lambda e, h=h, go1=go1: e.tensor_copy(
                        out=o1s_h(h), in_=go1[:, 0:256]), reads=[kgo1],
                        writes=[f"o1s{h}"], extra=[fence5])
                    go2, kgo2 = gp_slot()
                    sch.task("pe", lambda e, h=h, go2=go2: e.matmul(
                        go2[:, 0:256], qf_h(h), S_bf[:, h * 256:(h + 1) * 256],
                        start=True, stop=True),
                        reads=[f"qf{h}", f"Sbf{h}"], writes=[kgo2])
                    sch.task("dve", lambda e, h=h, cc=cc, go2=go2: e.scalar_tensor_tensor(
                        out=o1s_h(h), in0=go2[:, 0:256], scalar=dcol(G_GAM, cc, h),
                        in1=o1s_h(h), op0=AL.mult, op1=AL.add),
                        reads=[kgo2, "dcG", f"o1s{h}"], writes=[f"o1s{h}"])
                    sch.task("dve", lambda e, h=h, cc=cc: e.tensor_scalar(
                        out=osc_h(h), in0=o1s_h(h),
                        scalar1=dcol(G_CQ, cc, h), scalar2=None, op0=AL.mult),
                        reads=[f"o1s{h}", f"dcCQ{h}"], writes=[f"osc{h}"])
                gd, kgd = gp_slot()
                sch.task("pe", lambda e, h=h, gd=gd: e.matmul(
                    gd[:, 0:256], k2tm_h(h), u_h(h), start=True, stop=True),
                    reads=[f"k2tm{h}", f"u{h}"], writes=[kgd])
                sch.task("dve", lambda e, h=h, cc=cc, gd=gd: e.scalar_tensor_tensor(
                    out=S_sb[:, h * 256:(h + 1) * 256],
                    in0=S_sb[:, h * 256:(h + 1) * 256],
                    scalar=dcol(G_GC, cc, h), in1=gd[:, 0:256],
                    op0=AL.mult, op1=AL.add),
                    reads=[kgd, "dcGC", f"S{h}"], writes=[f"S{h}"])
                sch.task("dve", lambda e, h=h: e.tensor_copy(
                    out=S_bf[:, h * 256:(h + 1) * 256],
                    in_=S_sb[:, h * 256:(h + 1) * 256]),
                    reads=[f"S{h}"], writes=[f"Sbf{h}"])
            for h in HS:
                sch.task("dve", lambda e, h=h: e.scalar_tensor_tensor(
                    out=jnk[:] if False else bv_h(h), in0=osc_h(h), scalar=1.0,
                    in1=osc_h(h), op0=AL.bypass, op1=AL.mult,
                    accum_out=sso[:, h * 4 + 3:h * 4 + 4]),
                    reads=[f"osc{h}"], writes=[f"bv{h}", f"ssoo{h}"])
                sch.task("act", lambda e, h=h: e.activation(
                    out=sso[:, h * 4 + 3:h * 4 + 4],
                    in_=sso[:, h * 4 + 3:h * 4 + 4], func=AF.Sqrt,
                    scale=1.0 / 256.0, bias=1e-5),
                    reads=[f"ssoo{h}"], writes=[f"ssoo{h}"])
                sch.task("dve", lambda e, h=h: e.reciprocal(
                    out=sso[:, h * 4 + 3:h * 4 + 4],
                    in_=sso[:, h * 4 + 3:h * 4 + 4]),
                    reads=[f"ssoo{h}"], writes=[f"ssoo{h}"])
                sch.task("dve", lambda e, h=h: e.scalar_tensor_tensor(
                    out=osc_h(h), in0=osc_h(h), scalar=sso[:, h * 4 + 3:h * 4 + 4],
                    in1=gsn_sb[:, h * 256:(h + 1) * 256], op0=AL.mult,
                    op1=AL.mult),
                    reads=[f"ssoo{h}", "gsn", f"osc{h}"], writes=[f"osc{h}"])
                for half in range(2):
                    to, kto, bko = tf_slot()
                    sch.task("pe", lambda e, h=h, half=half, to=to: e.transpose(
                        to, osc_h(h)[:, half * 128:(half + 1) * 128],
                        sb_id_f32[:]), reads=[f"osc{h}", "idf"],
                        writes=[kto, bko])
                    sch.task("dve", lambda e, h=h, half=half, to=to: e.tensor_copy(
                        out=o2fm[:, (2 * h + half) * 128:(2 * h + half + 1) * 128],
                        in_=to), reads=[kto, bko], writes=["o2fm"])
            for dhalf in range(2):
                bank = gp[dhalf]
                kb = f"gp{dhalf}"
                for vc in range(12):
                    sch.task("pe", lambda e, vc=vc, dhalf=dhalf, bank=bank: e.matmul(
                        bank[:, 0:512], o2fm[:, vc * 128:(vc + 1) * 128],
                        wo_sb[:, vc * 1024 + dhalf * 512:vc * 1024 + (dhalf + 1) * 512],
                        start=(vc == 0), stop=(vc == 11)),
                        reads=["o2fm", "wo"], writes=[kb])
                sch.task("dve", lambda e, dhalf=dhalf, bank=bank: e.tensor_copy(
                    out=sbf[:, dhalf * 512:(dhalf + 1) * 512], in_=bank[:, 0:512]),
                    reads=[kb], writes=["sbf"])
            dmat("sp", hy[cc], sbf[:, 0:1024], reads=["sbf"], writes=["hy"])

        maybe_stop(5)
        w1sT = AB[:, 0:16384]
        w2sT = AB[:, 16384:32768]
        p5keys = []
        for h in range(NH):
            p5keys += [f"qf{h}", f"kf{h}", f"vf{h}", f"ktm{h}", f"k2tm{h}",
                       f"khf{h}", f"PT{h}", f"TSb{h}", f"TT{h}_0", f"TT{h}_1",
                       f"MT{h}", f"vtm{h}", f"bv{h}", f"ru{h}", f"u{h}",
                       f"ePT{h}", f"eTT{h}", f"eTS{h}", f"osc{h}", f"o1s{h}",
                       f"ccb{h}", f"ein{h}"]
            p5keys += [f"TSP{h}_{j}" for j in range(6)]
        fence6 = sch.task("sp", lambda e: e.nop(), writes=p5keys)
        w1c_bf = sbf[:, 0:1024]
        w2c_bf = sbf[:, 0:1024]
        alcols = sbuf("alcols", (128, 32), F32)
        b1_sb = sbuf("b1_sb", (128, 32), F32)
        a2cols = sbuf("a2cols", (128, 8), F32)
        a2b = sbuf("a2b", (128, 1024), BF16)
        h_bf = sbuf("h_bf", (128, 256), BF16)
        yt_bf = q_bf
        wfT = AB2[:, 0:8192]
        src8 = qB[:, 0:1024]
        src16 = S_sb[:, 0:1024]

        def fc_unit(fc):
            dmat("pool", q32[:], wf[fc * 128:(fc + 1) * 128, :], writes=["q32"])
            round_pipe(q32[:], q_bf[:], "q32", "qbf", mant_bits=3, min_p=-6,
                       clip=448.0, bb=xntm, kbb="xntm", dd=xtile, kdd="xtile")
            for cc2 in range(8):
                ts_, kts_, bks_ = tb_slot()
                sch.task("pe", lambda e, cc2=cc2, ts_=ts_: e.transpose(
                    ts_, q_bf[:, cc2 * 128:(cc2 + 1) * 128], sb_id_bf[:]),
                    reads=["qbf", "idbf"], writes=[kts_, bks_])
                sch.task("dve", lambda e, cc2=cc2, fc=fc, ts_=ts_: e.tensor_copy(
                    out=wfT[:, cc2 * 1024 + fc * 128:cc2 * 1024 + (fc + 1) * 128],
                    in_=ts_), reads=[kts_, bks_, bks_], writes=["wfT", "wo"])

        def merged_tile(tt):
            dmat("sp", xtile, hx1[tt], reads=["hx1w"], writes=["xtile"])
            ln_tile(gdst=xtile, gkey="xtile")
            for dc in range(8):
                sch.task("pe", lambda e, dc=dc: e.transpose(
                    tf[:, 0:128], xntm[:, dc * 128:(dc + 1) * 128], sb_id_f32[:]),
                    reads=["xntm", "idf"], writes=["tfA", "tfbank"])
                sch.task("dve", lambda e, dc=dc: e.tensor_copy(
                    out=xnfm_bf[:, dc * 128:(dc + 1) * 128], in_=tf[:, 0:128]),
                    reads=["tfA"], writes=["sbf"])
                sch.task("dve", lambda e, dc=dc: e.tensor_copy(
                    out=src8[:, dc * 128:(dc + 1) * 128], in_=tf[:, 0:128]),
                    reads=["tfA", "tfbank"], writes=["rpB"])
            for dc in range(8):
                dmat("sp", hxn1[dc, :, tt * 128:(tt + 1) * 128],
                     xnfm_bf[:, dc * 128:(dc + 1) * 128],
                     reads=["sbf"], writes=["hxn1"])
            round_pipe(src8[:], q_bf[:], "rpB", "qbf", mant_bits=3, min_p=-6,
                       clip=448.0, bb=xntm, kbb="xntm", dd=xtile, kdd="xtile")
            for dc in range(8):
                dmat("sp", hxq[dc, :, tt * 128:(tt + 1) * 128],
                     q_bf[:, dc * 128:(dc + 1) * 128],
                     reads=["qbf"], writes=["hxq"])
            for dh in range(2):
                bank = gp[dh]
                kb = f"gp{dh}"
                for cc2 in range(8):
                    sch.task("pe", lambda e, cc2=cc2, dh=dh, bank=bank: e.matmul(
                        bank[:, 0:512], q_bf[:, cc2 * 128:(cc2 + 1) * 128],
                        wfT[:, cc2 * 1024 + dh * 512:cc2 * 1024 + (dh + 1) * 512],
                        start=(cc2 == 0), stop=(cc2 == 7)),
                        reads=["qbf", "wfT"], writes=[kb])
                sch.task("dve", lambda e, dh=dh, bank=bank: e.tensor_copy(
                    out=src16[:, dh * 512:(dh + 1) * 512], in_=bank[:, 0:512]),
                    reads=[kb], writes=["srcm"])
            round_pipe(src16[:], f16b[:], "srcm", "f16b", mant_bits=10, min_p=-14,
                       clip=None, tie_even=True, bb=xntm, kbb="xntm",
                       dd=xtile, kdd="xtile")
            dmat("sp", hf1[tt], f16b[:], reads=["f16b"], writes=["hf1"])

        dmat("sp", b1_sb[:], b1c[:, :], writes=["b1"])
        for dc2 in range(8):
            for hhx in range(2):
                dmat("pool", w2c_bf[:, 0:1024],
                     w2[dc2 * 128:(dc2 + 1) * 128, hhx * 1024:(hhx + 1) * 1024],
                     writes=["sbf"])
                dmat("pool", xstat[:, 0:1024],
                     w2[dc2 * 128:(dc2 + 1) * 128, 2048 + hhx * 1024:2048 + (hhx + 1) * 1024],
                     writes=["xstat"])
                if hhx == 0:
                    sch.task("dve", lambda e, dc2=dc2: e.tensor_reduce(
                        out=a2cols[:, dc2:dc2 + 1], in_=w2c_bf[:, 0:1024], axis=AX.X,
                        op=AL.add, apply_absolute_value=True),
                        reads=["sbf"], writes=["a2c"])
                else:
                    sch.task("dve", lambda e, dc2=dc2: e.tensor_reduce(
                        out=lncols[:, 0:1], in_=w2c_bf[:, 0:1024], axis=AX.X,
                        op=AL.add, apply_absolute_value=True),
                        reads=["sbf"], writes=["lnc"])
                    sch.task("dve", lambda e, dc2=dc2: e.scalar_tensor_tensor(
                        out=a2cols[:, dc2:dc2 + 1],
                        in0=a2cols[:, dc2:dc2 + 1] if True else None,
                        scalar=1.0, in1=lncols[:, 0:1], op0=AL.mult, op1=AL.add),
                        reads=["lnc", "a2c"], writes=["a2c"])
                sch.task("dve", lambda e: e.tensor_reduce(
                    out=lncols[:, 1:2], in_=xstat[:, 0:1024], axis=AX.X,
                    op=AL.add, apply_absolute_value=True),
                    reads=["xstat"], writes=["lnc1"])
                sch.task("dve", lambda e, dc2=dc2: e.scalar_tensor_tensor(
                    out=a2cols[:, dc2:dc2 + 1], in0=a2cols[:, dc2:dc2 + 1],
                    scalar=1.0, in1=lncols[:, 1:2], op0=AL.mult, op1=AL.add),
                    reads=["lnc1", "a2c"], writes=["a2c"])
        sch.task("dve", lambda e: e.tensor_scalar(
            out=a2cols[:], in0=a2cols[:], scalar1=1.0 / HFF, scalar2=None,
            op0=AL.mult), reads=["a2c"], writes=["a2c"])
        for dc2 in range(8):
            dmat("pool", a2row[0:1, dc2 * 128:(dc2 + 1) * 128],
                 a2cols[:, dc2:dc2 + 1], reads=["a2c"], writes=["rpA"],
                 extra=[fence6], slow=True)
        for half in range(2):
            sch.task("pe", lambda e, half=half: e.matmul(
                tf[:, 0:512], sb_onesr[:], a2row[0:1, half * 512:(half + 1) * 512],
                start=True, stop=True), reads=["rpA", "onesr"],
                writes=["tfA", "tfB", "tfC", "tfD", "tfbank"])
            sch.task("dve", lambda e, half=half: e.tensor_copy(
                out=a2b[:, half * 512:(half + 1) * 512], in_=tf[:, 0:512]),
                reads=["tfA", "tfbank"], writes=["a2b"])
        dmat("sp", b2row, b2r[:, :], writes=["rpC"], extra=[fence6])
        sch.task("dve", lambda e: e.reciprocal(out=bzrow, in_=a2row),
                 reads=["rpA"], writes=["rpB"], extra=[fence6])
        sch.task("dve", lambda e: e.tensor_tensor(
            out=bzrow, in0=bzrow, in1=b2row, op=AL.mult),
            reads=["rpB", "rpC"], writes=["rpB"])

        NT2 = L // 256
        sch.task("sp", lambda e: e.nop(),
                 writes=["srcm", "rpB"] + [f"S{h}" for h in range(NH)])
        for hh in range(2):
            for hcl in range(16):
                hcg = hh * 16 + hcl
                dmat("pool", w1c_bf[:, 0:1024], w1[hcg * 128:(hcg + 1) * 128, :],
                     writes=["sbf"])
                sch.task("dve", lambda e, hcg=hcg: e.tensor_reduce(
                    out=alcols[:, hcg:hcg + 1], in_=w1c_bf[:, 0:1024], axis=AX.X,
                    op=AL.add, apply_absolute_value=True),
                    reads=["sbf"], writes=["al1"])
                for dc in range(8):
                    ts_, kts_, bks_ = tb_slot()
                    sch.task("pe", lambda e, dc=dc, ts_=ts_: e.transpose(
                        ts_, w1c_bf[:, dc * 128:(dc + 1) * 128], sb_id_bf[:]),
                        reads=["sbf", "idbf"], writes=[kts_, bks_])
                    sch.task("act", lambda e, dc=dc, hcl=hcl, ts_=ts_: e.activation(
                        out=w1sT[:, dc * 2048 + hcl * 128:dc * 2048 + (hcl + 1) * 128],
                        in_=ts_, func=AF.Sign),
                        reads=[kts_, bks_, bks_], writes=["w1sT", "xblk"])
            for dc2 in range(8):
                for hq2 in range(2):
                    dmat("pool", w2c_bf[:, 0:1024],
                         w2[dc2 * 128:(dc2 + 1) * 128,
                            hh * 2048 + hq2 * 1024:hh * 2048 + (hq2 + 1) * 1024],
                         writes=["sbf"])
                    for hcl8 in range(8):
                        hcl = hq2 * 8 + hcl8
                        ts_, kts_, bks_ = tb_slot()
                        sch.task("pe", lambda e, hcl8=hcl8, ts_=ts_: e.transpose(
                            ts_, w2c_bf[:, hcl8 * 128:(hcl8 + 1) * 128], sb_id_bf[:]),
                            reads=["sbf", "idbf"], writes=[kts_, bks_])
                        sch.task("act", lambda e, dc2=dc2, hcl=hcl, ts_=ts_:
                                 e.activation(
                            out=w2sT[:, hcl * 1024 + dc2 * 128:hcl * 1024 + (dc2 + 1) * 128],
                            in_=ts_, func=AF.Sign), reads=[kts_, bks_, bks_], writes=["w2sT"])
            sch.task("dve", lambda e, hh=hh: e.tensor_scalar(
                out=alcols[:, hh * 16:(hh + 1) * 16],
                in0=alcols[:, hh * 16:(hh + 1) * 16], scalar1=1.0 / D, scalar2=None,
                op0=AL.mult), reads=["al1"], writes=["al1"])

            for tblk in range(NT2):
                t0 = tblk * 256
                if hh == 0:
                    for fc in range(tblk * 8 // NT2, (tblk + 1) * 8 // NT2):
                        fc_unit(fc)
                for dc in range(8):
                    dmat("sp", xnb2[:, dc * 256:(dc + 1) * 256],
                         hxn0[dc, :, t0:t0 + 256], reads=["hxn0"], writes=["xstat"])
                if hh == 0:
                    for j in range(2):
                        for dh in range(2):
                            bank = gp[j * 2 + dh]
                            kb = f"gp{j * 2 + dh}"
                            sch.task("pe", lambda e, dh=dh, bank=bank: e.matmul(
                                bank[:, 0:512], sb_onesr[:],
                                bzrow[0:1, dh * 512:(dh + 1) * 512],
                                start=True, stop=False),
                                reads=["rpB", "onesr"], writes=[kb])
                for hcl in range(16):
                    for dc in range(8):
                        sch.task("pe", lambda e, hcl=hcl, dc=dc: e.matmul(
                            gp[4][:, 0:256],
                            w1sT[:, dc * 2048 + hcl * 128:dc * 2048 + (hcl + 1) * 128],
                            xnb2[:, dc * 256:(dc + 1) * 256],
                            start=(dc == 0), stop=(dc == 7)),
                            reads=["w1sT", "xstat"], writes=["gp4"])
                    hcg = hh * 16 + hcl
                    sch.task("act", lambda e, hcg=hcg: e.activation(
                        out=h_bf[:], in_=gp[4][:, 0:256], func=AF.Gelu,
                        bias=b1_sb[:, hcg:hcg + 1], scale=alcols[:, hcg:hcg + 1]),
                        reads=["gp4", "b1", "al1"], writes=["h_bf"])
                    for j in range(2):
                        for dh in range(2):
                            bank = gp[j * 2 + dh]
                            kb = f"gp{j * 2 + dh}"
                            first = (hh == 1 and hcl == 0)
                            sch.task("pe", lambda e, j=j, dh=dh, hcl=hcl, bank=bank,
                                     first=first: e.matmul(
                                bank[:, 0:512], h_bf[:, j * 128:(j + 1) * 128],
                                w2sT[:, hcl * 1024 + dh * 512:hcl * 1024 + (dh + 1) * 512],
                                start=first, stop=(hcl == 15)),
                                reads=["h_bf", "w2sT"], writes=[kb])
                for j in range(2):
                    tt = t0 // 128 + j
                    if hh == 0:
                        dmat("sp", qD[:], x_in[tt * 128:(tt + 1) * 128, :],
                             writes=["qD"])
                        dmat("sp", yt_bf[:], hy[tt], reads=["hy"], writes=["qbf"])
                        sch.task("dve", lambda e: e.tensor_tensor(
                            out=qD[:], in0=qD[:], in1=yt_bf[:], op=AL.add),
                            reads=["qbf", "qD"], writes=["qD"])
                    else:
                        dmat("sp", qD[:], hx1[tt], reads=["hx1w"], writes=["qD"])
                    for dh in range(2):
                        bank = gp[j * 2 + dh]
                        kb = f"gp{j * 2 + dh}"
                        sch.task("dve", lambda e, dh=dh, bank=bank: e.tensor_tensor(
                            out=q32[:, dh * 512:(dh + 1) * 512], in0=bank[:, 0:512],
                            in1=a2b[:, dh * 512:(dh + 1) * 512], op=AL.mult),
                            reads=[kb, "a2b"], writes=["q32"])
                    sch.task("dve", lambda e: e.tensor_tensor(
                        out=qD[:], in0=qD[:], in1=q32[:], op=AL.add),
                        reads=["q32", "qD"], writes=["qD"])
                    dmat("sp", hx1[tt], qD[:], reads=["qD"], writes=["hx1w"])
                    if hh == 1 and tblk >= 1:
                        merged_tile(2 * (tblk - 1) + j)

        merged_tile(L // 128 - 2)
        merged_tile(L // 128 - 1)
        maybe_stop(6)
        maybe_stop(8)
        KR = AB2[:, 0:32768]
        VR = AB[:, 0:32768]
        AB5 = sbuf("AB5", (128, 4096), BF16)
        wl1 = AB5[:, 0:4096]
        qp_bf = gsn_sb[:, 0:512]
        qx_bf = gsn_sb[:, 512:1024]
        fence8 = sch.task("sp", lambda e: e.nop(),
                          writes=["wl1", "qp", "qx", "xblk0", "xblk1",
                                  "xblk", "w1sT", "gsn"])

        def l1_proj(wsrc, cost, sint, is_q):
            for wh in range(2):
                for dc in range(8):
                    dmat("pool", wl1[:, dc * 512:(dc + 1) * 512],
                         wsrc[dc * 128:(dc + 1) * 128, wh * 512:(wh + 1) * 512],
                         writes=["wl1"], extra=[fence8])
                for tb8 in range(NB):
                    t0 = tb8 * 512
                    par = (wh * NB + tb8) % 2
                    xbuf = AB[:, par * 4096:(par + 1) * 4096]
                    kxb = f"xblk{par}"
                    for dc in range(8):
                        dmat("sp", xbuf[:, dc * 512:(dc + 1) * 512],
                             hxn1[dc, :, t0:t0 + 512], reads=["hxn1"],
                             writes=[kxb], extra=[fence8])
                    for dc2l in range(4):
                        dc2 = wh * 4 + dc2l
                        bank = gp[dc2 % 2]
                        kb = f"gp{dc2 % 2}"
                        for dc in range(8):
                            sch.task("pe", lambda e, dc=dc, dc2l=dc2l, bank=bank,
                                     xbuf=xbuf: e.matmul(
                                bank[:, 0:512],
                                wl1[:, dc * 512 + dc2l * 128:dc * 512 + (dc2l + 1) * 128],
                                xbuf[:, dc * 512:(dc + 1) * 512],
                                start=(dc == 0), stop=(dc == 7)),
                                reads=["wl1", kxb], writes=[kb])
                        sch.task("dve", lambda e, bank=bank: e.tensor_copy(
                            out=qp_bf[:], in_=bank[:, 0:512]), reads=[kb], writes=["qp"])
                        bank2 = gp[2 + dc2 % 2]
                        kb2 = f"gp{2 + dc2 % 2}"
                        sch.task("pe", lambda e, bank2=bank2: e.matmul(
                            bank2[:, 0:512], sb_rotT[:], qp_bf[:],
                            start=True, stop=True),
                            reads=["qp", "rotT"], writes=[kb2])
                        dmat("sp", trigc,
                             cost[dc2 * 128:(dc2 + 1) * 128, t0:t0 + 512],
                             writes=["rpC"])
                        dmat("sp", trigs,
                             sint[dc2 * 128:(dc2 + 1) * 128, t0:t0 + 512],
                             writes=["rpC2"])
                        sch.task("dve", lambda e: e.tensor_tensor(
                            out=qA[:, 0:512], in0=qp_bf[:], in1=trigc,
                            op=AL.mult), reads=["qp", "rpC"], writes=["rpA"])
                        sch.task("dve", lambda e, bank2=bank2: e.tensor_tensor(
                            out=qB[:, 0:512], in0=bank2[:, 0:512], in1=trigs,
                            op=AL.mult), reads=[kb2, "rpC2"], writes=["rpB"])
                        sch.task("dve", lambda e: e.tensor_tensor(
                            out=qx_bf[:], in0=qA[:, 0:512], in1=qB[:, 0:512],
                            op=AL.add), reads=["rpA", "rpB"], writes=["qx"])
                        if is_q:
                            for j in range(4):
                                dmat("sp", hq1[t0 // 128 + j, dc2],
                                     qx_bf[:, j * 128:(j + 1) * 128],
                                     reads=["qx"], writes=["hq1"])
                        else:
                            sch.task("dve", lambda e, dc2=dc2, t0=t0: e.tensor_copy(
                                out=KR[:, dc2 * L + t0:dc2 * L + t0 + 512],
                                in_=qx_bf[:]),
                                reads=["qx"], writes=["KR", "wfT"])

        l1_proj(WQ1, ctq, stq, True)
        l1_proj(WK1, ctk, stk, False)
        for wh in range(2):
            for dc in range(8):
                dmat("pool", wl1[:, dc * 512:(dc + 1) * 512],
                     WV1[dc * 128:(dc + 1) * 128, wh * 512:(wh + 1) * 512],
                     writes=["wl1"])
            for tt in range(NT):
                for dc in range(8):
                    dmat("sp", xstat[:, dc * 128:(dc + 1) * 128],
                         hxn1[dc, :, tt * 128:(tt + 1) * 128],
                         reads=["hxn1"], writes=["xstat"])
                bank = gp[tt % 2]
                kb = f"gp{tt % 2}"
                for dc in range(8):
                    sch.task("pe", lambda e, dc=dc, bank=bank: e.matmul(
                        bank[:, 0:512], xstat[:, dc * 128:(dc + 1) * 128],
                        wl1[:, dc * 512:(dc + 1) * 512],
                        start=(dc == 0), stop=(dc == 7)),
                        reads=["xstat", "wl1"], writes=[kb])
                sch.task("dve", lambda e, tt=tt, wh=wh, bank=bank: e.tensor_copy(
                    out=VR[:, tt * 1024 + wh * 512:tt * 1024 + (wh + 1) * 512],
                    in_=bank[:, 0:512]),
                    reads=[kb], writes=["VR", "xblk0", "xblk1", "w1sT", "w2sT"])

        maybe_stop(9)
        qstat = xnb2[:, 0:1024]
        mcols = sbuf("mcols", (128, 16), F32)
        ed_bf = qp_bf
        pt_full = o2fm[:, 0:512]
        zc = sbuf("zc", (128, 8), F32)
        bfb_sb = xnb2[:, 1024:2048]
        out_sb = q32
        fence9 = sch.task("sp", lambda e: e.nop(),
                          writes=["xstat", "wl1", "o2fm", "ptf"])
        dmat("sp", bfb_sb[:], bf_b[:, :], writes=["bfb"], extra=[fence9])

        for qt in range(NT):
            q0 = qt * 128
            dmat("sp", qstat[:], hq1[qt].rearrange("dc p c -> p dc c"),
                 reads=["hq1"], writes=["qstat"], extra=[fence9])
            for kc in range(NKC):
                bank = gp[kc % 2]
                kb = f"gp{kc % 2}"
                for dc in range(8):
                    sch.task("pe", lambda e, dc=dc, kc=kc, bank=bank: e.matmul(
                        bank[:, 0:512], qstat[:, dc * 128:(dc + 1) * 128],
                        KR[:, dc * L + kc * 512:dc * L + kc * 512 + 512],
                        start=(dc == 0), stop=(dc == 7)),
                        reads=["qstat", "KR"], writes=[kb])
                sch.task("dve", lambda e, kc=kc, bank=bank: e.reduce_max(
                    mcols[:, kc:kc + 1], bank[:, 0:512], axis=AX.X),
                    reads=[kb], writes=["mc"])
                sch.task("dve", lambda e, kc=kc, bank=bank: e.tensor_copy(
                    out=srow[:, kc * 512:(kc + 1) * 512], in_=bank[:, 0:512]),
                    reads=[kb], writes=SROW_W)
            sch.task("dve", lambda e: e.reduce_max(
                mcols[:, 9:10], mcols[:, 0:NKC], axis=AX.X),
                reads=["mc"], writes=["mfin"])
            sch.task("dve", lambda e: e.tensor_scalar(
                out=mcols[:, 10:11], in0=mcols[:, 9:10], scalar1=-1.0, scalar2=None,
                op0=AL.mult), reads=["mfin"], writes=["negm"])
            live = [kc for kc in range(NKC)
                    if max(0, q0 - (kc * 512 + 511), kc * 512 - (q0 + 127)) <= DEC_SKIP]
            for kc in range(NKC):
                k0 = kc * 512
                sch.task("act", lambda e, kc=kc: e.activation(
                    out=sbf[:, 0:512], in_=srow[:, kc * 512:(kc + 1) * 512],
                    func=AF.Exp, bias=mcols[:, 10:11], accum_out=zc[:, kc:kc + 1]),
                    reads=["srow", "negm"], writes=["sbf", f"zc{kc}"])
                if kc not in live:
                    continue
                j0 = DEC_OFF + k0 - q0
                dmat("sp", dlt[:, 0:512], dlog[:, j0:j0 + 512], writes=["q32"])
                sch.task("dve", lambda e, kc=kc: e.tensor_tensor(
                    out=sdec, in0=srow[:, kc * 512:(kc + 1) * 512], in1=dlt[:, 0:512],
                    op=AL.add), reads=["srow", "q32"], writes=SDEC_W)
                sch.task("act", lambda e: e.activation(
                    out=ed_bf[:], in_=sdec, func=AF.Exp, bias=mcols[:, 10:11]),
                    reads=["sdec", "negm"], writes=["qp"])
                for j in range(4):
                    tsl, ktsl, bksl = tb_slot()
                    sch.task("pe", lambda e, j=j, tsl=tsl: e.transpose(
                        tsl, ed_bf[:, j * 128:(j + 1) * 128], sb_id_bf[:]),
                        reads=["qp", "idbf"], writes=[ktsl, bksl])
                    sch.task("dve", lambda e, j=j, tsl=tsl: e.tensor_copy(
                        out=pt_full[:, j * 128:(j + 1) * 128], in_=tsl),
                        reads=[ktsl, bksl, bksl], writes=["ptf"], extra=[fence9])
                lastkc = (kc == live[-1])
                for j in range(4):
                    ktile = kc * 4 + j
                    st = (kc == live[0] and j == 0)
                    sp_ = (lastkc and j == 3)
                    for dh in range(2):
                        bank = gp[2 + dh]
                        kb = f"gp{2 + dh}"
                        sch.task("pe", lambda e, j=j, dh=dh, ktile=ktile, st=st,
                                 sp_=sp_, bank=bank: e.matmul(
                            bank[:, 0:512], pt_full[:, j * 128:(j + 1) * 128],
                            VR[:, ktile * 1024 + dh * 512:ktile * 1024 + (dh + 1) * 512],
                            start=st, stop=sp_),
                            reads=["ptf", "VR"], writes=[kb])
            sch.task("dve", lambda e: e.reduce_sum(
                mcols[:, 11:12], zc[:, 0:NKC], axis=AX.X),
                reads=[f"zc{kc}" for kc in range(NKC)], writes=["Z"])
            sch.task("dve", lambda e: e.reciprocal(out=mcols[:, 12:13],
                                                   in_=mcols[:, 11:12]),
                     reads=["Z"], writes=["rZ"])
            dmat("sp", qD[:], hx1[qt], reads=["hx1w"], writes=["qD"])
            dmat("sp", f16b[:], hf1[qt], reads=["hf1"], writes=["f16b"])
            sch.task("dve", lambda e: e.tensor_tensor(
                out=qD[:], in0=qD[:], in1=f16b[:], op=AL.add),
                reads=["f16b", "qD"], writes=["qD"])
            sch.task("dve", lambda e: e.tensor_tensor(
                out=qD[:], in0=qD[:], in1=bfb_sb[:], op=AL.add),
                reads=["bfb", "qD"], writes=["qD"])
            for dh in range(2):
                bank = gp[2 + dh]
                kb = f"gp{2 + dh}"
                sch.task("dve", lambda e, dh=dh, bank=bank: e.scalar_tensor_tensor(
                    out=out_sb[:, dh * 512:(dh + 1) * 512], in0=bank[:, 0:512],
                    scalar=mcols[:, 12:13], in1=qD[:, dh * 512:(dh + 1) * 512],
                    op0=AL.mult, op1=AL.add),
                    reads=[kb, "rZ", "qD"], writes=["q32"])
            dmat("sp", out[qt * 128:(qt + 1) * 128, :], out_sb[:],
                 reads=["q32"], writes=["outw"])


    except _StopBuild:
        dmat("sp", out[0:128, :], x_in[0:128, :], writes=["outw"])

    sems = {}
    for e in Sched.ENGS:
        g = nc.semaphore(f"sem_{e}")
        sems[("eng", e)] = g.__enter__()
        ctx.append(g)
    for e in Sched.DMAQ:
        for si in range(Sched.NSEM[e]):
            g = nc.semaphore(f"dmas_{e}{si}")
            sems[("dma", e, si)] = g.__enter__()
            ctx.append(g)
    blk = nc.Block()
    blk.__enter__()
    sch.emit(nc, blk, sems)
    blk.__exit__(None, None, None)
    for g in reversed(ctx):
        g.__exit__(None, None, None)
    nc._sched = sch
    return nc


def make_inputs_for_core(xb, I, consts):
    m = {"x": np.ascontiguousarray(xb, dtype=np.float32)}
    m["Wq"] = I["L0_attn_Wq"]
    m["Wk"] = I["L0_attn_Wk"]
    m["Wv"] = I["L0_attn_Wv"]
    wba = np.zeros((D, 16), dtype=np.float32)
    wba[:, 0:6] = I["L0_attn_Wb"]
    wba[:, 6:12] = I["L0_attn_Wa"]
    m["Wba"] = wba
    m["A_log"] = np.ascontiguousarray(I["L0_attn_A_log"]).reshape(NH, 1).astype(np.float32)
    m["dt_bias"] = np.ascontiguousarray(I["L0_attn_dt_bias"]).reshape(NH, 1).astype(np.float32)
    m["convq"] = np.ascontiguousarray(
        np.asarray(I["L0_attn_conv_q"]).reshape(NH, 128, 4).transpose(1, 0, 2).reshape(128, 24))
    m["convk"] = np.ascontiguousarray(
        np.asarray(I["L0_attn_conv_k"]).reshape(NH, 128, 4).transpose(1, 0, 2).reshape(128, 24))
    m["convv"] = np.ascontiguousarray(
        np.asarray(I["L0_attn_conv_v"]).reshape(12, 128, 4).transpose(1, 0, 2).reshape(128, 48))
    m["Wg"] = I["L0_attn_Wg"]
    m["nw_b"] = np.ascontiguousarray(
        np.broadcast_to(np.tile(np.asarray(I["L0_attn_norm_w"]), NH), (128, VD)))
    m["Wo"] = I["L0_attn_Wo"]
    m["w1"] = I["L0_ffn_w1"]
    m["b1c"] = np.ascontiguousarray(np.asarray(I["L0_ffn_b1"]).reshape(32, 128).T)
    m["w2"] = I["L0_ffn_w2"]
    m["b2r"] = np.ascontiguousarray(np.asarray(I["L0_ffn_b2"]).reshape(1, D))
    m["WQ1"] = I["L1_attn_W_Q"]
    m["WK1"] = I["L1_attn_W_K"]
    m["WV1"] = I["L1_attn_W_V"]
    m["wf"] = I["L1_ffn_weight"]
    m["bf_b"] = np.ascontiguousarray(np.broadcast_to(np.asarray(I["L1_ffn_bias"]), (128, D)))
    for k, v in consts.items():
        m[k] = v
    res = {}
    for k, v in m.items():
        a = np.asarray(v)
        if a.dtype == np.float64:
            a = a.astype(np.float32)
        res[k] = np.ascontiguousarray(a)
    return res


_NC_CACHE = {}


def kernel(**inputs):
    L = np.asarray(inputs["L0_attn_x"]).shape[1]
    if L not in _NC_CACHE:
        _NC_CACHE[L] = build_core(L)
    nc = _NC_CACHE[L]
    consts = host_constants(L)
    x = np.ascontiguousarray(np.asarray(inputs["L0_attn_x"], dtype=np.float32))
    n = x.shape[0]
    in_maps = [make_inputs_for_core(x[i], inputs, consts) for i in range(n)]
    res = run_bass_kernel_spmd(nc, in_maps, core_ids=list(range(n)))
    return np.stack([r["out"] for r in res.results], axis=0)
